# version 18
# baseline (speedup 1.0000x reference)
import numpy as np
from contextlib import ExitStack

import concourse.bass as bass
import concourse.mybir as mybir
from concourse.bass_utils import run_bass_kernel_spmd

F32 = mybir.dt.float32
BF16 = mybir.dt.bfloat16
I8 = mybir.dt.int8
AF = mybir.ActivationFunctionType
ALU = mybir.AluOpType
AXX = mybir.AxisListType.X

NCORE = 8
SEQ = 4096
D = 1024
KD = 8
NS = 8
R = 512
SUBT = 4
DS = 128
DFF = 2816
NIC = 22
MD = 6144
MC = 48
EPS = 1e-5
QMAX = 127.0
MAGIC = 12582912.0


class Sem:
    def __init__(self, stack, nc, name):
        self.h = stack.enter_context(nc.semaphore(name))
        self.n = 0

    def inc(self, instr, k=1):
        instr.then_inc(self.h, k)
        self.n += k
        return self.n


class Ctx:
    pass


_INTERCEPT = {
    "tensor_copy", "tensor_tensor", "tensor_add", "tensor_sub", "tensor_mul",
    "tensor_max", "tensor_scalar", "tensor_single_scalar", "scalar_tensor_tensor",
    "tensor_reduce", "reduce_max", "reduce_sum", "reciprocal", "memset",
    "memzero", "activation", "bn_stats", "bn_aggr", "select", "copy_predicated",
}


def _is_ap(x):
    return hasattr(x, "tensor") and hasattr(x, "offset") and hasattr(x, "ap")


def _fp(apobj):
    t = apobj.tensor
    shape = list(t.shape)
    row = 1
    for d in shape[1:]:
        row *= d
    off = int(apobj.offset)
    span = 0
    for pair in apobj.ap:
        st, cnt = int(pair[0]), int(pair[1])
        if row > 1 and st >= row:
            continue
        span += (cnt - 1) * abs(st)
    lo = off % row if row > 1 else 0
    return (t.name, lo, lo + span + 1)


class EngProxy:

    def __init__(self, eng):
        self._eng = eng
        self._w = []
        self._r = []

    def _touch(self, writes, reads):
        need = False
        for f in writes + reads:
            for g in self._w:
                if f[0] == g[0] and f[1] < g[2] and g[1] < f[2]:
                    need = True
        for f in writes:
            for g in self._r:
                if f[0] == g[0] and f[1] < g[2] and g[1] < f[2]:
                    need = True
        if need:
            self._eng.drain()
            self._w, self._r = [], []
        self._w += writes
        self._r += reads

    def __getattr__(self, name):
        real = getattr(self._eng, name)
        if name not in _INTERCEPT:
            return real

        def call(*args, **kw):
            writes, reads = [], []
            first = True
            for a in args:
                if _is_ap(a):
                    (writes if first else reads).append(_fp(a))
                    first = False
            for k, a in kw.items():
                if _is_ap(a):
                    (writes if k in ("out", "out_ap", "accum_out") else reads).append(_fp(a))
            self._touch(writes, reads)
            return real(*args, **kw)

        return call




def build(debug=False, upto="full"):
    nc = bass.Bass("TRN2", target_bir_lowering=False)
    c = Ctx()
    c.nc = nc
    c.debug = debug

    dt = nc.dram_tensor
    c.x = dt("x", [SEQ, D], F32, kind="ExternalInput").ap()
    c.A = dt("A", [DS, DS], F32, kind="ExternalInput").ap()
    c.B_w = dt("B_w", [DS, D], F32, kind="ExternalInput").ap()
    c.B_b = dt("B_b", [DS, 1], F32, kind="ExternalInput").ap()
    c.C_w = dt("C_w", [D, DS], F32, kind="ExternalInput").ap()
    c.C_b = dt("C_b", [128, KD], F32, kind="ExternalInput").ap()
    c.D_w = dt("D_w", [D, D], F32, kind="ExternalInput").ap()
    c.D_b = dt("D_b", [128, KD], F32, kind="ExternalInput").ap()
    c.s_w1 = dt("s_w1", [512, D], F32, kind="ExternalInput").ap()
    c.s_b1 = dt("s_b1", [128, 4], F32, kind="ExternalInput").ap()
    c.s_w2 = dt("s_w2", [DS, 512], F32, kind="ExternalInput").ap()
    c.s_b2 = dt("s_b2", [DS, 1], F32, kind="ExternalInput").ap()
    c.g_w = dt("g_w", [D, D], F32, kind="ExternalInput").ap()
    c.g_b = dt("g_b", [128, KD], F32, kind="ExternalInput").ap()
    c.w_gate = dt("w_gate", [DFF, D], F32, kind="ExternalInput").ap()
    c.w_up = dt("w_up", [DFF, D], F32, kind="ExternalInput").ap()
    c.w_down = dt("w_down", [D, DFF], F32, kind="ExternalInput").ap()
    c.clip_x = dt("clip_x", [1, 1], F32, kind="ExternalInput").ap()
    c.clip_down = dt("clip_down", [1, 1], F32, kind="ExternalInput").ap()
    c.c_ks = dt("c_ks", [NCORE, 128], F32, kind="ExternalInput").ap()
    c.wmod_ks = dt("wmod_ks", [128, MD], F32, kind="ExternalInput").ap()
    c.ident = dt("ident", [128, 128], F32, kind="ExternalInput").ap()
    c.ones1 = dt("ones1", [1, 128], F32, kind="ExternalInput").ap()

    c.out = dt("out", [SEQ, D], F32, kind="ExternalOutput").ap()

    skind = "ExternalOutput" if debug else "Internal"
    c.x2_s = dt("x2_s", [NS, 128, SUBT * D], F32, kind=skind).ap()
    c.h2_s = dt("h2_s", [NS, 128, KD * R], F32, kind=skind).ap()
    c.v_s = dt("v_s", [NS, 128, NIC * R], F32, kind=skind).ap()
    c.wqg_s = dt("wqg_s", [NIC * 128, KD * 128], BF16, kind="Internal").ap()
    c.wqu_s = dt("wqu_s", [NIC * 128, KD * 128], BF16, kind="Internal").ap()
    c.wqd_s = dt("wqd_s", [KD * 128, NIC * 128], BF16, kind="Internal").ap()
    c.wqd_t = dt("wqd_t", [NIC * 128, KD * 128], BF16, kind="Internal").ap()
    c.cc_mod_in = dt("cc_mod_in", [NCORE, MD], F32).ap()
    c.cc_mod_out = dt("cc_mod_out", [1, MD], F32).ap()
    c.cc_m1_in = dt("cc_m1_in", [1, 8], F32).ap()
    c.cc_m1_out = dt("cc_m1_out", [1, 8], F32).ap()
    c.cc_m2_in = dt("cc_m2_in", [1, 8], F32).ap()
    c.cc_m2_out = dt("cc_m2_out", [1, 8], F32).ap()
    if debug:
        c.dbg_mod = dt("dbg_mod", [128, MC], F32, kind="ExternalOutput").ap()
        c.dbg_u = dt("dbg_u", [128, SEQ], F32, kind="ExternalOutput").ap()
        c.dbg_hs = dt("dbg_hs", [128, SEQ], F32, kind="ExternalOutput").ap()
        c.dbg_h = dt("dbg_h", [128, KD * R], F32, kind="ExternalOutput").ap()
        c.dbg_sx = dt("dbg_sx", [1, 8], F32, kind="ExternalOutput").ap()
        c.dbg_sd = dt("dbg_sd", [1, 8], F32, kind="ExternalOutput").ap()

    c.sem_stack = ExitStack()

    al = nc.alloc_sbuf_tensor
    c.ID = al("ID", [128, 128], F32)
    c.ONES = al("ONES", [1, 128], F32)
    c.SUMX = al("SUMX", [128, 32], F32)
    c.SUMSQ = al("SUMSQ", [128, 32], F32)
    c.MU = al("MU", [128, 32], F32)
    c.INV = al("INV", [128, 32], F32)
    c.NMI = al("NMI", [128, 32], F32)
    c.MU2 = al("MU2", [128, 32], F32)
    c.INV2 = al("INV2", [128, 32], F32)
    c.NMI2 = al("NMI2", [128, 32], F32)
    c.STSC = al("STSC", [128, 8], F32)
    c.MOD = al("MOD", [128, MC], F32)
    c.ONE_PCA = al("ONE_PCA", [128, KD], F32)
    c.GA = al("GA", [128, KD], F32)
    c.ONE_PCB = al("ONE_PCB", [128, KD], F32)
    c.GB = al("GB", [128, KD], F32)
    c.BCD = al("BCD", [128, KD], F32)
    c.BIAS_B = al("BIAS_B", [DS, 1], F32)
    c.BIAS_S1 = al("BIAS_S1", [128, 4], F32)
    c.BIAS_S2 = al("BIAS_S2", [DS, 1], F32)
    c.BIAS_G = al("BIAS_G", [128, KD], F32)
    c.CB_T = al("CB_T", [128, KD], F32)
    c.DB_T = al("DB_T", [128, KD], F32)
    c.CLIPX = al("CLIPX", [1, 2], F32)
    c.CLIPD = al("CLIPD", [1, 2], F32)
    c.RMAX = al("RMAX", [128, 2], F32)
    c.SWG = al("SWG", [128, NIC], F32)
    c.SWU = al("SWU", [128, NIC], F32)
    c.SWD = al("SWD", [128, KD], F32)
    c.RSG = al("RSG", [128, NIC], F32)
    c.RSU = al("RSU", [128, NIC], F32)
    c.RSD = al("RSD", [128, KD], F32)
    c.SXG = al("SXG", [128, NIC], F32)
    c.SXU = al("SXU", [128, NIC], F32)
    c.SDD = al("SDD", [128, KD], F32)
    c.SXB = al("SXB", [128, 1], F32)
    c.ISXB = al("ISXB", [128, 1], F32)
    c.ISDB = al("ISDB", [128, 1], F32)
    c.SDB = al("SDB", [128, 1], F32)
    c.SCL = al("SCL", [1, 8], F32)
    c.CMAGIC = al("CMAGIC", [128, 1], F32)
    c.NMAGIC = al("NMAGIC", [128, 1], F32)
    c.LMX = al("LMX", [1, 8], F32)
    c.GMX = al("GMX", [1, 8], F32)
    c.H2MAX = al("H2MAX", [128, 64], F32)
    c.VMAX = al("VMAX", [128, NS * NIC], F32)
    c.REDT = al("REDT", [128, 2], F32)
    c.CSL = al("CSL", [NCORE, 128], F32)

    c.PS = [nc.alloc_psum_tensor(f"PS{i}", [128, 512], F32) for i in range(8)]

    c.stage1 = ExitStack()
    s1t = lambda name, shape, dtp=F32: c.stage1.enter_context(nc.sbuf_tensor(name, shape, dtp))
    c.X_TM = s1t("X_TM", [128, 2 * SUBT * D])
    c.XN = s1t("XN", [128, SUBT * D])
    c.H_FM = s1t("H_FM", [128, KD * R])
    c.U = s1t("U", [128, SEQ])
    c.HS = s1t("HS", [128, SEQ])
    c.SGT = s1t("SGT", [128, R])
    c.A_DIR = s1t("A_DIR", [128, 128])
    c.CSLT = s1t("CSLT", [128, NCORE])
    c.H_HI = c.stage1.enter_context(nc.sbuf_tensor("H_HI", [128, KD * R], BF16))
    c.H_LO = c.stage1.enter_context(nc.sbuf_tensor("H_LO", [128, KD * R], BF16))

    c.modstack = ExitStack()
    mst = lambda name, shape: c.modstack.enter_context(nc.sbuf_tensor(name, shape, F32))
    c.WMODS = mst("WMODS", [128, MD])
    c.MODP = mst("MODP", [NCORE, MD])
    c.MODROW = mst("MODROW", [1, MD])

    from types import SimpleNamespace
    sems = {}
    for name in ["dx", "dw", "dw2", "dw3", "dwl", "dws", "dxe", "dxo", "dsc", "dgp", "ct", "cv", "cs", "cg", "cc", "dst"]:
        sems[name] = Sem(c.sem_stack, nc, name)
    c.s = SimpleNamespace(**sems)
    c.m = {}

    with nc.Block() as block:
        _emit(c, block, upto)
    c.sem_stack.close()
    return nc, c




def _emit(c, bk, upto):
    nc = c.nc
    s = c.s
    m = c.m
    c.bank_free = [None] * 8
    pv = EngProxy(nc.vector)
    ps_ = EngProxy(nc.scalar)
    pg = EngProxy(nc.gpsimd)

    _bk0 = bk

    class BK:
        tensor = staticmethod(_bk0.tensor)
        sync = staticmethod(_bk0.sync)

        @staticmethod
        def vector(fn):
            _bk0.vector(lambda _e: fn(pv))

        @staticmethod
        def scalar(fn):
            _bk0.scalar(lambda _e: fn(ps_))

        @staticmethod
        def gpsimd(fn):
            _bk0.gpsimd(lambda _e: fn(pg))

    bk = BK

    def bank_wait(t, i):
        bf = c.bank_free[i]
        if bf is not None:
            t.wait_ge(bf[0].h, bf[1])
            c.bank_free[i] = None

    @bk.sync
    def _(e):
        for dst, src in [(c.ID[:], c.ident), (c.ONES[:], c.ones1), (c.A_DIR[:], c.A),
                         (c.CSL[:], c.c_ks), (c.BIAS_B[:], c.B_b), (c.BIAS_S1[:], c.s_b1),
                         (c.BIAS_S2[:], c.s_b2), (c.BIAS_G[:], c.g_b), (c.CB_T[:], c.C_b),
                         (c.DB_T[:], c.D_b), (c.CLIPX[0:1, 0:1], c.clip_x),
                         (c.CLIPD[0:1, 0:1], c.clip_down)]:
            s.dx.inc(e.dma_start(out=dst, in_=src), 16)
        m["ld_small"] = s.dx.n
        s.dx.inc(e.dma_start(out=c.WMODS[:], in_=c.wmod_ks), 16)
        m["ld_wmod"] = s.dx.n

    @bk.vector
    def _(v):
        v.memset(c.SCL[:], 0.0)
        v.memset(c.LMX[:], 0.0)
        v.memset(c.GMX[:], 0.0)
        v.memset(c.CMAGIC[:], MAGIC)
        v.memset(c.NMAGIC[:], -MAGIC)

    @bk.scalar
    def _(e):
        e.wait_ge(s.dx.h, s.dx.n)
        s.cs.inc(e.activation(c.CSL[:], c.CSL[:], AF.Silu))
        m["csl_silu"] = s.cs.n

    @bk.tensor
    def _(t):
        t.wait_ge(s.cs.h, m["csl_silu"])
        bank_wait(t, 4)
        s.ct.inc(t.transpose(c.PS[4][0:128, 0:NCORE], c.CSL[:], c.ID[0:NCORE, 0:NCORE]))
        m["cslt_t"] = s.ct.n

    @bk.vector
    def _(v):
        v.wait_ge(s.ct.h, m["cslt_t"])
        s.cv.inc(v.tensor_copy(c.CSLT[:], c.PS[4][:, 0:NCORE]))
        m["cslt"] = s.cv.n
    c.bank_free[4] = (s.cv, m["cslt"])

    for n in range(12):
        @bk.tensor
        def _(t, n=n):
            if n == 0:
                t.wait_ge(s.cv.h, m["cslt"])
                t.wait_ge(s.dx.h, s.dx.n)
                bank_wait(t, 4)
                bank_wait(t, 5)
            if n >= 2:
                t.wait_ge(s.cv.h, m["modp_e%d" % (n - 2)])
            s.ct.inc(t.matmul(c.PS[4 + n % 2][0:NCORE, 0:512], c.CSLT[:],
                              c.WMODS[:, n * 512:(n + 1) * 512], start=True, stop=True))
            m["modp_t%d" % n] = s.ct.n

        @bk.vector
        def _(v, n=n):
            v.wait_ge(s.ct.h, m["modp_t%d" % n])
            s.cv.inc(v.tensor_copy(c.MODP[:, n * 512:(n + 1) * 512], c.PS[4 + n % 2][0:NCORE, 0:512]))
            m["modp_e%d" % n] = s.cv.n
    m["modp"] = s.cv.n
    m["modp_mm"] = m["modp_t%d" % 11]
    c.bank_free[4] = (s.cv, m["modp"])
    c.bank_free[5] = (s.cv, m["modp"])

    @bk.gpsimd
    def _(g):
        g.wait_ge(s.cv.h, m["modp"])
        s.dgp.inc(g.dma_start(out=c.cc_mod_in, in_=c.MODP[:]), 16)
        g.wait_ge(s.dgp.h, s.dgp.n)
        s.cc.inc(g.collective_compute("ReduceScatter", ALU.add,
                                      replica_groups=[list(range(NCORE))],
                                      ins=[c.cc_mod_in], outs=[c.cc_mod_out]))
        g.wait_ge(s.cc.h, s.cc.n)
        s.dgp.inc(g.dma_start(out=c.MODROW[:], in_=c.cc_mod_out), 16)
        m["modrow"] = s.dgp.n

    c.modstack.close()
    c.powstack = ExitStack()
    pst = lambda name, shape: c.powstack.enter_context(nc.sbuf_tensor(name, shape, F32))
    c.M_ = [pst(f"Mp{k}", [128, 128]) for k in range(1, 9)]
    c.N_ = [pst(f"Np{k}", [128, 128]) for k in range(1, 9)]
    c.R_ = [pst(f"Rp{k}", [128, 128]) for k in range(1, 9)]
    c.Z_ = [pst(f"Zp{k}", [128, 128]) for k in range(1, 7)]
    c.TDIR = pst("TDIR", [128, 128])
    @bk.tensor
    def _(t):
        t.wait_ge(s.dx.h, s.dx.n)
        for _b in range(4):
            bank_wait(t, _b)
        s.ct.inc(t.matmul(c.PS[0][:, 0:128], c.A_DIR[:], c.ID[:], start=True, stop=True))
        m["pow_m1t"] = s.ct.n

    @bk.vector
    def _(v):
        v.wait_ge(s.ct.h, m["pow_m1t"])
        s.cv.inc(v.tensor_copy(c.M_[0][:], c.PS[0][:, 0:128]))
        m["pow_M1"] = s.cv.n

    for k in range(1, 8):
        @bk.tensor
        def _(t, k=k):
            t.wait_ge(s.cv.h, m["pow_M%d" % k])
            s.ct.inc(t.matmul(c.PS[k % 2][:, 0:128], c.A_DIR[:], c.M_[k - 1][:],
                              start=True, stop=True))
            m["pow_Mt%d" % (k + 1)] = s.ct.n

        @bk.vector
        def _(v, k=k):
            v.wait_ge(s.ct.h, m["pow_Mt%d" % (k + 1)])
            s.cv.inc(v.tensor_copy(c.M_[k][:], c.PS[k % 2][:, 0:128]))
            m["pow_M%d" % (k + 1)] = s.cv.n

    def build_level(tag, src_tile, src_ms, dst_list, nmax):
        @bk.tensor
        def _(t):
            t.wait_ge(s.cv.h, m[src_ms])
            s.ct.inc(t.transpose(c.PS[2][:, 0:128], src_tile[:], c.ID[:]))
            m[tag + "_tpt"] = s.ct.n

        @bk.vector
        def _(v):
            v.wait_ge(s.ct.h, m[tag + "_tpt"])
            v.tensor_copy(c.TDIR[:], c.PS[2][:, 0:128])
            s.cv.inc(v.tensor_copy(dst_list[0][:], src_tile[:]))
            m[tag + "_1"] = s.cv.n

        for k in range(1, nmax):
            @bk.tensor
            def _(t, k=k):
                t.wait_ge(s.cv.h, m[tag + "_%d" % k])
                s.ct.inc(t.matmul(c.PS[k % 2][:, 0:128], c.TDIR[:], dst_list[k - 1][:],
                                  start=True, stop=True))
                m[tag + "_t%d" % (k + 1)] = s.ct.n

            @bk.vector
            def _(v, k=k):
                v.wait_ge(s.ct.h, m[tag + "_t%d" % (k + 1)])
                s.cv.inc(v.tensor_copy(dst_list[k][:], c.PS[k % 2][:, 0:128]))
                m[tag + "_%d" % (k + 1)] = s.cv.n

    build_level("powN", c.M_[7], "pow_M8", c.N_, 8)
    build_level("powR", c.N_[7], "powN_8", c.R_, 8)
    build_level("powZ", c.R_[7], "powR_8", c.Z_, 6)
    m["powers_done"] = s.cv.n
    for _b in range(3):
        c.bank_free[_b] = (s.cv, m["powers_done"])


    @bk.tensor
    def _(t):
        t.wait_ge(s.dgp.h, s.dgp.n)
        for half in range(2):
            bank_wait(t, 6 + half)
            for jj in range(24):
                col = half * 24 + jj
                ins = t.transpose(c.PS[6 + half][:, jj:jj + 1],
                                  c.MODROW[0:1, col * 128:(col + 1) * 128], c.ID[0:1, 0:1])
            s.ct.inc(ins)
            m["mod_t%d" % half] = s.ct.n

    @bk.vector
    def _(v):
        v.wait_ge(s.ct.h, m["mod_t0"])
        v.wait_ge(s.ct.h, m["mod_t1"])
        v.tensor_copy(c.MOD[:, 0:24], c.PS[6][:, 0:24])
        s.cv.inc(v.tensor_copy(c.MOD[:, 24:48], c.PS[7][:, 0:24]))
        sa = c.MOD[:, 0:8]
        ca = c.MOD[:, 8:16]
        ga = c.MOD[:, 16:24]
        cb = c.MOD[:, 32:40]
        gb = c.MOD[:, 40:48]
        v.tensor_scalar(c.ONE_PCA[:], ca, 1.0, None, op0=ALU.add)
        v.tensor_scalar(c.ONE_PCB[:], cb, 1.0, None, op0=ALU.add)
        for src, dst in ((ga, c.GA), (gb, c.GB)):
            v.tensor_scalar(c.STSC[:], src, -0.25, 0.25, op0=ALU.max, op1=ALU.min)
            v.tensor_scalar(dst[:], src, 0.0, None, op0=ALU.is_ge)
            v.scalar_tensor_tensor(dst[:], dst[:], 0.5, src, op0=ALU.mult, op1=ALU.add)
            v.tensor_tensor(dst[:], dst[:], c.STSC[:], op=ALU.subtract)
            v.tensor_scalar(dst[:], dst[:], 0.25, None, op0=ALU.subtract)
        v.tensor_tensor(c.BCD[:], c.CB_T[:], c.DB_T[:], op=ALU.add)
        s.cv.inc(v.tensor_tensor(c.BCD[:], c.BCD[:], c.GA[:], op=ALU.mult))
        m["mod_done"] = s.cv.n
    c.bank_free[6] = (s.cv, m["mod_done"])
    c.bank_free[7] = (s.cv, m["mod_done"])

    if c.debug:
        @bk.sync
        def _(e):
            e.wait_ge(s.cv.h, m["mod_done"])
            s.dst.inc(e.dma_start(out=c.dbg_mod, in_=c.MOD[:]), 16)


    c.astack = ExitStack()
    sba = lambda name, shape: c.astack.enter_context(nc.sbuf_tensor(name, shape, F32))
    c.S_W1T = sba("S_W1T", [128, 32 * 128])
    c.S_W2T = sba("S_W2T", [128, 4 * 128])
    c.B_WT = sba("B_WT", [128, 8 * 128])
    c.WLD = sba("WLD", [128, D])
    c.S1A = sba("S1A", [128, 4 * R])

    def wprep(name, w_ap, nrc, ncc, dst, scale_ga, dst_lo=None):
        for rc in range(nrc):
            @bk.sync
            def _(e, rc=rc):
                if name == "sw1" and rc == 0:
                    e.wait_ge(s.ct.h, m["modp_mm"])
                    e.wait_ge(s.dgp.h, s.dgp.n - 16)
                if c.m.get("wld_free") is not None:
                    e.wait_ge(s.ct.h, c.m["wld_free"])
                s.dw.inc(e.dma_start(out=c.WLD[:, 0:ncc * 128],
                                     in_=w_ap[rc * 128:(rc + 1) * 128, :]), 16)
                c.m[name + "_ld%d" % rc] = s.dw.n

            if scale_ga:
                @bk.vector
                def _(v, rc=rc):
                    v.wait_ge(s.dw.h, s.dw.n)
                    v.wait_ge(s.cv.h, m["mod_done"])
                    s.cv.inc(v.tensor_scalar(c.WLD[:, 0:ncc * 128], c.WLD[:, 0:ncc * 128],
                                             c.GA[:, rc:rc + 1], None, op0=ALU.mult))
                    c.m[name + "_sc%d" % rc] = s.cv.n

            for cc2 in range(ncc):
                @bk.tensor
                def _(t, rc=rc, cc2=cc2):
                    if cc2 == 0:
                        if scale_ga:
                            t.wait_ge(s.cv.h, c.m[name + "_sc%d" % rc])
                        else:
                            t.wait_ge(s.dw.h, s.dw.n)
                    b = cc2 % 4
                    bank_wait(t, b)
                    key = name + "_evk%d" % b
                    if c.m.get(key) is not None:
                        t.wait_ge(s.cv.h, c.m[key])
                    s.ct.inc(t.transpose(c.PS[b][:, 0:128],
                                         c.WLD[:, cc2 * 128:(cc2 + 1) * 128], c.ID[:]))
                    c.m[name + "_tp_%d_%d" % (rc, cc2)] = s.ct.n
                    if cc2 == ncc - 1:
                        c.m["wld_free"] = s.ct.n

                @bk.vector
                def _(v, rc=rc, cc2=cc2):
                    v.wait_ge(s.ct.h, c.m[name + "_tp_%d_%d" % (rc, cc2)])
                    col = (cc2 * nrc + rc) * 128
                    if dst_lo is None:
                        s.cv.inc(v.tensor_copy(dst[:, col:col + 128], c.PS[cc2 % 4][:, 0:128]))
                    else:
                        v.tensor_copy(dst[:, col:col + 128], c.PS[cc2 % 4][:, 0:128])
                        s.cv.inc(v.tensor_tensor(dst_lo[:, col:col + 128], c.PS[cc2 % 4][:, 0:128],
                                                 dst[:, col:col + 128], op=ALU.subtract))
                    c.m[name + "_evk%d" % (cc2 % 4)] = s.cv.n

        c.m[name + "_done"] = s.cv.n
        c.bank_free[0] = (s.cv, s.cv.n)
        c.bank_free[1] = (s.cv, s.cv.n)

    wprep("sw1", c.s_w1, 4, 8, c.S_W1T, False)
    wprep("sw2", c.s_w2, 1, 4, c.S_W2T, False)
    wprep("bw", c.B_w, 1, 8, c.B_WT, False)
    m["wprep_done"] = s.cv.n

    c.qstack = ExitStack()
    qst = lambda name, shape, dtp: c.qstack.enter_context(nc.sbuf_tensor(name, shape, dtp))
    c.WST = qst("WST", [128, NIC * 128], F32)
    c.WQF = qst("WQF", [128, NIC * 128], F32)
    c.WQB = qst("WQB", [128, NIC * 128], BF16)

    def wq_chunk(name, w_ap, ncc, rc, sw_tile, rs_tile, scratch, first):
        @bk.sync
        def _(e):
            if first:
                e.wait_ge(s.ct.h, m["mod_t1"])
                e.wait_ge(s.dgp.h, s.dgp.n)
            if c.m.get("wst_free") is not None:
                e.wait_ge(s.cs.h, c.m["wst_free"])
            s.dwl.inc(e.dma_start(out=c.WST[:, 0:ncc * 128],
                                  in_=w_ap[rc * 128:(rc + 1) * 128, :]), 16)

        @bk.vector
        def _(v):
            v.wait_ge(s.dwl.h, s.dwl.n)
            v.reduce_max(c.RMAX[:, 0:1], c.WST[:, 0:ncc * 128], AXX,
                         apply_absolute_value=True)
            v.tensor_scalar(sw_tile[:, rc:rc + 1], c.RMAX[:, 0:1], 1.0 / QMAX, None,
                            op0=ALU.mult)
            v.reciprocal(c.RMAX[:, 1:2], c.RMAX[:, 0:1])
            s.cv.inc(v.tensor_scalar(rs_tile[:, rc:rc + 1], c.RMAX[:, 1:2], QMAX, None,
                                     op0=ALU.mult))
            c.m["wq_sc"] = s.cv.n

        @bk.scalar
        def _(e):
            e.wait_ge(s.cv.h, c.m["wq_sc"])
            s.cs.inc(e.activation(c.WQF[:, 0:ncc * 128], c.WST[:, 0:ncc * 128], AF.Identity,
                                  scale=rs_tile[:, rc:rc + 1], bias=c.CMAGIC[:, 0:1]))
            c.m["wst_free"] = s.cs.n
            c.m["wq_q"] = s.cs.n

        @bk.vector
        def _(v):
            v.wait_ge(s.cs.h, c.m["wq_q"])
            v.wait_ge(s.dws.h, s.dws.n)
            s.cv.inc(v.tensor_scalar(c.WQB[:, 0:ncc * 128], c.WQF[:, 0:ncc * 128],
                                     -MAGIC, None, op0=ALU.add))
            c.m["wq_b"] = s.cv.n

        @bk.sync
        def _(e):
            e.wait_ge(s.cv.h, c.m["wq_b"])
            s.dws.inc(e.dma_start(out=scratch[rc * 128:(rc + 1) * 128, :],
                                  in_=c.WQB[:, 0:ncc * 128]), 16)

    c.wq_chunks = [("wdq", c.w_down, NIC, rc, c.SWD, c.RSD, c.wqd_s) for rc in range(KD)]
    c.wq_pos = 0

    def emit_wq(n):
        while c.wq_pos < len(c.wq_chunks) and n > 0:
            nm, w_ap, ncc, rc, swt, rst, scr = c.wq_chunks[c.wq_pos]
            wq_chunk(nm, w_ap, ncc, rc, swt, rst, scr, c.wq_pos == 0)
            c.wq_pos += 1
            n -= 1
    c.emit_wq = emit_wq

    x_sl = c.x.rearrange("(r e) d -> e r d", e=NS)

    def xbuf(j):
        return c.X_TM[:, (j % 2) * SUBT * D:(j % 2 + 1) * SUBT * D]
    c.xbuf = xbuf

    def a_xload(j):
        @bk.sync
        def _(e, j=j):
            if j >= 2:
                e.wait_ge(s.cs.h, c.m["A_aff%d" % (j - 1)])
            sp = s.dxe if j % 2 == 0 else s.dxo
            src2 = x_sl[j].rearrange("(t p) d -> p t d", p=128)
            dstv = xbuf(j).rearrange("p (t d) -> p t d", d=D)
            sp.inc(e.dma_start(out=dstv, in_=src2), 16)

    def a_vprep(j, sq_last):
        @bk.vector
        def _(v, j=j, sq_last=sq_last):
            sp = s.dxe if j % 2 == 0 else s.dxo
            v.wait_ge(sp.h, sp.n)
            xb = xbuf(j)
            for t in range(SUBT):
                ci = j * 4 + t
                s.cv.inc(v.reduce_sum(c.SUMX[:, ci:ci + 1], xb[:, t * D:(t + 1) * D], AXX))
            cl = slice(j * 4, j * 4 + 4)
            v.wait_ge(s.cs.h, sq_last)
            v.tensor_scalar(c.MU[:, cl], c.SUMX[:, cl], 1.0 / D, None, op0=ALU.mult)
            v.tensor_scalar(c.STSC[:, 0:4], c.SUMSQ[:, cl], 1.0 / D, None, op0=ALU.mult)
            v.tensor_tensor(c.STSC[:, 4:8], c.MU[:, cl], c.MU[:, cl], op=ALU.mult)
            v.tensor_tensor(c.STSC[:, 0:4], c.STSC[:, 0:4], c.STSC[:, 4:8], op=ALU.subtract)
            v.tensor_scalar(c.STSC[:, 0:4], c.STSC[:, 0:4], EPS, None, op0=ALU.add)
            s.cv.inc(v.reciprocal(c.STSC[:, 4:8], c.STSC[:, 0:4]))
            v.wait_ge(s.cs.h, sq_last + 1)
            s.cv.inc(v.scalar_tensor_tensor(c.NMI[:, cl], c.MU[:, cl], -1.0, c.INV[:, cl],
                                            op0=ALU.mult, op1=ALU.mult))
            c.m["A_nmi%d" % j] = s.cv.n

    def a_sprep(j, recip_ms, nmi_ms, tp_ms):
        @bk.scalar
        def _(e, j=j):
            sp = s.dxe if j % 2 == 0 else s.dxo
            e.wait_ge(sp.h, sp.n)
            xb = xbuf(j)
            for t in range(SUBT):
                ci = j * 4 + t
                s.cs.inc(e.activation(c.S1A[:, (t % 2) * D:(t % 2 + 1) * D],
                                      xb[:, t * D:(t + 1) * D], AF.Square,
                                      accum_out=c.SUMSQ[:, ci:ci + 1]))
            cl = slice(j * 4, j * 4 + 4)
            e.wait_ge(s.cv.h, recip_ms)
            s.cs.inc(e.activation(c.INV[:, cl], c.STSC[:, 4:8], AF.Sqrt))
            e.wait_ge(s.cv.h, nmi_ms)
            if tp_ms is not None:
                e.wait_ge(s.ct.h, tp_ms)
            else:
                e.wait_ge(s.cv.h, m["mod_done"])
            for t in range(SUBT):
                ci = j * 4 + t
                s.cs.inc(e.activation(c.XN[:, t * D:(t + 1) * D], xb[:, t * D:(t + 1) * D],
                                      AF.Identity, bias=c.NMI[:, ci:ci + 1],
                                      scale=c.INV[:, ci:ci + 1]))
            c.m["A_aff%d" % j] = s.cs.n

    a_xload(0)
    _sq0 = s.cs.n + 4
    _bv0 = s.cv.n
    a_vprep(0, _sq0)
    a_sprep(0, _bv0 + 5, _bv0 + 6, None)

    for j in range(NS):
        bsI = s.cs.n
        btI = s.ct.n
        bvI = s.cv.n
        has_prep = (j + 1) < NS
        silu_base = bsI + 8 + (9 if has_prep else 0)
        c.m["A_silu_pred%d" % j] = silu_base + 4

        @bk.tensor
        def _(t, j=j, bsI=bsI, bvI=bvI):
            if j == 0:
                t.wait_ge(s.cv.h, m["wprep_done"])
                for _b in range(8):
                    bank_wait(t, _b)
            t.wait_ge(s.cs.h, c.m["A_aff%d" % j])
            for k in range(KD):
                if k >= 2:
                    t.wait_ge(s.cs.h, bsI + k - 1)
                b = 6 + k % 2
                for tt_ in range(SUBT):
                    ins = t.transpose(c.PS[b][:, tt_ * 128:(tt_ + 1) * 128],
                                      c.XN[:, tt_ * D + k * 128:tt_ * D + (k + 1) * 128],
                                      c.ID[:])
                s.ct.inc(ins)
            t.wait_ge(s.cs.h, bsI + 8)
            for mm in range(4):
                for k in range(KD):
                    ins = t.matmul(c.PS[mm][:, 0:R],
                                   c.S_W1T[:, (k * 4 + mm) * 128:(k * 4 + mm + 1) * 128],
                                   c.H_FM[:, k * R:(k + 1) * R], start=(k == 0), stop=(k == KD - 1))
                s.ct.inc(ins)
            t.wait_ge(s.cs.h, c.m["A_silu_pred%d" % j])
            for k2 in range(4):
                ins = t.matmul(c.PS[4][:, 0:R], c.S_W2T[:, k2 * 128:(k2 + 1) * 128],
                               c.S1A[:, k2 * R:(k2 + 1) * R], start=(k2 == 0), stop=(k2 == 3))
            s.ct.inc(ins)
            if j > 0:
                t.wait_ge(s.cv.h, c.m["A_u%d" % (j - 1)])
            for k in range(KD):
                ins = t.matmul(c.PS[5][:, 0:R], c.B_WT[:, k * 128:(k + 1) * 128],
                               c.H_FM[:, k * R:(k + 1) * R], start=(k == 0), stop=(k == KD - 1))
            s.ct.inc(ins)

        if j + 1 < NS:
            a_xload(j + 1)

        @bk.scalar
        def _(e, j=j, btI=btI):
            if c.debug and j == 1:
                e.wait_ge(s.dst.h, s.dst.n)
            for k in range(KD):
                e.wait_ge(s.ct.h, btI + k + 1)
                s.cs.inc(e.activation(c.H_FM[:, k * R:(k + 1) * R], c.PS[6 + k % 2][:, 0:R],
                                      AF.Identity, bias=c.MOD[:, k:k + 1],
                                      scale=c.ONE_PCA[:, k:k + 1]))
            c.m["A_evac%d" % j] = s.cs.n

        if j + 1 < NS:
            _sq = s.cs.n + 4
            _bv = s.cv.n
            a_vprep(j + 1, _sq)
            a_sprep(j + 1, _bv + 5, _bv + 6, btI + 8)

        @bk.scalar
        def _(e, j=j, btI=btI):
            for mm in range(4):
                e.wait_ge(s.ct.h, btI + 9 + mm)
                s.cs.inc(e.activation(c.S1A[:, mm * R:(mm + 1) * R], c.PS[mm][:, 0:R], AF.Silu,
                                      bias=c.BIAS_S1[:, mm:mm + 1]))
            e.wait_ge(s.ct.h, btI + 13)
            if j > 0:
                e.wait_ge(s.cv.h, c.m["A_u%d" % (j - 1)])
            s.cs.inc(e.activation(c.SGT[:], c.PS[4][:, 0:R], AF.Sigmoid, bias=c.BIAS_S2[:]))
            c.m["A_sgt%d" % j] = s.cs.n

        @bk.vector
        def _(v, j=j, btI=btI):
            v.wait_ge(s.ct.h, btI + 14)
            v.wait_ge(s.cs.h, c.m["A_sgt%d" % j])
            s.cv.inc(v.scalar_tensor_tensor(c.U[:, j * R:(j + 1) * R], c.PS[5][:, 0:R],
                                            c.BIAS_B[:], c.SGT[:], op0=ALU.add, op1=ALU.mult))
            c.m["A_u%d" % j] = s.cv.n
        c.m["A_bt%d" % j] = s.ct.n
        c.m["A_xnlast%d" % j] = c.m["A_aff%d" % j]

        if c.debug and j == 0:
            @bk.sync
            def _(e, j=j):
                e.wait_ge(s.cs.h, c.m["A_silu_pred%d" % j])
                s.dst.inc(e.dma_start(out=c.dbg_h, in_=c.H_FM[:]), 16)

        c.emit_wq(1)

    for _b in range(6):
        c.bank_free[_b] = (s.cv, s.cv.n)
    c.bank_free[6] = (s.cs, s.cs.n)
    c.bank_free[7] = (s.cs, s.cs.n)
    m["A_done"] = s.cv.n
    c.emit_wq(999)
    m["wqd_all"] = s.dws.n
    c.qstack.close()
    c.astack.close()
    c.scanstack = ExitStack()
    sst = lambda name, shape: c.scanstack.enter_context(nc.sbuf_tensor(name, shape, F32))
    c.MIX2 = sst("MIX2", [128, 512])
    c.MIX3 = sst("MIX3", [128, 64])
    c.H4S = sst("H4S", [128, 8])
    c.H3S = sst("H3S", [128, 64])
    c.H2SS = sst("H2SS", [128, 512])


    @bk.tensor
    def _(t):
        t.wait_ge(s.cv.h, m["A_done"])
        for b in range(8):
            bank_wait(t, b)
        for tl in range(8):
            for jj in range(tl + 1):
                lhsT = c.ID if jj == tl else c.M_[tl - jj - 1]
                ins = t.matmul(c.PS[tl][:, 0:R], lhsT[:], c.U[:, jj * R:(jj + 1) * R],
                               start=(jj == 0), stop=(jj == tl))
        s.ct.inc(ins)
        m["sc_l1"] = s.ct.n

    @bk.vector
    def _(v):
        v.wait_ge(s.ct.h, m["sc_l1"])
        for tl in range(8):
            s.cv.inc(v.tensor_copy(c.HS[:, tl * R:(tl + 1) * R], c.PS[tl][:, 0:R]))
        m["sc_l1e"] = s.cv.n

    @bk.tensor
    def _(t):
        t.wait_ge(s.cv.h, m["sc_l1e"])
        T1 = c.HS[:, 7 * R:8 * R].rearrange("p (b e) -> p e b", e=8)
        for tl in range(8):
            for jj in range(tl + 1):
                lhsT = c.ID if jj == tl else c.N_[tl - jj - 1]
                ins = t.matmul(c.PS[tl][:, 0:64], lhsT[:], T1[:, jj, :],
                               start=(jj == 0), stop=(jj == tl))
        s.ct.inc(ins)
        m["sc_l2"] = s.ct.n

    @bk.vector
    def _(v):
        v.wait_ge(s.ct.h, m["sc_l2"])
        for tl in range(8):
            s.cv.inc(v.tensor_copy(c.MIX2[:, tl * 64:(tl + 1) * 64], c.PS[tl][:, 0:64]))
        m["sc_l2e"] = s.cv.n

    @bk.tensor
    def _(t):
        t.wait_ge(s.cv.h, m["sc_l2e"])
        T2 = c.MIX2[:, 7 * 64:8 * 64].rearrange("p (b e) -> p e b", e=8)
        for tl in range(8):
            for jj in range(tl + 1):
                lhsT = c.ID if jj == tl else c.R_[tl - jj - 1]
                ins = t.matmul(c.PS[tl][:, 0:8], lhsT[:], T2[:, jj, :],
                               start=(jj == 0), stop=(jj == tl))
        s.ct.inc(ins)
        m["sc_l3"] = s.ct.n

    @bk.vector
    def _(v):
        v.wait_ge(s.ct.h, m["sc_l3"])
        for tl in range(8):
            s.cv.inc(v.tensor_copy(c.MIX3[:, tl * 8:(tl + 1) * 8], c.PS[tl][:, 0:8]))
        s.cv.inc(v.memset(c.H4S[:, 0:1], 0.0))
        m["sc_l3e"] = s.cv.n

    @bk.tensor
    def _(t):
        t.wait_ge(s.cv.h, m["sc_l3e"])
        for cc2 in range(1, 8):
            for jj in range(cc2):
                p = cc2 - 1 - jj
                lhsT = c.ID if p == 0 else c.Z_[p - 1]
                ins = t.matmul(c.PS[cc2][:, 0:1], lhsT[:], c.MIX3[:, 56 + jj:57 + jj],
                               start=(jj == 0), stop=(jj == cc2 - 1))
        s.ct.inc(ins)
        m["sc_l4"] = s.ct.n

    @bk.vector
    def _(v):
        v.wait_ge(s.ct.h, m["sc_l4"])
        for cc2 in range(1, 8):
            s.cv.inc(v.tensor_copy(c.H4S[:, cc2:cc2 + 1], c.PS[cc2][:, 0:1]))
        m["sc_h4"] = s.cv.n

    @bk.tensor
    def _(t):
        t.wait_ge(s.cv.h, m["sc_h4"])
        for tl in range(8):
            lhsT = c.ID if tl == 0 else c.R_[tl - 1]
            ins = t.matmul(c.PS[tl][:, 0:8], lhsT[:], c.H4S[:], start=True, stop=True)
        s.ct.inc(ins)
        m["sc_e3t"] = s.ct.n

    @bk.vector
    def _(v):
        v.wait_ge(s.ct.h, m["sc_e3t"])
        for tl in range(8):
            if tl == 0:
                s.cv.inc(v.tensor_copy(c.H3S[:, 0:8], c.PS[0][:, 0:8]))
            else:
                s.cv.inc(v.tensor_tensor(c.H3S[:, tl * 8:(tl + 1) * 8], c.PS[tl][:, 0:8],
                                         c.MIX3[:, (tl - 1) * 8:tl * 8], op=ALU.add))
        m["sc_e3"] = s.cv.n

    @bk.tensor
    def _(t):
        t.wait_ge(s.cv.h, m["sc_e3"])
        e3nat = c.H3S[:, 0:64].rearrange("p (t b) -> p b t", t=8)
        for tl in range(8):
            lhsT = c.ID if tl == 0 else c.N_[tl - 1]
            ins = t.matmul(c.PS[tl][:, 0:64], lhsT[:], e3nat, start=True, stop=True)
        s.ct.inc(ins)
        m["sc_e2t"] = s.ct.n

    @bk.vector
    def _(v):
        v.wait_ge(s.ct.h, m["sc_e2t"])
        for tl in range(8):
            if tl == 0:
                s.cv.inc(v.tensor_copy(c.H2SS[:, 0:64], c.PS[0][:, 0:64]))
            else:
                s.cv.inc(v.tensor_tensor(c.H2SS[:, tl * 64:(tl + 1) * 64], c.PS[tl][:, 0:64],
                                         c.MIX2[:, (tl - 1) * 64:tl * 64], op=ALU.add))
        m["sc_e2"] = s.cv.n

    @bk.tensor
    def _(t):
        t.wait_ge(s.cv.h, m["sc_e2"])
        e2nat = c.H2SS[:, 0:512].rearrange("p (t b) -> p b t", t=8)
        for tl in range(8):
            ins = t.matmul(c.PS[tl][:, 0:R], c.M_[tl][:], e2nat, start=True, stop=True)
        s.ct.inc(ins)
        m["sc_corrt"] = s.ct.n

    @bk.vector
    def _(v):
        v.wait_ge(s.ct.h, m["sc_corrt"])
        for tl in range(8):
            s.cv.inc(v.tensor_tensor(c.HS[:, tl * R:(tl + 1) * R],
                                     c.HS[:, tl * R:(tl + 1) * R], c.PS[tl][:, 0:R], op=ALU.add))
        m["sc_done"] = s.cv.n
    for k in range(8):
        c.bank_free[k] = (s.cv, m["sc_done"])

    if c.debug:
        @bk.sync
        def _(e):
            e.wait_ge(s.cv.h, m["A_done"])
            s.dst.inc(e.dma_start(out=c.dbg_u, in_=c.U[:]), 16)

        @bk.sync
        def _(e):
            e.wait_ge(s.cv.h, m["sc_done"])
            s.dst.inc(e.dma_start(out=c.dbg_hs, in_=c.HS[:]), 16)

    if upto == "A":
        _finish(c, bk)
        return

    c.scanstack.close()
    c.powstack.close()
    c.cdgstack = ExitStack()
    cst = lambda name, shape, dtp: c.cdgstack.enter_context(nc.sbuf_tensor(name, shape, dtp))
    c.C_WT = cst("C_WT", [128, 8 * 128], F32)
    c.D_HI = cst("D_HI", [128, 64 * 128], BF16)
    c.D_LO = cst("D_LO", [128, 64 * 128], BF16)
    c.G_HI = cst("G_HI", [128, 64 * 128], BF16)
    c.G_LO = cst("G_LO", [128, 64 * 128], BF16)
    c.WLD = cst("WLDC", [128, D], F32)
    c.WSTC = cst("WSTC", [128, D], F32)
    c.WQFC = cst("WQFC", [128, D], F32)
    c.WQBC = cst("WQBC", [128, 2 * D], BF16)
    c.TBNC = cst("TBNC", [128, KD * 128], BF16)

    @bk.sync
    def _(e):
        e.wait_ge(s.ct.h, m["sc_corrt"])
        e.wait_ge(s.cv.h, m["sc_done"])
        e.wait_ge(s.dws.h, s.dws.n)
        e.wait_ge(s.dwl.h, s.dwl.n)

    @bk.vector
    def _(v):
        v.wait_ge(s.dws.h, s.dws.n)
        v.wait_ge(s.dwl.h, s.dwl.n)
        v.wait_ge(s.cs.h, s.cs.n)
    c.m["wld_free"] = m["sc_corrt"]

    wprep("cw", c.C_w, 8, 1, c.C_WT, True)
    wprep("dw", c.D_w, 8, 8, c.D_HI, True, dst_lo=c.D_LO)
    wprep("gw", c.g_w, 8, 8, c.G_HI, False, dst_lo=c.G_LO)
    m["wprepC_done"] = s.cv.n

    H2FM = c.U
    c.bankC = list(c.bank_free)
    c.bank_free = [None] * 8

    def cbw(t, b):
        e_ = c.bankC[b]
        if e_ is not None:
            t.wait_ge(e_[0].h, e_[1])
            c.bankC[b] = None

    c.wq2_chunks = ([(c.w_gate, rc, c.SWG, c.RSG, c.wqg_s) for rc in range(NIC)]
                    + [(c.w_up, rc, c.SWU, c.RSU, c.wqu_s) for rc in range(NIC)])
    c.wq2_pos = 0

    def wq2_batch(qs):
        csb = s.cs.n
        for bi, q in enumerate(qs):
            @bk.sync
            def _(e, bi=bi, q=q, csb=csb):
                w_ap, rc, sw, rs, scr = c.wq2_chunks[q]
                if bi > 0:
                    e.wait_ge(s.cs.h, csb + 2 * bi - 1)
                elif q >= 1:
                    e.wait_ge(s.cs.h, c.m["wq2_round%d" % (q - 1)])
                e.wait_ge(s.dwl.h, s.dwl.n)
                s.dwl.inc(e.dma_start(out=c.WSTC[:],
                                      in_=w_ap[rc * 128:(rc + 1) * 128, :]), 16)

            @bk.vector
            def _(v, q=q):
                w_ap, rc, sw, rs, scr = c.wq2_chunks[q]
                v.wait_ge(s.dwl.h, s.dwl.n)
                v.reduce_max(c.RMAX[:, 0:1], c.WSTC[:], AXX, apply_absolute_value=True)
                v.tensor_scalar(sw[:, rc:rc + 1], c.RMAX[:, 0:1], 1.0 / QMAX, None,
                                op0=ALU.mult)
                v.reciprocal(c.RMAX[:, 1:2], c.RMAX[:, 0:1])
                s.cv.inc(v.tensor_scalar(rs[:, rc:rc + 1], c.RMAX[:, 1:2], QMAX, None,
                                         op0=ALU.mult))
                c.m["wq2_rs%d" % q] = s.cv.n

        @bk.scalar
        def _(e, qs=qs):
            for q in qs:
                w_ap, rc, sw, rs, scr = c.wq2_chunks[q]
                e.wait_ge(s.cv.h, c.m["wq2_rs%d" % q])
                if q >= 2:
                    e.wait_ge(s.dws.h, c.m["wq2_st%d" % (q - 2)])
                s.cs.inc(e.activation(c.WQFC[:], c.WSTC[:],
                                      AF.Identity, scale=rs[:, rc:rc + 1],
                                      bias=c.CMAGIC[:, 0:1]))
                c.m["wq2_round%d" % q] = s.cs.n
                s.cs.inc(e.activation(c.WQBC[:, (q % 2) * D:(q % 2 + 1) * D], c.WQFC[:],
                                      AF.Identity, bias=c.NMAGIC[:, 0:1]))
                c.m["wq2_ub%d" % q] = s.cs.n

        @bk.sync
        def _(e, qs=qs):
            e.wait_ge(s.dws.h, s.dws.n)
            for q in qs:
                w_ap, rc, sw, rs, scr = c.wq2_chunks[q]
                e.wait_ge(s.cs.h, c.m["wq2_ub%d" % q])
                s.dws.inc(e.dma_start(out=scr[rc * 128:(rc + 1) * 128, :],
                                      in_=c.WQBC[:, (q % 2) * D:(q % 2 + 1) * D]), 16)
            for q in qs:
                c.m["wq2_st%d" % q] = s.dws.n

    def emit_wq2(n):
        while c.wq2_pos < len(c.wq2_chunks) and n > 0:
            take = min(2, n, len(c.wq2_chunks) - c.wq2_pos)
            wq2_batch(list(range(c.wq2_pos, c.wq2_pos + take)))
            c.wq2_pos += take
            n -= take

    c.bnc_pos = 0

    def emit_bounce(n):
        while c.bnc_pos < NIC and n > 0:
            ic = c.bnc_pos

            @bk.sync
            def _(e, ic=ic):
                e.wait_ge(s.dws.h, m["wqd_all"])
                if ic >= 1:
                    e.wait_ge(s.dws.h, c.m["bnc_st%d" % (ic - 1)])
                e.wait_ge(s.dw3.h, s.dw3.n)
                s.dw3.inc(e.dma_start(out=c.TBNC[:],
                                      in_=c.wqd_s[:, ic * 128:(ic + 1) * 128],
                                      transpose=True), 16)
                e.wait_ge(s.dw3.h, s.dw3.n)
                e.wait_ge(s.dws.h, s.dws.n)
                s.dws.inc(e.dma_start(out=c.wqd_t[ic * 128:(ic + 1) * 128, :],
                                      in_=c.TBNC[:]), 16)
                c.m["bnc_st%d" % ic] = s.dws.n
            c.bnc_pos += 1
            n -= 1

    @bk.sync
    def _(e):
        e.wait_ge(s.cs.h, c.m["A_xnlast%d" % (NS - 1)])
        src2 = x_sl[0].rearrange("(t p) d -> p t d", p=128)
        dstv = c.xbuf(0).rearrange("p (t d) -> p t d", d=D)
        s.dxe.inc(e.dma_start(out=dstv, in_=src2), 16)

    c.mmrot = 0
    for j in range(NS):
        TAIL = j >= 1
        bsI = s.cs.n
        btI = s.ct.n
        bvI = s.cv.n
        offA = bsI + 4 + (8 if TAIL else 0)
        offB = offA + 8
        sq_last_p = offB + 4
        sqrt_p = offB + 5
        aff2_last_p = offB + 9
        yev_base = offB + (9 if TAIL else 0)
        wq2_take = min(6, len(c.wq2_chunks) - c.wq2_pos)
        h2ev_base = yev_base + 8 + 2 * wq2_take
        t1tp_base = btI + 8
        y_base = t1tp_base + (8 if TAIL else 0)
        h2tp_base = y_base + 8
        hsplit_p = bvI + (8 if TAIL else 0) + 1
        x2add_base = hsplit_p

        @bk.scalar
        def _(e, j=j, TAIL=TAIL):
            sp = s.dxe if j % 2 == 0 else s.dxo
            e.wait_ge(sp.h, sp.n)
            if j == 0:
                e.wait_ge(s.cv.h, m["sc_done"])
                e.wait_ge(s.ct.h, m["sc_corrt"])
            elif j == 1:
                e.wait_ge(s.ct.h, c.m["C_tpL0"])
            else:
                e.wait_ge(s.ct.h, c.m["C_h2tpL%d" % (j - 2)])
            for t in range(SUBT):
                ci = j * 4 + t
                s.cs.inc(e.activation(c.XN[:, t * D:(t + 1) * D],
                                      c.xbuf(j)[:, t * D:(t + 1) * D], AF.Identity,
                                      bias=c.NMI[:, ci:ci + 1], scale=c.INV[:, ci:ci + 1]))
            c.m["C_aff1_%d" % j] = s.cs.n
            if TAIL:
                if j == 1:
                    if c.debug:
                        e.wait_ge(s.dst.h, s.dst.n)
                else:
                    e.wait_ge(s.cv.h, c.m["C_h2maxL%d" % (j - 2)])
                    e.wait_ge(s.dsc.h, c.m["C_h2st%d" % (j - 2)])
                for mc, b_, ctv in c.gch:
                    e.wait_ge(s.ct.h, ctv)
                    s.cs.inc(e.activation(H2FM[:, mc * R:(mc + 1) * R], c.PS[b_][:, 0:R],
                                          AF.Sigmoid, bias=c.BIAS_G[:, mc:mc + 1]))
                c.m["C_gevL%d" % (j - 1)] = s.cs.n
        assert s.cs.n == offA

        if TAIL:
            @bk.vector
            def _(v, j=j, bsI=bsI):
                for k in range(KD):
                    v.wait_ge(s.cs.h, bsI + 4 + k + 1)
                    gt = H2FM[:, k * R:(k + 1) * R]
                    s.cv.inc(v.tensor_tensor(gt, gt, c.H_FM[:, k * R:(k + 1) * R],
                                             op=ALU.mult))
                c.m["C_t1L%d" % (j - 1)] = s.cv.n

        @bk.tensor
        def _(t, j=j, TAIL=TAIL, bvI=bvI, offA=offA, x2add_base=x2add_base):
            t.wait_ge(s.cs.h, c.m["C_aff1_%d" % j])
            for k in range(KD):
                b = 6 + k % 2
                if k < 2:
                    if j == 0:
                        cbw(t, b)
                    else:
                        t.wait_ge(s.cs.h, c.m["C_gevL%d" % (j - 1)])
                else:
                    t.wait_ge(s.cs.h, offA + (k - 2) + 1)
                for tt_ in range(SUBT):
                    ins = t.transpose(c.PS[b][:, tt_ * 128:(tt_ + 1) * 128],
                                      c.XN[:, tt_ * D + k * 128:tt_ * D + (k + 1) * 128],
                                      c.ID[:])
                s.ct.inc(ins)
            c.m["C_tpL%d" % j] = s.ct.n
            if TAIL:
                for k in range(KD):
                    b = 6 + k % 2
                    t.wait_ge(s.cv.h, bvI + k + 1)
                    if k < 2:
                        t.wait_ge(s.cs.h, offA + (6 + k) + 1)
                    else:
                        t.wait_ge(s.cv.h, x2add_base + (k - 2) + 1)
                    for tt_ in range(SUBT):
                        ins = t.transpose(c.PS[b][:, tt_ * 128:(tt_ + 1) * 128],
                                          H2FM[:, k * R + tt_ * 128:k * R + (tt_ + 1) * 128],
                                          c.ID[:])
                    s.ct.inc(ins)
                c.m["C_t1tpL%d" % (j - 1)] = s.ct.n
        assert s.ct.n == t1tp_base + (8 if TAIL else 0)

        @bk.vector
        def _(v, j=j, TAIL=TAIL, offA=offA, sq_last_p=sq_last_p, sqrt_p=sqrt_p,
              t1tp_base=t1tp_base):
            v.wait_ge(s.cs.h, offA + 8)
            if TAIL:
                v.wait_ge(s.ct.h, c.m["C_gL%d" % (j - 1)])
            v.tensor_copy(c.H_HI[:], c.H_FM[:])
            s.cv.inc(v.tensor_tensor(c.H_LO[:], c.H_FM[:], c.H_HI[:], op=ALU.subtract))
            c.m["C_hsplit%d" % j] = s.cv.n
            if TAIL:
                for k in range(KD):
                    v.wait_ge(s.ct.h, t1tp_base + k + 1)
                    src = c.PS[6 + k % 2][:, 0:SUBT * 128].rearrange("p (t q) -> p t q", q=128)
                    dst = c.xbuf(j - 1).rearrange("p (t d) -> p t d", d=D)[:, :, k * 128:(k + 1) * 128]
                    s.cv.inc(v.tensor_tensor(dst, dst, src, op=ALU.add))
                for t in range(SUBT):
                    ci = (j - 1) * 4 + t
                    s.cv.inc(v.reduce_sum(c.SUMX[:, ci:ci + 1],
                                          c.xbuf(j - 1)[:, t * D:(t + 1) * D], AXX))
                cl = slice((j - 1) * 4, (j - 1) * 4 + 4)
                v.wait_ge(s.cs.h, sq_last_p)
                v.tensor_scalar(c.MU2[:, cl], c.SUMX[:, cl], 1.0 / D, None, op0=ALU.mult)
                v.tensor_scalar(c.STSC[:, 0:4], c.SUMSQ[:, cl], 1.0 / D, None, op0=ALU.mult)
                v.tensor_tensor(c.STSC[:, 4:8], c.MU2[:, cl], c.MU2[:, cl], op=ALU.mult)
                v.tensor_tensor(c.STSC[:, 0:4], c.STSC[:, 0:4], c.STSC[:, 4:8], op=ALU.subtract)
                v.tensor_scalar(c.STSC[:, 0:4], c.STSC[:, 0:4], EPS, None, op0=ALU.add)
                s.cv.inc(v.reciprocal(c.STSC[:, 4:8], c.STSC[:, 0:4]))
                v.wait_ge(s.cs.h, sqrt_p)
                s.cv.inc(v.scalar_tensor_tensor(c.NMI2[:, cl], c.MU2[:, cl], -1.0,
                                                c.INV2[:, cl], op0=ALU.mult, op1=ALU.mult))
                c.m["C_nmiL%d" % (j - 1)] = s.cv.n
        assert s.cv.n == x2add_base + (14 if TAIL else 0)

        @bk.tensor
        def _(t, j=j, TAIL=TAIL, yev_base=yev_base, h2ev_base=h2ev_base, offA=offA,
              aff2_last_p=aff2_last_p, x2add_base=x2add_base):
            t.wait_ge(s.cv.h, c.m["C_hsplit%d" % j])
            if j == 0:
                t.wait_ge(s.cv.h, m["wprepC_done"])
            c.ych = []
            for i in range(8):
                b_ = (c.mmrot + i) % 6
                if i < 6:
                    if TAIL:
                        t.wait_ge(s.cs.h, c.gevpos[i])
                    else:
                        cbw(t, b_)
                else:
                    t.wait_ge(s.cs.h, yev_base + (i - 6) + 1)
                t.matmul(c.PS[b_][:, 0:R], c.C_WT[:, i * 128:(i + 1) * 128],
                         c.HS[:, j * R:(j + 1) * R], start=True, stop=False)
                for k in range(KD):
                    col = (k * 8 + i) * 128
                    t.matmul(c.PS[b_][:, 0:R], c.D_HI[:, col:col + 128],
                             c.H_HI[:, k * R:(k + 1) * R], start=False, stop=False)
                    t.matmul(c.PS[b_][:, 0:R], c.D_LO[:, col:col + 128],
                             c.H_HI[:, k * R:(k + 1) * R], start=False, stop=False)
                    ins = t.matmul(c.PS[b_][:, 0:R], c.D_HI[:, col:col + 128],
                                   c.H_LO[:, k * R:(k + 1) * R], start=False, stop=(k == KD - 1))
                s.ct.inc(ins)
                c.ych.append((i, b_, s.ct.n))
            if TAIL:
                t.wait_ge(s.cs.h, aff2_last_p)
                for k in range(KD):
                    b = 6 + k % 2
                    if k < 2:
                        t.wait_ge(s.cv.h, x2add_base + (6 + k) + 1)
                    else:
                        t.wait_ge(s.cs.h, h2ev_base + (k - 2) + 1)
                    for tt_ in range(SUBT):
                        ins = t.transpose(c.PS[b][:, tt_ * 128:(tt_ + 1) * 128],
                                          c.XN[:, tt_ * D + k * 128:tt_ * D + (k + 1) * 128],
                                          c.ID[:])
                    s.ct.inc(ins)
                c.m["C_h2tpL%d" % (j - 1)] = s.ct.n
            c.gch = []
            for i in range(8):
                if i < 6:
                    b_ = (c.mmrot + 8 + i) % 6
                    t.wait_ge(s.cs.h, yev_base + (i + 2) + 1)
                else:
                    b_ = i
                    if TAIL:
                        t.wait_ge(s.cs.h, h2ev_base + i + 1)
                    else:
                        t.wait_ge(s.cs.h, offA + (i + 1))
                for k in range(KD):
                    col = (k * 8 + i) * 128
                    t.matmul(c.PS[b_][:, 0:R], c.G_HI[:, col:col + 128],
                             c.H_HI[:, k * R:(k + 1) * R], start=(k == 0), stop=False)
                    t.matmul(c.PS[b_][:, 0:R], c.G_LO[:, col:col + 128],
                             c.H_HI[:, k * R:(k + 1) * R], start=False, stop=False)
                    ins = t.matmul(c.PS[b_][:, 0:R], c.G_HI[:, col:col + 128],
                                   c.H_LO[:, k * R:(k + 1) * R], start=False, stop=(k == KD - 1))
                s.ct.inc(ins)
                c.gch.append((i, b_, s.ct.n))
            c.mmrot += 14
            c.m["C_gL%d" % j] = s.ct.n

        @bk.scalar
        def _(e, j=j, TAIL=TAIL, btI=btI, x2add_base=x2add_base):
            if TAIL:
                e.wait_ge(s.cv.h, c.m["C_t1L%d" % (j - 1)])
            for k in range(KD):
                e.wait_ge(s.ct.h, btI + k + 1)
                s.cs.inc(e.activation(c.H_FM[:, k * R:(k + 1) * R], c.PS[6 + k % 2][:, 0:R],
                                      AF.Identity, bias=c.MOD[:, k:k + 1],
                                      scale=c.ONE_PCA[:, k:k + 1]))
            c.m["C_hevL%d" % j] = s.cs.n
            if TAIL:
                e.wait_ge(s.cv.h, x2add_base + 8)
                e.wait_ge(s.dsc.h, s.dsc.n)
                s.dsc.inc(e.dma_start(out=c.x2_s[j - 1], in_=c.xbuf(j - 1)), 16)
                c.m["C_x2st%d" % (j - 1)] = s.dsc.n
                for t in range(SUBT):
                    ci = (j - 1) * 4 + t
                    s.cs.inc(e.activation(c.H_FM[:, (t % 2) * D:(t % 2 + 1) * D],
                                          c.xbuf(j - 1)[:, t * D:(t + 1) * D], AF.Square,
                                          accum_out=c.SUMSQ[:, ci:ci + 1]))
                cl = slice((j - 1) * 4, (j - 1) * 4 + 4)
                e.wait_ge(s.cv.h, x2add_base + 13)
                s.cs.inc(e.activation(c.INV2[:, cl], c.STSC[:, 4:8], AF.Sqrt))
                e.wait_ge(s.cv.h, x2add_base + 14)
                e.wait_ge(s.ct.h, c.m["C_tpL%d" % j])
                for t in range(SUBT):
                    ci = (j - 1) * 4 + t
                    s.cs.inc(e.activation(c.XN[:, t * D:(t + 1) * D],
                                          c.xbuf(j - 1)[:, t * D:(t + 1) * D], AF.Identity,
                                          bias=c.NMI2[:, ci:ci + 1], scale=c.INV2[:, ci:ci + 1]))
                c.m["C_aff2L%d" % (j - 1)] = s.cs.n
            e.wait_ge(s.cv.h, c.m["C_hsplit%d" % j])
            for mc, b_, ctv in c.ych:
                e.wait_ge(s.ct.h, ctv)
                s.cs.inc(e.activation(c.H_FM[:, mc * R:(mc + 1) * R], c.PS[b_][:, 0:R],
                                      AF.Identity, bias=c.BCD[:, mc:mc + 1]))
            c.m["C_yevL%d" % j] = s.cs.n
        assert s.cs.n == yev_base + 8

        emit_wq2(wq2_take)

        if TAIL:
            @bk.scalar
            def _(e, j=j):
                for k in range(KD):
                    e.wait_ge(s.ct.h, c.m["C_h2tpL%d" % (j - 1)] - 8 + k + 1)
                    s.cs.inc(e.activation(H2FM[:, k * R:(k + 1) * R], c.PS[6 + k % 2][:, 0:R],
                                          AF.Identity, bias=c.MOD[:, 24 + k:24 + k + 1],
                                          scale=c.ONE_PCB[:, k:k + 1]))
                c.m["C_h2evL%d" % (j - 1)] = s.cs.n
                e.wait_ge(s.dsc.h, s.dsc.n)
                s.dsc.inc(e.dma_start(out=c.h2_s[j - 1], in_=H2FM[:]), 16)
                c.m["C_h2st%d" % (j - 1)] = s.dsc.n
        assert s.cs.n == h2ev_base + (8 if TAIL else 0), (j, s.cs.n, h2ev_base)
        assert s.ct.n == h2tp_base + (8 if TAIL else 0) + 8
        c.gevpos = [s.cs.n + 4 + i + 1 for i in range(8)]

        if TAIL:
            @bk.vector
            def _(v, j=j, h2ev_base=h2ev_base):
                for k in range(KD):
                    v.wait_ge(s.cs.h, h2ev_base + k + 1)
                    s.cv.inc(v.reduce_max(c.H2MAX[:, (j - 1) * 8 + k:(j - 1) * 8 + k + 1],
                                          H2FM[:, k * R:(k + 1) * R], AXX,
                                          apply_absolute_value=True))
                c.m["C_h2maxL%d" % (j - 1)] = s.cv.n

        if j + 1 < NS:
            @bk.sync
            def _(e, j=j, TAIL=TAIL):
                if TAIL:
                    e.wait_ge(s.cs.h, c.m["C_aff2L%d" % (j - 1)])
                    e.wait_ge(s.cv.h, c.m["C_nmiL%d" % (j - 1)])
                    e.wait_ge(s.dsc.h, c.m["C_x2st%d" % (j - 1)])
                else:
                    e.wait_ge(s.cs.h, c.m["A_xnlast%d" % (NS - 1)])
                sp = s.dxe if (j + 1) % 2 == 0 else s.dxo
                src2 = x_sl[j + 1].rearrange("(t p) d -> p t d", p=128)
                dstv = c.xbuf(j + 1).rearrange("p (t d) -> p t d", d=D)
                sp.inc(e.dma_start(out=dstv, in_=src2), 16)
        emit_bounce(3)

    bsE = s.cs.n
    btE = s.ct.n
    bvE = s.cv.n

    @bk.scalar
    def _(e):
        e.wait_ge(s.cv.h, c.m["C_h2maxL%d" % (NS - 2)])
        e.wait_ge(s.dsc.h, c.m["C_h2st%d" % (NS - 2)])
        for mc, b_, ctv in c.gch:
            e.wait_ge(s.ct.h, ctv)
            s.cs.inc(e.activation(H2FM[:, mc * R:(mc + 1) * R], c.PS[b_][:, 0:R],
                                  AF.Sigmoid, bias=c.BIAS_G[:, mc:mc + 1]))
        c.m["C_gevL%d" % (NS - 1)] = s.cs.n

    @bk.vector
    def _(v):
        for k in range(KD):
            v.wait_ge(s.cs.h, bsE + k + 1)
            gt = H2FM[:, k * R:(k + 1) * R]
            s.cv.inc(v.tensor_tensor(gt, gt, c.H_FM[:, k * R:(k + 1) * R], op=ALU.mult))
        c.m["C_t1L%d" % (NS - 1)] = s.cv.n

    @bk.tensor
    def _(t):
        for k in range(KD):
            b = 6 + k % 2
            t.wait_ge(s.cv.h, bvE + k + 1)
            if k < 2:
                t.wait_ge(s.cs.h, c.m["C_gevL%d" % (NS - 1)])
            else:
                t.wait_ge(s.cv.h, bvE + 8 + (k - 2) + 1)
            for tt_ in range(SUBT):
                ins = t.transpose(c.PS[b][:, tt_ * 128:(tt_ + 1) * 128],
                                  H2FM[:, k * R + tt_ * 128:k * R + (tt_ + 1) * 128],
                                  c.ID[:])
            s.ct.inc(ins)

    @bk.vector
    def _(v):
        for k in range(KD):
            v.wait_ge(s.ct.h, btE + k + 1)
            src = c.PS[6 + k % 2][:, 0:SUBT * 128].rearrange("p (t q) -> p t q", q=128)
            dst = c.xbuf(NS - 1).rearrange("p (t d) -> p t d", d=D)[:, :, k * 128:(k + 1) * 128]
            s.cv.inc(v.tensor_tensor(dst, dst, src, op=ALU.add))
        for t in range(SUBT):
            ci = (NS - 1) * 4 + t
            s.cv.inc(v.reduce_sum(c.SUMX[:, ci:ci + 1],
                                  c.xbuf(NS - 1)[:, t * D:(t + 1) * D], AXX))
        cl = slice((NS - 1) * 4, (NS - 1) * 4 + 4)
        v.wait_ge(s.cs.h, bsE + 8 + 4)
        v.tensor_scalar(c.MU2[:, cl], c.SUMX[:, cl], 1.0 / D, None, op0=ALU.mult)
        v.tensor_scalar(c.STSC[:, 0:4], c.SUMSQ[:, cl], 1.0 / D, None, op0=ALU.mult)
        v.tensor_tensor(c.STSC[:, 4:8], c.MU2[:, cl], c.MU2[:, cl], op=ALU.mult)
        v.tensor_tensor(c.STSC[:, 0:4], c.STSC[:, 0:4], c.STSC[:, 4:8], op=ALU.subtract)
        v.tensor_scalar(c.STSC[:, 0:4], c.STSC[:, 0:4], EPS, None, op0=ALU.add)
        s.cv.inc(v.reciprocal(c.STSC[:, 4:8], c.STSC[:, 0:4]))
        v.wait_ge(s.cs.h, bsE + 8 + 5)
        s.cv.inc(v.scalar_tensor_tensor(c.NMI2[:, cl], c.MU2[:, cl], -1.0, c.INV2[:, cl],
                                        op0=ALU.mult, op1=ALU.mult))

    @bk.scalar
    def _(e):
        e.wait_ge(s.cv.h, bvE + 16)
        e.wait_ge(s.dsc.h, s.dsc.n)
        s.dsc.inc(e.dma_start(out=c.x2_s[NS - 1], in_=c.xbuf(NS - 1)), 16)
        c.m["C_x2st%d" % (NS - 1)] = s.dsc.n
        for t in range(SUBT):
            ci = (NS - 1) * 4 + t
            s.cs.inc(e.activation(c.H_FM[:, (t % 2) * D:(t % 2 + 1) * D],
                                  c.xbuf(NS - 1)[:, t * D:(t + 1) * D], AF.Square,
                                  accum_out=c.SUMSQ[:, ci:ci + 1]))
        cl = slice((NS - 1) * 4, (NS - 1) * 4 + 4)
        e.wait_ge(s.cv.h, bvE + 21)
        s.cs.inc(e.activation(c.INV2[:, cl], c.STSC[:, 4:8], AF.Sqrt))
        e.wait_ge(s.cv.h, bvE + 22)
        for t in range(SUBT):
            ci = (NS - 1) * 4 + t
            s.cs.inc(e.activation(c.XN[:, t * D:(t + 1) * D],
                                  c.xbuf(NS - 1)[:, t * D:(t + 1) * D], AF.Identity,
                                  bias=c.NMI2[:, ci:ci + 1], scale=c.INV2[:, ci:ci + 1]))
        c.m["C_aff2L%d" % (NS - 1)] = s.cs.n
    assert s.cs.n == bsE + 17

    @bk.tensor
    def _(t):
        t.wait_ge(s.cs.h, c.m["C_aff2L%d" % (NS - 1)])
        for k in range(KD):
            b = 6 + k % 2
            if k < 2:
                t.wait_ge(s.cv.h, bvE + 8 + (6 + k) + 1)
            else:
                t.wait_ge(s.cs.h, bsE + 17 + (k - 2) + 1)
            for tt_ in range(SUBT):
                ins = t.transpose(c.PS[b][:, tt_ * 128:(tt_ + 1) * 128],
                                  c.XN[:, tt_ * D + k * 128:tt_ * D + (k + 1) * 128],
                                  c.ID[:])
            s.ct.inc(ins)
        c.m["C_h2tpL%d" % (NS - 1)] = s.ct.n

    @bk.scalar
    def _(e):
        for k in range(KD):
            e.wait_ge(s.ct.h, btE + 8 + k + 1)
            s.cs.inc(e.activation(H2FM[:, k * R:(k + 1) * R], c.PS[6 + k % 2][:, 0:R],
                                  AF.Identity, bias=c.MOD[:, 24 + k:24 + k + 1],
                                  scale=c.ONE_PCB[:, k:k + 1]))
        c.m["C_h2evL%d" % (NS - 1)] = s.cs.n
        e.wait_ge(s.dsc.h, s.dsc.n)
        s.dsc.inc(e.dma_start(out=c.h2_s[NS - 1], in_=H2FM[:]), 16)
        c.m["C_h2st%d" % (NS - 1)] = s.dsc.n

    @bk.vector
    def _(v):
        for k in range(KD):
            v.wait_ge(s.cs.h, bsE + 17 + k + 1)
            s.cv.inc(v.reduce_max(c.H2MAX[:, (NS - 1) * 8 + k:(NS - 1) * 8 + k + 1],
                                  H2FM[:, k * R:(k + 1) * R], AXX,
                                  apply_absolute_value=True))
        c.m["C_h2maxL%d" % (NS - 1)] = s.cv.n

    c.bank_free = [(s.cs, s.cs.n) for _ in range(8)]
    m["C_done_v"] = s.cv.n
    m["C_done_t"] = s.ct.n
    m["C_done_s"] = s.cs.n
    m["C_done_st"] = s.dsc.n

    if upto == "C":
        _finish(c, bk)
        return

    @bk.vector
    def _(v):
        v.wait_ge(s.cv.h, m["C_done_v"])
        s.cv.inc(v.reduce_max(c.REDT[:, 0:1], c.H2MAX[:], AXX))
        m["sx_red"] = s.cv.n

    @bk.tensor
    def _(t):
        t.wait_ge(s.cv.h, m["sx_red"])
        bank_wait(t, 0)
        s.ct.inc(t.transpose(c.PS[0][0:1, 0:128], c.REDT[:, 0:1], c.ID[:]))
        m["sx_tp"] = s.ct.n

    @bk.vector
    def _(v):
        v.wait_ge(s.ct.h, m["sx_tp"])
        v.memset(c.LMX[:], 0.0)
        s.cv.inc(v.reduce_max(c.LMX[0:1, 0:1], c.PS[0][0:1, 0:128], AXX))
        m["sx_loc"] = s.cv.n
    c.bank_free[0] = (s.cv, m["sx_loc"])

    @bk.gpsimd
    def _(g):
        g.wait_ge(s.cv.h, m["sx_loc"])
        s.dgp.inc(g.dma_start(out=c.cc_m1_in, in_=c.LMX[:]), 16)
        g.wait_ge(s.dgp.h, s.dgp.n)
        s.cc.inc(g.collective_compute("AllReduce", ALU.max,
                                      replica_groups=[list(range(NCORE))],
                                      ins=[c.cc_m1_in], outs=[c.cc_m1_out]))
        g.wait_ge(s.cc.h, s.cc.n)
        s.dgp.inc(g.dma_start(out=c.GMX[:], in_=c.cc_m1_out), 16)
        m["sx_gmax"] = s.dgp.n

    @bk.scalar
    def _(e):
        e.wait_ge(s.dx.h, s.dx.n)
        e.activation(c.CLIPX[0:1, 1:2], c.CLIPX[0:1, 0:1], AF.Sigmoid)
        s.cs.inc(e.activation(c.CLIPD[0:1, 1:2], c.CLIPD[0:1, 0:1], AF.Sigmoid))
        m["clip_sig"] = s.cs.n

    @bk.vector
    def _(v):
        v.wait_ge(s.dgp.h, s.dgp.n)
        v.wait_ge(s.cs.h, m["clip_sig"])
        v.tensor_tensor(c.SCL[0:1, 0:1], c.GMX[0:1, 0:1], c.CLIPX[0:1, 1:2], op=ALU.mult)
        v.tensor_scalar(c.SCL[0:1, 0:1], c.SCL[0:1, 0:1], 1.0 / QMAX, None, op0=ALU.mult)
        s.cv.inc(v.reciprocal(c.SCL[0:1, 1:2], c.SCL[0:1, 0:1]))
        m["sx_val"] = s.cv.n

    @bk.tensor
    def _(t):
        t.wait_ge(s.cv.h, m["sx_val"])
        bank_wait(t, 1)
        t.matmul(c.PS[1][:, 0:1], c.ONES[:], c.SCL[0:1, 0:1], start=True, stop=True)
        s.ct.inc(t.matmul(c.PS[1][:, 1:2], c.ONES[:], c.SCL[0:1, 1:2], start=True, stop=True))
        m["sx_bc"] = s.ct.n

    @bk.vector
    def _(v):
        v.wait_ge(s.ct.h, m["sx_bc"])
        v.tensor_copy(c.SXB[:], c.PS[1][:, 0:1])
        s.cv.inc(v.tensor_copy(c.ISXB[:], c.PS[1][:, 1:2]))
        v.wait_ge(s.dw.h, s.dw.n)
        v.tensor_scalar(c.SXG[:], c.SWG[:], c.SXB[:, 0:1], None, op0=ALU.mult)
        s.cv.inc(v.tensor_scalar(c.SXU[:], c.SWU[:], c.SXB[:, 0:1], None, op0=ALU.mult))
        m["sx_done"] = s.cv.n
    c.bank_free[1] = (s.cv, m["sx_done"])

    if c.debug:
        @bk.sync
        def _(e):
            e.wait_ge(s.cv.h, m["sx_done"])
            s.dst.inc(e.dma_start(out=c.dbg_sx, in_=c.SCL[0:1, :]), 16)

    if upto == "SX":
        _finish(c, bk)
        return

    c.cdgstack.close()
    c.stage1.close()
    c.p2stack = ExitStack()
    sb2 = lambda name, shape, dtp: c.p2stack.enter_context(nc.sbuf_tensor(name, shape, dtp))
    c.H2RD = sb2("H2RD", [128, 2 * KD * R], F32)
    c.XQF = sb2("XQF", [128, KD * R], F32)
    c.XQB = sb2("XQB", [128, 2 * KD * R], BF16)
    c.SGW = sb2("SGW", [128, 2 * R], F32)
    c.VT = sb2("VT", [128, NIC * R], F32)
    c.WGT = sb2("WGT", [128, NIC * KD * 128], BF16)
    c.WUT = sb2("WUT", [128, NIC * KD * 128], BF16)

    @bk.sync
    def _(e):
        e.wait_ge(s.ct.h, c.m["C_gL%d" % (NS - 1)])
        e.wait_ge(s.dws.h, c.m["wq2_st%d" % (NIC - 1)])
        e.wait_ge(s.dw.h, s.dw.n)
        for k in range(KD):
            s.dw.inc(e.dma_start(out=c.WGT[:, k * NIC * 128:(k + 1) * NIC * 128],
                                 in_=c.wqg_s[:, k * 128:(k + 1) * 128], transpose=True), 16)
        m["p2_wg"] = s.dw.n

    @bk.sync
    def _(e):
        e.wait_ge(s.dws.h, s.dws.n)
        e.wait_ge(s.dwl.h, s.dwl.n)
        for k in range(KD):
            s.dwl.inc(e.dma_start(out=c.WUT[:, k * NIC * 128:(k + 1) * NIC * 128],
                                  in_=c.wqu_s[:, k * 128:(k + 1) * 128], transpose=True), 16)
        m["p2_wu"] = s.dwl.n

    @bk.sync
    def _(e):
        e.wait_ge(s.ct.h, m["C_done_t"])
        e.wait_ge(s.cv.h, m["C_done_v"])
        e.wait_ge(s.cs.h, m["C_done_s"])
        e.wait_ge(s.dsc.h, s.dsc.n)
        s.dxe.inc(e.dma_start(out=c.H2RD[:, 0:KD * R], in_=c.h2_s[0]), 16)
        s.dxo.inc(e.dma_start(out=c.H2RD[:, KD * R:2 * KD * R], in_=c.h2_s[1]), 16)

    def p2_quant(e, j):
        sp = s.dxe if j % 2 == 0 else s.dxo
        e.wait_ge(sp.h, sp.n)
        if j >= 1:
            e.wait_ge(s.cv.h, c.m["p2_clip%d" % (j - 1)])
        s.cs.inc(e.activation(c.XQF[:], c.H2RD[:, (j % 2) * KD * R:(j % 2 + 1) * KD * R],
                              AF.Identity, scale=c.ISXB[:, 0:1],
                              bias=c.CMAGIC[:, 0:1]))
        c.m["p2_q%d" % j] = s.cs.n

    def p2_clip(v, j):
        xqb = c.XQB[:, (j % 2) * KD * R:(j % 2 + 1) * KD * R]
        v.tensor_scalar(c.XQF[:], c.XQF[:], -MAGIC, None, op0=ALU.add)
        s.cv.inc(v.tensor_scalar(xqb, c.XQF[:], -128.0, 127.0,
                                 op0=ALU.max, op1=ALU.min))
        c.m["p2_clip%d" % j] = s.cv.n

    @bk.scalar
    def _(e):
        e.wait_ge(s.cv.h, m["sx_done"])
        e.wait_ge(s.dsc.h, s.dsc.n)
        p2_quant(e, 0)

    @bk.vector
    def _(v):
        v.wait_ge(s.cs.h, c.m["p2_q0"])
        v.wait_ge(s.dsc.h, s.dsc.n)
        p2_clip(v, 0)

    @bk.scalar
    def _(e):
        p2_quant(e, 1)

    @bk.vector
    def _(v):
        v.wait_ge(s.cs.h, c.m["p2_q1"])
        p2_clip(v, 1)

    for j in range(NS):
        bs = s.cs.n
        bt = s.ct.n
        bv = s.cv.n
        c.m["p2_bt%d" % j] = bt
        has_nx = j + 2 < NS
        qoff = 1 if has_nx else 0

        ldq = [2, 3] if j == 0 else ([j + 3] if j + 3 < NS else [])
        for q_ in ldq:
            @bk.sync
            def _(e, q_=q_):
                e.wait_ge(s.cs.h, c.m["p2_q%d" % (q_ - 2)])
                sp = s.dxe if q_ % 2 == 0 else s.dxo
                sp.inc(e.dma_start(out=c.H2RD[:, (q_ % 2) * KD * R:(q_ % 2 + 1) * KD * R],
                                   in_=c.h2_s[q_]), 16)

        @bk.scalar
        def _(e, j=j, bs=bs, bt=bt, bv=bv, has_nx=has_nx):
            if has_nx:
                p2_quant(e, j + 2)
            for ic in range(NIC):
                e.wait_ge(s.ct.h, bt + 2 * ic + 1)
                if ic >= 2:
                    e.wait_ge(s.cv.h, bv + (ic - 1))
                s.cs.inc(e.activation(c.SGW[:, (ic % 2) * R:(ic % 2 + 1) * R],
                                      c.PS[(2 * ic) % 8][:, 0:R],
                                      AF.Silu, scale=c.SXG[:, ic:ic + 1]))

        @bk.tensor
        def _(t, j=j, bs=bs, bt=bt, bv=bv):
            if j == 0:
                t.wait_ge(s.dw.h, m["p2_wg"])
            t.wait_ge(s.cv.h, c.m["p2_clip%d" % j])
            xqb = c.XQB[:, (j % 2) * KD * R:(j % 2 + 1) * KD * R]
            for ic in range(NIC):
                gb_, ub_ = (2 * ic) % 8, (2 * ic + 1) % 8
                if ic < 4:
                    bank_wait(t, gb_)
                    bank_wait(t, ub_)
                else:
                    t.wait_ge(s.cv.h, bv + (ic - 3))
                for k in range(KD):
                    ins = t.matmul(c.PS[gb_][:, 0:R],
                                   c.WGT[:, (k * NIC + ic) * 128:(k * NIC + ic + 1) * 128],
                                   xqb[:, k * R:(k + 1) * R], start=(k == 0), stop=(k == KD - 1))
                s.ct.inc(ins)
                if j == 0 and ic == 0:
                    t.wait_ge(s.dwl.h, m["p2_wu"])
                for k in range(KD):
                    ins = t.matmul(c.PS[ub_][:, 0:R],
                                   c.WUT[:, (k * NIC + ic) * 128:(k * NIC + ic + 1) * 128],
                                   xqb[:, k * R:(k + 1) * R], start=(k == 0), stop=(k == KD - 1))
                s.ct.inc(ins)

        @bk.vector
        def _(v, j=j, bs=bs, bt=bt, bv=bv, has_nx=has_nx, qoff=qoff):
            v.wait_ge(s.dsc.h, s.dsc.n)
            for ic in range(NIC):
                ub_ = (2 * ic + 1) % 8
                v.wait_ge(s.cs.h, bs + qoff + ic + 1)
                v.wait_ge(s.ct.h, bt + 2 * ic + 2)
                vt = c.VT[:, ic * R:(ic + 1) * R]
                v.scalar_tensor_tensor(vt, c.PS[ub_][:, 0:R], c.SXU[:, ic:ic + 1],
                                       c.SGW[:, (ic % 2) * R:(ic % 2 + 1) * R],
                                       op0=ALU.mult, op1=ALU.mult)
                s.cv.inc(v.reduce_max(c.VMAX[:, j * NIC + ic:j * NIC + ic + 1], vt, AXX,
                                      apply_absolute_value=True))
            c.m["p2_stt_last%d" % j] = s.cv.n
            if has_nx:
                v.wait_ge(s.cs.h, bs + 1)
                v.wait_ge(s.ct.h, bt + 2 * NIC)
                p2_clip(v, j + 2)
            c.m["p2_v_last%d" % j] = s.cv.n
        for b in range(8):
            c.bank_free[b] = (s.cv, c.m["p2_stt_last%d" % j])

        @bk.scalar
        def _(e, j=j, bv=bv):
            e.wait_ge(s.cv.h, bv + NIC)
            e.wait_ge(s.dsc.h, s.dsc.n)
            s.dsc.inc(e.dma_start(out=c.v_s[j], in_=c.VT[:]), 16)
            c.m["p2_vst%d" % j] = s.dsc.n

    m["p2_done_v"] = s.cv.n
    m["p2_done_st"] = s.dsc.n
    m["p2_done_t"] = s.ct.n
    m["p2_done_s"] = s.cs.n

    @bk.vector
    def _(v):
        v.wait_ge(s.cv.h, m["p2_done_v"])
        s.cv.inc(v.reduce_max(c.REDT[:, 1:2], c.VMAX[:, 0:NS * NIC], AXX))
        m["sd_red"] = s.cv.n

    @bk.tensor
    def _(t):
        t.wait_ge(s.cv.h, m["sd_red"])
        bank_wait(t, 0)
        s.ct.inc(t.transpose(c.PS[0][0:1, 0:128], c.REDT[:, 1:2], c.ID[:]))
        m["sd_tp"] = s.ct.n

    @bk.vector
    def _(v):
        v.wait_ge(s.ct.h, m["sd_tp"])
        v.memset(c.LMX[:], 0.0)
        s.cv.inc(v.reduce_max(c.LMX[0:1, 0:1], c.PS[0][0:1, 0:128], AXX))
        m["sd_loc"] = s.cv.n
    c.bank_free[0] = (s.cv, m["sd_loc"])

    @bk.gpsimd
    def _(g):
        g.wait_ge(s.cv.h, m["sd_loc"])
        s.dgp.inc(g.dma_start(out=c.cc_m2_in, in_=c.LMX[:]), 16)
        g.wait_ge(s.dgp.h, s.dgp.n)
        s.cc.inc(g.collective_compute("AllReduce", ALU.max,
                                      replica_groups=[list(range(NCORE))],
                                      ins=[c.cc_m2_in], outs=[c.cc_m2_out]))
        g.wait_ge(s.cc.h, s.cc.n)
        s.dgp.inc(g.dma_start(out=c.GMX[:], in_=c.cc_m2_out), 16)
        m["sd_gmax"] = s.dgp.n

    @bk.vector
    def _(v):
        v.wait_ge(s.dgp.h, s.dgp.n)
        v.wait_ge(s.cs.h, m["clip_sig"])
        if c.debug:
            v.wait_ge(s.dst.h, s.dst.n)
        v.tensor_tensor(c.SCL[0:1, 2:3], c.GMX[0:1, 0:1], c.CLIPD[0:1, 1:2], op=ALU.mult)
        v.tensor_scalar(c.SCL[0:1, 2:3], c.SCL[0:1, 2:3], 1.0 / QMAX, None, op0=ALU.mult)
        s.cv.inc(v.reciprocal(c.SCL[0:1, 3:4], c.SCL[0:1, 2:3]))
        m["sd_val"] = s.cv.n

    @bk.tensor
    def _(t):
        t.wait_ge(s.cv.h, m["sd_val"])
        bank_wait(t, 1)
        t.matmul(c.PS[1][:, 0:1], c.ONES[:], c.SCL[0:1, 2:3], start=True, stop=True)
        s.ct.inc(t.matmul(c.PS[1][:, 1:2], c.ONES[:], c.SCL[0:1, 3:4], start=True, stop=True))
        m["sd_bc"] = s.ct.n

    @bk.vector
    def _(v):
        v.wait_ge(s.ct.h, m["sd_bc"])
        v.tensor_copy(c.SDB[:], c.PS[1][:, 0:1])
        s.cv.inc(v.tensor_copy(c.ISDB[:], c.PS[1][:, 1:2]))
        v.wait_ge(s.dw.h, s.dw.n)
        v.tensor_scalar(c.SDD[:], c.SWD[:], c.SDB[:, 0:1], None, op0=ALU.mult)
        s.cv.inc(v.tensor_tensor(c.SDD[:], c.SDD[:], c.GB[:], op=ALU.mult))
        m["sd_done"] = s.cv.n
    c.bank_free[1] = (s.cv, m["sd_done"])

    if c.debug:
        @bk.sync
        def _(e):
            e.wait_ge(s.cv.h, m["sd_done"])
            s.dst.inc(e.dma_start(out=c.dbg_sd, in_=c.SCL[0:1, :]), 16)

    c.p2stack.close()
    c.p3stack = ExitStack()
    sb3 = lambda name, shape, dtp: c.p3stack.enter_context(nc.sbuf_tensor(name, shape, dtp))
    HNC = NIC // 2 + 1
    c.WDT = sb3("WDT", [128, NIC * KD * 128], BF16)
    c.V3H = sb3("V3H", [128, 2 * 11 * R], F32)
    c.V3F = sb3("V3F", [128, 2 * 11 * R], F32)
    c.V3B = sb3("V3B", [128, 2 * 11 * R], BF16)
    c.OFM = sb3("OFM", [128, KD * R], F32)
    c.X3T = sb3("X3T", [128, SUBT * D], F32)

    wdt_src = c.wqd_t.rearrange("(ic p) d -> p ic d", p=128)
    wdt_dst = c.WDT.rearrange("p (ic d) -> p ic d", d=KD * 128)

    @bk.sync
    def _(e):
        e.wait_ge(s.ct.h, m["p2_done_t"])
        e.wait_ge(s.cv.h, m["p2_done_v"])
        e.wait_ge(s.cs.h, m["p2_done_s"])
        e.wait_ge(s.dst.h, s.dst.n)
        e.wait_ge(s.dsc.h, m["C_done_st"])
        s.dx.inc(e.dma_start(out=c.X3T[:], in_=c.x2_s[0]), 16)
        e.wait_ge(s.dws.h, s.dws.n)
        e.wait_ge(s.dw.h, s.dw.n)
        s.dw.inc(e.dma_start(out=wdt_dst[:, 0:11], in_=wdt_src[:, 0:11]), 16)
        m["p3_wld_a"] = s.dw.n

    @bk.sync
    def _(e):
        e.wait_ge(s.dwl.h, s.dwl.n)
        s.dwl.inc(e.dma_start(out=wdt_dst[:, 11:NIC], in_=wdt_src[:, 11:NIC]), 16)
        m["p3_wld_b"] = s.dwl.n

    @bk.sync
    def _(e):
        e.wait_ge(s.dsc.h, s.dsc.n)
        s.dxe.inc(e.dma_start(out=c.V3H[:, 0:11 * R], in_=c.v_s[0][:, 0:11 * R]), 16)
        s.dxo.inc(e.dma_start(out=c.V3H[:, 11 * R:22 * R], in_=c.v_s[0][:, 11 * R:22 * R]), 16)

    def p3_quant(e, q):
        sp = s.dxe if q % 2 == 0 else s.dxo
        e.wait_ge(sp.h, sp.n)
        if q >= 2:
            e.wait_ge(s.cv.h, c.m["p3_clip%d" % (q - 2)])
        s.cs.inc(e.activation(c.V3F[:, (q % 2) * 11 * R:(q % 2 + 1) * 11 * R],
                              c.V3H[:, (q % 2) * 11 * R:(q % 2 + 1) * 11 * R],
                              AF.Identity, scale=c.ISDB[:, 0:1], bias=c.CMAGIC[:, 0:1]))
        c.m["p3_q%d" % q] = s.cs.n

    def p3_clip(v, q):
        v3f = c.V3F[:, (q % 2) * 11 * R:(q % 2 + 1) * 11 * R]
        v.tensor_scalar(v3f, v3f, -MAGIC, None, op0=ALU.add)
        s.cv.inc(v.tensor_scalar(c.V3B[:, (q % 2) * 11 * R:(q % 2 + 1) * 11 * R],
                                 v3f, -128.0, 127.0, op0=ALU.max, op1=ALU.min))
        c.m["p3_clip%d" % q] = s.cv.n

    @bk.scalar
    def _(e):
        e.wait_ge(s.cv.h, m["sd_done"])
        e.wait_ge(s.dsc.h, s.dsc.n)
        p3_quant(e, 0)
        p3_quant(e, 1)

    @bk.vector
    def _(v):
        v.wait_ge(s.cs.h, c.m["p3_q0"])
        p3_clip(v, 0)
        v.wait_ge(s.cs.h, c.m["p3_q1"])
        p3_clip(v, 1)

    for j in range(NS):
        bs3 = s.cs.n
        bt3 = s.ct.n
        bv3 = s.cv.n
        has_nx3 = j + 1 < NS
        nq = 2 if has_nx3 else 0

        if has_nx3:
            @bk.sync
            def _(e, j=j):
                e.wait_ge(s.dsc.h, c.m["p2_vst%d" % (NS - 1)])
                e.wait_ge(s.cs.h, c.m["p3_q%d" % (2 * j)])
                s.dxe.inc(e.dma_start(out=c.V3H[:, 0:11 * R],
                                      in_=c.v_s[j + 1][:, 0:11 * R]), 16)
                e.wait_ge(s.cs.h, c.m["p3_q%d" % (2 * j + 1)])
                s.dxo.inc(e.dma_start(out=c.V3H[:, 11 * R:22 * R],
                                      in_=c.v_s[j + 1][:, 11 * R:22 * R]), 16)

        @bk.scalar
        def _(e, j=j, bs3=bs3, bt3=bt3, has_nx3=has_nx3, nq=nq):
            if has_nx3:
                p3_quant(e, 2 * j + 2)
                p3_quant(e, 2 * j + 3)
            if j == 0:
                e.wait_ge(s.dsc.h, c.m["p2_vst%d" % (NS - 1)])
            for mc in range(8):
                e.wait_ge(s.ct.h, bt3 + 8 + mc + 1)
                s.cs.inc(e.activation(c.OFM[:, mc * R:(mc + 1) * R], c.PS[mc][:, 0:R],
                                      AF.Identity, scale=c.SDD[:, mc:mc + 1]))

        @bk.tensor
        def _(t, j=j, bt3=bt3, bv3=bv3, has_nx3=has_nx3):
            if j == 0:
                t.wait_ge(s.dw.h, m["p3_wld_a"])
                t.wait_ge(s.dwl.h, m["p3_wld_b"])
            for h in range(2):
                t.wait_ge(s.cv.h, c.m["p3_clip%d" % (2 * j + h)])
                for mc in range(8):
                    if j == 0:
                        if h == 0:
                            bank_wait(t, mc)
                    elif h == 0:
                        t.wait_ge(s.cv.h, c.m["p3_x3_%d_%d" % (j - 1, mc)])
                    for il in range(11):
                        ic = h * 11 + il
                        ins = t.matmul(c.PS[mc][:, 0:R],
                                       c.WDT[:, (ic * KD + mc) * 128:(ic * KD + mc + 1) * 128],
                                       c.V3B[:, (h % 2) * 11 * R + il * R:
                                              (h % 2) * 11 * R + (il + 1) * R],
                                       start=(ic == 0), stop=(ic == NIC - 1))
                    s.ct.inc(ins)
            for k in range(KD):
                t.wait_ge(s.cs.h, bs3 + (2 if has_nx3 else 0) + k + 1)
                for t_ in range(SUBT):
                    ins = t.transpose(c.PS[k][:, t_ * 128:(t_ + 1) * 128],
                                      c.OFM[:, k * R + t_ * 128:k * R + (t_ + 1) * 128],
                                      c.ID[:])
                s.ct.inc(ins)

        @bk.vector
        def _(v, j=j, bs3=bs3, bt3=bt3, has_nx3=has_nx3):
            if has_nx3:
                v.wait_ge(s.cs.h, bs3 + 1)
                v.wait_ge(s.ct.h, bt3 + 8)
                p3_clip(v, 2 * j + 2)
                v.wait_ge(s.cs.h, bs3 + 2)
                v.wait_ge(s.ct.h, bt3 + 16)
                p3_clip(v, 2 * j + 3)
            v.wait_ge(s.dx.h, s.dx.n)
            for k in range(KD):
                v.wait_ge(s.ct.h, bt3 + 16 + k + 1)
                src2 = c.PS[k][:, 0:SUBT * 128].rearrange("p (t q) -> p t q", q=128)
                dst2 = c.X3T[:].rearrange("p (t d) -> p t d", d=D)[:, :, k * 128:(k + 1) * 128]
                s.cv.inc(v.tensor_tensor(dst2, dst2, src2, op=ALU.add))
                c.m["p3_x3_%d_%d" % (j, k)] = s.cv.n

        @bk.scalar
        def _(e, j=j):
            e.wait_ge(s.cv.h, c.m["p3_x3_%d_%d" % (j, KD - 1)])
            dstv = c.out.rearrange("(r e) d -> e r d", e=NS)[j].rearrange("(t p) d -> p t d", p=128)
            srcv = c.X3T[:].rearrange("p (t d) -> p t d", d=D)
            s.dst.inc(e.dma_start(out=dstv, in_=srcv), 16)
            c.m["p3_out%d" % j] = s.dst.n

        if has_nx3:
            @bk.sync
            def _(e, j=j):
                e.wait_ge(s.dst.h, c.m["p3_out%d" % j])
                s.dx.inc(e.dma_start(out=c.X3T[:], in_=c.x2_s[j + 1]), 16)

    c.p3stack.close()
    _finish(c, bk)


def _finish(c, bk):
    s = c.s

    @bk.sync
    def _(e):
        e.wait_ge(s.dst.h, s.dst.n)
        e.wait_ge(s.dsc.h, s.dsc.n)
        e.wait_ge(s.dgp.h, s.dgp.n)
        e.wait_ge(s.dw2.h, s.dw2.n)
        e.wait_ge(s.dw3.h, s.dw3.n)




def _bias_fm(vec):
    n = vec.shape[0] // 128
    return np.ascontiguousarray(vec.reshape(n, 128).T)


def _prep_inputs(inputs):
    x = np.ascontiguousarray(np.asarray(inputs["L0_attn_x"], np.float32))
    c_full = np.asarray(inputs["L0_c"], np.float32)
    wmod = np.asarray(inputs["L0_wmod"], np.float32)
    base = {
        "A": np.ascontiguousarray(np.asarray(inputs["L0_attn_A"], np.float32)),
        "B_w": np.ascontiguousarray(np.asarray(inputs["L0_attn_B_w"], np.float32)),
        "B_b": np.ascontiguousarray(np.asarray(inputs["L0_attn_B_b"], np.float32).reshape(DS, 1)),
        "C_w": np.ascontiguousarray(np.asarray(inputs["L0_attn_C_w"], np.float32)),
        "C_b": _bias_fm(np.asarray(inputs["L0_attn_C_b"], np.float32)),
        "D_w": np.ascontiguousarray(np.asarray(inputs["L0_attn_D_w"], np.float32)),
        "D_b": _bias_fm(np.asarray(inputs["L0_attn_D_b"], np.float32)),
        "s_w1": np.ascontiguousarray(np.asarray(inputs["L0_attn_s_w1"], np.float32)),
        "s_b1": _bias_fm(np.asarray(inputs["L0_attn_s_b1"], np.float32)),
        "s_w2": np.ascontiguousarray(np.asarray(inputs["L0_attn_s_w2"], np.float32)),
        "s_b2": np.ascontiguousarray(np.asarray(inputs["L0_attn_s_b2"], np.float32).reshape(DS, 1)),
        "g_w": np.ascontiguousarray(np.asarray(inputs["L0_attn_g_w"], np.float32)),
        "g_b": _bias_fm(np.asarray(inputs["L0_attn_g_b"], np.float32)),
        "w_gate": np.ascontiguousarray(np.asarray(inputs["L0_ffn_w_gate"], np.float32)),
        "w_up": np.ascontiguousarray(np.asarray(inputs["L0_ffn_w_up"], np.float32)),
        "w_down": np.ascontiguousarray(np.asarray(inputs["L0_ffn_w_down"], np.float32)),
        "clip_x": np.asarray(inputs["L0_ffn_clip_x"], np.float32).reshape(1, 1),
        "clip_down": np.asarray(inputs["L0_ffn_clip_down"], np.float32).reshape(1, 1),
        "ident": np.eye(128, dtype=np.float32),
        "ones1": np.ones((1, 128), dtype=np.float32),
    }
    in_maps = []
    for i in range(NCORE):
        im = dict(base)
        im["x"] = np.ascontiguousarray(x[i])
        im["c_ks"] = np.ascontiguousarray(c_full[:, i * 128:(i + 1) * 128])
        im["wmod_ks"] = np.ascontiguousarray(wmod[i * 128:(i + 1) * 128, :])
        in_maps.append(im)
    return in_maps


def kernel(**inputs):
    in_maps = _prep_inputs(inputs)
    nc, _ = build(debug=False, upto="full")
    res = run_bass_kernel_spmd(nc, in_maps, core_ids=list(range(NCORE)))
    return np.stack([np.asarray(r["out"]) for r in res.results], axis=0)


# version 19
# speedup vs baseline: 1.0931x; 1.0931x over previous
import numpy as np
from contextlib import ExitStack

import concourse.bass as bass
import concourse.mybir as mybir
from concourse.bass_utils import run_bass_kernel_spmd

F32 = mybir.dt.float32
BF16 = mybir.dt.bfloat16
I8 = mybir.dt.int8
AF = mybir.ActivationFunctionType
ALU = mybir.AluOpType
AXX = mybir.AxisListType.X

NCORE = 8
SEQ = 4096
D = 1024
KD = 8
NS = 8
R = 512
SUBT = 4
DS = 128
DFF = 2816
NIC = 22
MD = 6144
MC = 48
EPS = 1e-5
QMAX = 127.0
MAGIC = 12582912.0


class Sem:
    def __init__(self, stack, nc, name):
        self.h = stack.enter_context(nc.semaphore(name))
        self.n = 0

    def inc(self, instr, k=1):
        instr.then_inc(self.h, k)
        self.n += k
        return self.n


class Ctx:
    pass


_INTERCEPT = {
    "tensor_copy", "tensor_tensor", "tensor_add", "tensor_sub", "tensor_mul",
    "tensor_max", "tensor_scalar", "tensor_single_scalar", "scalar_tensor_tensor",
    "tensor_reduce", "reduce_max", "reduce_sum", "reciprocal", "memset",
    "memzero", "activation", "bn_stats", "bn_aggr", "select", "copy_predicated",
}


def _is_ap(x):
    return hasattr(x, "tensor") and hasattr(x, "offset") and hasattr(x, "ap")


def _fp(apobj):
    t = apobj.tensor
    shape = list(t.shape)
    row = 1
    for d in shape[1:]:
        row *= d
    off = int(apobj.offset)
    span = 0
    for pair in apobj.ap:
        st, cnt = int(pair[0]), int(pair[1])
        if row > 1 and st >= row:
            continue
        span += (cnt - 1) * abs(st)
    lo = off % row if row > 1 else 0
    return (t.name, lo, lo + span + 1)


class EngProxy:

    def __init__(self, eng):
        self._eng = eng
        self._w = []
        self._r = []

    def _touch(self, writes, reads):
        need = False
        for f in writes + reads:
            for g in self._w:
                if f[0] == g[0] and f[1] < g[2] and g[1] < f[2]:
                    need = True
        for f in writes:
            for g in self._r:
                if f[0] == g[0] and f[1] < g[2] and g[1] < f[2]:
                    need = True
        if need:
            self._eng.drain()
            self._w, self._r = [], []
        self._w += writes
        self._r += reads

    def __getattr__(self, name):
        real = getattr(self._eng, name)
        if name not in _INTERCEPT:
            return real

        def call(*args, **kw):
            writes, reads = [], []
            first = True
            for a in args:
                if _is_ap(a):
                    (writes if first else reads).append(_fp(a))
                    first = False
            for k, a in kw.items():
                if _is_ap(a):
                    (writes if k in ("out", "out_ap", "accum_out") else reads).append(_fp(a))
            self._touch(writes, reads)
            return real(*args, **kw)

        return call




def build(debug=False, upto="full"):
    nc = bass.Bass("TRN2", target_bir_lowering=False)
    c = Ctx()
    c.nc = nc
    c.debug = debug

    dt = nc.dram_tensor
    c.x = dt("x", [SEQ, D], F32, kind="ExternalInput").ap()
    c.A = dt("A", [DS, DS], F32, kind="ExternalInput").ap()
    c.B_w = dt("B_w", [DS, D], F32, kind="ExternalInput").ap()
    c.B_b = dt("B_b", [DS, 1], F32, kind="ExternalInput").ap()
    c.C_w = dt("C_w", [D, DS], F32, kind="ExternalInput").ap()
    c.C_b = dt("C_b", [128, KD], F32, kind="ExternalInput").ap()
    c.D_w = dt("D_w", [D, D], F32, kind="ExternalInput").ap()
    c.D_b = dt("D_b", [128, KD], F32, kind="ExternalInput").ap()
    c.s_w1 = dt("s_w1", [512, D], F32, kind="ExternalInput").ap()
    c.s_b1 = dt("s_b1", [128, 4], F32, kind="ExternalInput").ap()
    c.s_w2 = dt("s_w2", [DS, 512], F32, kind="ExternalInput").ap()
    c.s_b2 = dt("s_b2", [DS, 1], F32, kind="ExternalInput").ap()
    c.g_w = dt("g_w", [D, D], F32, kind="ExternalInput").ap()
    c.g_b = dt("g_b", [128, KD], F32, kind="ExternalInput").ap()
    c.w_gate = dt("w_gate", [DFF, D], F32, kind="ExternalInput").ap()
    c.w_up = dt("w_up", [DFF, D], F32, kind="ExternalInput").ap()
    c.w_down = dt("w_down", [D, DFF], F32, kind="ExternalInput").ap()
    c.clip_x = dt("clip_x", [1, 1], F32, kind="ExternalInput").ap()
    c.clip_down = dt("clip_down", [1, 1], F32, kind="ExternalInput").ap()
    c.c_ks = dt("c_ks", [NCORE, 128], F32, kind="ExternalInput").ap()
    c.wmod_ks = dt("wmod_ks", [128, MD], F32, kind="ExternalInput").ap()
    c.ident = dt("ident", [128, 128], F32, kind="ExternalInput").ap()
    c.ones1 = dt("ones1", [1, 128], F32, kind="ExternalInput").ap()

    c.out = dt("out", [SEQ, D], F32, kind="ExternalOutput").ap()

    skind = "ExternalOutput" if debug else "Internal"
    c.x2_s = dt("x2_s", [NS, 128, SUBT * D], F32, kind=skind).ap()
    c.h2_s = dt("h2_s", [NS, 128, KD * R], F32, kind=skind).ap()
    c.v_s = dt("v_s", [NS, 128, NIC * R], F32, kind=skind).ap()
    c.wqg_s = dt("wqg_s", [NIC * 128, KD * 128], BF16, kind="Internal").ap()
    c.wqu_s = dt("wqu_s", [NIC * 128, KD * 128], BF16, kind="Internal").ap()
    c.wqd_s = dt("wqd_s", [KD * 128, NIC * 128], BF16, kind="Internal").ap()
    c.wqd_t = dt("wqd_t", [NIC * 128, KD * 128], BF16, kind="Internal").ap()
    c.cc_mod_in = dt("cc_mod_in", [NCORE, MD], F32).ap()
    c.cc_mod_out = dt("cc_mod_out", [1, MD], F32).ap()
    c.cc_m1_in = dt("cc_m1_in", [1, 8], F32).ap()
    c.cc_m1_out = dt("cc_m1_out", [1, 8], F32).ap()
    c.cc_m2_in = dt("cc_m2_in", [1, 8], F32).ap()
    c.cc_m2_out = dt("cc_m2_out", [1, 8], F32).ap()
    if debug:
        c.dbg_mod = dt("dbg_mod", [128, MC], F32, kind="ExternalOutput").ap()
        c.dbg_u = dt("dbg_u", [128, SEQ], F32, kind="ExternalOutput").ap()
        c.dbg_hs = dt("dbg_hs", [128, SEQ], F32, kind="ExternalOutput").ap()
        c.dbg_h = dt("dbg_h", [128, KD * R], F32, kind="ExternalOutput").ap()
        c.dbg_sx = dt("dbg_sx", [1, 8], F32, kind="ExternalOutput").ap()
        c.dbg_sd = dt("dbg_sd", [1, 8], F32, kind="ExternalOutput").ap()

    c.sem_stack = ExitStack()

    al = nc.alloc_sbuf_tensor
    c.ID = al("ID", [128, 128], F32)
    c.ONES = al("ONES", [1, 128], F32)
    c.SUMX = al("SUMX", [128, 32], F32)
    c.SUMSQ = al("SUMSQ", [128, 32], F32)
    c.MU = al("MU", [128, 32], F32)
    c.INV = al("INV", [128, 32], F32)
    c.NMI = al("NMI", [128, 32], F32)
    c.MU2 = al("MU2", [128, 32], F32)
    c.INV2 = al("INV2", [128, 32], F32)
    c.NMI2 = al("NMI2", [128, 32], F32)
    c.STSC = al("STSC", [128, 8], F32)
    c.MOD = al("MOD", [128, MC], F32)
    c.ONE_PCA = al("ONE_PCA", [128, KD], F32)
    c.GA = al("GA", [128, KD], F32)
    c.ONE_PCB = al("ONE_PCB", [128, KD], F32)
    c.GB = al("GB", [128, KD], F32)
    c.BCD = al("BCD", [128, KD], F32)
    c.BIAS_B = al("BIAS_B", [DS, 1], F32)
    c.BIAS_S1 = al("BIAS_S1", [128, 4], F32)
    c.BIAS_S2 = al("BIAS_S2", [DS, 1], F32)
    c.BIAS_G = al("BIAS_G", [128, KD], F32)
    c.CB_T = al("CB_T", [128, KD], F32)
    c.DB_T = al("DB_T", [128, KD], F32)
    c.CLIPX = al("CLIPX", [1, 2], F32)
    c.CLIPD = al("CLIPD", [1, 2], F32)
    c.RMAX = al("RMAX", [128, 2], F32)
    c.SWG = al("SWG", [128, NIC], F32)
    c.SWU = al("SWU", [128, NIC], F32)
    c.SWD = al("SWD", [128, KD], F32)
    c.RSG = al("RSG", [128, NIC], F32)
    c.RSU = al("RSU", [128, NIC], F32)
    c.RSD = al("RSD", [128, KD], F32)
    c.SXG = al("SXG", [128, NIC], F32)
    c.SXU = al("SXU", [128, NIC], F32)
    c.SDD = al("SDD", [128, KD], F32)
    c.SXB = al("SXB", [128, 1], F32)
    c.ISXB = al("ISXB", [128, 1], F32)
    c.ISDB = al("ISDB", [128, 1], F32)
    c.SDB = al("SDB", [128, 1], F32)
    c.SCL = al("SCL", [1, 8], F32)
    c.CMAGIC = al("CMAGIC", [128, 1], F32)
    c.NMAGIC = al("NMAGIC", [128, 1], F32)
    c.LMX = al("LMX", [1, 8], F32)
    c.GMX = al("GMX", [1, 8], F32)
    c.H2MAX = al("H2MAX", [128, 64], F32)
    c.VMAX = al("VMAX", [128, NS * NIC], F32)
    c.REDT = al("REDT", [128, 2], F32)
    c.CSL = al("CSL", [NCORE, 128], F32)

    c.PS = [nc.alloc_psum_tensor(f"PS{i}", [128, 512], F32) for i in range(8)]

    c.stage1 = ExitStack()
    s1t = lambda name, shape, dtp=F32: c.stage1.enter_context(nc.sbuf_tensor(name, shape, dtp))
    c.X_TM = s1t("X_TM", [128, 2 * SUBT * D])
    c.XN = s1t("XN", [128, SUBT * D])
    c.H_FM = s1t("H_FM", [128, KD * R])
    c.U = s1t("U", [128, SEQ])
    c.HS = s1t("HS", [128, SEQ])
    c.SGT = s1t("SGT", [128, R])
    c.A_DIR = s1t("A_DIR", [128, 128])
    c.CSLT = s1t("CSLT", [128, NCORE])
    c.H_HI = c.stage1.enter_context(nc.sbuf_tensor("H_HI", [128, KD * R], BF16))
    c.H_LO = c.stage1.enter_context(nc.sbuf_tensor("H_LO", [128, KD * R], BF16))

    c.modstack = ExitStack()
    mst = lambda name, shape: c.modstack.enter_context(nc.sbuf_tensor(name, shape, F32))
    c.WMODS = mst("WMODS", [128, MD])
    c.MODP = mst("MODP", [NCORE, MD])
    c.MODROW = mst("MODROW", [1, MD])

    from types import SimpleNamespace
    sems = {}
    for name in ["dx", "dw", "dw2", "dw3", "dwl", "dws", "dxe", "dxo", "dsc", "dgp", "ct", "cv", "cs", "cg", "cc", "dst"]:
        sems[name] = Sem(c.sem_stack, nc, name)
    c.s = SimpleNamespace(**sems)
    c.m = {}

    with nc.Block() as block:
        _emit(c, block, upto)
    c.sem_stack.close()
    return nc, c




def _emit(c, bk, upto):
    nc = c.nc
    s = c.s
    m = c.m
    c.bank_free = [None] * 8
    pv = EngProxy(nc.vector)
    ps_ = EngProxy(nc.scalar)
    pg = EngProxy(nc.gpsimd)

    _bk0 = bk

    class BK:
        tensor = staticmethod(_bk0.tensor)
        sync = staticmethod(_bk0.sync)

        @staticmethod
        def vector(fn):
            _bk0.vector(lambda _e: fn(pv))

        @staticmethod
        def scalar(fn):
            _bk0.scalar(lambda _e: fn(ps_))

        @staticmethod
        def gpsimd(fn):
            _bk0.gpsimd(lambda _e: fn(pg))

    bk = BK

    def bank_wait(t, i):
        bf = c.bank_free[i]
        if bf is not None:
            t.wait_ge(bf[0].h, bf[1])
            c.bank_free[i] = None

    @bk.sync
    def _(e):
        for dst, src in [(c.ID[:], c.ident), (c.ONES[:], c.ones1), (c.A_DIR[:], c.A),
                         (c.CSL[:], c.c_ks), (c.BIAS_B[:], c.B_b), (c.BIAS_S1[:], c.s_b1),
                         (c.BIAS_S2[:], c.s_b2), (c.BIAS_G[:], c.g_b), (c.CB_T[:], c.C_b),
                         (c.DB_T[:], c.D_b), (c.CLIPX[0:1, 0:1], c.clip_x),
                         (c.CLIPD[0:1, 0:1], c.clip_down)]:
            s.dx.inc(e.dma_start(out=dst, in_=src), 16)
        m["ld_small"] = s.dx.n
        s.dx.inc(e.dma_start(out=c.WMODS[:], in_=c.wmod_ks), 16)
        m["ld_wmod"] = s.dx.n

    @bk.vector
    def _(v):
        v.memset(c.SCL[:], 0.0)
        v.memset(c.LMX[:], 0.0)
        v.memset(c.GMX[:], 0.0)
        v.memset(c.CMAGIC[:], MAGIC)
        v.memset(c.NMAGIC[:], -MAGIC)

    @bk.scalar
    def _(e):
        e.wait_ge(s.dx.h, s.dx.n)
        s.cs.inc(e.activation(c.CSL[:], c.CSL[:], AF.Silu))
        m["csl_silu"] = s.cs.n

    @bk.tensor
    def _(t):
        t.wait_ge(s.cs.h, m["csl_silu"])
        bank_wait(t, 4)
        s.ct.inc(t.transpose(c.PS[4][0:128, 0:NCORE], c.CSL[:], c.ID[0:NCORE, 0:NCORE]))
        m["cslt_t"] = s.ct.n

    @bk.vector
    def _(v):
        v.wait_ge(s.ct.h, m["cslt_t"])
        s.cv.inc(v.tensor_copy(c.CSLT[:], c.PS[4][:, 0:NCORE]))
        m["cslt"] = s.cv.n
    c.bank_free[4] = (s.cv, m["cslt"])

    for n in range(12):
        @bk.tensor
        def _(t, n=n):
            if n == 0:
                t.wait_ge(s.cv.h, m["cslt"])
                t.wait_ge(s.dx.h, s.dx.n)
                bank_wait(t, 4)
                bank_wait(t, 5)
            if n >= 2:
                t.wait_ge(s.cv.h, m["modp_e%d" % (n - 2)])
            s.ct.inc(t.matmul(c.PS[4 + n % 2][0:NCORE, 0:512], c.CSLT[:],
                              c.WMODS[:, n * 512:(n + 1) * 512], start=True, stop=True))
            m["modp_t%d" % n] = s.ct.n

        @bk.vector
        def _(v, n=n):
            v.wait_ge(s.ct.h, m["modp_t%d" % n])
            s.cv.inc(v.tensor_copy(c.MODP[:, n * 512:(n + 1) * 512], c.PS[4 + n % 2][0:NCORE, 0:512]))
            m["modp_e%d" % n] = s.cv.n
    m["modp"] = s.cv.n
    m["modp_mm"] = m["modp_t%d" % 11]
    c.bank_free[4] = (s.cv, m["modp"])
    c.bank_free[5] = (s.cv, m["modp"])

    @bk.gpsimd
    def _(g):
        g.wait_ge(s.cv.h, m["modp"])
        s.dgp.inc(g.dma_start(out=c.cc_mod_in, in_=c.MODP[:]), 16)
        g.wait_ge(s.dgp.h, s.dgp.n)
        s.cc.inc(g.collective_compute("ReduceScatter", ALU.add,
                                      replica_groups=[list(range(NCORE))],
                                      ins=[c.cc_mod_in], outs=[c.cc_mod_out]))
        g.wait_ge(s.cc.h, s.cc.n)
        s.dgp.inc(g.dma_start(out=c.MODROW[:], in_=c.cc_mod_out), 16)
        m["modrow"] = s.dgp.n

    c.modstack.close()
    c.powstack = ExitStack()
    pst = lambda name, shape: c.powstack.enter_context(nc.sbuf_tensor(name, shape, F32))
    c.M_ = [pst(f"Mp{k}", [128, 128]) for k in range(1, 9)]
    c.N_ = [pst(f"Np{k}", [128, 128]) for k in range(1, 9)]
    c.R_ = [pst(f"Rp{k}", [128, 128]) for k in range(1, 9)]
    c.Z_ = [pst(f"Zp{k}", [128, 128]) for k in range(1, 7)]
    c.TDIR = pst("TDIR", [128, 128])
    @bk.tensor
    def _(t):
        t.wait_ge(s.dx.h, s.dx.n)
        for _b in range(4):
            bank_wait(t, _b)
        s.ct.inc(t.matmul(c.PS[0][:, 0:128], c.A_DIR[:], c.ID[:], start=True, stop=True))
        m["pow_m1t"] = s.ct.n

    @bk.vector
    def _(v):
        v.wait_ge(s.ct.h, m["pow_m1t"])
        s.cv.inc(v.tensor_copy(c.M_[0][:], c.PS[0][:, 0:128]))
        m["pow_M1"] = s.cv.n

    for k in range(1, 8):
        @bk.tensor
        def _(t, k=k):
            t.wait_ge(s.cv.h, m["pow_M%d" % k])
            s.ct.inc(t.matmul(c.PS[k % 2][:, 0:128], c.A_DIR[:], c.M_[k - 1][:],
                              start=True, stop=True))
            m["pow_Mt%d" % (k + 1)] = s.ct.n

        @bk.vector
        def _(v, k=k):
            v.wait_ge(s.ct.h, m["pow_Mt%d" % (k + 1)])
            s.cv.inc(v.tensor_copy(c.M_[k][:], c.PS[k % 2][:, 0:128]))
            m["pow_M%d" % (k + 1)] = s.cv.n

    def build_level(tag, src_tile, src_ms, dst_list, nmax):
        @bk.tensor
        def _(t):
            t.wait_ge(s.cv.h, m[src_ms])
            s.ct.inc(t.transpose(c.PS[2][:, 0:128], src_tile[:], c.ID[:]))
            m[tag + "_tpt"] = s.ct.n

        @bk.vector
        def _(v):
            v.wait_ge(s.ct.h, m[tag + "_tpt"])
            v.tensor_copy(c.TDIR[:], c.PS[2][:, 0:128])
            s.cv.inc(v.tensor_copy(dst_list[0][:], src_tile[:]))
            m[tag + "_1"] = s.cv.n

        for k in range(1, nmax):
            @bk.tensor
            def _(t, k=k):
                t.wait_ge(s.cv.h, m[tag + "_%d" % k])
                s.ct.inc(t.matmul(c.PS[k % 2][:, 0:128], c.TDIR[:], dst_list[k - 1][:],
                                  start=True, stop=True))
                m[tag + "_t%d" % (k + 1)] = s.ct.n

            @bk.vector
            def _(v, k=k):
                v.wait_ge(s.ct.h, m[tag + "_t%d" % (k + 1)])
                s.cv.inc(v.tensor_copy(dst_list[k][:], c.PS[k % 2][:, 0:128]))
                m[tag + "_%d" % (k + 1)] = s.cv.n

    build_level("powN", c.M_[7], "pow_M8", c.N_, 8)
    build_level("powR", c.N_[7], "powN_8", c.R_, 8)
    build_level("powZ", c.R_[7], "powR_8", c.Z_, 6)
    m["powers_done"] = s.cv.n
    for _b in range(3):
        c.bank_free[_b] = (s.cv, m["powers_done"])


    @bk.tensor
    def _(t):
        t.wait_ge(s.dgp.h, s.dgp.n)
        for half in range(2):
            bank_wait(t, 6 + half)
            for jj in range(24):
                col = half * 24 + jj
                ins = t.transpose(c.PS[6 + half][:, jj:jj + 1],
                                  c.MODROW[0:1, col * 128:(col + 1) * 128], c.ID[0:1, 0:1])
            s.ct.inc(ins)
            m["mod_t%d" % half] = s.ct.n

    @bk.vector
    def _(v):
        v.wait_ge(s.ct.h, m["mod_t0"])
        v.wait_ge(s.ct.h, m["mod_t1"])
        v.tensor_copy(c.MOD[:, 0:24], c.PS[6][:, 0:24])
        s.cv.inc(v.tensor_copy(c.MOD[:, 24:48], c.PS[7][:, 0:24]))
        sa = c.MOD[:, 0:8]
        ca = c.MOD[:, 8:16]
        ga = c.MOD[:, 16:24]
        cb = c.MOD[:, 32:40]
        gb = c.MOD[:, 40:48]
        v.tensor_scalar(c.ONE_PCA[:], ca, 1.0, None, op0=ALU.add)
        v.tensor_scalar(c.ONE_PCB[:], cb, 1.0, None, op0=ALU.add)
        for src, dst in ((ga, c.GA), (gb, c.GB)):
            v.tensor_scalar(c.STSC[:], src, -0.25, 0.25, op0=ALU.max, op1=ALU.min)
            v.tensor_scalar(dst[:], src, 0.0, None, op0=ALU.is_ge)
            v.scalar_tensor_tensor(dst[:], dst[:], 0.5, src, op0=ALU.mult, op1=ALU.add)
            v.tensor_tensor(dst[:], dst[:], c.STSC[:], op=ALU.subtract)
            v.tensor_scalar(dst[:], dst[:], 0.25, None, op0=ALU.subtract)
        v.tensor_tensor(c.BCD[:], c.CB_T[:], c.DB_T[:], op=ALU.add)
        s.cv.inc(v.tensor_tensor(c.BCD[:], c.BCD[:], c.GA[:], op=ALU.mult))
        m["mod_done"] = s.cv.n
    c.bank_free[6] = (s.cv, m["mod_done"])
    c.bank_free[7] = (s.cv, m["mod_done"])

    if c.debug:
        @bk.sync
        def _(e):
            e.wait_ge(s.cv.h, m["mod_done"])
            s.dst.inc(e.dma_start(out=c.dbg_mod, in_=c.MOD[:]), 16)


    c.astack = ExitStack()
    sba = lambda name, shape: c.astack.enter_context(nc.sbuf_tensor(name, shape, F32))
    c.S_W1T = sba("S_W1T", [128, 32 * 128])
    c.S_W2T = sba("S_W2T", [128, 4 * 128])
    c.B_WT = sba("B_WT", [128, 8 * 128])
    c.WLD = sba("WLD", [128, D])
    c.S1A = sba("S1A", [128, 4 * R])

    def wprep(name, w_ap, nrc, ncc, dst, scale_ga, dst_lo=None):
        for rc in range(nrc):
            @bk.sync
            def _(e, rc=rc):
                if name == "sw1" and rc == 0:
                    e.wait_ge(s.ct.h, m["modp_mm"])
                    e.wait_ge(s.dgp.h, s.dgp.n - 16)
                if c.m.get("wld_free") is not None:
                    e.wait_ge(s.ct.h, c.m["wld_free"])
                s.dw.inc(e.dma_start(out=c.WLD[:, 0:ncc * 128],
                                     in_=w_ap[rc * 128:(rc + 1) * 128, :]), 16)
                c.m[name + "_ld%d" % rc] = s.dw.n

            if scale_ga:
                @bk.vector
                def _(v, rc=rc):
                    v.wait_ge(s.dw.h, s.dw.n)
                    v.wait_ge(s.cv.h, m["mod_done"])
                    s.cv.inc(v.tensor_scalar(c.WLD[:, 0:ncc * 128], c.WLD[:, 0:ncc * 128],
                                             c.GA[:, rc:rc + 1], None, op0=ALU.mult))
                    c.m[name + "_sc%d" % rc] = s.cv.n

            for cc2 in range(ncc):
                @bk.tensor
                def _(t, rc=rc, cc2=cc2):
                    if cc2 == 0:
                        if scale_ga:
                            t.wait_ge(s.cv.h, c.m[name + "_sc%d" % rc])
                        else:
                            t.wait_ge(s.dw.h, s.dw.n)
                    b = cc2 % 4
                    bank_wait(t, b)
                    key = name + "_evk%d" % b
                    if c.m.get(key) is not None:
                        t.wait_ge(s.cv.h, c.m[key])
                    s.ct.inc(t.transpose(c.PS[b][:, 0:128],
                                         c.WLD[:, cc2 * 128:(cc2 + 1) * 128], c.ID[:]))
                    c.m[name + "_tp_%d_%d" % (rc, cc2)] = s.ct.n
                    if cc2 == ncc - 1:
                        c.m["wld_free"] = s.ct.n

                @bk.vector
                def _(v, rc=rc, cc2=cc2):
                    v.wait_ge(s.ct.h, c.m[name + "_tp_%d_%d" % (rc, cc2)])
                    col = (cc2 * nrc + rc) * 128
                    if dst_lo is None:
                        s.cv.inc(v.tensor_copy(dst[:, col:col + 128], c.PS[cc2 % 4][:, 0:128]))
                    else:
                        v.tensor_copy(dst[:, col:col + 128], c.PS[cc2 % 4][:, 0:128])
                        s.cv.inc(v.tensor_tensor(dst_lo[:, col:col + 128], c.PS[cc2 % 4][:, 0:128],
                                                 dst[:, col:col + 128], op=ALU.subtract))
                    c.m[name + "_evk%d" % (cc2 % 4)] = s.cv.n

        c.m[name + "_done"] = s.cv.n
        c.bank_free[0] = (s.cv, s.cv.n)
        c.bank_free[1] = (s.cv, s.cv.n)

    wprep("sw1", c.s_w1, 4, 8, c.S_W1T, False)
    wprep("sw2", c.s_w2, 1, 4, c.S_W2T, False)
    wprep("bw", c.B_w, 1, 8, c.B_WT, False)
    m["wprep_done"] = s.cv.n

    c.qstack = ExitStack()
    qst = lambda name, shape, dtp: c.qstack.enter_context(nc.sbuf_tensor(name, shape, dtp))
    c.WST = qst("WST", [128, NIC * 128], F32)
    c.WQF = qst("WQF", [128, NIC * 128], F32)
    c.WQB = qst("WQB", [128, NIC * 128], BF16)

    def wq_chunk(name, w_ap, ncc, rc, sw_tile, rs_tile, scratch, first):
        @bk.sync
        def _(e):
            if first:
                e.wait_ge(s.ct.h, m["mod_t1"])
                e.wait_ge(s.dgp.h, s.dgp.n)
            if c.m.get("wst_free") is not None:
                e.wait_ge(s.cv.h, c.m["wst_free"])
            s.dwl.inc(e.dma_start(out=c.WST[:, 0:ncc * 128],
                                  in_=w_ap[rc * 128:(rc + 1) * 128, :]), 16)

        @bk.vector
        def _(v):
            v.wait_ge(s.dwl.h, s.dwl.n)
            v.reduce_max(c.RMAX[:, 0:1], c.WST[:, 0:ncc * 128], AXX,
                         apply_absolute_value=True)
            v.tensor_scalar(sw_tile[:, rc:rc + 1], c.RMAX[:, 0:1], 1.0 / QMAX, None,
                            op0=ALU.mult)
            v.reciprocal(c.RMAX[:, 1:2], c.RMAX[:, 0:1])
            s.cv.inc(v.tensor_scalar(rs_tile[:, rc:rc + 1], c.RMAX[:, 1:2], QMAX, None,
                                     op0=ALU.mult))
            s.cv.inc(v.tensor_scalar(c.WQF[:, 0:ncc * 128], c.WST[:, 0:ncc * 128],
                                     rs_tile[:, rc:rc + 1], None, op0=ALU.mult))
            c.m["wst_free"] = s.cv.n
            v.tensor_scalar(c.WQF[:, 0:ncc * 128], c.WQF[:, 0:ncc * 128], MAGIC, None,
                            op0=ALU.add)
            v.wait_ge(s.dws.h, s.dws.n)
            s.cv.inc(v.tensor_scalar(c.WQB[:, 0:ncc * 128], c.WQF[:, 0:ncc * 128],
                                     -MAGIC, None, op0=ALU.add))
            c.m["wq_b"] = s.cv.n

        @bk.sync
        def _(e):
            e.wait_ge(s.cv.h, c.m["wq_b"])
            s.dws.inc(e.dma_start(out=scratch[rc * 128:(rc + 1) * 128, :],
                                  in_=c.WQB[:, 0:ncc * 128]), 16)

    c.wq_chunks = [("wdq", c.w_down, NIC, rc, c.SWD, c.RSD, c.wqd_s) for rc in range(KD)]
    c.wq_pos = 0

    def emit_wq(n):
        while c.wq_pos < len(c.wq_chunks) and n > 0:
            nm, w_ap, ncc, rc, swt, rst, scr = c.wq_chunks[c.wq_pos]
            wq_chunk(nm, w_ap, ncc, rc, swt, rst, scr, c.wq_pos == 0)
            c.wq_pos += 1
            n -= 1
    c.emit_wq = emit_wq

    x_sl = c.x.rearrange("(r e) d -> e r d", e=NS)

    def xbuf(j):
        return c.X_TM[:, (j % 2) * SUBT * D:(j % 2 + 1) * SUBT * D]
    c.xbuf = xbuf

    def a_xload(j):
        @bk.sync
        def _(e, j=j):
            if j >= 2:
                e.wait_ge(s.cs.h, c.m["A_aff%d" % (j - 1)])
            sp = s.dxe if j % 2 == 0 else s.dxo
            src2 = x_sl[j].rearrange("(t p) d -> p t d", p=128)
            dstv = xbuf(j).rearrange("p (t d) -> p t d", d=D)
            sp.inc(e.dma_start(out=dstv, in_=src2), 16)

    def a_vprep(j, sq_last):
        @bk.vector
        def _(v, j=j, sq_last=sq_last):
            sp = s.dxe if j % 2 == 0 else s.dxo
            v.wait_ge(sp.h, sp.n)
            xb = xbuf(j)
            for t in range(SUBT):
                ci = j * 4 + t
                s.cv.inc(v.reduce_sum(c.SUMX[:, ci:ci + 1], xb[:, t * D:(t + 1) * D], AXX))
            cl = slice(j * 4, j * 4 + 4)
            v.wait_ge(s.cs.h, sq_last)
            v.tensor_scalar(c.MU[:, cl], c.SUMX[:, cl], 1.0 / D, None, op0=ALU.mult)
            v.tensor_scalar(c.STSC[:, 0:4], c.SUMSQ[:, cl], 1.0 / D, None, op0=ALU.mult)
            v.tensor_tensor(c.STSC[:, 4:8], c.MU[:, cl], c.MU[:, cl], op=ALU.mult)
            v.tensor_tensor(c.STSC[:, 0:4], c.STSC[:, 0:4], c.STSC[:, 4:8], op=ALU.subtract)
            v.tensor_scalar(c.STSC[:, 0:4], c.STSC[:, 0:4], EPS, None, op0=ALU.add)
            s.cv.inc(v.reciprocal(c.STSC[:, 4:8], c.STSC[:, 0:4]))
            v.wait_ge(s.cs.h, sq_last + 1)
            s.cv.inc(v.scalar_tensor_tensor(c.NMI[:, cl], c.MU[:, cl], -1.0, c.INV[:, cl],
                                            op0=ALU.mult, op1=ALU.mult))
            c.m["A_nmi%d" % j] = s.cv.n

    def a_sprep(j, recip_ms, nmi_ms, tp_ms):
        @bk.scalar
        def _(e, j=j):
            sp = s.dxe if j % 2 == 0 else s.dxo
            e.wait_ge(sp.h, sp.n)
            xb = xbuf(j)
            for t in range(SUBT):
                ci = j * 4 + t
                s.cs.inc(e.activation(c.S1A[:, (t % 2) * D:(t % 2 + 1) * D],
                                      xb[:, t * D:(t + 1) * D], AF.Square,
                                      accum_out=c.SUMSQ[:, ci:ci + 1]))
            cl = slice(j * 4, j * 4 + 4)
            e.wait_ge(s.cv.h, recip_ms)
            s.cs.inc(e.activation(c.INV[:, cl], c.STSC[:, 4:8], AF.Sqrt))
            e.wait_ge(s.cv.h, nmi_ms)
            if tp_ms is not None:
                e.wait_ge(s.ct.h, tp_ms)
            else:
                e.wait_ge(s.cv.h, m["mod_done"])
            for t in range(SUBT):
                ci = j * 4 + t
                s.cs.inc(e.activation(c.XN[:, t * D:(t + 1) * D], xb[:, t * D:(t + 1) * D],
                                      AF.Identity, bias=c.NMI[:, ci:ci + 1],
                                      scale=c.INV[:, ci:ci + 1]))
            c.m["A_aff%d" % j] = s.cs.n

    a_xload(0)
    _sq0 = s.cs.n + 4
    _bv0 = s.cv.n
    a_vprep(0, _sq0)
    a_sprep(0, _bv0 + 5, _bv0 + 6, None)

    for j in range(NS):
        bsI = s.cs.n
        btI = s.ct.n
        bvI = s.cv.n
        has_prep = (j + 1) < NS
        silu_base = bsI + 8 + (9 if has_prep else 0)
        c.m["A_silu_pred%d" % j] = silu_base + 4

        @bk.tensor
        def _(t, j=j, bsI=bsI, bvI=bvI):
            if j == 0:
                t.wait_ge(s.cv.h, m["wprep_done"])
                for _b in range(8):
                    bank_wait(t, _b)
            t.wait_ge(s.cs.h, c.m["A_aff%d" % j])
            for k in range(KD):
                if k >= 2:
                    t.wait_ge(s.cs.h, bsI + k - 1)
                b = 6 + k % 2
                for tt_ in range(SUBT):
                    ins = t.transpose(c.PS[b][:, tt_ * 128:(tt_ + 1) * 128],
                                      c.XN[:, tt_ * D + k * 128:tt_ * D + (k + 1) * 128],
                                      c.ID[:])
                s.ct.inc(ins)
            t.wait_ge(s.cs.h, bsI + 8)
            for mm in range(4):
                for k in range(KD):
                    ins = t.matmul(c.PS[mm][:, 0:R],
                                   c.S_W1T[:, (k * 4 + mm) * 128:(k * 4 + mm + 1) * 128],
                                   c.H_FM[:, k * R:(k + 1) * R], start=(k == 0), stop=(k == KD - 1))
                s.ct.inc(ins)
            t.wait_ge(s.cs.h, c.m["A_silu_pred%d" % j])
            for k2 in range(4):
                ins = t.matmul(c.PS[4][:, 0:R], c.S_W2T[:, k2 * 128:(k2 + 1) * 128],
                               c.S1A[:, k2 * R:(k2 + 1) * R], start=(k2 == 0), stop=(k2 == 3))
            s.ct.inc(ins)
            if j > 0:
                t.wait_ge(s.cv.h, c.m["A_u%d" % (j - 1)])
            for k in range(KD):
                ins = t.matmul(c.PS[5][:, 0:R], c.B_WT[:, k * 128:(k + 1) * 128],
                               c.H_FM[:, k * R:(k + 1) * R], start=(k == 0), stop=(k == KD - 1))
            s.ct.inc(ins)

        if j + 1 < NS:
            a_xload(j + 1)

        @bk.scalar
        def _(e, j=j, btI=btI):
            if c.debug and j == 1:
                e.wait_ge(s.dst.h, s.dst.n)
            for k in range(KD):
                e.wait_ge(s.ct.h, btI + k + 1)
                s.cs.inc(e.activation(c.H_FM[:, k * R:(k + 1) * R], c.PS[6 + k % 2][:, 0:R],
                                      AF.Identity, bias=c.MOD[:, k:k + 1],
                                      scale=c.ONE_PCA[:, k:k + 1]))
            c.m["A_evac%d" % j] = s.cs.n

        if j + 1 < NS:
            _sq = s.cs.n + 4
            _bv = s.cv.n
            a_vprep(j + 1, _sq)
            a_sprep(j + 1, _bv + 5, _bv + 6, btI + 8)

        @bk.scalar
        def _(e, j=j, btI=btI):
            for mm in range(4):
                e.wait_ge(s.ct.h, btI + 9 + mm)
                s.cs.inc(e.activation(c.S1A[:, mm * R:(mm + 1) * R], c.PS[mm][:, 0:R], AF.Silu,
                                      bias=c.BIAS_S1[:, mm:mm + 1]))
            e.wait_ge(s.ct.h, btI + 13)
            if j > 0:
                e.wait_ge(s.cv.h, c.m["A_u%d" % (j - 1)])
            s.cs.inc(e.activation(c.SGT[:], c.PS[4][:, 0:R], AF.Sigmoid, bias=c.BIAS_S2[:]))
            c.m["A_sgt%d" % j] = s.cs.n

        @bk.vector
        def _(v, j=j, btI=btI):
            v.wait_ge(s.ct.h, btI + 14)
            v.wait_ge(s.cs.h, c.m["A_sgt%d" % j])
            s.cv.inc(v.scalar_tensor_tensor(c.U[:, j * R:(j + 1) * R], c.PS[5][:, 0:R],
                                            c.BIAS_B[:], c.SGT[:], op0=ALU.add, op1=ALU.mult))
            c.m["A_u%d" % j] = s.cv.n
        c.m["A_bt%d" % j] = s.ct.n
        c.m["A_xnlast%d" % j] = c.m["A_aff%d" % j]

        if c.debug and j == 0:
            @bk.sync
            def _(e, j=j):
                e.wait_ge(s.cs.h, c.m["A_silu_pred%d" % j])
                s.dst.inc(e.dma_start(out=c.dbg_h, in_=c.H_FM[:]), 16)

        c.emit_wq(1)

    for _b in range(6):
        c.bank_free[_b] = (s.cv, s.cv.n)
    c.bank_free[6] = (s.cs, s.cs.n)
    c.bank_free[7] = (s.cs, s.cs.n)
    m["A_done"] = s.cv.n
    c.emit_wq(999)
    m["wqd_all"] = s.dws.n
    c.qstack.close()
    c.astack.close()
    c.scanstack = ExitStack()
    sst = lambda name, shape: c.scanstack.enter_context(nc.sbuf_tensor(name, shape, F32))
    c.MIX2 = sst("MIX2", [128, 512])
    c.MIX3 = sst("MIX3", [128, 64])
    c.H4S = sst("H4S", [128, 8])
    c.H3S = sst("H3S", [128, 64])
    c.H2SS = sst("H2SS", [128, 512])


    @bk.tensor
    def _(t):
        t.wait_ge(s.cv.h, m["A_done"])
        for b in range(8):
            bank_wait(t, b)
        for tl in range(8):
            for jj in range(tl + 1):
                lhsT = c.ID if jj == tl else c.M_[tl - jj - 1]
                ins = t.matmul(c.PS[tl][:, 0:R], lhsT[:], c.U[:, jj * R:(jj + 1) * R],
                               start=(jj == 0), stop=(jj == tl))
        s.ct.inc(ins)
        m["sc_l1"] = s.ct.n

    @bk.vector
    def _(v):
        v.wait_ge(s.ct.h, m["sc_l1"])
        for tl in range(8):
            s.cv.inc(v.tensor_copy(c.HS[:, tl * R:(tl + 1) * R], c.PS[tl][:, 0:R]))
        m["sc_l1e"] = s.cv.n

    @bk.tensor
    def _(t):
        t.wait_ge(s.cv.h, m["sc_l1e"])
        T1 = c.HS[:, 7 * R:8 * R].rearrange("p (b e) -> p e b", e=8)
        for tl in range(8):
            for jj in range(tl + 1):
                lhsT = c.ID if jj == tl else c.N_[tl - jj - 1]
                ins = t.matmul(c.PS[tl][:, 0:64], lhsT[:], T1[:, jj, :],
                               start=(jj == 0), stop=(jj == tl))
        s.ct.inc(ins)
        m["sc_l2"] = s.ct.n

    @bk.vector
    def _(v):
        v.wait_ge(s.ct.h, m["sc_l2"])
        for tl in range(8):
            s.cv.inc(v.tensor_copy(c.MIX2[:, tl * 64:(tl + 1) * 64], c.PS[tl][:, 0:64]))
        m["sc_l2e"] = s.cv.n

    @bk.tensor
    def _(t):
        t.wait_ge(s.cv.h, m["sc_l2e"])
        T2 = c.MIX2[:, 7 * 64:8 * 64].rearrange("p (b e) -> p e b", e=8)
        for tl in range(8):
            for jj in range(tl + 1):
                lhsT = c.ID if jj == tl else c.R_[tl - jj - 1]
                ins = t.matmul(c.PS[tl][:, 0:8], lhsT[:], T2[:, jj, :],
                               start=(jj == 0), stop=(jj == tl))
        s.ct.inc(ins)
        m["sc_l3"] = s.ct.n

    @bk.vector
    def _(v):
        v.wait_ge(s.ct.h, m["sc_l3"])
        for tl in range(8):
            s.cv.inc(v.tensor_copy(c.MIX3[:, tl * 8:(tl + 1) * 8], c.PS[tl][:, 0:8]))
        s.cv.inc(v.memset(c.H4S[:, 0:1], 0.0))
        m["sc_l3e"] = s.cv.n

    @bk.tensor
    def _(t):
        t.wait_ge(s.cv.h, m["sc_l3e"])
        for cc2 in range(1, 8):
            for jj in range(cc2):
                p = cc2 - 1 - jj
                lhsT = c.ID if p == 0 else c.Z_[p - 1]
                ins = t.matmul(c.PS[cc2][:, 0:1], lhsT[:], c.MIX3[:, 56 + jj:57 + jj],
                               start=(jj == 0), stop=(jj == cc2 - 1))
        s.ct.inc(ins)
        m["sc_l4"] = s.ct.n

    @bk.vector
    def _(v):
        v.wait_ge(s.ct.h, m["sc_l4"])
        for cc2 in range(1, 8):
            s.cv.inc(v.tensor_copy(c.H4S[:, cc2:cc2 + 1], c.PS[cc2][:, 0:1]))
        m["sc_h4"] = s.cv.n

    @bk.tensor
    def _(t):
        t.wait_ge(s.cv.h, m["sc_h4"])
        for tl in range(8):
            lhsT = c.ID if tl == 0 else c.R_[tl - 1]
            ins = t.matmul(c.PS[tl][:, 0:8], lhsT[:], c.H4S[:], start=True, stop=True)
        s.ct.inc(ins)
        m["sc_e3t"] = s.ct.n

    @bk.vector
    def _(v):
        v.wait_ge(s.ct.h, m["sc_e3t"])
        for tl in range(8):
            if tl == 0:
                s.cv.inc(v.tensor_copy(c.H3S[:, 0:8], c.PS[0][:, 0:8]))
            else:
                s.cv.inc(v.tensor_tensor(c.H3S[:, tl * 8:(tl + 1) * 8], c.PS[tl][:, 0:8],
                                         c.MIX3[:, (tl - 1) * 8:tl * 8], op=ALU.add))
        m["sc_e3"] = s.cv.n

    @bk.tensor
    def _(t):
        t.wait_ge(s.cv.h, m["sc_e3"])
        e3nat = c.H3S[:, 0:64].rearrange("p (t b) -> p b t", t=8)
        for tl in range(8):
            lhsT = c.ID if tl == 0 else c.N_[tl - 1]
            ins = t.matmul(c.PS[tl][:, 0:64], lhsT[:], e3nat, start=True, stop=True)
        s.ct.inc(ins)
        m["sc_e2t"] = s.ct.n

    @bk.vector
    def _(v):
        v.wait_ge(s.ct.h, m["sc_e2t"])
        for tl in range(8):
            if tl == 0:
                s.cv.inc(v.tensor_copy(c.H2SS[:, 0:64], c.PS[0][:, 0:64]))
            else:
                s.cv.inc(v.tensor_tensor(c.H2SS[:, tl * 64:(tl + 1) * 64], c.PS[tl][:, 0:64],
                                         c.MIX2[:, (tl - 1) * 64:tl * 64], op=ALU.add))
        m["sc_e2"] = s.cv.n

    @bk.tensor
    def _(t):
        t.wait_ge(s.cv.h, m["sc_e2"])
        e2nat = c.H2SS[:, 0:512].rearrange("p (t b) -> p b t", t=8)
        for tl in range(8):
            ins = t.matmul(c.PS[tl][:, 0:R], c.M_[tl][:], e2nat, start=True, stop=True)
        s.ct.inc(ins)
        m["sc_corrt"] = s.ct.n

    @bk.vector
    def _(v):
        v.wait_ge(s.ct.h, m["sc_corrt"])
        for tl in range(8):
            s.cv.inc(v.tensor_tensor(c.HS[:, tl * R:(tl + 1) * R],
                                     c.HS[:, tl * R:(tl + 1) * R], c.PS[tl][:, 0:R], op=ALU.add))
        m["sc_done"] = s.cv.n
    for k in range(8):
        c.bank_free[k] = (s.cv, m["sc_done"])

    if c.debug:
        @bk.sync
        def _(e):
            e.wait_ge(s.cv.h, m["A_done"])
            s.dst.inc(e.dma_start(out=c.dbg_u, in_=c.U[:]), 16)

        @bk.sync
        def _(e):
            e.wait_ge(s.cv.h, m["sc_done"])
            s.dst.inc(e.dma_start(out=c.dbg_hs, in_=c.HS[:]), 16)

    if upto == "A":
        _finish(c, bk)
        return

    c.scanstack.close()
    c.powstack.close()
    c.cdgstack = ExitStack()
    cst = lambda name, shape, dtp: c.cdgstack.enter_context(nc.sbuf_tensor(name, shape, dtp))
    c.C_WT = cst("C_WT", [128, 8 * 128], F32)
    c.D_HI = cst("D_HI", [128, 64 * 128], BF16)
    c.D_LO = cst("D_LO", [128, 64 * 128], BF16)
    c.G_HI = cst("G_HI", [128, 64 * 128], BF16)
    c.G_LO = cst("G_LO", [128, 64 * 128], BF16)
    c.WLD = cst("WLDC", [128, D], F32)
    c.WSTC = cst("WSTC", [128, D], F32)
    c.WQFC = cst("WQFC", [128, D], F32)
    c.WQBC = cst("WQBC", [128, 2 * D], BF16)
    c.TBNC = cst("TBNC", [128, KD * 128], BF16)

    @bk.sync
    def _(e):
        e.wait_ge(s.ct.h, m["sc_corrt"])
        e.wait_ge(s.cv.h, m["sc_done"])
        e.wait_ge(s.dws.h, s.dws.n)
        e.wait_ge(s.dwl.h, s.dwl.n)

    @bk.vector
    def _(v):
        v.wait_ge(s.dws.h, s.dws.n)
        v.wait_ge(s.dwl.h, s.dwl.n)
        v.wait_ge(s.cs.h, s.cs.n)
    c.m["wld_free"] = m["sc_corrt"]

    wprep("cw", c.C_w, 8, 1, c.C_WT, True)
    wprep("dw", c.D_w, 8, 8, c.D_HI, True, dst_lo=c.D_LO)
    wprep("gw", c.g_w, 8, 8, c.G_HI, False, dst_lo=c.G_LO)
    m["wprepC_done"] = s.cv.n

    H2FM = c.U
    c.bankC = list(c.bank_free)
    c.bank_free = [None] * 8

    def cbw(t, b):
        e_ = c.bankC[b]
        if e_ is not None:
            t.wait_ge(e_[0].h, e_[1])
            c.bankC[b] = None

    c.wq2_chunks = ([(c.w_gate, rc, c.SWG, c.RSG, c.wqg_s) for rc in range(NIC)]
                    + [(c.w_up, rc, c.SWU, c.RSU, c.wqu_s) for rc in range(NIC)])
    c.wq2_pos = 0

    def wq2_batch(qs):
        csb = s.cs.n
        for bi, q in enumerate(qs):
            @bk.sync
            def _(e, bi=bi, q=q, csb=csb):
                w_ap, rc, sw, rs, scr = c.wq2_chunks[q]
                if bi > 0:
                    e.wait_ge(s.cs.h, csb + 2 * bi - 1)
                elif q >= 1:
                    e.wait_ge(s.cs.h, c.m["wq2_round%d" % (q - 1)])
                e.wait_ge(s.dwl.h, s.dwl.n)
                s.dwl.inc(e.dma_start(out=c.WSTC[:],
                                      in_=w_ap[rc * 128:(rc + 1) * 128, :]), 16)

            @bk.vector
            def _(v, q=q):
                w_ap, rc, sw, rs, scr = c.wq2_chunks[q]
                v.wait_ge(s.dwl.h, s.dwl.n)
                v.reduce_max(c.RMAX[:, 0:1], c.WSTC[:], AXX, apply_absolute_value=True)
                v.tensor_scalar(sw[:, rc:rc + 1], c.RMAX[:, 0:1], 1.0 / QMAX, None,
                                op0=ALU.mult)
                v.reciprocal(c.RMAX[:, 1:2], c.RMAX[:, 0:1])
                s.cv.inc(v.tensor_scalar(rs[:, rc:rc + 1], c.RMAX[:, 1:2], QMAX, None,
                                         op0=ALU.mult))
                c.m["wq2_rs%d" % q] = s.cv.n

        @bk.scalar
        def _(e, qs=qs):
            for q in qs:
                w_ap, rc, sw, rs, scr = c.wq2_chunks[q]
                e.wait_ge(s.cv.h, c.m["wq2_rs%d" % q])
                if q >= 2:
                    e.wait_ge(s.dws.h, c.m["wq2_st%d" % (q - 2)])
                s.cs.inc(e.activation(c.WQFC[:], c.WSTC[:],
                                      AF.Identity, scale=rs[:, rc:rc + 1],
                                      bias=c.CMAGIC[:, 0:1]))
                c.m["wq2_round%d" % q] = s.cs.n
                s.cs.inc(e.activation(c.WQBC[:, (q % 2) * D:(q % 2 + 1) * D], c.WQFC[:],
                                      AF.Identity, bias=c.NMAGIC[:, 0:1]))
                c.m["wq2_ub%d" % q] = s.cs.n

        @bk.sync
        def _(e, qs=qs):
            e.wait_ge(s.dws.h, s.dws.n)
            for q in qs:
                w_ap, rc, sw, rs, scr = c.wq2_chunks[q]
                e.wait_ge(s.cs.h, c.m["wq2_ub%d" % q])
                s.dws.inc(e.dma_start(out=scr[rc * 128:(rc + 1) * 128, :],
                                      in_=c.WQBC[:, (q % 2) * D:(q % 2 + 1) * D]), 16)
            for q in qs:
                c.m["wq2_st%d" % q] = s.dws.n

    def emit_wq2(n):
        while c.wq2_pos < len(c.wq2_chunks) and n > 0:
            take = min(2, n, len(c.wq2_chunks) - c.wq2_pos)
            wq2_batch(list(range(c.wq2_pos, c.wq2_pos + take)))
            c.wq2_pos += take
            n -= take

    c.bnc_pos = 0

    def emit_bounce(n):
        while c.bnc_pos < NIC and n > 0:
            ic = c.bnc_pos

            @bk.sync
            def _(e, ic=ic):
                e.wait_ge(s.dws.h, m["wqd_all"])
                if ic >= 1:
                    e.wait_ge(s.dws.h, c.m["bnc_st%d" % (ic - 1)])
                e.wait_ge(s.dw3.h, s.dw3.n)
                s.dw3.inc(e.dma_start(out=c.TBNC[:],
                                      in_=c.wqd_s[:, ic * 128:(ic + 1) * 128],
                                      transpose=True), 16)
                e.wait_ge(s.dw3.h, s.dw3.n)
                e.wait_ge(s.dws.h, s.dws.n)
                s.dws.inc(e.dma_start(out=c.wqd_t[ic * 128:(ic + 1) * 128, :],
                                      in_=c.TBNC[:]), 16)
                c.m["bnc_st%d" % ic] = s.dws.n
            c.bnc_pos += 1
            n -= 1

    @bk.sync
    def _(e):
        e.wait_ge(s.cs.h, c.m["A_xnlast%d" % (NS - 1)])
        src2 = x_sl[0].rearrange("(t p) d -> p t d", p=128)
        dstv = c.xbuf(0).rearrange("p (t d) -> p t d", d=D)
        s.dxe.inc(e.dma_start(out=dstv, in_=src2), 16)

    c.mmrot = 0
    for j in range(NS):
        TAIL = j >= 1
        bsI = s.cs.n
        btI = s.ct.n
        bvI = s.cv.n
        offA = bsI + 4 + (8 if TAIL else 0)
        offB = offA + 8
        sq_last_p = offB + 4
        sqrt_p = offB + 5
        aff2_last_p = offB + 9
        yev_base = offB + (9 if TAIL else 0)
        h2ev_base = yev_base + 8
        t1tp_base = btI + 8
        y_base = t1tp_base + (8 if TAIL else 0)
        h2tp_base = y_base + 8
        hsplit_p = bvI + (8 if TAIL else 0) + 1
        x2add_base = hsplit_p

        @bk.scalar
        def _(e, j=j, TAIL=TAIL):
            sp = s.dxe if j % 2 == 0 else s.dxo
            e.wait_ge(sp.h, sp.n)
            if j == 0:
                e.wait_ge(s.cv.h, m["sc_done"])
                e.wait_ge(s.ct.h, m["sc_corrt"])
            elif j == 1:
                e.wait_ge(s.ct.h, c.m["C_tpL0"])
            else:
                e.wait_ge(s.ct.h, c.m["C_h2tpL%d" % (j - 2)])
            for t in range(SUBT):
                ci = j * 4 + t
                s.cs.inc(e.activation(c.XN[:, t * D:(t + 1) * D],
                                      c.xbuf(j)[:, t * D:(t + 1) * D], AF.Identity,
                                      bias=c.NMI[:, ci:ci + 1], scale=c.INV[:, ci:ci + 1]))
            c.m["C_aff1_%d" % j] = s.cs.n
            if TAIL:
                if j == 1:
                    if c.debug:
                        e.wait_ge(s.dst.h, s.dst.n)
                else:
                    e.wait_ge(s.cv.h, c.m["C_h2maxL%d" % (j - 2)])
                    e.wait_ge(s.dsc.h, c.m["C_h2st%d" % (j - 2)])
                for mc, b_, ctv in c.gch:
                    e.wait_ge(s.ct.h, ctv)
                    s.cs.inc(e.activation(H2FM[:, mc * R:(mc + 1) * R], c.PS[b_][:, 0:R],
                                          AF.Sigmoid, bias=c.BIAS_G[:, mc:mc + 1]))
                c.m["C_gevL%d" % (j - 1)] = s.cs.n
        assert s.cs.n == offA

        if TAIL:
            @bk.vector
            def _(v, j=j, bsI=bsI):
                for k in range(KD):
                    v.wait_ge(s.cs.h, bsI + 4 + k + 1)
                    gt = H2FM[:, k * R:(k + 1) * R]
                    s.cv.inc(v.tensor_tensor(gt, gt, c.H_FM[:, k * R:(k + 1) * R],
                                             op=ALU.mult))
                c.m["C_t1L%d" % (j - 1)] = s.cv.n

        @bk.tensor
        def _(t, j=j, TAIL=TAIL, bvI=bvI, offA=offA, x2add_base=x2add_base):
            t.wait_ge(s.cs.h, c.m["C_aff1_%d" % j])
            for k in range(KD):
                b = 6 + k % 2
                if k < 2:
                    if j == 0:
                        cbw(t, b)
                    else:
                        t.wait_ge(s.cs.h, c.m["C_gevL%d" % (j - 1)])
                else:
                    t.wait_ge(s.cs.h, offA + (k - 2) + 1)
                for tt_ in range(SUBT):
                    ins = t.transpose(c.PS[b][:, tt_ * 128:(tt_ + 1) * 128],
                                      c.XN[:, tt_ * D + k * 128:tt_ * D + (k + 1) * 128],
                                      c.ID[:])
                s.ct.inc(ins)
            c.m["C_tpL%d" % j] = s.ct.n
            if TAIL:
                for k in range(KD):
                    b = 6 + k % 2
                    t.wait_ge(s.cv.h, bvI + k + 1)
                    if k < 2:
                        t.wait_ge(s.cs.h, offA + (6 + k) + 1)
                    else:
                        t.wait_ge(s.cv.h, x2add_base + (k - 2) + 1)
                    for tt_ in range(SUBT):
                        ins = t.transpose(c.PS[b][:, tt_ * 128:(tt_ + 1) * 128],
                                          H2FM[:, k * R + tt_ * 128:k * R + (tt_ + 1) * 128],
                                          c.ID[:])
                    s.ct.inc(ins)
                c.m["C_t1tpL%d" % (j - 1)] = s.ct.n
        assert s.ct.n == t1tp_base + (8 if TAIL else 0)

        @bk.vector
        def _(v, j=j, TAIL=TAIL, offA=offA, sq_last_p=sq_last_p, sqrt_p=sqrt_p,
              t1tp_base=t1tp_base):
            v.wait_ge(s.cs.h, offA + 8)
            if TAIL:
                v.wait_ge(s.ct.h, c.m["C_gL%d" % (j - 1)])
            v.tensor_copy(c.H_HI[:], c.H_FM[:])
            s.cv.inc(v.tensor_tensor(c.H_LO[:], c.H_FM[:], c.H_HI[:], op=ALU.subtract))
            c.m["C_hsplit%d" % j] = s.cv.n
            if TAIL:
                for k in range(KD):
                    v.wait_ge(s.ct.h, t1tp_base + k + 1)
                    src = c.PS[6 + k % 2][:, 0:SUBT * 128].rearrange("p (t q) -> p t q", q=128)
                    dst = c.xbuf(j - 1).rearrange("p (t d) -> p t d", d=D)[:, :, k * 128:(k + 1) * 128]
                    s.cv.inc(v.tensor_tensor(dst, dst, src, op=ALU.add))
                for t in range(SUBT):
                    ci = (j - 1) * 4 + t
                    s.cv.inc(v.reduce_sum(c.SUMX[:, ci:ci + 1],
                                          c.xbuf(j - 1)[:, t * D:(t + 1) * D], AXX))
                cl = slice((j - 1) * 4, (j - 1) * 4 + 4)
                v.wait_ge(s.cs.h, sq_last_p)
                v.tensor_scalar(c.MU2[:, cl], c.SUMX[:, cl], 1.0 / D, None, op0=ALU.mult)
                v.tensor_scalar(c.STSC[:, 0:4], c.SUMSQ[:, cl], 1.0 / D, None, op0=ALU.mult)
                v.tensor_tensor(c.STSC[:, 4:8], c.MU2[:, cl], c.MU2[:, cl], op=ALU.mult)
                v.tensor_tensor(c.STSC[:, 0:4], c.STSC[:, 0:4], c.STSC[:, 4:8], op=ALU.subtract)
                v.tensor_scalar(c.STSC[:, 0:4], c.STSC[:, 0:4], EPS, None, op0=ALU.add)
                s.cv.inc(v.reciprocal(c.STSC[:, 4:8], c.STSC[:, 0:4]))
                v.wait_ge(s.cs.h, sqrt_p)
                s.cv.inc(v.scalar_tensor_tensor(c.NMI2[:, cl], c.MU2[:, cl], -1.0,
                                                c.INV2[:, cl], op0=ALU.mult, op1=ALU.mult))
                c.m["C_nmiL%d" % (j - 1)] = s.cv.n
        assert s.cv.n == x2add_base + (14 if TAIL else 0)

        @bk.tensor
        def _(t, j=j, TAIL=TAIL, yev_base=yev_base, h2ev_base=h2ev_base, offA=offA,
              aff2_last_p=aff2_last_p, x2add_base=x2add_base):
            t.wait_ge(s.cv.h, c.m["C_hsplit%d" % j])
            if j == 0:
                t.wait_ge(s.cv.h, m["wprepC_done"])
            c.ych = []
            for i in range(8):
                b_ = (c.mmrot + i) % 6
                if i < 6:
                    if TAIL:
                        t.wait_ge(s.cs.h, c.gevpos[i])
                    else:
                        cbw(t, b_)
                else:
                    t.wait_ge(s.cs.h, yev_base + (i - 6) + 1)
                t.matmul(c.PS[b_][:, 0:R], c.C_WT[:, i * 128:(i + 1) * 128],
                         c.HS[:, j * R:(j + 1) * R], start=True, stop=False)
                for k in range(KD):
                    col = (k * 8 + i) * 128
                    t.matmul(c.PS[b_][:, 0:R], c.D_HI[:, col:col + 128],
                             c.H_HI[:, k * R:(k + 1) * R], start=False, stop=False)
                    t.matmul(c.PS[b_][:, 0:R], c.D_LO[:, col:col + 128],
                             c.H_HI[:, k * R:(k + 1) * R], start=False, stop=False)
                    ins = t.matmul(c.PS[b_][:, 0:R], c.D_HI[:, col:col + 128],
                                   c.H_LO[:, k * R:(k + 1) * R], start=False, stop=(k == KD - 1))
                s.ct.inc(ins)
                c.ych.append((i, b_, s.ct.n))
            if TAIL:
                t.wait_ge(s.cs.h, aff2_last_p)
                for k in range(KD):
                    b = 6 + k % 2
                    if k < 2:
                        t.wait_ge(s.cv.h, x2add_base + (6 + k) + 1)
                    else:
                        t.wait_ge(s.cs.h, h2ev_base + (k - 2) + 1)
                    for tt_ in range(SUBT):
                        ins = t.transpose(c.PS[b][:, tt_ * 128:(tt_ + 1) * 128],
                                          c.XN[:, tt_ * D + k * 128:tt_ * D + (k + 1) * 128],
                                          c.ID[:])
                    s.ct.inc(ins)
                c.m["C_h2tpL%d" % (j - 1)] = s.ct.n
            c.gch = []
            for i in range(8):
                if i < 6:
                    b_ = (c.mmrot + 8 + i) % 6
                    t.wait_ge(s.cs.h, yev_base + (i + 2) + 1)
                else:
                    b_ = i
                    if TAIL:
                        t.wait_ge(s.cs.h, h2ev_base + i + 1)
                    else:
                        t.wait_ge(s.cs.h, offA + (i + 1))
                for k in range(KD):
                    col = (k * 8 + i) * 128
                    t.matmul(c.PS[b_][:, 0:R], c.G_HI[:, col:col + 128],
                             c.H_HI[:, k * R:(k + 1) * R], start=(k == 0), stop=False)
                    t.matmul(c.PS[b_][:, 0:R], c.G_LO[:, col:col + 128],
                             c.H_HI[:, k * R:(k + 1) * R], start=False, stop=False)
                    ins = t.matmul(c.PS[b_][:, 0:R], c.G_HI[:, col:col + 128],
                                   c.H_LO[:, k * R:(k + 1) * R], start=False, stop=(k == KD - 1))
                s.ct.inc(ins)
                c.gch.append((i, b_, s.ct.n))
            c.mmrot += 14
            c.m["C_gL%d" % j] = s.ct.n

        @bk.scalar
        def _(e, j=j, TAIL=TAIL, btI=btI, x2add_base=x2add_base):
            if TAIL:
                e.wait_ge(s.cv.h, c.m["C_t1L%d" % (j - 1)])
            for k in range(KD):
                e.wait_ge(s.ct.h, btI + k + 1)
                s.cs.inc(e.activation(c.H_FM[:, k * R:(k + 1) * R], c.PS[6 + k % 2][:, 0:R],
                                      AF.Identity, bias=c.MOD[:, k:k + 1],
                                      scale=c.ONE_PCA[:, k:k + 1]))
            c.m["C_hevL%d" % j] = s.cs.n
            if TAIL:
                e.wait_ge(s.cv.h, x2add_base + 8)
                e.wait_ge(s.dsc.h, s.dsc.n)
                s.dsc.inc(e.dma_start(out=c.x2_s[j - 1], in_=c.xbuf(j - 1)), 16)
                c.m["C_x2st%d" % (j - 1)] = s.dsc.n
                for t in range(SUBT):
                    ci = (j - 1) * 4 + t
                    s.cs.inc(e.activation(c.H_FM[:, (t % 2) * D:(t % 2 + 1) * D],
                                          c.xbuf(j - 1)[:, t * D:(t + 1) * D], AF.Square,
                                          accum_out=c.SUMSQ[:, ci:ci + 1]))
                cl = slice((j - 1) * 4, (j - 1) * 4 + 4)
                e.wait_ge(s.cv.h, x2add_base + 13)
                s.cs.inc(e.activation(c.INV2[:, cl], c.STSC[:, 4:8], AF.Sqrt))
                e.wait_ge(s.cv.h, x2add_base + 14)
                e.wait_ge(s.ct.h, c.m["C_tpL%d" % j])
                for t in range(SUBT):
                    ci = (j - 1) * 4 + t
                    s.cs.inc(e.activation(c.XN[:, t * D:(t + 1) * D],
                                          c.xbuf(j - 1)[:, t * D:(t + 1) * D], AF.Identity,
                                          bias=c.NMI2[:, ci:ci + 1], scale=c.INV2[:, ci:ci + 1]))
                c.m["C_aff2L%d" % (j - 1)] = s.cs.n
            e.wait_ge(s.cv.h, c.m["C_hsplit%d" % j])
            for mc, b_, ctv in c.ych:
                e.wait_ge(s.ct.h, ctv)
                s.cs.inc(e.activation(c.H_FM[:, mc * R:(mc + 1) * R], c.PS[b_][:, 0:R],
                                      AF.Identity, bias=c.BCD[:, mc:mc + 1]))
            c.m["C_yevL%d" % j] = s.cs.n
            if TAIL:
                for k in range(KD):
                    e.wait_ge(s.ct.h, c.m["C_h2tpL%d" % (j - 1)] - 8 + k + 1)
                    s.cs.inc(e.activation(H2FM[:, k * R:(k + 1) * R], c.PS[6 + k % 2][:, 0:R],
                                          AF.Identity, bias=c.MOD[:, 24 + k:24 + k + 1],
                                          scale=c.ONE_PCB[:, k:k + 1]))
                c.m["C_h2evL%d" % (j - 1)] = s.cs.n
                e.wait_ge(s.dsc.h, s.dsc.n)
                s.dsc.inc(e.dma_start(out=c.h2_s[j - 1], in_=H2FM[:]), 16)
                c.m["C_h2st%d" % (j - 1)] = s.dsc.n
        assert s.cs.n == h2ev_base + (8 if TAIL else 0), (j, s.cs.n, h2ev_base)
        assert s.ct.n == h2tp_base + (8 if TAIL else 0) + 8
        c.gevpos = [s.cs.n + 4 + i + 1 for i in range(8)]

        if TAIL:
            @bk.vector
            def _(v, j=j, h2ev_base=h2ev_base):
                for k in range(KD):
                    v.wait_ge(s.cs.h, h2ev_base + k + 1)
                    s.cv.inc(v.reduce_max(c.H2MAX[:, (j - 1) * 8 + k:(j - 1) * 8 + k + 1],
                                          H2FM[:, k * R:(k + 1) * R], AXX,
                                          apply_absolute_value=True))
                c.m["C_h2maxL%d" % (j - 1)] = s.cv.n

        if j + 1 < NS:
            @bk.sync
            def _(e, j=j, TAIL=TAIL):
                if TAIL:
                    e.wait_ge(s.cs.h, c.m["C_aff2L%d" % (j - 1)])
                    e.wait_ge(s.cv.h, c.m["C_nmiL%d" % (j - 1)])
                    e.wait_ge(s.dsc.h, c.m["C_x2st%d" % (j - 1)])
                else:
                    e.wait_ge(s.cs.h, c.m["A_xnlast%d" % (NS - 1)])
                sp = s.dxe if (j + 1) % 2 == 0 else s.dxo
                src2 = x_sl[j + 1].rearrange("(t p) d -> p t d", p=128)
                dstv = c.xbuf(j + 1).rearrange("p (t d) -> p t d", d=D)
                sp.inc(e.dma_start(out=dstv, in_=src2), 16)
        emit_wq2(6)
        emit_bounce(3)

    bsE = s.cs.n
    btE = s.ct.n
    bvE = s.cv.n

    @bk.scalar
    def _(e):
        e.wait_ge(s.cv.h, c.m["C_h2maxL%d" % (NS - 2)])
        e.wait_ge(s.dsc.h, c.m["C_h2st%d" % (NS - 2)])
        for mc, b_, ctv in c.gch:
            e.wait_ge(s.ct.h, ctv)
            s.cs.inc(e.activation(H2FM[:, mc * R:(mc + 1) * R], c.PS[b_][:, 0:R],
                                  AF.Sigmoid, bias=c.BIAS_G[:, mc:mc + 1]))
        c.m["C_gevL%d" % (NS - 1)] = s.cs.n

    @bk.vector
    def _(v):
        for k in range(KD):
            v.wait_ge(s.cs.h, bsE + k + 1)
            gt = H2FM[:, k * R:(k + 1) * R]
            s.cv.inc(v.tensor_tensor(gt, gt, c.H_FM[:, k * R:(k + 1) * R], op=ALU.mult))
        c.m["C_t1L%d" % (NS - 1)] = s.cv.n

    @bk.tensor
    def _(t):
        for k in range(KD):
            b = 6 + k % 2
            t.wait_ge(s.cv.h, bvE + k + 1)
            if k < 2:
                t.wait_ge(s.cs.h, c.m["C_gevL%d" % (NS - 1)])
            else:
                t.wait_ge(s.cv.h, bvE + 8 + (k - 2) + 1)
            for tt_ in range(SUBT):
                ins = t.transpose(c.PS[b][:, tt_ * 128:(tt_ + 1) * 128],
                                  H2FM[:, k * R + tt_ * 128:k * R + (tt_ + 1) * 128],
                                  c.ID[:])
            s.ct.inc(ins)

    @bk.vector
    def _(v):
        for k in range(KD):
            v.wait_ge(s.ct.h, btE + k + 1)
            src = c.PS[6 + k % 2][:, 0:SUBT * 128].rearrange("p (t q) -> p t q", q=128)
            dst = c.xbuf(NS - 1).rearrange("p (t d) -> p t d", d=D)[:, :, k * 128:(k + 1) * 128]
            s.cv.inc(v.tensor_tensor(dst, dst, src, op=ALU.add))
        for t in range(SUBT):
            ci = (NS - 1) * 4 + t
            s.cv.inc(v.reduce_sum(c.SUMX[:, ci:ci + 1],
                                  c.xbuf(NS - 1)[:, t * D:(t + 1) * D], AXX))
        cl = slice((NS - 1) * 4, (NS - 1) * 4 + 4)
        v.wait_ge(s.cs.h, bsE + 8 + 4)
        v.tensor_scalar(c.MU2[:, cl], c.SUMX[:, cl], 1.0 / D, None, op0=ALU.mult)
        v.tensor_scalar(c.STSC[:, 0:4], c.SUMSQ[:, cl], 1.0 / D, None, op0=ALU.mult)
        v.tensor_tensor(c.STSC[:, 4:8], c.MU2[:, cl], c.MU2[:, cl], op=ALU.mult)
        v.tensor_tensor(c.STSC[:, 0:4], c.STSC[:, 0:4], c.STSC[:, 4:8], op=ALU.subtract)
        v.tensor_scalar(c.STSC[:, 0:4], c.STSC[:, 0:4], EPS, None, op0=ALU.add)
        s.cv.inc(v.reciprocal(c.STSC[:, 4:8], c.STSC[:, 0:4]))
        v.wait_ge(s.cs.h, bsE + 8 + 5)
        s.cv.inc(v.scalar_tensor_tensor(c.NMI2[:, cl], c.MU2[:, cl], -1.0, c.INV2[:, cl],
                                        op0=ALU.mult, op1=ALU.mult))

    @bk.scalar
    def _(e):
        e.wait_ge(s.cv.h, bvE + 16)
        e.wait_ge(s.dsc.h, s.dsc.n)
        s.dsc.inc(e.dma_start(out=c.x2_s[NS - 1], in_=c.xbuf(NS - 1)), 16)
        c.m["C_x2st%d" % (NS - 1)] = s.dsc.n
        for t in range(SUBT):
            ci = (NS - 1) * 4 + t
            s.cs.inc(e.activation(c.H_FM[:, (t % 2) * D:(t % 2 + 1) * D],
                                  c.xbuf(NS - 1)[:, t * D:(t + 1) * D], AF.Square,
                                  accum_out=c.SUMSQ[:, ci:ci + 1]))
        cl = slice((NS - 1) * 4, (NS - 1) * 4 + 4)
        e.wait_ge(s.cv.h, bvE + 21)
        s.cs.inc(e.activation(c.INV2[:, cl], c.STSC[:, 4:8], AF.Sqrt))
        e.wait_ge(s.cv.h, bvE + 22)
        for t in range(SUBT):
            ci = (NS - 1) * 4 + t
            s.cs.inc(e.activation(c.XN[:, t * D:(t + 1) * D],
                                  c.xbuf(NS - 1)[:, t * D:(t + 1) * D], AF.Identity,
                                  bias=c.NMI2[:, ci:ci + 1], scale=c.INV2[:, ci:ci + 1]))
        c.m["C_aff2L%d" % (NS - 1)] = s.cs.n
    assert s.cs.n == bsE + 17

    @bk.tensor
    def _(t):
        t.wait_ge(s.cs.h, c.m["C_aff2L%d" % (NS - 1)])
        for k in range(KD):
            b = 6 + k % 2
            if k < 2:
                t.wait_ge(s.cv.h, bvE + 8 + (6 + k) + 1)
            else:
                t.wait_ge(s.cs.h, bsE + 17 + (k - 2) + 1)
            for tt_ in range(SUBT):
                ins = t.transpose(c.PS[b][:, tt_ * 128:(tt_ + 1) * 128],
                                  c.XN[:, tt_ * D + k * 128:tt_ * D + (k + 1) * 128],
                                  c.ID[:])
            s.ct.inc(ins)
        c.m["C_h2tpL%d" % (NS - 1)] = s.ct.n

    @bk.scalar
    def _(e):
        for k in range(KD):
            e.wait_ge(s.ct.h, btE + 8 + k + 1)
            s.cs.inc(e.activation(H2FM[:, k * R:(k + 1) * R], c.PS[6 + k % 2][:, 0:R],
                                  AF.Identity, bias=c.MOD[:, 24 + k:24 + k + 1],
                                  scale=c.ONE_PCB[:, k:k + 1]))
        c.m["C_h2evL%d" % (NS - 1)] = s.cs.n
        e.wait_ge(s.dsc.h, s.dsc.n)
        s.dsc.inc(e.dma_start(out=c.h2_s[NS - 1], in_=H2FM[:]), 16)
        c.m["C_h2st%d" % (NS - 1)] = s.dsc.n

    @bk.vector
    def _(v):
        for k in range(KD):
            v.wait_ge(s.cs.h, bsE + 17 + k + 1)
            s.cv.inc(v.reduce_max(c.H2MAX[:, (NS - 1) * 8 + k:(NS - 1) * 8 + k + 1],
                                  H2FM[:, k * R:(k + 1) * R], AXX,
                                  apply_absolute_value=True))
        c.m["C_h2maxL%d" % (NS - 1)] = s.cv.n

    c.bank_free = [(s.cs, s.cs.n) for _ in range(8)]
    m["C_done_v"] = s.cv.n
    m["C_done_t"] = s.ct.n
    m["C_done_s"] = s.cs.n
    m["C_done_st"] = s.dsc.n

    if upto == "C":
        _finish(c, bk)
        return

    @bk.vector
    def _(v):
        v.wait_ge(s.cv.h, m["C_done_v"])
        s.cv.inc(v.reduce_max(c.REDT[:, 0:1], c.H2MAX[:], AXX))
        m["sx_red"] = s.cv.n

    @bk.tensor
    def _(t):
        t.wait_ge(s.cv.h, m["sx_red"])
        bank_wait(t, 0)
        s.ct.inc(t.transpose(c.PS[0][0:1, 0:128], c.REDT[:, 0:1], c.ID[:]))
        m["sx_tp"] = s.ct.n

    @bk.vector
    def _(v):
        v.wait_ge(s.ct.h, m["sx_tp"])
        v.memset(c.LMX[:], 0.0)
        s.cv.inc(v.reduce_max(c.LMX[0:1, 0:1], c.PS[0][0:1, 0:128], AXX))
        m["sx_loc"] = s.cv.n
    c.bank_free[0] = (s.cv, m["sx_loc"])

    @bk.gpsimd
    def _(g):
        g.wait_ge(s.cv.h, m["sx_loc"])
        s.dgp.inc(g.dma_start(out=c.cc_m1_in, in_=c.LMX[:]), 16)
        g.wait_ge(s.dgp.h, s.dgp.n)
        s.cc.inc(g.collective_compute("AllReduce", ALU.max,
                                      replica_groups=[list(range(NCORE))],
                                      ins=[c.cc_m1_in], outs=[c.cc_m1_out]))
        g.wait_ge(s.cc.h, s.cc.n)
        s.dgp.inc(g.dma_start(out=c.GMX[:], in_=c.cc_m1_out), 16)
        m["sx_gmax"] = s.dgp.n

    @bk.scalar
    def _(e):
        e.wait_ge(s.dx.h, s.dx.n)
        e.activation(c.CLIPX[0:1, 1:2], c.CLIPX[0:1, 0:1], AF.Sigmoid)
        s.cs.inc(e.activation(c.CLIPD[0:1, 1:2], c.CLIPD[0:1, 0:1], AF.Sigmoid))
        m["clip_sig"] = s.cs.n

    @bk.vector
    def _(v):
        v.wait_ge(s.dgp.h, s.dgp.n)
        v.wait_ge(s.cs.h, m["clip_sig"])
        v.tensor_tensor(c.SCL[0:1, 0:1], c.GMX[0:1, 0:1], c.CLIPX[0:1, 1:2], op=ALU.mult)
        v.tensor_scalar(c.SCL[0:1, 0:1], c.SCL[0:1, 0:1], 1.0 / QMAX, None, op0=ALU.mult)
        s.cv.inc(v.reciprocal(c.SCL[0:1, 1:2], c.SCL[0:1, 0:1]))
        m["sx_val"] = s.cv.n

    @bk.tensor
    def _(t):
        t.wait_ge(s.cv.h, m["sx_val"])
        bank_wait(t, 1)
        t.matmul(c.PS[1][:, 0:1], c.ONES[:], c.SCL[0:1, 0:1], start=True, stop=True)
        s.ct.inc(t.matmul(c.PS[1][:, 1:2], c.ONES[:], c.SCL[0:1, 1:2], start=True, stop=True))
        m["sx_bc"] = s.ct.n

    @bk.vector
    def _(v):
        v.wait_ge(s.ct.h, m["sx_bc"])
        v.tensor_copy(c.SXB[:], c.PS[1][:, 0:1])
        s.cv.inc(v.tensor_copy(c.ISXB[:], c.PS[1][:, 1:2]))
        v.wait_ge(s.dw.h, s.dw.n)
        v.tensor_scalar(c.SXG[:], c.SWG[:], c.SXB[:, 0:1], None, op0=ALU.mult)
        s.cv.inc(v.tensor_scalar(c.SXU[:], c.SWU[:], c.SXB[:, 0:1], None, op0=ALU.mult))
        m["sx_done"] = s.cv.n
    c.bank_free[1] = (s.cv, m["sx_done"])

    if c.debug:
        @bk.sync
        def _(e):
            e.wait_ge(s.cv.h, m["sx_done"])
            s.dst.inc(e.dma_start(out=c.dbg_sx, in_=c.SCL[0:1, :]), 16)

    if upto == "SX":
        _finish(c, bk)
        return

    c.cdgstack.close()
    c.stage1.close()
    c.p2stack = ExitStack()
    sb2 = lambda name, shape, dtp: c.p2stack.enter_context(nc.sbuf_tensor(name, shape, dtp))
    c.H2RD = sb2("H2RD", [128, 2 * KD * R], F32)
    c.XQF = sb2("XQF", [128, KD * R], F32)
    c.XQB = sb2("XQB", [128, 2 * KD * R], BF16)
    c.SGW = sb2("SGW", [128, 2 * R], F32)
    c.VT = sb2("VT", [128, NIC * R], F32)
    c.WGT = sb2("WGT", [128, NIC * KD * 128], BF16)
    c.WUT = sb2("WUT", [128, NIC * KD * 128], BF16)

    @bk.sync
    def _(e):
        e.wait_ge(s.ct.h, c.m["C_gL%d" % (NS - 1)])
        e.wait_ge(s.dws.h, c.m["wq2_st%d" % (NIC - 1)])
        e.wait_ge(s.dw.h, s.dw.n)
        for k in range(KD):
            s.dw.inc(e.dma_start(out=c.WGT[:, k * NIC * 128:(k + 1) * NIC * 128],
                                 in_=c.wqg_s[:, k * 128:(k + 1) * 128], transpose=True), 16)
        m["p2_wg"] = s.dw.n

    @bk.sync
    def _(e):
        e.wait_ge(s.dws.h, s.dws.n)
        e.wait_ge(s.dwl.h, s.dwl.n)
        for k in range(KD):
            s.dwl.inc(e.dma_start(out=c.WUT[:, k * NIC * 128:(k + 1) * NIC * 128],
                                  in_=c.wqu_s[:, k * 128:(k + 1) * 128], transpose=True), 16)
        m["p2_wu"] = s.dwl.n

    @bk.sync
    def _(e):
        e.wait_ge(s.ct.h, m["C_done_t"])
        e.wait_ge(s.cv.h, m["C_done_v"])
        e.wait_ge(s.cs.h, m["C_done_s"])
        e.wait_ge(s.dsc.h, s.dsc.n)
        s.dxe.inc(e.dma_start(out=c.H2RD[:, 0:KD * R], in_=c.h2_s[0]), 16)
        s.dxo.inc(e.dma_start(out=c.H2RD[:, KD * R:2 * KD * R], in_=c.h2_s[1]), 16)

    def p2_quant(e, j):
        sp = s.dxe if j % 2 == 0 else s.dxo
        e.wait_ge(sp.h, sp.n)
        if j >= 1:
            e.wait_ge(s.cv.h, c.m["p2_clip%d" % (j - 1)])
        s.cs.inc(e.activation(c.XQF[:], c.H2RD[:, (j % 2) * KD * R:(j % 2 + 1) * KD * R],
                              AF.Identity, scale=c.ISXB[:, 0:1],
                              bias=c.CMAGIC[:, 0:1]))
        c.m["p2_q%d" % j] = s.cs.n

    def p2_clip(v, j):
        xqb = c.XQB[:, (j % 2) * KD * R:(j % 2 + 1) * KD * R]
        v.tensor_scalar(c.XQF[:], c.XQF[:], -MAGIC, None, op0=ALU.add)
        s.cv.inc(v.tensor_scalar(xqb, c.XQF[:], -128.0, 127.0,
                                 op0=ALU.max, op1=ALU.min))
        c.m["p2_clip%d" % j] = s.cv.n

    @bk.scalar
    def _(e):
        e.wait_ge(s.cv.h, m["sx_done"])
        e.wait_ge(s.dsc.h, s.dsc.n)
        p2_quant(e, 0)

    @bk.vector
    def _(v):
        v.wait_ge(s.cs.h, c.m["p2_q0"])
        v.wait_ge(s.dsc.h, s.dsc.n)
        p2_clip(v, 0)

    @bk.scalar
    def _(e):
        p2_quant(e, 1)

    @bk.vector
    def _(v):
        v.wait_ge(s.cs.h, c.m["p2_q1"])
        p2_clip(v, 1)

    for j in range(NS):
        bs = s.cs.n
        bt = s.ct.n
        bv = s.cv.n
        c.m["p2_bt%d" % j] = bt
        has_nx = j + 2 < NS
        qoff = 1 if has_nx else 0

        ldq = [2, 3] if j == 0 else ([j + 3] if j + 3 < NS else [])
        for q_ in ldq:
            @bk.sync
            def _(e, q_=q_):
                e.wait_ge(s.cs.h, c.m["p2_q%d" % (q_ - 2)])
                sp = s.dxe if q_ % 2 == 0 else s.dxo
                sp.inc(e.dma_start(out=c.H2RD[:, (q_ % 2) * KD * R:(q_ % 2 + 1) * KD * R],
                                   in_=c.h2_s[q_]), 16)

        @bk.scalar
        def _(e, j=j, bs=bs, bt=bt, bv=bv, has_nx=has_nx):
            if has_nx:
                p2_quant(e, j + 2)
            for ic in range(NIC):
                e.wait_ge(s.ct.h, bt + 2 * ic + 1)
                if ic >= 2:
                    e.wait_ge(s.cv.h, bv + (ic - 1))
                s.cs.inc(e.activation(c.SGW[:, (ic % 2) * R:(ic % 2 + 1) * R],
                                      c.PS[(2 * ic) % 8][:, 0:R],
                                      AF.Silu, scale=c.SXG[:, ic:ic + 1]))

        @bk.tensor
        def _(t, j=j, bs=bs, bt=bt, bv=bv):
            if j == 0:
                t.wait_ge(s.dw.h, m["p2_wg"])
            t.wait_ge(s.cv.h, c.m["p2_clip%d" % j])
            xqb = c.XQB[:, (j % 2) * KD * R:(j % 2 + 1) * KD * R]
            for ic in range(NIC):
                gb_, ub_ = (2 * ic) % 8, (2 * ic + 1) % 8
                if ic < 4:
                    bank_wait(t, gb_)
                    bank_wait(t, ub_)
                else:
                    t.wait_ge(s.cv.h, bv + (ic - 3))
                for k in range(KD):
                    ins = t.matmul(c.PS[gb_][:, 0:R],
                                   c.WGT[:, (k * NIC + ic) * 128:(k * NIC + ic + 1) * 128],
                                   xqb[:, k * R:(k + 1) * R], start=(k == 0), stop=(k == KD - 1))
                s.ct.inc(ins)
                if j == 0 and ic == 0:
                    t.wait_ge(s.dwl.h, m["p2_wu"])
                for k in range(KD):
                    ins = t.matmul(c.PS[ub_][:, 0:R],
                                   c.WUT[:, (k * NIC + ic) * 128:(k * NIC + ic + 1) * 128],
                                   xqb[:, k * R:(k + 1) * R], start=(k == 0), stop=(k == KD - 1))
                s.ct.inc(ins)

        @bk.vector
        def _(v, j=j, bs=bs, bt=bt, bv=bv, has_nx=has_nx, qoff=qoff):
            v.wait_ge(s.dsc.h, s.dsc.n)
            for ic in range(NIC):
                ub_ = (2 * ic + 1) % 8
                v.wait_ge(s.cs.h, bs + qoff + ic + 1)
                v.wait_ge(s.ct.h, bt + 2 * ic + 2)
                vt = c.VT[:, ic * R:(ic + 1) * R]
                v.scalar_tensor_tensor(vt, c.PS[ub_][:, 0:R], c.SXU[:, ic:ic + 1],
                                       c.SGW[:, (ic % 2) * R:(ic % 2 + 1) * R],
                                       op0=ALU.mult, op1=ALU.mult)
                s.cv.inc(v.reduce_max(c.VMAX[:, j * NIC + ic:j * NIC + ic + 1], vt, AXX,
                                      apply_absolute_value=True))
            c.m["p2_stt_last%d" % j] = s.cv.n
            if has_nx:
                v.wait_ge(s.cs.h, bs + 1)
                v.wait_ge(s.ct.h, bt + 2 * NIC)
                p2_clip(v, j + 2)
            c.m["p2_v_last%d" % j] = s.cv.n
        for b in range(8):
            c.bank_free[b] = (s.cv, c.m["p2_stt_last%d" % j])

        @bk.scalar
        def _(e, j=j, bv=bv):
            e.wait_ge(s.cv.h, bv + NIC)
            e.wait_ge(s.dsc.h, s.dsc.n)
            s.dsc.inc(e.dma_start(out=c.v_s[j], in_=c.VT[:]), 16)
            c.m["p2_vst%d" % j] = s.dsc.n

    m["p2_done_v"] = s.cv.n
    m["p2_done_st"] = s.dsc.n
    m["p2_done_t"] = s.ct.n
    m["p2_done_s"] = s.cs.n

    @bk.vector
    def _(v):
        v.wait_ge(s.cv.h, m["p2_done_v"])
        s.cv.inc(v.reduce_max(c.REDT[:, 1:2], c.VMAX[:, 0:NS * NIC], AXX))
        m["sd_red"] = s.cv.n

    @bk.tensor
    def _(t):
        t.wait_ge(s.cv.h, m["sd_red"])
        bank_wait(t, 0)
        s.ct.inc(t.transpose(c.PS[0][0:1, 0:128], c.REDT[:, 1:2], c.ID[:]))
        m["sd_tp"] = s.ct.n

    @bk.vector
    def _(v):
        v.wait_ge(s.ct.h, m["sd_tp"])
        v.memset(c.LMX[:], 0.0)
        s.cv.inc(v.reduce_max(c.LMX[0:1, 0:1], c.PS[0][0:1, 0:128], AXX))
        m["sd_loc"] = s.cv.n
    c.bank_free[0] = (s.cv, m["sd_loc"])

    @bk.gpsimd
    def _(g):
        g.wait_ge(s.cv.h, m["sd_loc"])
        s.dgp.inc(g.dma_start(out=c.cc_m2_in, in_=c.LMX[:]), 16)
        g.wait_ge(s.dgp.h, s.dgp.n)
        s.cc.inc(g.collective_compute("AllReduce", ALU.max,
                                      replica_groups=[list(range(NCORE))],
                                      ins=[c.cc_m2_in], outs=[c.cc_m2_out]))
        g.wait_ge(s.cc.h, s.cc.n)
        s.dgp.inc(g.dma_start(out=c.GMX[:], in_=c.cc_m2_out), 16)
        m["sd_gmax"] = s.dgp.n

    @bk.vector
    def _(v):
        v.wait_ge(s.dgp.h, s.dgp.n)
        v.wait_ge(s.cs.h, m["clip_sig"])
        if c.debug:
            v.wait_ge(s.dst.h, s.dst.n)
        v.tensor_tensor(c.SCL[0:1, 2:3], c.GMX[0:1, 0:1], c.CLIPD[0:1, 1:2], op=ALU.mult)
        v.tensor_scalar(c.SCL[0:1, 2:3], c.SCL[0:1, 2:3], 1.0 / QMAX, None, op0=ALU.mult)
        s.cv.inc(v.reciprocal(c.SCL[0:1, 3:4], c.SCL[0:1, 2:3]))
        m["sd_val"] = s.cv.n

    @bk.tensor
    def _(t):
        t.wait_ge(s.cv.h, m["sd_val"])
        bank_wait(t, 1)
        t.matmul(c.PS[1][:, 0:1], c.ONES[:], c.SCL[0:1, 2:3], start=True, stop=True)
        s.ct.inc(t.matmul(c.PS[1][:, 1:2], c.ONES[:], c.SCL[0:1, 3:4], start=True, stop=True))
        m["sd_bc"] = s.ct.n

    @bk.vector
    def _(v):
        v.wait_ge(s.ct.h, m["sd_bc"])
        v.tensor_copy(c.SDB[:], c.PS[1][:, 0:1])
        s.cv.inc(v.tensor_copy(c.ISDB[:], c.PS[1][:, 1:2]))
        v.wait_ge(s.dw.h, s.dw.n)
        v.tensor_scalar(c.SDD[:], c.SWD[:], c.SDB[:, 0:1], None, op0=ALU.mult)
        s.cv.inc(v.tensor_tensor(c.SDD[:], c.SDD[:], c.GB[:], op=ALU.mult))
        m["sd_done"] = s.cv.n
    c.bank_free[1] = (s.cv, m["sd_done"])

    if c.debug:
        @bk.sync
        def _(e):
            e.wait_ge(s.cv.h, m["sd_done"])
            s.dst.inc(e.dma_start(out=c.dbg_sd, in_=c.SCL[0:1, :]), 16)

    c.p2stack.close()
    c.p3stack = ExitStack()
    sb3 = lambda name, shape, dtp: c.p3stack.enter_context(nc.sbuf_tensor(name, shape, dtp))
    HNC = NIC // 2 + 1
    c.WDT = sb3("WDT", [128, NIC * KD * 128], BF16)
    c.V3H = sb3("V3H", [128, 2 * 11 * R], F32)
    c.V3F = sb3("V3F", [128, 2 * 11 * R], F32)
    c.V3B = sb3("V3B", [128, 2 * 11 * R], BF16)
    c.OFM = sb3("OFM", [128, KD * R], F32)
    c.X3T = sb3("X3T", [128, SUBT * D], F32)

    wdt_src = c.wqd_t.rearrange("(ic p) d -> p ic d", p=128)
    wdt_dst = c.WDT.rearrange("p (ic d) -> p ic d", d=KD * 128)

    @bk.sync
    def _(e):
        e.wait_ge(s.ct.h, m["p2_done_t"])
        e.wait_ge(s.cv.h, m["p2_done_v"])
        e.wait_ge(s.cs.h, m["p2_done_s"])
        e.wait_ge(s.dst.h, s.dst.n)
        e.wait_ge(s.dsc.h, m["C_done_st"])
        s.dx.inc(e.dma_start(out=c.X3T[:], in_=c.x2_s[0]), 16)
        e.wait_ge(s.dws.h, s.dws.n)
        e.wait_ge(s.dw.h, s.dw.n)
        s.dw.inc(e.dma_start(out=wdt_dst[:, 0:11], in_=wdt_src[:, 0:11]), 16)
        m["p3_wld_a"] = s.dw.n

    @bk.sync
    def _(e):
        e.wait_ge(s.dwl.h, s.dwl.n)
        s.dwl.inc(e.dma_start(out=wdt_dst[:, 11:NIC], in_=wdt_src[:, 11:NIC]), 16)
        m["p3_wld_b"] = s.dwl.n

    @bk.sync
    def _(e):
        e.wait_ge(s.dsc.h, s.dsc.n)
        s.dxe.inc(e.dma_start(out=c.V3H[:, 0:11 * R], in_=c.v_s[0][:, 0:11 * R]), 16)
        s.dxo.inc(e.dma_start(out=c.V3H[:, 11 * R:22 * R], in_=c.v_s[0][:, 11 * R:22 * R]), 16)

    def p3_quant(e, q):
        sp = s.dxe if q % 2 == 0 else s.dxo
        e.wait_ge(sp.h, sp.n)
        if q >= 2:
            e.wait_ge(s.cv.h, c.m["p3_clip%d" % (q - 2)])
        s.cs.inc(e.activation(c.V3F[:, (q % 2) * 11 * R:(q % 2 + 1) * 11 * R],
                              c.V3H[:, (q % 2) * 11 * R:(q % 2 + 1) * 11 * R],
                              AF.Identity, scale=c.ISDB[:, 0:1], bias=c.CMAGIC[:, 0:1]))
        c.m["p3_q%d" % q] = s.cs.n

    def p3_clip(v, q):
        v3f = c.V3F[:, (q % 2) * 11 * R:(q % 2 + 1) * 11 * R]
        v.tensor_scalar(v3f, v3f, -MAGIC, None, op0=ALU.add)
        s.cv.inc(v.tensor_scalar(c.V3B[:, (q % 2) * 11 * R:(q % 2 + 1) * 11 * R],
                                 v3f, -128.0, 127.0, op0=ALU.max, op1=ALU.min))
        c.m["p3_clip%d" % q] = s.cv.n

    @bk.scalar
    def _(e):
        e.wait_ge(s.cv.h, m["sd_done"])
        e.wait_ge(s.dsc.h, s.dsc.n)
        p3_quant(e, 0)
        p3_quant(e, 1)

    @bk.vector
    def _(v):
        v.wait_ge(s.cs.h, c.m["p3_q0"])
        p3_clip(v, 0)
        v.wait_ge(s.cs.h, c.m["p3_q1"])
        p3_clip(v, 1)

    for j in range(NS):
        bs3 = s.cs.n
        bt3 = s.ct.n
        bv3 = s.cv.n
        has_nx3 = j + 1 < NS
        nq = 2 if has_nx3 else 0

        if has_nx3:
            @bk.sync
            def _(e, j=j):
                e.wait_ge(s.dsc.h, c.m["p2_vst%d" % (NS - 1)])
                e.wait_ge(s.cs.h, c.m["p3_q%d" % (2 * j)])
                s.dxe.inc(e.dma_start(out=c.V3H[:, 0:11 * R],
                                      in_=c.v_s[j + 1][:, 0:11 * R]), 16)
                e.wait_ge(s.cs.h, c.m["p3_q%d" % (2 * j + 1)])
                s.dxo.inc(e.dma_start(out=c.V3H[:, 11 * R:22 * R],
                                      in_=c.v_s[j + 1][:, 11 * R:22 * R]), 16)

        @bk.scalar
        def _(e, j=j, bs3=bs3, bt3=bt3, has_nx3=has_nx3, nq=nq):
            if has_nx3:
                p3_quant(e, 2 * j + 2)
                p3_quant(e, 2 * j + 3)
            if j == 0:
                e.wait_ge(s.dsc.h, c.m["p2_vst%d" % (NS - 1)])
            for mc in range(8):
                e.wait_ge(s.ct.h, bt3 + 8 + mc + 1)
                s.cs.inc(e.activation(c.OFM[:, mc * R:(mc + 1) * R], c.PS[mc][:, 0:R],
                                      AF.Identity, scale=c.SDD[:, mc:mc + 1]))

        @bk.tensor
        def _(t, j=j, bt3=bt3, bv3=bv3, has_nx3=has_nx3):
            if j == 0:
                t.wait_ge(s.dw.h, m["p3_wld_a"])
                t.wait_ge(s.dwl.h, m["p3_wld_b"])
            for h in range(2):
                t.wait_ge(s.cv.h, c.m["p3_clip%d" % (2 * j + h)])
                for mc in range(8):
                    if j == 0:
                        if h == 0:
                            bank_wait(t, mc)
                    elif h == 0:
                        t.wait_ge(s.cv.h, c.m["p3_x3_%d_%d" % (j - 1, mc)])
                    for il in range(11):
                        ic = h * 11 + il
                        ins = t.matmul(c.PS[mc][:, 0:R],
                                       c.WDT[:, (ic * KD + mc) * 128:(ic * KD + mc + 1) * 128],
                                       c.V3B[:, (h % 2) * 11 * R + il * R:
                                              (h % 2) * 11 * R + (il + 1) * R],
                                       start=(ic == 0), stop=(ic == NIC - 1))
                    s.ct.inc(ins)
            for k in range(KD):
                t.wait_ge(s.cs.h, bs3 + (2 if has_nx3 else 0) + k + 1)
                for t_ in range(SUBT):
                    ins = t.transpose(c.PS[k][:, t_ * 128:(t_ + 1) * 128],
                                      c.OFM[:, k * R + t_ * 128:k * R + (t_ + 1) * 128],
                                      c.ID[:])
                s.ct.inc(ins)

        @bk.vector
        def _(v, j=j, bs3=bs3, bt3=bt3, has_nx3=has_nx3):
            if has_nx3:
                v.wait_ge(s.cs.h, bs3 + 1)
                v.wait_ge(s.ct.h, bt3 + 8)
                p3_clip(v, 2 * j + 2)
                v.wait_ge(s.cs.h, bs3 + 2)
                v.wait_ge(s.ct.h, bt3 + 16)
                p3_clip(v, 2 * j + 3)
            v.wait_ge(s.dx.h, s.dx.n)
            for k in range(KD):
                v.wait_ge(s.ct.h, bt3 + 16 + k + 1)
                src2 = c.PS[k][:, 0:SUBT * 128].rearrange("p (t q) -> p t q", q=128)
                dst2 = c.X3T[:].rearrange("p (t d) -> p t d", d=D)[:, :, k * 128:(k + 1) * 128]
                s.cv.inc(v.tensor_tensor(dst2, dst2, src2, op=ALU.add))
                c.m["p3_x3_%d_%d" % (j, k)] = s.cv.n

        @bk.scalar
        def _(e, j=j):
            e.wait_ge(s.cv.h, c.m["p3_x3_%d_%d" % (j, KD - 1)])
            dstv = c.out.rearrange("(r e) d -> e r d", e=NS)[j].rearrange("(t p) d -> p t d", p=128)
            srcv = c.X3T[:].rearrange("p (t d) -> p t d", d=D)
            s.dst.inc(e.dma_start(out=dstv, in_=srcv), 16)
            c.m["p3_out%d" % j] = s.dst.n

        if has_nx3:
            @bk.sync
            def _(e, j=j):
                e.wait_ge(s.dst.h, c.m["p3_out%d" % j])
                s.dx.inc(e.dma_start(out=c.X3T[:], in_=c.x2_s[j + 1]), 16)

    c.p3stack.close()
    _finish(c, bk)


def _finish(c, bk):
    s = c.s

    @bk.sync
    def _(e):
        e.wait_ge(s.dst.h, s.dst.n)
        e.wait_ge(s.dsc.h, s.dsc.n)
        e.wait_ge(s.dgp.h, s.dgp.n)
        e.wait_ge(s.dw2.h, s.dw2.n)
        e.wait_ge(s.dw3.h, s.dw3.n)




def _bias_fm(vec):
    n = vec.shape[0] // 128
    return np.ascontiguousarray(vec.reshape(n, 128).T)


def _prep_inputs(inputs):
    x = np.ascontiguousarray(np.asarray(inputs["L0_attn_x"], np.float32))
    c_full = np.asarray(inputs["L0_c"], np.float32)
    wmod = np.asarray(inputs["L0_wmod"], np.float32)
    base = {
        "A": np.ascontiguousarray(np.asarray(inputs["L0_attn_A"], np.float32)),
        "B_w": np.ascontiguousarray(np.asarray(inputs["L0_attn_B_w"], np.float32)),
        "B_b": np.ascontiguousarray(np.asarray(inputs["L0_attn_B_b"], np.float32).reshape(DS, 1)),
        "C_w": np.ascontiguousarray(np.asarray(inputs["L0_attn_C_w"], np.float32)),
        "C_b": _bias_fm(np.asarray(inputs["L0_attn_C_b"], np.float32)),
        "D_w": np.ascontiguousarray(np.asarray(inputs["L0_attn_D_w"], np.float32)),
        "D_b": _bias_fm(np.asarray(inputs["L0_attn_D_b"], np.float32)),
        "s_w1": np.ascontiguousarray(np.asarray(inputs["L0_attn_s_w1"], np.float32)),
        "s_b1": _bias_fm(np.asarray(inputs["L0_attn_s_b1"], np.float32)),
        "s_w2": np.ascontiguousarray(np.asarray(inputs["L0_attn_s_w2"], np.float32)),
        "s_b2": np.ascontiguousarray(np.asarray(inputs["L0_attn_s_b2"], np.float32).reshape(DS, 1)),
        "g_w": np.ascontiguousarray(np.asarray(inputs["L0_attn_g_w"], np.float32)),
        "g_b": _bias_fm(np.asarray(inputs["L0_attn_g_b"], np.float32)),
        "w_gate": np.ascontiguousarray(np.asarray(inputs["L0_ffn_w_gate"], np.float32)),
        "w_up": np.ascontiguousarray(np.asarray(inputs["L0_ffn_w_up"], np.float32)),
        "w_down": np.ascontiguousarray(np.asarray(inputs["L0_ffn_w_down"], np.float32)),
        "clip_x": np.asarray(inputs["L0_ffn_clip_x"], np.float32).reshape(1, 1),
        "clip_down": np.asarray(inputs["L0_ffn_clip_down"], np.float32).reshape(1, 1),
        "ident": np.eye(128, dtype=np.float32),
        "ones1": np.ones((1, 128), dtype=np.float32),
    }
    in_maps = []
    for i in range(NCORE):
        im = dict(base)
        im["x"] = np.ascontiguousarray(x[i])
        im["c_ks"] = np.ascontiguousarray(c_full[:, i * 128:(i + 1) * 128])
        im["wmod_ks"] = np.ascontiguousarray(wmod[i * 128:(i + 1) * 128, :])
        in_maps.append(im)
    return in_maps


def kernel(**inputs):
    in_maps = _prep_inputs(inputs)
    nc, _ = build(debug=False, upto="full")
    res = run_bass_kernel_spmd(nc, in_maps, core_ids=list(range(NCORE)))
    return np.stack([np.asarray(r["out"]) for r in res.results], axis=0)


# version 20 (final)
# speedup vs baseline: 1.1092x; 1.0147x over previous
import numpy as np
from contextlib import ExitStack

import concourse.bass as bass
import concourse.mybir as mybir
from concourse.bass_utils import run_bass_kernel_spmd

F32 = mybir.dt.float32
BF16 = mybir.dt.bfloat16
I8 = mybir.dt.int8
AF = mybir.ActivationFunctionType
ALU = mybir.AluOpType
AXX = mybir.AxisListType.X

NCORE = 8
SEQ = 4096
D = 1024
KD = 8
NS = 8
R = 512
SUBT = 4
DS = 128
DFF = 2816
NIC = 22
MD = 6144
MC = 48
EPS = 1e-5
QMAX = 127.0
MAGIC = 12582912.0


class Sem:
    def __init__(self, stack, nc, name):
        self.h = stack.enter_context(nc.semaphore(name))
        self.n = 0

    def inc(self, instr, k=1):
        instr.then_inc(self.h, k)
        self.n += k
        return self.n


class Ctx:
    pass


_INTERCEPT = {
    "tensor_copy", "tensor_tensor", "tensor_add", "tensor_sub", "tensor_mul",
    "tensor_max", "tensor_scalar", "tensor_single_scalar", "scalar_tensor_tensor",
    "tensor_reduce", "reduce_max", "reduce_sum", "reciprocal", "memset",
    "memzero", "activation", "bn_stats", "bn_aggr", "select", "copy_predicated",
}


def _is_ap(x):
    return hasattr(x, "tensor") and hasattr(x, "offset") and hasattr(x, "ap")


def _fp(apobj):
    t = apobj.tensor
    shape = list(t.shape)
    row = 1
    for d in shape[1:]:
        row *= d
    off = int(apobj.offset)
    span = 0
    for pair in apobj.ap:
        st, cnt = int(pair[0]), int(pair[1])
        if row > 1 and st >= row:
            continue
        span += (cnt - 1) * abs(st)
    lo = off % row if row > 1 else 0
    return (t.name, lo, lo + span + 1)


class EngProxy:

    def __init__(self, eng):
        self._eng = eng
        self._w = []
        self._r = []

    def _touch(self, writes, reads):
        need = False
        for f in writes + reads:
            for g in self._w:
                if f[0] == g[0] and f[1] < g[2] and g[1] < f[2]:
                    need = True
        for f in writes:
            for g in self._r:
                if f[0] == g[0] and f[1] < g[2] and g[1] < f[2]:
                    need = True
        if need:
            self._eng.drain()
            self._w, self._r = [], []
        self._w += writes
        self._r += reads

    def __getattr__(self, name):
        real = getattr(self._eng, name)
        if name not in _INTERCEPT:
            return real

        def call(*args, **kw):
            writes, reads = [], []
            first = True
            for a in args:
                if _is_ap(a):
                    (writes if first else reads).append(_fp(a))
                    first = False
            for k, a in kw.items():
                if _is_ap(a):
                    (writes if k in ("out", "out_ap", "accum_out") else reads).append(_fp(a))
            self._touch(writes, reads)
            return real(*args, **kw)

        return call




def build(debug=False, upto="full"):
    nc = bass.Bass("TRN2", target_bir_lowering=False)
    c = Ctx()
    c.nc = nc
    c.debug = debug

    dt = nc.dram_tensor
    c.x = dt("x", [SEQ, D], F32, kind="ExternalInput").ap()
    c.A = dt("A", [DS, DS], F32, kind="ExternalInput").ap()
    c.B_w = dt("B_w", [DS, D], F32, kind="ExternalInput").ap()
    c.B_b = dt("B_b", [DS, 1], F32, kind="ExternalInput").ap()
    c.C_w = dt("C_w", [D, DS], F32, kind="ExternalInput").ap()
    c.C_b = dt("C_b", [128, KD], F32, kind="ExternalInput").ap()
    c.D_w = dt("D_w", [D, D], F32, kind="ExternalInput").ap()
    c.D_b = dt("D_b", [128, KD], F32, kind="ExternalInput").ap()
    c.s_w1 = dt("s_w1", [512, D], F32, kind="ExternalInput").ap()
    c.s_b1 = dt("s_b1", [128, 4], F32, kind="ExternalInput").ap()
    c.s_w2 = dt("s_w2", [DS, 512], F32, kind="ExternalInput").ap()
    c.s_b2 = dt("s_b2", [DS, 1], F32, kind="ExternalInput").ap()
    c.g_w = dt("g_w", [D, D], F32, kind="ExternalInput").ap()
    c.g_b = dt("g_b", [128, KD], F32, kind="ExternalInput").ap()
    c.w_gate = dt("w_gate", [DFF, D], F32, kind="ExternalInput").ap()
    c.w_up = dt("w_up", [DFF, D], F32, kind="ExternalInput").ap()
    c.w_down = dt("w_down", [D, DFF], F32, kind="ExternalInput").ap()
    c.clip_x = dt("clip_x", [1, 1], F32, kind="ExternalInput").ap()
    c.clip_down = dt("clip_down", [1, 1], F32, kind="ExternalInput").ap()
    c.c_ks = dt("c_ks", [NCORE, 128], F32, kind="ExternalInput").ap()
    c.wmod_ks = dt("wmod_ks", [128, MD], F32, kind="ExternalInput").ap()
    c.ident = dt("ident", [128, 128], F32, kind="ExternalInput").ap()
    c.ones1 = dt("ones1", [1, 128], F32, kind="ExternalInput").ap()

    c.out = dt("out", [SEQ, D], F32, kind="ExternalOutput").ap()

    skind = "ExternalOutput" if debug else "Internal"
    c.x2_s = dt("x2_s", [NS, 128, SUBT * D], F32, kind=skind).ap()
    c.h2_s = dt("h2_s", [NS, 128, KD * R], F32, kind=skind).ap()
    c.v_s = dt("v_s", [NS, 128, NIC * R], F32, kind=skind).ap()
    c.wqg_s = dt("wqg_s", [NIC * 128, KD * 128], BF16, kind="Internal").ap()
    c.wqu_s = dt("wqu_s", [NIC * 128, KD * 128], BF16, kind="Internal").ap()
    c.wqd_s = dt("wqd_s", [KD * 128, NIC * 128], BF16, kind="Internal").ap()
    c.wqd_t = dt("wqd_t", [NIC * 128, KD * 128], BF16, kind="Internal").ap()
    c.cc_mod_in = dt("cc_mod_in", [NCORE, MD], F32).ap()
    c.cc_mod_out = dt("cc_mod_out", [1, MD], F32).ap()
    c.cc_m1_in = dt("cc_m1_in", [1, 8], F32).ap()
    c.cc_m1_out = dt("cc_m1_out", [1, 8], F32).ap()
    c.cc_m2_in = dt("cc_m2_in", [1, 8], F32).ap()
    c.cc_m2_out = dt("cc_m2_out", [1, 8], F32).ap()
    if debug:
        c.dbg_mod = dt("dbg_mod", [128, MC], F32, kind="ExternalOutput").ap()
        c.dbg_u = dt("dbg_u", [128, SEQ], F32, kind="ExternalOutput").ap()
        c.dbg_hs = dt("dbg_hs", [128, SEQ], F32, kind="ExternalOutput").ap()
        c.dbg_h = dt("dbg_h", [128, KD * R], F32, kind="ExternalOutput").ap()
        c.dbg_sx = dt("dbg_sx", [1, 8], F32, kind="ExternalOutput").ap()
        c.dbg_sd = dt("dbg_sd", [1, 8], F32, kind="ExternalOutput").ap()

    c.sem_stack = ExitStack()

    al = nc.alloc_sbuf_tensor
    c.ID = al("ID", [128, 128], F32)
    c.ONES = al("ONES", [1, 128], F32)
    c.SUMX = al("SUMX", [128, 32], F32)
    c.SUMSQ = al("SUMSQ", [128, 32], F32)
    c.MU = al("MU", [128, 32], F32)
    c.INV = al("INV", [128, 32], F32)
    c.NMI = al("NMI", [128, 32], F32)
    c.MU2 = al("MU2", [128, 32], F32)
    c.INV2 = al("INV2", [128, 32], F32)
    c.NMI2 = al("NMI2", [128, 32], F32)
    c.STSC = al("STSC", [128, 8], F32)
    c.MOD = al("MOD", [128, MC], F32)
    c.ONE_PCA = al("ONE_PCA", [128, KD], F32)
    c.GA = al("GA", [128, KD], F32)
    c.ONE_PCB = al("ONE_PCB", [128, KD], F32)
    c.GB = al("GB", [128, KD], F32)
    c.BCD = al("BCD", [128, KD], F32)
    c.BIAS_B = al("BIAS_B", [DS, 1], F32)
    c.BIAS_S1 = al("BIAS_S1", [128, 4], F32)
    c.BIAS_S2 = al("BIAS_S2", [DS, 1], F32)
    c.BIAS_G = al("BIAS_G", [128, KD], F32)
    c.CB_T = al("CB_T", [128, KD], F32)
    c.DB_T = al("DB_T", [128, KD], F32)
    c.CLIPX = al("CLIPX", [1, 2], F32)
    c.CLIPD = al("CLIPD", [1, 2], F32)
    c.RMAX = al("RMAX", [128, 2], F32)
    c.SWG = al("SWG", [128, NIC], F32)
    c.SWU = al("SWU", [128, NIC], F32)
    c.SWD = al("SWD", [128, KD], F32)
    c.RSG = al("RSG", [128, NIC], F32)
    c.RSU = al("RSU", [128, NIC], F32)
    c.RSD = al("RSD", [128, KD], F32)
    c.SXG = al("SXG", [128, NIC], F32)
    c.SXU = al("SXU", [128, NIC], F32)
    c.SDD = al("SDD", [128, KD], F32)
    c.SXB = al("SXB", [128, 1], F32)
    c.ISXB = al("ISXB", [128, 1], F32)
    c.ISDB = al("ISDB", [128, 1], F32)
    c.SDB = al("SDB", [128, 1], F32)
    c.SCL = al("SCL", [1, 8], F32)
    c.CMAGIC = al("CMAGIC", [128, 1], F32)
    c.NMAGIC = al("NMAGIC", [128, 1], F32)
    c.LMX = al("LMX", [1, 8], F32)
    c.GMX = al("GMX", [1, 8], F32)
    c.H2MAX = al("H2MAX", [128, 64], F32)
    c.VMAX = al("VMAX", [128, NS * NIC], F32)
    c.REDT = al("REDT", [128, 2], F32)
    c.CSL = al("CSL", [NCORE, 128], F32)

    c.PS = [nc.alloc_psum_tensor(f"PS{i}", [128, 512], F32) for i in range(8)]

    c.stage1 = ExitStack()
    s1t = lambda name, shape, dtp=F32: c.stage1.enter_context(nc.sbuf_tensor(name, shape, dtp))
    c.X_TM = s1t("X_TM", [128, 2 * SUBT * D])
    c.XN = s1t("XN", [128, SUBT * D])
    c.H_FM = s1t("H_FM", [128, KD * R])
    c.U = s1t("U", [128, SEQ])
    c.HS = s1t("HS", [128, SEQ])
    c.SGT = s1t("SGT", [128, R])
    c.A_DIR = s1t("A_DIR", [128, 128])
    c.CSLT = s1t("CSLT", [128, NCORE])
    c.H_HI = c.stage1.enter_context(nc.sbuf_tensor("H_HI", [128, KD * R], BF16))
    c.H_LO = c.stage1.enter_context(nc.sbuf_tensor("H_LO", [128, KD * R], BF16))

    c.modstack = ExitStack()
    mst = lambda name, shape: c.modstack.enter_context(nc.sbuf_tensor(name, shape, F32))
    c.WMODS = mst("WMODS", [128, MD])
    c.MODP = mst("MODP", [NCORE, MD])
    c.MODROW = mst("MODROW", [1, MD])

    from types import SimpleNamespace
    sems = {}
    for name in ["dx", "dw", "dw2", "dw3", "dwl", "dws", "dxe", "dxo", "dsc", "dgp", "ct", "cv", "cs", "cg", "cc", "dst"]:
        sems[name] = Sem(c.sem_stack, nc, name)
    c.s = SimpleNamespace(**sems)
    c.m = {}

    with nc.Block() as block:
        _emit(c, block, upto)
    c.sem_stack.close()
    return nc, c




def _emit(c, bk, upto):
    nc = c.nc
    s = c.s
    m = c.m
    c.bank_free = [None] * 8
    pv = EngProxy(nc.vector)
    ps_ = EngProxy(nc.scalar)
    pg = EngProxy(nc.gpsimd)

    _bk0 = bk

    class BK:
        tensor = staticmethod(_bk0.tensor)
        sync = staticmethod(_bk0.sync)

        @staticmethod
        def vector(fn):
            _bk0.vector(lambda _e: fn(pv))

        @staticmethod
        def scalar(fn):
            _bk0.scalar(lambda _e: fn(ps_))

        @staticmethod
        def gpsimd(fn):
            _bk0.gpsimd(lambda _e: fn(pg))

    bk = BK

    def bank_wait(t, i):
        bf = c.bank_free[i]
        if bf is not None:
            t.wait_ge(bf[0].h, bf[1])
            c.bank_free[i] = None

    @bk.sync
    def _(e):
        for dst, src in [(c.ID[:], c.ident), (c.ONES[:], c.ones1), (c.A_DIR[:], c.A),
                         (c.CSL[:], c.c_ks), (c.BIAS_B[:], c.B_b), (c.BIAS_S1[:], c.s_b1),
                         (c.BIAS_S2[:], c.s_b2), (c.BIAS_G[:], c.g_b), (c.CB_T[:], c.C_b),
                         (c.DB_T[:], c.D_b), (c.CLIPX[0:1, 0:1], c.clip_x),
                         (c.CLIPD[0:1, 0:1], c.clip_down)]:
            s.dx.inc(e.dma_start(out=dst, in_=src), 16)
        m["ld_small"] = s.dx.n
        s.dx.inc(e.dma_start(out=c.WMODS[:], in_=c.wmod_ks), 16)
        m["ld_wmod"] = s.dx.n

    @bk.vector
    def _(v):
        v.memset(c.SCL[:], 0.0)
        v.memset(c.LMX[:], 0.0)
        v.memset(c.GMX[:], 0.0)
        v.memset(c.CMAGIC[:], MAGIC)
        v.memset(c.NMAGIC[:], -MAGIC)

    @bk.scalar
    def _(e):
        e.wait_ge(s.dx.h, s.dx.n)
        s.cs.inc(e.activation(c.CSL[:], c.CSL[:], AF.Silu))
        m["csl_silu"] = s.cs.n

    @bk.tensor
    def _(t):
        t.wait_ge(s.cs.h, m["csl_silu"])
        bank_wait(t, 4)
        s.ct.inc(t.transpose(c.PS[4][0:128, 0:NCORE], c.CSL[:], c.ID[0:NCORE, 0:NCORE]))
        m["cslt_t"] = s.ct.n

    @bk.vector
    def _(v):
        v.wait_ge(s.ct.h, m["cslt_t"])
        s.cv.inc(v.tensor_copy(c.CSLT[:], c.PS[4][:, 0:NCORE]))
        m["cslt"] = s.cv.n
    c.bank_free[4] = (s.cv, m["cslt"])

    for n in range(12):
        @bk.tensor
        def _(t, n=n):
            if n == 0:
                t.wait_ge(s.cv.h, m["cslt"])
                t.wait_ge(s.dx.h, s.dx.n)
                bank_wait(t, 4)
                bank_wait(t, 5)
            if n >= 2:
                t.wait_ge(s.cv.h, m["modp_e%d" % (n - 2)])
            s.ct.inc(t.matmul(c.PS[4 + n % 2][0:NCORE, 0:512], c.CSLT[:],
                              c.WMODS[:, n * 512:(n + 1) * 512], start=True, stop=True))
            m["modp_t%d" % n] = s.ct.n

        @bk.vector
        def _(v, n=n):
            v.wait_ge(s.ct.h, m["modp_t%d" % n])
            s.cv.inc(v.tensor_copy(c.MODP[:, n * 512:(n + 1) * 512], c.PS[4 + n % 2][0:NCORE, 0:512]))
            m["modp_e%d" % n] = s.cv.n
    m["modp"] = s.cv.n
    m["modp_mm"] = m["modp_t%d" % 11]
    c.bank_free[4] = (s.cv, m["modp"])
    c.bank_free[5] = (s.cv, m["modp"])

    @bk.gpsimd
    def _(g):
        g.wait_ge(s.cv.h, m["modp"])
        s.dgp.inc(g.dma_start(out=c.cc_mod_in, in_=c.MODP[:]), 16)
        g.wait_ge(s.dgp.h, s.dgp.n)
        s.cc.inc(g.collective_compute("ReduceScatter", ALU.add,
                                      replica_groups=[list(range(NCORE))],
                                      ins=[c.cc_mod_in], outs=[c.cc_mod_out]))
        g.wait_ge(s.cc.h, s.cc.n)
        s.dgp.inc(g.dma_start(out=c.MODROW[:], in_=c.cc_mod_out), 16)
        m["modrow"] = s.dgp.n

    c.modstack.close()
    c.powstack = ExitStack()
    pst = lambda name, shape: c.powstack.enter_context(nc.sbuf_tensor(name, shape, F32))
    c.M_ = [pst(f"Mp{k}", [128, 128]) for k in range(1, 9)]
    c.N_ = [pst(f"Np{k}", [128, 128]) for k in range(1, 9)]
    c.R_ = [pst(f"Rp{k}", [128, 128]) for k in range(1, 9)]
    c.Z_ = [pst(f"Zp{k}", [128, 128]) for k in range(1, 7)]
    c.TDIR = pst("TDIR", [128, 128])
    @bk.tensor
    def _(t):
        t.wait_ge(s.dx.h, s.dx.n)
        for _b in range(4):
            bank_wait(t, _b)
        s.ct.inc(t.matmul(c.PS[0][:, 0:128], c.A_DIR[:], c.ID[:], start=True, stop=True))
        m["pow_m1t"] = s.ct.n

    @bk.vector
    def _(v):
        v.wait_ge(s.ct.h, m["pow_m1t"])
        s.cv.inc(v.tensor_copy(c.M_[0][:], c.PS[0][:, 0:128]))
        m["pow_M1"] = s.cv.n

    for k in range(1, 8):
        @bk.tensor
        def _(t, k=k):
            t.wait_ge(s.cv.h, m["pow_M%d" % k])
            s.ct.inc(t.matmul(c.PS[k % 2][:, 0:128], c.A_DIR[:], c.M_[k - 1][:],
                              start=True, stop=True))
            m["pow_Mt%d" % (k + 1)] = s.ct.n

        @bk.vector
        def _(v, k=k):
            v.wait_ge(s.ct.h, m["pow_Mt%d" % (k + 1)])
            s.cv.inc(v.tensor_copy(c.M_[k][:], c.PS[k % 2][:, 0:128]))
            m["pow_M%d" % (k + 1)] = s.cv.n

    def build_level(tag, src_tile, src_ms, dst_list, nmax):
        @bk.tensor
        def _(t):
            t.wait_ge(s.cv.h, m[src_ms])
            s.ct.inc(t.transpose(c.PS[2][:, 0:128], src_tile[:], c.ID[:]))
            m[tag + "_tpt"] = s.ct.n

        @bk.vector
        def _(v):
            v.wait_ge(s.ct.h, m[tag + "_tpt"])
            v.tensor_copy(c.TDIR[:], c.PS[2][:, 0:128])
            s.cv.inc(v.tensor_copy(dst_list[0][:], src_tile[:]))
            m[tag + "_1"] = s.cv.n

        for k in range(1, nmax):
            @bk.tensor
            def _(t, k=k):
                t.wait_ge(s.cv.h, m[tag + "_%d" % k])
                s.ct.inc(t.matmul(c.PS[k % 2][:, 0:128], c.TDIR[:], dst_list[k - 1][:],
                                  start=True, stop=True))
                m[tag + "_t%d" % (k + 1)] = s.ct.n

            @bk.vector
            def _(v, k=k):
                v.wait_ge(s.ct.h, m[tag + "_t%d" % (k + 1)])
                s.cv.inc(v.tensor_copy(dst_list[k][:], c.PS[k % 2][:, 0:128]))
                m[tag + "_%d" % (k + 1)] = s.cv.n

    build_level("powN", c.M_[7], "pow_M8", c.N_, 8)
    build_level("powR", c.N_[7], "powN_8", c.R_, 8)
    build_level("powZ", c.R_[7], "powR_8", c.Z_, 6)
    m["powers_done"] = s.cv.n
    for _b in range(3):
        c.bank_free[_b] = (s.cv, m["powers_done"])


    @bk.tensor
    def _(t):
        t.wait_ge(s.dgp.h, s.dgp.n)
        for half in range(2):
            bank_wait(t, 6 + half)
            for jj in range(24):
                col = half * 24 + jj
                ins = t.transpose(c.PS[6 + half][:, jj:jj + 1],
                                  c.MODROW[0:1, col * 128:(col + 1) * 128], c.ID[0:1, 0:1])
            s.ct.inc(ins)
            m["mod_t%d" % half] = s.ct.n

    @bk.vector
    def _(v):
        v.wait_ge(s.ct.h, m["mod_t0"])
        v.wait_ge(s.ct.h, m["mod_t1"])
        v.tensor_copy(c.MOD[:, 0:24], c.PS[6][:, 0:24])
        s.cv.inc(v.tensor_copy(c.MOD[:, 24:48], c.PS[7][:, 0:24]))
        sa = c.MOD[:, 0:8]
        ca = c.MOD[:, 8:16]
        ga = c.MOD[:, 16:24]
        cb = c.MOD[:, 32:40]
        gb = c.MOD[:, 40:48]
        v.tensor_scalar(c.ONE_PCA[:], ca, 1.0, None, op0=ALU.add)
        v.tensor_scalar(c.ONE_PCB[:], cb, 1.0, None, op0=ALU.add)
        for src, dst in ((ga, c.GA), (gb, c.GB)):
            v.tensor_scalar(c.STSC[:], src, -0.25, 0.25, op0=ALU.max, op1=ALU.min)
            v.tensor_scalar(dst[:], src, 0.0, None, op0=ALU.is_ge)
            v.scalar_tensor_tensor(dst[:], dst[:], 0.5, src, op0=ALU.mult, op1=ALU.add)
            v.tensor_tensor(dst[:], dst[:], c.STSC[:], op=ALU.subtract)
            v.tensor_scalar(dst[:], dst[:], 0.25, None, op0=ALU.subtract)
        v.tensor_tensor(c.BCD[:], c.CB_T[:], c.DB_T[:], op=ALU.add)
        s.cv.inc(v.tensor_tensor(c.BCD[:], c.BCD[:], c.GA[:], op=ALU.mult))
        m["mod_done"] = s.cv.n
    c.bank_free[6] = (s.cv, m["mod_done"])
    c.bank_free[7] = (s.cv, m["mod_done"])

    if c.debug:
        @bk.sync
        def _(e):
            e.wait_ge(s.cv.h, m["mod_done"])
            s.dst.inc(e.dma_start(out=c.dbg_mod, in_=c.MOD[:]), 16)


    c.astack = ExitStack()
    sba = lambda name, shape: c.astack.enter_context(nc.sbuf_tensor(name, shape, F32))
    c.S_W1T = sba("S_W1T", [128, 32 * 128])
    c.S_W2T = sba("S_W2T", [128, 4 * 128])
    c.B_WT = sba("B_WT", [128, 8 * 128])
    c.WLD = sba("WLD", [128, D])
    c.S1A = sba("S1A", [128, 4 * R])

    def wprep(name, w_ap, nrc, ncc, dst, scale_ga, dst_lo=None):
        for rc in range(nrc):
            @bk.sync
            def _(e, rc=rc):
                if name == "sw1" and rc == 0:
                    e.wait_ge(s.ct.h, m["modp_mm"])
                    e.wait_ge(s.dgp.h, s.dgp.n - 16)
                if c.m.get("wld_free") is not None:
                    e.wait_ge(s.ct.h, c.m["wld_free"])
                s.dw.inc(e.dma_start(out=c.WLD[:, 0:ncc * 128],
                                     in_=w_ap[rc * 128:(rc + 1) * 128, :]), 16)
                c.m[name + "_ld%d" % rc] = s.dw.n

            if scale_ga:
                @bk.vector
                def _(v, rc=rc):
                    v.wait_ge(s.dw.h, s.dw.n)
                    v.wait_ge(s.cv.h, m["mod_done"])
                    s.cv.inc(v.tensor_scalar(c.WLD[:, 0:ncc * 128], c.WLD[:, 0:ncc * 128],
                                             c.GA[:, rc:rc + 1], None, op0=ALU.mult))
                    c.m[name + "_sc%d" % rc] = s.cv.n

            for cc2 in range(ncc):
                @bk.tensor
                def _(t, rc=rc, cc2=cc2):
                    if cc2 == 0:
                        if scale_ga:
                            t.wait_ge(s.cv.h, c.m[name + "_sc%d" % rc])
                        else:
                            t.wait_ge(s.dw.h, s.dw.n)
                    b = cc2 % 4
                    bank_wait(t, b)
                    key = name + "_evk%d" % b
                    if c.m.get(key) is not None:
                        t.wait_ge(s.cv.h, c.m[key])
                    s.ct.inc(t.transpose(c.PS[b][:, 0:128],
                                         c.WLD[:, cc2 * 128:(cc2 + 1) * 128], c.ID[:]))
                    c.m[name + "_tp_%d_%d" % (rc, cc2)] = s.ct.n
                    if cc2 == ncc - 1:
                        c.m["wld_free"] = s.ct.n

                @bk.vector
                def _(v, rc=rc, cc2=cc2):
                    v.wait_ge(s.ct.h, c.m[name + "_tp_%d_%d" % (rc, cc2)])
                    col = (cc2 * nrc + rc) * 128
                    if dst_lo is None:
                        s.cv.inc(v.tensor_copy(dst[:, col:col + 128], c.PS[cc2 % 4][:, 0:128]))
                    else:
                        v.tensor_copy(dst[:, col:col + 128], c.PS[cc2 % 4][:, 0:128])
                        s.cv.inc(v.tensor_tensor(dst_lo[:, col:col + 128], c.PS[cc2 % 4][:, 0:128],
                                                 dst[:, col:col + 128], op=ALU.subtract))
                    c.m[name + "_evk%d" % (cc2 % 4)] = s.cv.n

        c.m[name + "_done"] = s.cv.n
        c.bank_free[0] = (s.cv, s.cv.n)
        c.bank_free[1] = (s.cv, s.cv.n)

    wprep("sw1", c.s_w1, 4, 8, c.S_W1T, False)
    wprep("sw2", c.s_w2, 1, 4, c.S_W2T, False)
    wprep("bw", c.B_w, 1, 8, c.B_WT, False)
    m["wprep_done"] = s.cv.n

    c.qstack = ExitStack()
    qst = lambda name, shape, dtp: c.qstack.enter_context(nc.sbuf_tensor(name, shape, dtp))
    c.WST = qst("WST", [128, NIC * 128], F32)
    c.WQF = qst("WQF", [128, NIC * 128], F32)
    c.WQB = qst("WQB", [128, NIC * 128], BF16)

    def wq_chunk(name, w_ap, ncc, rc, sw_tile, rs_tile, scratch, first):
        @bk.sync
        def _(e):
            if first:
                e.wait_ge(s.ct.h, m["mod_t1"])
                e.wait_ge(s.dgp.h, s.dgp.n)
            if c.m.get("wst_free") is not None:
                e.wait_ge(s.cv.h, c.m["wst_free"])
            s.dwl.inc(e.dma_start(out=c.WST[:, 0:ncc * 128],
                                  in_=w_ap[rc * 128:(rc + 1) * 128, :]), 16)

        @bk.vector
        def _(v):
            v.wait_ge(s.dwl.h, s.dwl.n)
            v.reduce_max(c.RMAX[:, 0:1], c.WST[:, 0:ncc * 128], AXX,
                         apply_absolute_value=True)
            v.tensor_scalar(sw_tile[:, rc:rc + 1], c.RMAX[:, 0:1], 1.0 / QMAX, None,
                            op0=ALU.mult)
            v.reciprocal(c.RMAX[:, 1:2], c.RMAX[:, 0:1])
            s.cv.inc(v.tensor_scalar(rs_tile[:, rc:rc + 1], c.RMAX[:, 1:2], QMAX, None,
                                     op0=ALU.mult))
            s.cv.inc(v.tensor_scalar(c.WQF[:, 0:ncc * 128], c.WST[:, 0:ncc * 128],
                                     rs_tile[:, rc:rc + 1], None, op0=ALU.mult))
            c.m["wst_free"] = s.cv.n
            v.tensor_scalar(c.WQF[:, 0:ncc * 128], c.WQF[:, 0:ncc * 128], MAGIC, None,
                            op0=ALU.add)
            v.wait_ge(s.dws.h, s.dws.n)
            s.cv.inc(v.tensor_scalar(c.WQB[:, 0:ncc * 128], c.WQF[:, 0:ncc * 128],
                                     -MAGIC, None, op0=ALU.add))
            c.m["wq_b"] = s.cv.n

        @bk.sync
        def _(e):
            e.wait_ge(s.cv.h, c.m["wq_b"])
            s.dws.inc(e.dma_start(out=scratch[rc * 128:(rc + 1) * 128, :],
                                  in_=c.WQB[:, 0:ncc * 128]), 16)

    c.wq_chunks = [("wdq", c.w_down, NIC, rc, c.SWD, c.RSD, c.wqd_s) for rc in range(KD)]
    c.wq_pos = 0

    def emit_wq(n):
        while c.wq_pos < len(c.wq_chunks) and n > 0:
            nm, w_ap, ncc, rc, swt, rst, scr = c.wq_chunks[c.wq_pos]
            wq_chunk(nm, w_ap, ncc, rc, swt, rst, scr, c.wq_pos == 0)
            c.wq_pos += 1
            n -= 1
    c.emit_wq = emit_wq

    x_sl = c.x.rearrange("(r e) d -> e r d", e=NS)

    def xbuf(j):
        return c.X_TM[:, (j % 2) * SUBT * D:(j % 2 + 1) * SUBT * D]
    c.xbuf = xbuf

    def a_xload(j):
        @bk.sync
        def _(e, j=j):
            if j >= 2:
                e.wait_ge(s.cs.h, c.m["A_aff%d" % (j - 1)])
            sp = s.dxe if j % 2 == 0 else s.dxo
            src2 = x_sl[j].rearrange("(t p) d -> p t d", p=128)
            dstv = xbuf(j).rearrange("p (t d) -> p t d", d=D)
            sp.inc(e.dma_start(out=dstv, in_=src2), 16)

    def a_vprep(j, sq_last):
        @bk.vector
        def _(v, j=j, sq_last=sq_last):
            sp = s.dxe if j % 2 == 0 else s.dxo
            v.wait_ge(sp.h, sp.n)
            xb = xbuf(j)
            for t in range(SUBT):
                ci = j * 4 + t
                s.cv.inc(v.reduce_sum(c.SUMX[:, ci:ci + 1], xb[:, t * D:(t + 1) * D], AXX))
            cl = slice(j * 4, j * 4 + 4)
            v.wait_ge(s.cs.h, sq_last)
            v.tensor_scalar(c.MU[:, cl], c.SUMX[:, cl], 1.0 / D, None, op0=ALU.mult)
            v.tensor_scalar(c.STSC[:, 0:4], c.SUMSQ[:, cl], 1.0 / D, None, op0=ALU.mult)
            v.tensor_tensor(c.STSC[:, 4:8], c.MU[:, cl], c.MU[:, cl], op=ALU.mult)
            v.tensor_tensor(c.STSC[:, 0:4], c.STSC[:, 0:4], c.STSC[:, 4:8], op=ALU.subtract)
            v.tensor_scalar(c.STSC[:, 0:4], c.STSC[:, 0:4], EPS, None, op0=ALU.add)
            s.cv.inc(v.reciprocal(c.STSC[:, 4:8], c.STSC[:, 0:4]))
            v.wait_ge(s.cs.h, sq_last + 1)
            s.cv.inc(v.scalar_tensor_tensor(c.NMI[:, cl], c.MU[:, cl], -1.0, c.INV[:, cl],
                                            op0=ALU.mult, op1=ALU.mult))
            c.m["A_nmi%d" % j] = s.cv.n

    def a_sprep(j, recip_ms, nmi_ms, tp_ms):
        @bk.scalar
        def _(e, j=j):
            sp = s.dxe if j % 2 == 0 else s.dxo
            e.wait_ge(sp.h, sp.n)
            xb = xbuf(j)
            for t in range(SUBT):
                ci = j * 4 + t
                s.cs.inc(e.activation(c.S1A[:, (t % 2) * D:(t % 2 + 1) * D],
                                      xb[:, t * D:(t + 1) * D], AF.Square,
                                      accum_out=c.SUMSQ[:, ci:ci + 1]))
            cl = slice(j * 4, j * 4 + 4)
            e.wait_ge(s.cv.h, recip_ms)
            s.cs.inc(e.activation(c.INV[:, cl], c.STSC[:, 4:8], AF.Sqrt))
            e.wait_ge(s.cv.h, nmi_ms)
            if tp_ms is not None:
                e.wait_ge(s.ct.h, tp_ms)
            else:
                e.wait_ge(s.cv.h, m["mod_done"])
            for t in range(SUBT):
                ci = j * 4 + t
                s.cs.inc(e.activation(c.XN[:, t * D:(t + 1) * D], xb[:, t * D:(t + 1) * D],
                                      AF.Identity, bias=c.NMI[:, ci:ci + 1],
                                      scale=c.INV[:, ci:ci + 1]))
            c.m["A_aff%d" % j] = s.cs.n

    a_xload(0)
    _sq0 = s.cs.n + 4
    _bv0 = s.cv.n
    a_vprep(0, _sq0)
    a_sprep(0, _bv0 + 5, _bv0 + 6, None)

    for j in range(NS):
        bsI = s.cs.n
        btI = s.ct.n
        bvI = s.cv.n
        has_prep = (j + 1) < NS
        silu_base = bsI + 8 + (9 if has_prep else 0)
        c.m["A_silu_pred%d" % j] = silu_base + 4

        @bk.tensor
        def _(t, j=j, bsI=bsI, bvI=bvI):
            if j == 0:
                t.wait_ge(s.cv.h, m["wprep_done"])
                for _b in range(8):
                    bank_wait(t, _b)
            t.wait_ge(s.cs.h, c.m["A_aff%d" % j])
            for k in range(KD):
                if k >= 2:
                    t.wait_ge(s.cs.h, bsI + k - 1)
                b = 6 + k % 2
                for tt_ in range(SUBT):
                    ins = t.transpose(c.PS[b][:, tt_ * 128:(tt_ + 1) * 128],
                                      c.XN[:, tt_ * D + k * 128:tt_ * D + (k + 1) * 128],
                                      c.ID[:])
                s.ct.inc(ins)
            t.wait_ge(s.cs.h, bsI + 8)
            for mm in range(4):
                for k in range(KD):
                    ins = t.matmul(c.PS[mm][:, 0:R],
                                   c.S_W1T[:, (k * 4 + mm) * 128:(k * 4 + mm + 1) * 128],
                                   c.H_FM[:, k * R:(k + 1) * R], start=(k == 0), stop=(k == KD - 1))
                s.ct.inc(ins)
            t.wait_ge(s.cs.h, c.m["A_silu_pred%d" % j])
            for k2 in range(4):
                ins = t.matmul(c.PS[4][:, 0:R], c.S_W2T[:, k2 * 128:(k2 + 1) * 128],
                               c.S1A[:, k2 * R:(k2 + 1) * R], start=(k2 == 0), stop=(k2 == 3))
            s.ct.inc(ins)
            if j > 0:
                t.wait_ge(s.cv.h, c.m["A_u%d" % (j - 1)])
            for k in range(KD):
                ins = t.matmul(c.PS[5][:, 0:R], c.B_WT[:, k * 128:(k + 1) * 128],
                               c.H_FM[:, k * R:(k + 1) * R], start=(k == 0), stop=(k == KD - 1))
            s.ct.inc(ins)

        if j + 1 < NS:
            a_xload(j + 1)

        @bk.scalar
        def _(e, j=j, btI=btI):
            if c.debug and j == 1:
                e.wait_ge(s.dst.h, s.dst.n)
            for k in range(KD):
                e.wait_ge(s.ct.h, btI + k + 1)
                s.cs.inc(e.activation(c.H_FM[:, k * R:(k + 1) * R], c.PS[6 + k % 2][:, 0:R],
                                      AF.Identity, bias=c.MOD[:, k:k + 1],
                                      scale=c.ONE_PCA[:, k:k + 1]))
            c.m["A_evac%d" % j] = s.cs.n

        if j + 1 < NS:
            _sq = s.cs.n + 4
            _bv = s.cv.n
            a_vprep(j + 1, _sq)
            a_sprep(j + 1, _bv + 5, _bv + 6, btI + 8)

        @bk.scalar
        def _(e, j=j, btI=btI):
            for mm in range(4):
                e.wait_ge(s.ct.h, btI + 9 + mm)
                s.cs.inc(e.activation(c.S1A[:, mm * R:(mm + 1) * R], c.PS[mm][:, 0:R], AF.Silu,
                                      bias=c.BIAS_S1[:, mm:mm + 1]))
            e.wait_ge(s.ct.h, btI + 13)
            if j > 0:
                e.wait_ge(s.cv.h, c.m["A_u%d" % (j - 1)])
            s.cs.inc(e.activation(c.SGT[:], c.PS[4][:, 0:R], AF.Sigmoid, bias=c.BIAS_S2[:]))
            c.m["A_sgt%d" % j] = s.cs.n

        @bk.vector
        def _(v, j=j, btI=btI):
            v.wait_ge(s.ct.h, btI + 14)
            v.wait_ge(s.cs.h, c.m["A_sgt%d" % j])
            s.cv.inc(v.scalar_tensor_tensor(c.U[:, j * R:(j + 1) * R], c.PS[5][:, 0:R],
                                            c.BIAS_B[:], c.SGT[:], op0=ALU.add, op1=ALU.mult))
            c.m["A_u%d" % j] = s.cv.n
        c.m["A_bt%d" % j] = s.ct.n
        c.m["A_xnlast%d" % j] = c.m["A_aff%d" % j]

        if c.debug and j == 0:
            @bk.sync
            def _(e, j=j):
                e.wait_ge(s.cs.h, c.m["A_silu_pred%d" % j])
                s.dst.inc(e.dma_start(out=c.dbg_h, in_=c.H_FM[:]), 16)

        c.emit_wq(1)

    for _b in range(6):
        c.bank_free[_b] = (s.cv, s.cv.n)
    c.bank_free[6] = (s.cs, s.cs.n)
    c.bank_free[7] = (s.cs, s.cs.n)
    m["A_done"] = s.cv.n
    c.emit_wq(999)
    m["wqd_all"] = s.dws.n
    c.qstack.close()
    c.astack.close()
    c.scanstack = ExitStack()
    sst = lambda name, shape: c.scanstack.enter_context(nc.sbuf_tensor(name, shape, F32))
    c.MIX2 = sst("MIX2", [128, 512])
    c.MIX3 = sst("MIX3", [128, 64])
    c.H4S = sst("H4S", [128, 8])
    c.H3S = sst("H3S", [128, 64])
    c.H2SS = sst("H2SS", [128, 512])


    @bk.tensor
    def _(t):
        t.wait_ge(s.cv.h, m["A_done"])
        for b in range(8):
            bank_wait(t, b)
        for tl in range(8):
            for jj in range(tl + 1):
                lhsT = c.ID if jj == tl else c.M_[tl - jj - 1]
                ins = t.matmul(c.PS[tl][:, 0:R], lhsT[:], c.U[:, jj * R:(jj + 1) * R],
                               start=(jj == 0), stop=(jj == tl))
        s.ct.inc(ins)
        m["sc_l1"] = s.ct.n

    @bk.vector
    def _(v):
        v.wait_ge(s.ct.h, m["sc_l1"])
        for tl in range(8):
            s.cv.inc(v.tensor_copy(c.HS[:, tl * R:(tl + 1) * R], c.PS[tl][:, 0:R]))
        m["sc_l1e"] = s.cv.n

    @bk.tensor
    def _(t):
        t.wait_ge(s.cv.h, m["sc_l1e"])
        T1 = c.HS[:, 7 * R:8 * R].rearrange("p (b e) -> p e b", e=8)
        for tl in range(8):
            for jj in range(tl + 1):
                lhsT = c.ID if jj == tl else c.N_[tl - jj - 1]
                ins = t.matmul(c.PS[tl][:, 0:64], lhsT[:], T1[:, jj, :],
                               start=(jj == 0), stop=(jj == tl))
        s.ct.inc(ins)
        m["sc_l2"] = s.ct.n

    @bk.vector
    def _(v):
        v.wait_ge(s.ct.h, m["sc_l2"])
        for tl in range(8):
            s.cv.inc(v.tensor_copy(c.MIX2[:, tl * 64:(tl + 1) * 64], c.PS[tl][:, 0:64]))
        m["sc_l2e"] = s.cv.n

    @bk.tensor
    def _(t):
        t.wait_ge(s.cv.h, m["sc_l2e"])
        T2 = c.MIX2[:, 7 * 64:8 * 64].rearrange("p (b e) -> p e b", e=8)
        for tl in range(8):
            for jj in range(tl + 1):
                lhsT = c.ID if jj == tl else c.R_[tl - jj - 1]
                ins = t.matmul(c.PS[tl][:, 0:8], lhsT[:], T2[:, jj, :],
                               start=(jj == 0), stop=(jj == tl))
        s.ct.inc(ins)
        m["sc_l3"] = s.ct.n

    @bk.vector
    def _(v):
        v.wait_ge(s.ct.h, m["sc_l3"])
        for tl in range(8):
            s.cv.inc(v.tensor_copy(c.MIX3[:, tl * 8:(tl + 1) * 8], c.PS[tl][:, 0:8]))
        s.cv.inc(v.memset(c.H4S[:, 0:1], 0.0))
        m["sc_l3e"] = s.cv.n

    @bk.tensor
    def _(t):
        t.wait_ge(s.cv.h, m["sc_l3e"])
        for cc2 in range(1, 8):
            for jj in range(cc2):
                p = cc2 - 1 - jj
                lhsT = c.ID if p == 0 else c.Z_[p - 1]
                ins = t.matmul(c.PS[cc2][:, 0:1], lhsT[:], c.MIX3[:, 56 + jj:57 + jj],
                               start=(jj == 0), stop=(jj == cc2 - 1))
        s.ct.inc(ins)
        m["sc_l4"] = s.ct.n

    @bk.vector
    def _(v):
        v.wait_ge(s.ct.h, m["sc_l4"])
        for cc2 in range(1, 8):
            s.cv.inc(v.tensor_copy(c.H4S[:, cc2:cc2 + 1], c.PS[cc2][:, 0:1]))
        m["sc_h4"] = s.cv.n

    @bk.tensor
    def _(t):
        t.wait_ge(s.cv.h, m["sc_h4"])
        for tl in range(8):
            lhsT = c.ID if tl == 0 else c.R_[tl - 1]
            ins = t.matmul(c.PS[tl][:, 0:8], lhsT[:], c.H4S[:], start=True, stop=True)
        s.ct.inc(ins)
        m["sc_e3t"] = s.ct.n

    @bk.vector
    def _(v):
        v.wait_ge(s.ct.h, m["sc_e3t"])
        for tl in range(8):
            if tl == 0:
                s.cv.inc(v.tensor_copy(c.H3S[:, 0:8], c.PS[0][:, 0:8]))
            else:
                s.cv.inc(v.tensor_tensor(c.H3S[:, tl * 8:(tl + 1) * 8], c.PS[tl][:, 0:8],
                                         c.MIX3[:, (tl - 1) * 8:tl * 8], op=ALU.add))
        m["sc_e3"] = s.cv.n

    @bk.tensor
    def _(t):
        t.wait_ge(s.cv.h, m["sc_e3"])
        e3nat = c.H3S[:, 0:64].rearrange("p (t b) -> p b t", t=8)
        for tl in range(8):
            lhsT = c.ID if tl == 0 else c.N_[tl - 1]
            ins = t.matmul(c.PS[tl][:, 0:64], lhsT[:], e3nat, start=True, stop=True)
        s.ct.inc(ins)
        m["sc_e2t"] = s.ct.n

    @bk.vector
    def _(v):
        v.wait_ge(s.ct.h, m["sc_e2t"])
        for tl in range(8):
            if tl == 0:
                s.cv.inc(v.tensor_copy(c.H2SS[:, 0:64], c.PS[0][:, 0:64]))
            else:
                s.cv.inc(v.tensor_tensor(c.H2SS[:, tl * 64:(tl + 1) * 64], c.PS[tl][:, 0:64],
                                         c.MIX2[:, (tl - 1) * 64:tl * 64], op=ALU.add))
        m["sc_e2"] = s.cv.n

    @bk.tensor
    def _(t):
        t.wait_ge(s.cv.h, m["sc_e2"])
        e2nat = c.H2SS[:, 0:512].rearrange("p (t b) -> p b t", t=8)
        for tl in range(8):
            ins = t.matmul(c.PS[tl][:, 0:R], c.M_[tl][:], e2nat, start=True, stop=True)
        s.ct.inc(ins)
        m["sc_corrt"] = s.ct.n

    @bk.vector
    def _(v):
        v.wait_ge(s.ct.h, m["sc_corrt"])
        for tl in range(8):
            s.cv.inc(v.tensor_tensor(c.HS[:, tl * R:(tl + 1) * R],
                                     c.HS[:, tl * R:(tl + 1) * R], c.PS[tl][:, 0:R], op=ALU.add))
        m["sc_done"] = s.cv.n
    for k in range(8):
        c.bank_free[k] = (s.cv, m["sc_done"])

    if c.debug:
        @bk.sync
        def _(e):
            e.wait_ge(s.cv.h, m["A_done"])
            s.dst.inc(e.dma_start(out=c.dbg_u, in_=c.U[:]), 16)

        @bk.sync
        def _(e):
            e.wait_ge(s.cv.h, m["sc_done"])
            s.dst.inc(e.dma_start(out=c.dbg_hs, in_=c.HS[:]), 16)

    if upto == "A":
        _finish(c, bk)
        return

    c.scanstack.close()
    c.powstack.close()
    c.cdgstack = ExitStack()
    cst = lambda name, shape, dtp: c.cdgstack.enter_context(nc.sbuf_tensor(name, shape, dtp))
    c.C_WT = cst("C_WT", [128, 8 * 128], F32)
    c.D_HI = cst("D_HI", [128, 64 * 128], BF16)
    c.D_LO = cst("D_LO", [128, 64 * 128], BF16)
    c.G_HI = cst("G_HI", [128, 64 * 128], BF16)
    c.G_LO = cst("G_LO", [128, 64 * 128], BF16)
    c.WLD = cst("WLDC", [128, D], F32)
    c.WSTC = cst("WSTC", [128, D], F32)
    c.WQFC = cst("WQFC", [128, D], F32)
    c.WQBC = cst("WQBC", [128, 2 * D], BF16)
    c.TBNC = cst("TBNC", [128, KD * 128], BF16)

    @bk.sync
    def _(e):
        e.wait_ge(s.ct.h, m["sc_corrt"])
        e.wait_ge(s.cv.h, m["sc_done"])
        e.wait_ge(s.dws.h, s.dws.n)
        e.wait_ge(s.dwl.h, s.dwl.n)

    @bk.vector
    def _(v):
        v.wait_ge(s.dws.h, s.dws.n)
        v.wait_ge(s.dwl.h, s.dwl.n)
        v.wait_ge(s.cs.h, s.cs.n)
    c.m["wld_free"] = m["sc_corrt"]

    wprep("cw", c.C_w, 8, 1, c.C_WT, True)
    wprep("dw", c.D_w, 8, 8, c.D_HI, True, dst_lo=c.D_LO)
    wprep("gw", c.g_w, 8, 8, c.G_HI, False, dst_lo=c.G_LO)
    m["wprepC_done"] = s.cv.n

    H2FM = c.U
    c.bankC = list(c.bank_free)
    c.bank_free = [None] * 8

    def cbw(t, b):
        e_ = c.bankC[b]
        if e_ is not None:
            t.wait_ge(e_[0].h, e_[1])
            c.bankC[b] = None

    c.wq2_chunks = ([(c.w_gate, rc, c.SWG, c.RSG, c.wqg_s) for rc in range(NIC)]
                    + [(c.w_up, rc, c.SWU, c.RSU, c.wqu_s) for rc in range(NIC)])
    c.wq2_pos = 0

    def wq2_batch(qs):
        for bi, q in enumerate(qs):
            @bk.sync
            def _(e, q=q):
                w_ap, rc, sw, rs, scr = c.wq2_chunks[q]
                if q >= 1:
                    e.wait_ge(s.cv.h, c.m["wq2_round%d" % (q - 1)])
                e.wait_ge(s.dwl.h, s.dwl.n)
                s.dwl.inc(e.dma_start(out=c.WSTC[:],
                                      in_=w_ap[rc * 128:(rc + 1) * 128, :]), 16)

            @bk.vector
            def _(v, q=q):
                w_ap, rc, sw, rs, scr = c.wq2_chunks[q]
                v.wait_ge(s.dwl.h, s.dwl.n)
                v.reduce_max(c.RMAX[:, 0:1], c.WSTC[:], AXX, apply_absolute_value=True)
                v.tensor_scalar(sw[:, rc:rc + 1], c.RMAX[:, 0:1], 1.0 / QMAX, None,
                                op0=ALU.mult)
                v.reciprocal(c.RMAX[:, 1:2], c.RMAX[:, 0:1])
                s.cv.inc(v.tensor_scalar(rs[:, rc:rc + 1], c.RMAX[:, 1:2], QMAX, None,
                                         op0=ALU.mult))
                s.cv.inc(v.tensor_scalar(c.WQFC[:], c.WSTC[:], rs[:, rc:rc + 1], None,
                                         op0=ALU.mult))
                c.m["wq2_round%d" % q] = s.cv.n
                v.tensor_scalar(c.WQFC[:], c.WQFC[:], MAGIC, None, op0=ALU.add)
                if q >= 2:
                    v.wait_ge(s.dws.h, c.m["wq2_st%d" % (q - 2)])
                s.cv.inc(v.tensor_scalar(c.WQBC[:, (q % 2) * D:(q % 2 + 1) * D],
                                         c.WQFC[:], -MAGIC, None, op0=ALU.add))
                c.m["wq2_ub%d" % q] = s.cv.n

        @bk.sync
        def _(e, qs=qs):
            e.wait_ge(s.dws.h, s.dws.n)
            for q in qs:
                w_ap, rc, sw, rs, scr = c.wq2_chunks[q]
                e.wait_ge(s.cv.h, c.m["wq2_ub%d" % q])
                s.dws.inc(e.dma_start(out=scr[rc * 128:(rc + 1) * 128, :],
                                      in_=c.WQBC[:, (q % 2) * D:(q % 2 + 1) * D]), 16)
            for q in qs:
                c.m["wq2_st%d" % q] = s.dws.n

    def emit_wq2(n):
        while c.wq2_pos < len(c.wq2_chunks) and n > 0:
            take = min(2, n, len(c.wq2_chunks) - c.wq2_pos)
            wq2_batch(list(range(c.wq2_pos, c.wq2_pos + take)))
            c.wq2_pos += take
            n -= take

    c.bnc_pos = 0

    def emit_bounce(n):
        while c.bnc_pos < NIC and n > 0:
            ic = c.bnc_pos

            @bk.sync
            def _(e, ic=ic):
                e.wait_ge(s.dws.h, m["wqd_all"])
                if ic >= 1:
                    e.wait_ge(s.dws.h, c.m["bnc_st%d" % (ic - 1)])
                e.wait_ge(s.dw3.h, s.dw3.n)
                s.dw3.inc(e.dma_start(out=c.TBNC[:],
                                      in_=c.wqd_s[:, ic * 128:(ic + 1) * 128],
                                      transpose=True), 16)
                e.wait_ge(s.dw3.h, s.dw3.n)
                e.wait_ge(s.dws.h, s.dws.n)
                s.dws.inc(e.dma_start(out=c.wqd_t[ic * 128:(ic + 1) * 128, :],
                                      in_=c.TBNC[:]), 16)
                c.m["bnc_st%d" % ic] = s.dws.n
            c.bnc_pos += 1
            n -= 1

    @bk.sync
    def _(e):
        e.wait_ge(s.cs.h, c.m["A_xnlast%d" % (NS - 1)])
        src2 = x_sl[0].rearrange("(t p) d -> p t d", p=128)
        dstv = c.xbuf(0).rearrange("p (t d) -> p t d", d=D)
        s.dxe.inc(e.dma_start(out=dstv, in_=src2), 16)

    c.mmrot = 0
    for j in range(NS):
        TAIL = j >= 1
        bsI = s.cs.n
        btI = s.ct.n
        bvI = s.cv.n
        offA = bsI + 4 + (8 if TAIL else 0)
        offB = offA + 8
        sq_last_p = offB + 4
        sqrt_p = offB + 5
        aff2_last_p = offB + 9
        yev_base = offB + (9 if TAIL else 0)
        h2ev_base = yev_base + 8
        t1tp_base = btI + 8
        y_base = t1tp_base + (8 if TAIL else 0)
        h2tp_base = y_base + 8
        hsplit_p = bvI + (8 if TAIL else 0) + 1
        x2add_base = hsplit_p

        @bk.scalar
        def _(e, j=j, TAIL=TAIL):
            sp = s.dxe if j % 2 == 0 else s.dxo
            e.wait_ge(sp.h, sp.n)
            if j == 0:
                e.wait_ge(s.cv.h, m["sc_done"])
                e.wait_ge(s.ct.h, m["sc_corrt"])
            elif j == 1:
                e.wait_ge(s.ct.h, c.m["C_tpL0"])
            else:
                e.wait_ge(s.ct.h, c.m["C_h2tpL%d" % (j - 2)])
            for t in range(SUBT):
                ci = j * 4 + t
                s.cs.inc(e.activation(c.XN[:, t * D:(t + 1) * D],
                                      c.xbuf(j)[:, t * D:(t + 1) * D], AF.Identity,
                                      bias=c.NMI[:, ci:ci + 1], scale=c.INV[:, ci:ci + 1]))
            c.m["C_aff1_%d" % j] = s.cs.n
            if TAIL:
                if j == 1:
                    if c.debug:
                        e.wait_ge(s.dst.h, s.dst.n)
                else:
                    e.wait_ge(s.cv.h, c.m["C_h2maxL%d" % (j - 2)])
                    e.wait_ge(s.dsc.h, c.m["C_h2st%d" % (j - 2)])
                for mc, b_, ctv in c.gch:
                    e.wait_ge(s.ct.h, ctv)
                    s.cs.inc(e.activation(H2FM[:, mc * R:(mc + 1) * R], c.PS[b_][:, 0:R],
                                          AF.Sigmoid, bias=c.BIAS_G[:, mc:mc + 1]))
                c.m["C_gevL%d" % (j - 1)] = s.cs.n
        assert s.cs.n == offA

        if TAIL:
            @bk.vector
            def _(v, j=j, bsI=bsI):
                for k in range(KD):
                    v.wait_ge(s.cs.h, bsI + 4 + k + 1)
                    gt = H2FM[:, k * R:(k + 1) * R]
                    s.cv.inc(v.tensor_tensor(gt, gt, c.H_FM[:, k * R:(k + 1) * R],
                                             op=ALU.mult))
                c.m["C_t1L%d" % (j - 1)] = s.cv.n

        @bk.tensor
        def _(t, j=j, TAIL=TAIL, bvI=bvI, offA=offA, x2add_base=x2add_base):
            t.wait_ge(s.cs.h, c.m["C_aff1_%d" % j])
            for k in range(KD):
                b = 6 + k % 2
                if k < 2:
                    if j == 0:
                        cbw(t, b)
                    else:
                        t.wait_ge(s.cs.h, c.m["C_gevL%d" % (j - 1)])
                else:
                    t.wait_ge(s.cs.h, offA + (k - 2) + 1)
                for tt_ in range(SUBT):
                    ins = t.transpose(c.PS[b][:, tt_ * 128:(tt_ + 1) * 128],
                                      c.XN[:, tt_ * D + k * 128:tt_ * D + (k + 1) * 128],
                                      c.ID[:])
                s.ct.inc(ins)
            c.m["C_tpL%d" % j] = s.ct.n
            if TAIL:
                for k in range(KD):
                    b = 6 + k % 2
                    t.wait_ge(s.cv.h, bvI + k + 1)
                    if k < 2:
                        t.wait_ge(s.cs.h, offA + (6 + k) + 1)
                    else:
                        t.wait_ge(s.cv.h, x2add_base + (k - 2) + 1)
                    for tt_ in range(SUBT):
                        ins = t.transpose(c.PS[b][:, tt_ * 128:(tt_ + 1) * 128],
                                          H2FM[:, k * R + tt_ * 128:k * R + (tt_ + 1) * 128],
                                          c.ID[:])
                    s.ct.inc(ins)
                c.m["C_t1tpL%d" % (j - 1)] = s.ct.n
        assert s.ct.n == t1tp_base + (8 if TAIL else 0)

        @bk.vector
        def _(v, j=j, TAIL=TAIL, offA=offA, sq_last_p=sq_last_p, sqrt_p=sqrt_p,
              t1tp_base=t1tp_base):
            v.wait_ge(s.cs.h, offA + 8)
            if TAIL:
                v.wait_ge(s.ct.h, c.m["C_gL%d" % (j - 1)])
            v.tensor_copy(c.H_HI[:], c.H_FM[:])
            s.cv.inc(v.tensor_tensor(c.H_LO[:], c.H_FM[:], c.H_HI[:], op=ALU.subtract))
            c.m["C_hsplit%d" % j] = s.cv.n
            if TAIL:
                for k in range(KD):
                    v.wait_ge(s.ct.h, t1tp_base + k + 1)
                    src = c.PS[6 + k % 2][:, 0:SUBT * 128].rearrange("p (t q) -> p t q", q=128)
                    dst = c.xbuf(j - 1).rearrange("p (t d) -> p t d", d=D)[:, :, k * 128:(k + 1) * 128]
                    s.cv.inc(v.tensor_tensor(dst, dst, src, op=ALU.add))
                for t in range(SUBT):
                    ci = (j - 1) * 4 + t
                    s.cv.inc(v.reduce_sum(c.SUMX[:, ci:ci + 1],
                                          c.xbuf(j - 1)[:, t * D:(t + 1) * D], AXX))
                cl = slice((j - 1) * 4, (j - 1) * 4 + 4)
                v.wait_ge(s.cs.h, sq_last_p)
                v.tensor_scalar(c.MU2[:, cl], c.SUMX[:, cl], 1.0 / D, None, op0=ALU.mult)
                v.tensor_scalar(c.STSC[:, 0:4], c.SUMSQ[:, cl], 1.0 / D, None, op0=ALU.mult)
                v.tensor_tensor(c.STSC[:, 4:8], c.MU2[:, cl], c.MU2[:, cl], op=ALU.mult)
                v.tensor_tensor(c.STSC[:, 0:4], c.STSC[:, 0:4], c.STSC[:, 4:8], op=ALU.subtract)
                v.tensor_scalar(c.STSC[:, 0:4], c.STSC[:, 0:4], EPS, None, op0=ALU.add)
                s.cv.inc(v.reciprocal(c.STSC[:, 4:8], c.STSC[:, 0:4]))
                v.wait_ge(s.cs.h, sqrt_p)
                s.cv.inc(v.scalar_tensor_tensor(c.NMI2[:, cl], c.MU2[:, cl], -1.0,
                                                c.INV2[:, cl], op0=ALU.mult, op1=ALU.mult))
                c.m["C_nmiL%d" % (j - 1)] = s.cv.n
        assert s.cv.n == x2add_base + (14 if TAIL else 0)

        @bk.tensor
        def _(t, j=j, TAIL=TAIL, yev_base=yev_base, h2ev_base=h2ev_base, offA=offA,
              aff2_last_p=aff2_last_p, x2add_base=x2add_base):
            t.wait_ge(s.cv.h, c.m["C_hsplit%d" % j])
            if j == 0:
                t.wait_ge(s.cv.h, m["wprepC_done"])
            c.ych = []
            for i in range(8):
                b_ = (c.mmrot + i) % 6
                if i < 6:
                    if TAIL:
                        t.wait_ge(s.cs.h, c.gevpos[i])
                    else:
                        cbw(t, b_)
                else:
                    t.wait_ge(s.cs.h, yev_base + (i - 6) + 1)
                t.matmul(c.PS[b_][:, 0:R], c.C_WT[:, i * 128:(i + 1) * 128],
                         c.HS[:, j * R:(j + 1) * R], start=True, stop=False)
                for k in range(KD):
                    col = (k * 8 + i) * 128
                    t.matmul(c.PS[b_][:, 0:R], c.D_HI[:, col:col + 128],
                             c.H_HI[:, k * R:(k + 1) * R], start=False, stop=False)
                    t.matmul(c.PS[b_][:, 0:R], c.D_LO[:, col:col + 128],
                             c.H_HI[:, k * R:(k + 1) * R], start=False, stop=False)
                    ins = t.matmul(c.PS[b_][:, 0:R], c.D_HI[:, col:col + 128],
                                   c.H_LO[:, k * R:(k + 1) * R], start=False, stop=(k == KD - 1))
                s.ct.inc(ins)
                c.ych.append((i, b_, s.ct.n))
            if TAIL:
                t.wait_ge(s.cs.h, aff2_last_p)
                for k in range(KD):
                    b = 6 + k % 2
                    if k < 2:
                        t.wait_ge(s.cv.h, x2add_base + (6 + k) + 1)
                    else:
                        t.wait_ge(s.cs.h, h2ev_base + (k - 2) + 1)
                    for tt_ in range(SUBT):
                        ins = t.transpose(c.PS[b][:, tt_ * 128:(tt_ + 1) * 128],
                                          c.XN[:, tt_ * D + k * 128:tt_ * D + (k + 1) * 128],
                                          c.ID[:])
                    s.ct.inc(ins)
                c.m["C_h2tpL%d" % (j - 1)] = s.ct.n
            c.gch = []
            for i in range(8):
                if i < 6:
                    b_ = (c.mmrot + 8 + i) % 6
                    t.wait_ge(s.cs.h, yev_base + (i + 2) + 1)
                else:
                    b_ = i
                    if TAIL:
                        t.wait_ge(s.cs.h, h2ev_base + i + 1)
                    else:
                        t.wait_ge(s.cs.h, offA + (i + 1))
                for k in range(KD):
                    col = (k * 8 + i) * 128
                    t.matmul(c.PS[b_][:, 0:R], c.G_HI[:, col:col + 128],
                             c.H_HI[:, k * R:(k + 1) * R], start=(k == 0), stop=False)
                    t.matmul(c.PS[b_][:, 0:R], c.G_LO[:, col:col + 128],
                             c.H_HI[:, k * R:(k + 1) * R], start=False, stop=False)
                    ins = t.matmul(c.PS[b_][:, 0:R], c.G_HI[:, col:col + 128],
                                   c.H_LO[:, k * R:(k + 1) * R], start=False, stop=(k == KD - 1))
                s.ct.inc(ins)
                c.gch.append((i, b_, s.ct.n))
            c.mmrot += 14
            c.m["C_gL%d" % j] = s.ct.n

        @bk.scalar
        def _(e, j=j, TAIL=TAIL, btI=btI, x2add_base=x2add_base):
            if TAIL:
                e.wait_ge(s.cv.h, c.m["C_t1L%d" % (j - 1)])
            for k in range(KD):
                e.wait_ge(s.ct.h, btI + k + 1)
                s.cs.inc(e.activation(c.H_FM[:, k * R:(k + 1) * R], c.PS[6 + k % 2][:, 0:R],
                                      AF.Identity, bias=c.MOD[:, k:k + 1],
                                      scale=c.ONE_PCA[:, k:k + 1]))
            c.m["C_hevL%d" % j] = s.cs.n
            if TAIL:
                e.wait_ge(s.cv.h, x2add_base + 8)
                e.wait_ge(s.dsc.h, s.dsc.n)
                s.dsc.inc(e.dma_start(out=c.x2_s[j - 1], in_=c.xbuf(j - 1)), 16)
                c.m["C_x2st%d" % (j - 1)] = s.dsc.n
                for t in range(SUBT):
                    ci = (j - 1) * 4 + t
                    s.cs.inc(e.activation(c.H_FM[:, (t % 2) * D:(t % 2 + 1) * D],
                                          c.xbuf(j - 1)[:, t * D:(t + 1) * D], AF.Square,
                                          accum_out=c.SUMSQ[:, ci:ci + 1]))
                cl = slice((j - 1) * 4, (j - 1) * 4 + 4)
                e.wait_ge(s.cv.h, x2add_base + 13)
                s.cs.inc(e.activation(c.INV2[:, cl], c.STSC[:, 4:8], AF.Sqrt))
                e.wait_ge(s.cv.h, x2add_base + 14)
                e.wait_ge(s.ct.h, c.m["C_tpL%d" % j])
                for t in range(SUBT):
                    ci = (j - 1) * 4 + t
                    s.cs.inc(e.activation(c.XN[:, t * D:(t + 1) * D],
                                          c.xbuf(j - 1)[:, t * D:(t + 1) * D], AF.Identity,
                                          bias=c.NMI2[:, ci:ci + 1], scale=c.INV2[:, ci:ci + 1]))
                c.m["C_aff2L%d" % (j - 1)] = s.cs.n
            e.wait_ge(s.cv.h, c.m["C_hsplit%d" % j])
            for mc, b_, ctv in c.ych:
                e.wait_ge(s.ct.h, ctv)
                s.cs.inc(e.activation(c.H_FM[:, mc * R:(mc + 1) * R], c.PS[b_][:, 0:R],
                                      AF.Identity, bias=c.BCD[:, mc:mc + 1]))
            c.m["C_yevL%d" % j] = s.cs.n
            if TAIL:
                for k in range(KD):
                    e.wait_ge(s.ct.h, c.m["C_h2tpL%d" % (j - 1)] - 8 + k + 1)
                    s.cs.inc(e.activation(H2FM[:, k * R:(k + 1) * R], c.PS[6 + k % 2][:, 0:R],
                                          AF.Identity, bias=c.MOD[:, 24 + k:24 + k + 1],
                                          scale=c.ONE_PCB[:, k:k + 1]))
                c.m["C_h2evL%d" % (j - 1)] = s.cs.n
                e.wait_ge(s.dsc.h, s.dsc.n)
                s.dsc.inc(e.dma_start(out=c.h2_s[j - 1], in_=H2FM[:]), 16)
                c.m["C_h2st%d" % (j - 1)] = s.dsc.n
        assert s.cs.n == h2ev_base + (8 if TAIL else 0), (j, s.cs.n, h2ev_base)
        assert s.ct.n == h2tp_base + (8 if TAIL else 0) + 8
        c.gevpos = [s.cs.n + 4 + i + 1 for i in range(8)]

        if TAIL:
            @bk.vector
            def _(v, j=j, h2ev_base=h2ev_base):
                for k in range(KD):
                    v.wait_ge(s.cs.h, h2ev_base + k + 1)
                    s.cv.inc(v.reduce_max(c.H2MAX[:, (j - 1) * 8 + k:(j - 1) * 8 + k + 1],
                                          H2FM[:, k * R:(k + 1) * R], AXX,
                                          apply_absolute_value=True))
                c.m["C_h2maxL%d" % (j - 1)] = s.cv.n

        if j + 1 < NS:
            @bk.sync
            def _(e, j=j, TAIL=TAIL):
                if TAIL:
                    e.wait_ge(s.cs.h, c.m["C_aff2L%d" % (j - 1)])
                    e.wait_ge(s.cv.h, c.m["C_nmiL%d" % (j - 1)])
                    e.wait_ge(s.dsc.h, c.m["C_x2st%d" % (j - 1)])
                else:
                    e.wait_ge(s.cs.h, c.m["A_xnlast%d" % (NS - 1)])
                sp = s.dxe if (j + 1) % 2 == 0 else s.dxo
                src2 = x_sl[j + 1].rearrange("(t p) d -> p t d", p=128)
                dstv = c.xbuf(j + 1).rearrange("p (t d) -> p t d", d=D)
                sp.inc(e.dma_start(out=dstv, in_=src2), 16)
        emit_wq2(6)
        emit_bounce(3)

    bsE = s.cs.n
    btE = s.ct.n
    bvE = s.cv.n

    @bk.scalar
    def _(e):
        e.wait_ge(s.cv.h, c.m["C_h2maxL%d" % (NS - 2)])
        e.wait_ge(s.dsc.h, c.m["C_h2st%d" % (NS - 2)])
        for mc, b_, ctv in c.gch:
            e.wait_ge(s.ct.h, ctv)
            s.cs.inc(e.activation(H2FM[:, mc * R:(mc + 1) * R], c.PS[b_][:, 0:R],
                                  AF.Sigmoid, bias=c.BIAS_G[:, mc:mc + 1]))
        c.m["C_gevL%d" % (NS - 1)] = s.cs.n

    @bk.vector
    def _(v):
        for k in range(KD):
            v.wait_ge(s.cs.h, bsE + k + 1)
            gt = H2FM[:, k * R:(k + 1) * R]
            s.cv.inc(v.tensor_tensor(gt, gt, c.H_FM[:, k * R:(k + 1) * R], op=ALU.mult))
        c.m["C_t1L%d" % (NS - 1)] = s.cv.n

    @bk.tensor
    def _(t):
        for k in range(KD):
            b = 6 + k % 2
            t.wait_ge(s.cv.h, bvE + k + 1)
            if k < 2:
                t.wait_ge(s.cs.h, c.m["C_gevL%d" % (NS - 1)])
            else:
                t.wait_ge(s.cv.h, bvE + 8 + (k - 2) + 1)
            for tt_ in range(SUBT):
                ins = t.transpose(c.PS[b][:, tt_ * 128:(tt_ + 1) * 128],
                                  H2FM[:, k * R + tt_ * 128:k * R + (tt_ + 1) * 128],
                                  c.ID[:])
            s.ct.inc(ins)

    @bk.vector
    def _(v):
        for k in range(KD):
            v.wait_ge(s.ct.h, btE + k + 1)
            src = c.PS[6 + k % 2][:, 0:SUBT * 128].rearrange("p (t q) -> p t q", q=128)
            dst = c.xbuf(NS - 1).rearrange("p (t d) -> p t d", d=D)[:, :, k * 128:(k + 1) * 128]
            s.cv.inc(v.tensor_tensor(dst, dst, src, op=ALU.add))
        for t in range(SUBT):
            ci = (NS - 1) * 4 + t
            s.cv.inc(v.reduce_sum(c.SUMX[:, ci:ci + 1],
                                  c.xbuf(NS - 1)[:, t * D:(t + 1) * D], AXX))
        cl = slice((NS - 1) * 4, (NS - 1) * 4 + 4)
        v.wait_ge(s.cs.h, bsE + 8 + 4)
        v.tensor_scalar(c.MU2[:, cl], c.SUMX[:, cl], 1.0 / D, None, op0=ALU.mult)
        v.tensor_scalar(c.STSC[:, 0:4], c.SUMSQ[:, cl], 1.0 / D, None, op0=ALU.mult)
        v.tensor_tensor(c.STSC[:, 4:8], c.MU2[:, cl], c.MU2[:, cl], op=ALU.mult)
        v.tensor_tensor(c.STSC[:, 0:4], c.STSC[:, 0:4], c.STSC[:, 4:8], op=ALU.subtract)
        v.tensor_scalar(c.STSC[:, 0:4], c.STSC[:, 0:4], EPS, None, op0=ALU.add)
        s.cv.inc(v.reciprocal(c.STSC[:, 4:8], c.STSC[:, 0:4]))
        v.wait_ge(s.cs.h, bsE + 8 + 5)
        s.cv.inc(v.scalar_tensor_tensor(c.NMI2[:, cl], c.MU2[:, cl], -1.0, c.INV2[:, cl],
                                        op0=ALU.mult, op1=ALU.mult))

    @bk.scalar
    def _(e):
        e.wait_ge(s.cv.h, bvE + 16)
        e.wait_ge(s.dsc.h, s.dsc.n)
        s.dsc.inc(e.dma_start(out=c.x2_s[NS - 1], in_=c.xbuf(NS - 1)), 16)
        c.m["C_x2st%d" % (NS - 1)] = s.dsc.n
        for t in range(SUBT):
            ci = (NS - 1) * 4 + t
            s.cs.inc(e.activation(c.H_FM[:, (t % 2) * D:(t % 2 + 1) * D],
                                  c.xbuf(NS - 1)[:, t * D:(t + 1) * D], AF.Square,
                                  accum_out=c.SUMSQ[:, ci:ci + 1]))
        cl = slice((NS - 1) * 4, (NS - 1) * 4 + 4)
        e.wait_ge(s.cv.h, bvE + 21)
        s.cs.inc(e.activation(c.INV2[:, cl], c.STSC[:, 4:8], AF.Sqrt))
        e.wait_ge(s.cv.h, bvE + 22)
        for t in range(SUBT):
            ci = (NS - 1) * 4 + t
            s.cs.inc(e.activation(c.XN[:, t * D:(t + 1) * D],
                                  c.xbuf(NS - 1)[:, t * D:(t + 1) * D], AF.Identity,
                                  bias=c.NMI2[:, ci:ci + 1], scale=c.INV2[:, ci:ci + 1]))
        c.m["C_aff2L%d" % (NS - 1)] = s.cs.n
    assert s.cs.n == bsE + 17

    @bk.tensor
    def _(t):
        t.wait_ge(s.cs.h, c.m["C_aff2L%d" % (NS - 1)])
        for k in range(KD):
            b = 6 + k % 2
            if k < 2:
                t.wait_ge(s.cv.h, bvE + 8 + (6 + k) + 1)
            else:
                t.wait_ge(s.cs.h, bsE + 17 + (k - 2) + 1)
            for tt_ in range(SUBT):
                ins = t.transpose(c.PS[b][:, tt_ * 128:(tt_ + 1) * 128],
                                  c.XN[:, tt_ * D + k * 128:tt_ * D + (k + 1) * 128],
                                  c.ID[:])
            s.ct.inc(ins)
        c.m["C_h2tpL%d" % (NS - 1)] = s.ct.n

    @bk.scalar
    def _(e):
        for k in range(KD):
            e.wait_ge(s.ct.h, btE + 8 + k + 1)
            s.cs.inc(e.activation(H2FM[:, k * R:(k + 1) * R], c.PS[6 + k % 2][:, 0:R],
                                  AF.Identity, bias=c.MOD[:, 24 + k:24 + k + 1],
                                  scale=c.ONE_PCB[:, k:k + 1]))
        c.m["C_h2evL%d" % (NS - 1)] = s.cs.n
        e.wait_ge(s.dsc.h, s.dsc.n)
        s.dsc.inc(e.dma_start(out=c.h2_s[NS - 1], in_=H2FM[:]), 16)
        c.m["C_h2st%d" % (NS - 1)] = s.dsc.n

    @bk.vector
    def _(v):
        for k in range(KD):
            v.wait_ge(s.cs.h, bsE + 17 + k + 1)
            s.cv.inc(v.reduce_max(c.H2MAX[:, (NS - 1) * 8 + k:(NS - 1) * 8 + k + 1],
                                  H2FM[:, k * R:(k + 1) * R], AXX,
                                  apply_absolute_value=True))
        c.m["C_h2maxL%d" % (NS - 1)] = s.cv.n

    c.bank_free = [(s.cs, s.cs.n) for _ in range(8)]
    m["C_done_v"] = s.cv.n
    m["C_done_t"] = s.ct.n
    m["C_done_s"] = s.cs.n
    m["C_done_st"] = s.dsc.n

    if upto == "C":
        _finish(c, bk)
        return

    @bk.vector
    def _(v):
        v.wait_ge(s.cv.h, m["C_done_v"])
        s.cv.inc(v.reduce_max(c.REDT[:, 0:1], c.H2MAX[:], AXX))
        m["sx_red"] = s.cv.n

    @bk.tensor
    def _(t):
        t.wait_ge(s.cv.h, m["sx_red"])
        bank_wait(t, 0)
        s.ct.inc(t.transpose(c.PS[0][0:1, 0:128], c.REDT[:, 0:1], c.ID[:]))
        m["sx_tp"] = s.ct.n

    @bk.vector
    def _(v):
        v.wait_ge(s.ct.h, m["sx_tp"])
        v.memset(c.LMX[:], 0.0)
        s.cv.inc(v.reduce_max(c.LMX[0:1, 0:1], c.PS[0][0:1, 0:128], AXX))
        m["sx_loc"] = s.cv.n
    c.bank_free[0] = (s.cv, m["sx_loc"])

    @bk.gpsimd
    def _(g):
        g.wait_ge(s.cv.h, m["sx_loc"])
        s.dgp.inc(g.dma_start(out=c.cc_m1_in, in_=c.LMX[:]), 16)
        g.wait_ge(s.dgp.h, s.dgp.n)
        s.cc.inc(g.collective_compute("AllReduce", ALU.max,
                                      replica_groups=[list(range(NCORE))],
                                      ins=[c.cc_m1_in], outs=[c.cc_m1_out]))
        g.wait_ge(s.cc.h, s.cc.n)
        s.dgp.inc(g.dma_start(out=c.GMX[:], in_=c.cc_m1_out), 16)
        m["sx_gmax"] = s.dgp.n

    @bk.scalar
    def _(e):
        e.wait_ge(s.dx.h, s.dx.n)
        e.activation(c.CLIPX[0:1, 1:2], c.CLIPX[0:1, 0:1], AF.Sigmoid)
        s.cs.inc(e.activation(c.CLIPD[0:1, 1:2], c.CLIPD[0:1, 0:1], AF.Sigmoid))
        m["clip_sig"] = s.cs.n

    @bk.vector
    def _(v):
        v.wait_ge(s.dgp.h, s.dgp.n)
        v.wait_ge(s.cs.h, m["clip_sig"])
        v.tensor_tensor(c.SCL[0:1, 0:1], c.GMX[0:1, 0:1], c.CLIPX[0:1, 1:2], op=ALU.mult)
        v.tensor_scalar(c.SCL[0:1, 0:1], c.SCL[0:1, 0:1], 1.0 / QMAX, None, op0=ALU.mult)
        s.cv.inc(v.reciprocal(c.SCL[0:1, 1:2], c.SCL[0:1, 0:1]))
        m["sx_val"] = s.cv.n

    @bk.tensor
    def _(t):
        t.wait_ge(s.cv.h, m["sx_val"])
        bank_wait(t, 1)
        t.matmul(c.PS[1][:, 0:1], c.ONES[:], c.SCL[0:1, 0:1], start=True, stop=True)
        s.ct.inc(t.matmul(c.PS[1][:, 1:2], c.ONES[:], c.SCL[0:1, 1:2], start=True, stop=True))
        m["sx_bc"] = s.ct.n

    @bk.vector
    def _(v):
        v.wait_ge(s.ct.h, m["sx_bc"])
        v.tensor_copy(c.SXB[:], c.PS[1][:, 0:1])
        s.cv.inc(v.tensor_copy(c.ISXB[:], c.PS[1][:, 1:2]))
        v.wait_ge(s.dw.h, s.dw.n)
        v.tensor_scalar(c.SXG[:], c.SWG[:], c.SXB[:, 0:1], None, op0=ALU.mult)
        s.cv.inc(v.tensor_scalar(c.SXU[:], c.SWU[:], c.SXB[:, 0:1], None, op0=ALU.mult))
        m["sx_done"] = s.cv.n
    c.bank_free[1] = (s.cv, m["sx_done"])

    if c.debug:
        @bk.sync
        def _(e):
            e.wait_ge(s.cv.h, m["sx_done"])
            s.dst.inc(e.dma_start(out=c.dbg_sx, in_=c.SCL[0:1, :]), 16)

    if upto == "SX":
        _finish(c, bk)
        return

    c.cdgstack.close()
    c.stage1.close()
    c.p2stack = ExitStack()
    sb2 = lambda name, shape, dtp: c.p2stack.enter_context(nc.sbuf_tensor(name, shape, dtp))
    c.H2RD = sb2("H2RD", [128, 2 * KD * R], F32)
    c.XQF = sb2("XQF", [128, KD * R], F32)
    c.XQB = sb2("XQB", [128, 2 * KD * R], BF16)
    c.SGW = sb2("SGW", [128, 2 * R], F32)
    c.VT = sb2("VT", [128, NIC * R], F32)
    c.WGT = sb2("WGT", [128, NIC * KD * 128], BF16)
    c.WUT = sb2("WUT", [128, NIC * KD * 128], BF16)

    @bk.sync
    def _(e):
        e.wait_ge(s.ct.h, c.m["C_gL%d" % (NS - 1)])
        e.wait_ge(s.dws.h, c.m["wq2_st%d" % (NIC - 1)])
        e.wait_ge(s.dw.h, s.dw.n)
        for k in range(KD):
            s.dw.inc(e.dma_start(out=c.WGT[:, k * NIC * 128:(k + 1) * NIC * 128],
                                 in_=c.wqg_s[:, k * 128:(k + 1) * 128], transpose=True), 16)
        m["p2_wg"] = s.dw.n

    @bk.sync
    def _(e):
        e.wait_ge(s.dws.h, s.dws.n)
        e.wait_ge(s.dwl.h, s.dwl.n)
        for k in range(KD):
            s.dwl.inc(e.dma_start(out=c.WUT[:, k * NIC * 128:(k + 1) * NIC * 128],
                                  in_=c.wqu_s[:, k * 128:(k + 1) * 128], transpose=True), 16)
        m["p2_wu"] = s.dwl.n

    @bk.sync
    def _(e):
        e.wait_ge(s.ct.h, m["C_done_t"])
        e.wait_ge(s.cv.h, m["C_done_v"])
        e.wait_ge(s.cs.h, m["C_done_s"])
        e.wait_ge(s.dsc.h, s.dsc.n)
        s.dxe.inc(e.dma_start(out=c.H2RD[:, 0:KD * R], in_=c.h2_s[0]), 16)
        s.dxo.inc(e.dma_start(out=c.H2RD[:, KD * R:2 * KD * R], in_=c.h2_s[1]), 16)

    def p2_quant(e, j):
        sp = s.dxe if j % 2 == 0 else s.dxo
        e.wait_ge(sp.h, sp.n)
        if j >= 1:
            e.wait_ge(s.cv.h, c.m["p2_clip%d" % (j - 1)])
        s.cs.inc(e.activation(c.XQF[:], c.H2RD[:, (j % 2) * KD * R:(j % 2 + 1) * KD * R],
                              AF.Identity, scale=c.ISXB[:, 0:1],
                              bias=c.CMAGIC[:, 0:1]))
        c.m["p2_q%d" % j] = s.cs.n

    def p2_clip(v, j):
        xqb = c.XQB[:, (j % 2) * KD * R:(j % 2 + 1) * KD * R]
        v.tensor_scalar(c.XQF[:], c.XQF[:], -MAGIC, None, op0=ALU.add)
        s.cv.inc(v.tensor_scalar(xqb, c.XQF[:], -128.0, 127.0,
                                 op0=ALU.max, op1=ALU.min))
        c.m["p2_clip%d" % j] = s.cv.n

    @bk.scalar
    def _(e):
        e.wait_ge(s.cv.h, m["sx_done"])
        e.wait_ge(s.dsc.h, s.dsc.n)
        p2_quant(e, 0)

    @bk.vector
    def _(v):
        v.wait_ge(s.cs.h, c.m["p2_q0"])
        v.wait_ge(s.dsc.h, s.dsc.n)
        p2_clip(v, 0)

    @bk.scalar
    def _(e):
        p2_quant(e, 1)

    @bk.vector
    def _(v):
        v.wait_ge(s.cs.h, c.m["p2_q1"])
        p2_clip(v, 1)

    for j in range(NS):
        bs = s.cs.n
        bt = s.ct.n
        bv = s.cv.n
        c.m["p2_bt%d" % j] = bt
        has_nx = j + 2 < NS
        qoff = 1 if has_nx else 0

        ldq = [2, 3] if j == 0 else ([j + 3] if j + 3 < NS else [])
        for q_ in ldq:
            @bk.sync
            def _(e, q_=q_):
                e.wait_ge(s.cs.h, c.m["p2_q%d" % (q_ - 2)])
                sp = s.dxe if q_ % 2 == 0 else s.dxo
                sp.inc(e.dma_start(out=c.H2RD[:, (q_ % 2) * KD * R:(q_ % 2 + 1) * KD * R],
                                   in_=c.h2_s[q_]), 16)

        @bk.scalar
        def _(e, j=j, bs=bs, bt=bt, bv=bv, has_nx=has_nx):
            if has_nx:
                p2_quant(e, j + 2)
            for ic in range(NIC):
                e.wait_ge(s.ct.h, bt + 2 * ic + 1)
                if ic >= 2:
                    e.wait_ge(s.cv.h, bv + (ic - 1))
                s.cs.inc(e.activation(c.SGW[:, (ic % 2) * R:(ic % 2 + 1) * R],
                                      c.PS[(2 * ic) % 8][:, 0:R],
                                      AF.Silu, scale=c.SXG[:, ic:ic + 1]))

        @bk.tensor
        def _(t, j=j, bs=bs, bt=bt, bv=bv):
            if j == 0:
                t.wait_ge(s.dw.h, m["p2_wg"])
            t.wait_ge(s.cv.h, c.m["p2_clip%d" % j])
            xqb = c.XQB[:, (j % 2) * KD * R:(j % 2 + 1) * KD * R]
            for ic in range(NIC):
                gb_, ub_ = (2 * ic) % 8, (2 * ic + 1) % 8
                if ic < 4:
                    bank_wait(t, gb_)
                    bank_wait(t, ub_)
                else:
                    t.wait_ge(s.cv.h, bv + (ic - 3))
                for k in range(KD):
                    ins = t.matmul(c.PS[gb_][:, 0:R],
                                   c.WGT[:, (k * NIC + ic) * 128:(k * NIC + ic + 1) * 128],
                                   xqb[:, k * R:(k + 1) * R], start=(k == 0), stop=(k == KD - 1))
                s.ct.inc(ins)
                if j == 0 and ic == 0:
                    t.wait_ge(s.dwl.h, m["p2_wu"])
                for k in range(KD):
                    ins = t.matmul(c.PS[ub_][:, 0:R],
                                   c.WUT[:, (k * NIC + ic) * 128:(k * NIC + ic + 1) * 128],
                                   xqb[:, k * R:(k + 1) * R], start=(k == 0), stop=(k == KD - 1))
                s.ct.inc(ins)

        @bk.vector
        def _(v, j=j, bs=bs, bt=bt, bv=bv, has_nx=has_nx, qoff=qoff):
            v.wait_ge(s.dsc.h, s.dsc.n)
            for ic in range(NIC):
                ub_ = (2 * ic + 1) % 8
                v.wait_ge(s.cs.h, bs + qoff + ic + 1)
                v.wait_ge(s.ct.h, bt + 2 * ic + 2)
                vt = c.VT[:, ic * R:(ic + 1) * R]
                v.scalar_tensor_tensor(vt, c.PS[ub_][:, 0:R], c.SXU[:, ic:ic + 1],
                                       c.SGW[:, (ic % 2) * R:(ic % 2 + 1) * R],
                                       op0=ALU.mult, op1=ALU.mult)
                s.cv.inc(v.reduce_max(c.VMAX[:, j * NIC + ic:j * NIC + ic + 1], vt, AXX,
                                      apply_absolute_value=True))
            c.m["p2_stt_last%d" % j] = s.cv.n
            if has_nx:
                v.wait_ge(s.cs.h, bs + 1)
                v.wait_ge(s.ct.h, bt + 2 * NIC)
                p2_clip(v, j + 2)
            c.m["p2_v_last%d" % j] = s.cv.n
        for b in range(8):
            c.bank_free[b] = (s.cv, c.m["p2_stt_last%d" % j])

        @bk.scalar
        def _(e, j=j, bv=bv):
            e.wait_ge(s.cv.h, bv + NIC)
            e.wait_ge(s.dsc.h, s.dsc.n)
            s.dsc.inc(e.dma_start(out=c.v_s[j], in_=c.VT[:]), 16)
            c.m["p2_vst%d" % j] = s.dsc.n

    m["p2_done_v"] = s.cv.n
    m["p2_done_st"] = s.dsc.n
    m["p2_done_t"] = s.ct.n
    m["p2_done_s"] = s.cs.n

    @bk.vector
    def _(v):
        v.wait_ge(s.cv.h, m["p2_done_v"])
        s.cv.inc(v.reduce_max(c.REDT[:, 1:2], c.VMAX[:, 0:NS * NIC], AXX))
        m["sd_red"] = s.cv.n

    @bk.tensor
    def _(t):
        t.wait_ge(s.cv.h, m["sd_red"])
        bank_wait(t, 0)
        s.ct.inc(t.transpose(c.PS[0][0:1, 0:128], c.REDT[:, 1:2], c.ID[:]))
        m["sd_tp"] = s.ct.n

    @bk.vector
    def _(v):
        v.wait_ge(s.ct.h, m["sd_tp"])
        v.memset(c.LMX[:], 0.0)
        s.cv.inc(v.reduce_max(c.LMX[0:1, 0:1], c.PS[0][0:1, 0:128], AXX))
        m["sd_loc"] = s.cv.n
    c.bank_free[0] = (s.cv, m["sd_loc"])

    @bk.gpsimd
    def _(g):
        g.wait_ge(s.cv.h, m["sd_loc"])
        s.dgp.inc(g.dma_start(out=c.cc_m2_in, in_=c.LMX[:]), 16)
        g.wait_ge(s.dgp.h, s.dgp.n)
        s.cc.inc(g.collective_compute("AllReduce", ALU.max,
                                      replica_groups=[list(range(NCORE))],
                                      ins=[c.cc_m2_in], outs=[c.cc_m2_out]))
        g.wait_ge(s.cc.h, s.cc.n)
        s.dgp.inc(g.dma_start(out=c.GMX[:], in_=c.cc_m2_out), 16)
        m["sd_gmax"] = s.dgp.n

    @bk.vector
    def _(v):
        v.wait_ge(s.dgp.h, s.dgp.n)
        v.wait_ge(s.cs.h, m["clip_sig"])
        if c.debug:
            v.wait_ge(s.dst.h, s.dst.n)
        v.tensor_tensor(c.SCL[0:1, 2:3], c.GMX[0:1, 0:1], c.CLIPD[0:1, 1:2], op=ALU.mult)
        v.tensor_scalar(c.SCL[0:1, 2:3], c.SCL[0:1, 2:3], 1.0 / QMAX, None, op0=ALU.mult)
        s.cv.inc(v.reciprocal(c.SCL[0:1, 3:4], c.SCL[0:1, 2:3]))
        m["sd_val"] = s.cv.n

    @bk.tensor
    def _(t):
        t.wait_ge(s.cv.h, m["sd_val"])
        bank_wait(t, 1)
        t.matmul(c.PS[1][:, 0:1], c.ONES[:], c.SCL[0:1, 2:3], start=True, stop=True)
        s.ct.inc(t.matmul(c.PS[1][:, 1:2], c.ONES[:], c.SCL[0:1, 3:4], start=True, stop=True))
        m["sd_bc"] = s.ct.n

    @bk.vector
    def _(v):
        v.wait_ge(s.ct.h, m["sd_bc"])
        v.tensor_copy(c.SDB[:], c.PS[1][:, 0:1])
        s.cv.inc(v.tensor_copy(c.ISDB[:], c.PS[1][:, 1:2]))
        v.wait_ge(s.dw.h, s.dw.n)
        v.tensor_scalar(c.SDD[:], c.SWD[:], c.SDB[:, 0:1], None, op0=ALU.mult)
        s.cv.inc(v.tensor_tensor(c.SDD[:], c.SDD[:], c.GB[:], op=ALU.mult))
        m["sd_done"] = s.cv.n
    c.bank_free[1] = (s.cv, m["sd_done"])

    if c.debug:
        @bk.sync
        def _(e):
            e.wait_ge(s.cv.h, m["sd_done"])
            s.dst.inc(e.dma_start(out=c.dbg_sd, in_=c.SCL[0:1, :]), 16)

    c.p2stack.close()
    c.p3stack = ExitStack()
    sb3 = lambda name, shape, dtp: c.p3stack.enter_context(nc.sbuf_tensor(name, shape, dtp))
    HNC = NIC // 2 + 1
    c.WDT = sb3("WDT", [128, NIC * KD * 128], BF16)
    c.V3H = sb3("V3H", [128, 2 * 11 * R], F32)
    c.V3F = sb3("V3F", [128, 2 * 11 * R], F32)
    c.V3B = sb3("V3B", [128, 2 * 11 * R], BF16)
    c.OFM = sb3("OFM", [128, KD * R], F32)
    c.X3T = sb3("X3T", [128, SUBT * D], F32)

    wdt_src = c.wqd_t.rearrange("(ic p) d -> p ic d", p=128)
    wdt_dst = c.WDT.rearrange("p (ic d) -> p ic d", d=KD * 128)

    @bk.sync
    def _(e):
        e.wait_ge(s.ct.h, m["p2_done_t"])
        e.wait_ge(s.cv.h, m["p2_done_v"])
        e.wait_ge(s.cs.h, m["p2_done_s"])
        e.wait_ge(s.dst.h, s.dst.n)
        e.wait_ge(s.dsc.h, m["C_done_st"])
        s.dx.inc(e.dma_start(out=c.X3T[:], in_=c.x2_s[0]), 16)
        e.wait_ge(s.dws.h, s.dws.n)
        e.wait_ge(s.dw.h, s.dw.n)
        s.dw.inc(e.dma_start(out=wdt_dst[:, 0:11], in_=wdt_src[:, 0:11]), 16)
        m["p3_wld_a"] = s.dw.n

    @bk.sync
    def _(e):
        e.wait_ge(s.dwl.h, s.dwl.n)
        s.dwl.inc(e.dma_start(out=wdt_dst[:, 11:NIC], in_=wdt_src[:, 11:NIC]), 16)
        m["p3_wld_b"] = s.dwl.n

    @bk.sync
    def _(e):
        e.wait_ge(s.dsc.h, s.dsc.n)
        s.dxe.inc(e.dma_start(out=c.V3H[:, 0:11 * R], in_=c.v_s[0][:, 0:11 * R]), 16)
        s.dxo.inc(e.dma_start(out=c.V3H[:, 11 * R:22 * R], in_=c.v_s[0][:, 11 * R:22 * R]), 16)

    def p3_quant(e, q):
        sp = s.dxe if q % 2 == 0 else s.dxo
        e.wait_ge(sp.h, sp.n)
        if q >= 2:
            e.wait_ge(s.cv.h, c.m["p3_clip%d" % (q - 2)])
        s.cs.inc(e.activation(c.V3F[:, (q % 2) * 11 * R:(q % 2 + 1) * 11 * R],
                              c.V3H[:, (q % 2) * 11 * R:(q % 2 + 1) * 11 * R],
                              AF.Identity, scale=c.ISDB[:, 0:1], bias=c.CMAGIC[:, 0:1]))
        c.m["p3_q%d" % q] = s.cs.n

    def p3_clip(v, q):
        v3f = c.V3F[:, (q % 2) * 11 * R:(q % 2 + 1) * 11 * R]
        v.tensor_scalar(v3f, v3f, -MAGIC, None, op0=ALU.add)
        s.cv.inc(v.tensor_scalar(c.V3B[:, (q % 2) * 11 * R:(q % 2 + 1) * 11 * R],
                                 v3f, -128.0, 127.0, op0=ALU.max, op1=ALU.min))
        c.m["p3_clip%d" % q] = s.cv.n

    @bk.scalar
    def _(e):
        e.wait_ge(s.cv.h, m["sd_done"])
        e.wait_ge(s.dsc.h, s.dsc.n)
        p3_quant(e, 0)
        p3_quant(e, 1)

    @bk.vector
    def _(v):
        v.wait_ge(s.cs.h, c.m["p3_q0"])
        p3_clip(v, 0)
        v.wait_ge(s.cs.h, c.m["p3_q1"])
        p3_clip(v, 1)

    for j in range(NS):
        bs3 = s.cs.n
        bt3 = s.ct.n
        bv3 = s.cv.n
        has_nx3 = j + 1 < NS
        nq = 2 if has_nx3 else 0

        if has_nx3:
            @bk.sync
            def _(e, j=j):
                e.wait_ge(s.dsc.h, c.m["p2_vst%d" % (NS - 1)])
                e.wait_ge(s.cs.h, c.m["p3_q%d" % (2 * j)])
                s.dxe.inc(e.dma_start(out=c.V3H[:, 0:11 * R],
                                      in_=c.v_s[j + 1][:, 0:11 * R]), 16)
                e.wait_ge(s.cs.h, c.m["p3_q%d" % (2 * j + 1)])
                s.dxo.inc(e.dma_start(out=c.V3H[:, 11 * R:22 * R],
                                      in_=c.v_s[j + 1][:, 11 * R:22 * R]), 16)

        @bk.scalar
        def _(e, j=j, bs3=bs3, bt3=bt3, has_nx3=has_nx3, nq=nq):
            if has_nx3:
                p3_quant(e, 2 * j + 2)
                p3_quant(e, 2 * j + 3)
            if j == 0:
                e.wait_ge(s.dsc.h, c.m["p2_vst%d" % (NS - 1)])
            for mc in range(8):
                e.wait_ge(s.ct.h, bt3 + 8 + mc + 1)
                s.cs.inc(e.activation(c.OFM[:, mc * R:(mc + 1) * R], c.PS[mc][:, 0:R],
                                      AF.Identity, scale=c.SDD[:, mc:mc + 1]))

        @bk.tensor
        def _(t, j=j, bt3=bt3, bv3=bv3, has_nx3=has_nx3):
            if j == 0:
                t.wait_ge(s.dw.h, m["p3_wld_a"])
                t.wait_ge(s.dwl.h, m["p3_wld_b"])
            for h in range(2):
                t.wait_ge(s.cv.h, c.m["p3_clip%d" % (2 * j + h)])
                for mc in range(8):
                    if j == 0:
                        if h == 0:
                            bank_wait(t, mc)
                    elif h == 0:
                        t.wait_ge(s.cv.h, c.m["p3_x3_%d_%d" % (j - 1, mc)])
                    for il in range(11):
                        ic = h * 11 + il
                        ins = t.matmul(c.PS[mc][:, 0:R],
                                       c.WDT[:, (ic * KD + mc) * 128:(ic * KD + mc + 1) * 128],
                                       c.V3B[:, (h % 2) * 11 * R + il * R:
                                              (h % 2) * 11 * R + (il + 1) * R],
                                       start=(ic == 0), stop=(ic == NIC - 1))
                    s.ct.inc(ins)
            for k in range(KD):
                t.wait_ge(s.cs.h, bs3 + (2 if has_nx3 else 0) + k + 1)
                for t_ in range(SUBT):
                    ins = t.transpose(c.PS[k][:, t_ * 128:(t_ + 1) * 128],
                                      c.OFM[:, k * R + t_ * 128:k * R + (t_ + 1) * 128],
                                      c.ID[:])
                s.ct.inc(ins)

        @bk.vector
        def _(v, j=j, bs3=bs3, bt3=bt3, has_nx3=has_nx3):
            if has_nx3:
                v.wait_ge(s.cs.h, bs3 + 1)
                v.wait_ge(s.ct.h, bt3 + 8)
                p3_clip(v, 2 * j + 2)
                v.wait_ge(s.cs.h, bs3 + 2)
                v.wait_ge(s.ct.h, bt3 + 16)
                p3_clip(v, 2 * j + 3)
            v.wait_ge(s.dx.h, s.dx.n)
            for k in range(KD):
                v.wait_ge(s.ct.h, bt3 + 16 + k + 1)
                src2 = c.PS[k][:, 0:SUBT * 128].rearrange("p (t q) -> p t q", q=128)
                dst2 = c.X3T[:].rearrange("p (t d) -> p t d", d=D)[:, :, k * 128:(k + 1) * 128]
                s.cv.inc(v.tensor_tensor(dst2, dst2, src2, op=ALU.add))
                c.m["p3_x3_%d_%d" % (j, k)] = s.cv.n

        @bk.scalar
        def _(e, j=j):
            e.wait_ge(s.cv.h, c.m["p3_x3_%d_%d" % (j, KD - 1)])
            dstv = c.out.rearrange("(r e) d -> e r d", e=NS)[j].rearrange("(t p) d -> p t d", p=128)
            srcv = c.X3T[:].rearrange("p (t d) -> p t d", d=D)
            s.dst.inc(e.dma_start(out=dstv, in_=srcv), 16)
            c.m["p3_out%d" % j] = s.dst.n

        if has_nx3:
            @bk.sync
            def _(e, j=j):
                e.wait_ge(s.dst.h, c.m["p3_out%d" % j])
                s.dx.inc(e.dma_start(out=c.X3T[:], in_=c.x2_s[j + 1]), 16)

    c.p3stack.close()
    _finish(c, bk)


def _finish(c, bk):
    s = c.s

    @bk.sync
    def _(e):
        e.wait_ge(s.dst.h, s.dst.n)
        e.wait_ge(s.dsc.h, s.dsc.n)
        e.wait_ge(s.dgp.h, s.dgp.n)
        e.wait_ge(s.dw2.h, s.dw2.n)
        e.wait_ge(s.dw3.h, s.dw3.n)




def _bias_fm(vec):
    n = vec.shape[0] // 128
    return np.ascontiguousarray(vec.reshape(n, 128).T)


def _prep_inputs(inputs):
    x = np.ascontiguousarray(np.asarray(inputs["L0_attn_x"], np.float32))
    c_full = np.asarray(inputs["L0_c"], np.float32)
    wmod = np.asarray(inputs["L0_wmod"], np.float32)
    base = {
        "A": np.ascontiguousarray(np.asarray(inputs["L0_attn_A"], np.float32)),
        "B_w": np.ascontiguousarray(np.asarray(inputs["L0_attn_B_w"], np.float32)),
        "B_b": np.ascontiguousarray(np.asarray(inputs["L0_attn_B_b"], np.float32).reshape(DS, 1)),
        "C_w": np.ascontiguousarray(np.asarray(inputs["L0_attn_C_w"], np.float32)),
        "C_b": _bias_fm(np.asarray(inputs["L0_attn_C_b"], np.float32)),
        "D_w": np.ascontiguousarray(np.asarray(inputs["L0_attn_D_w"], np.float32)),
        "D_b": _bias_fm(np.asarray(inputs["L0_attn_D_b"], np.float32)),
        "s_w1": np.ascontiguousarray(np.asarray(inputs["L0_attn_s_w1"], np.float32)),
        "s_b1": _bias_fm(np.asarray(inputs["L0_attn_s_b1"], np.float32)),
        "s_w2": np.ascontiguousarray(np.asarray(inputs["L0_attn_s_w2"], np.float32)),
        "s_b2": np.ascontiguousarray(np.asarray(inputs["L0_attn_s_b2"], np.float32).reshape(DS, 1)),
        "g_w": np.ascontiguousarray(np.asarray(inputs["L0_attn_g_w"], np.float32)),
        "g_b": _bias_fm(np.asarray(inputs["L0_attn_g_b"], np.float32)),
        "w_gate": np.ascontiguousarray(np.asarray(inputs["L0_ffn_w_gate"], np.float32)),
        "w_up": np.ascontiguousarray(np.asarray(inputs["L0_ffn_w_up"], np.float32)),
        "w_down": np.ascontiguousarray(np.asarray(inputs["L0_ffn_w_down"], np.float32)),
        "clip_x": np.asarray(inputs["L0_ffn_clip_x"], np.float32).reshape(1, 1),
        "clip_down": np.asarray(inputs["L0_ffn_clip_down"], np.float32).reshape(1, 1),
        "ident": np.eye(128, dtype=np.float32),
        "ones1": np.ones((1, 128), dtype=np.float32),
    }
    in_maps = []
    for i in range(NCORE):
        im = dict(base)
        im["x"] = np.ascontiguousarray(x[i])
        im["c_ks"] = np.ascontiguousarray(c_full[:, i * 128:(i + 1) * 128])
        im["wmod_ks"] = np.ascontiguousarray(wmod[i * 128:(i + 1) * 128, :])
        in_maps.append(im)
    return in_maps


def kernel(**inputs):
    in_maps = _prep_inputs(inputs)
    nc, _ = build(debug=False, upto="full")
    res = run_bass_kernel_spmd(nc, in_maps, core_ids=list(range(NCORE)))
    return np.stack([np.asarray(r["out"]) for r in res.results], axis=0)


# version 21
# speedup vs baseline: 1.1227x; 1.0121x over previous
import numpy as np
from contextlib import ExitStack

import concourse.bass as bass
import concourse.mybir as mybir
from concourse.bass_utils import run_bass_kernel_spmd

F32 = mybir.dt.float32
BF16 = mybir.dt.bfloat16
I8 = mybir.dt.int8
AF = mybir.ActivationFunctionType
ALU = mybir.AluOpType
AXX = mybir.AxisListType.X

NCORE = 8
SEQ = 4096
D = 1024
KD = 8
NS = 8
R = 512
SUBT = 4
DS = 128
DFF = 2816
NIC = 22
MD = 6144
MC = 48
EPS = 1e-5
QMAX = 127.0
MAGIC = 12582912.0


class Sem:
    def __init__(self, stack, nc, name):
        self.h = stack.enter_context(nc.semaphore(name))
        self.n = 0

    def inc(self, instr, k=1):
        instr.then_inc(self.h, k)
        self.n += k
        return self.n


class Ctx:
    pass


_INTERCEPT = {
    "tensor_copy", "tensor_tensor", "tensor_add", "tensor_sub", "tensor_mul",
    "tensor_max", "tensor_scalar", "tensor_single_scalar", "scalar_tensor_tensor",
    "tensor_reduce", "reduce_max", "reduce_sum", "reciprocal", "memset",
    "memzero", "activation", "bn_stats", "bn_aggr", "select", "copy_predicated",
}


def _is_ap(x):
    return hasattr(x, "tensor") and hasattr(x, "offset") and hasattr(x, "ap")


def _fp(apobj):
    t = apobj.tensor
    shape = list(t.shape)
    row = 1
    for d in shape[1:]:
        row *= d
    off = int(apobj.offset)
    span = 0
    for pair in apobj.ap:
        st, cnt = int(pair[0]), int(pair[1])
        if row > 1 and st >= row:
            continue
        span += (cnt - 1) * abs(st)
    lo = off % row if row > 1 else 0
    return (t.name, lo, lo + span + 1)


class EngProxy:

    def __init__(self, eng):
        self._eng = eng
        self._w = []
        self._r = []

    def _touch(self, writes, reads):
        need = False
        for f in writes + reads:
            for g in self._w:
                if f[0] == g[0] and f[1] < g[2] and g[1] < f[2]:
                    need = True
        for f in writes:
            for g in self._r:
                if f[0] == g[0] and f[1] < g[2] and g[1] < f[2]:
                    need = True
        if need:
            self._eng.drain()
            self._w, self._r = [], []
        self._w += writes
        self._r += reads

    def __getattr__(self, name):
        real = getattr(self._eng, name)
        if name not in _INTERCEPT:
            return real

        def call(*args, **kw):
            writes, reads = [], []
            first = True
            for a in args:
                if _is_ap(a):
                    (writes if first else reads).append(_fp(a))
                    first = False
            for k, a in kw.items():
                if _is_ap(a):
                    (writes if k in ("out", "out_ap", "accum_out") else reads).append(_fp(a))
            self._touch(writes, reads)
            return real(*args, **kw)

        return call




def build(debug=False, upto="full"):
    nc = bass.Bass("TRN2", target_bir_lowering=False)
    c = Ctx()
    c.nc = nc
    c.debug = debug

    dt = nc.dram_tensor
    c.x = dt("x", [SEQ, D], F32, kind="ExternalInput").ap()
    c.A = dt("A", [DS, DS], F32, kind="ExternalInput").ap()
    c.B_w = dt("B_w", [DS, D], F32, kind="ExternalInput").ap()
    c.B_b = dt("B_b", [DS, 1], F32, kind="ExternalInput").ap()
    c.C_w = dt("C_w", [D, DS], F32, kind="ExternalInput").ap()
    c.C_b = dt("C_b", [128, KD], F32, kind="ExternalInput").ap()
    c.D_w = dt("D_w", [D, D], F32, kind="ExternalInput").ap()
    c.D_b = dt("D_b", [128, KD], F32, kind="ExternalInput").ap()
    c.s_w1 = dt("s_w1", [512, D], F32, kind="ExternalInput").ap()
    c.s_b1 = dt("s_b1", [128, 4], F32, kind="ExternalInput").ap()
    c.s_w2 = dt("s_w2", [DS, 512], F32, kind="ExternalInput").ap()
    c.s_b2 = dt("s_b2", [DS, 1], F32, kind="ExternalInput").ap()
    c.g_w = dt("g_w", [D, D], F32, kind="ExternalInput").ap()
    c.g_b = dt("g_b", [128, KD], F32, kind="ExternalInput").ap()
    c.w_gate = dt("w_gate", [DFF, D], F32, kind="ExternalInput").ap()
    c.w_up = dt("w_up", [DFF, D], F32, kind="ExternalInput").ap()
    c.w_down = dt("w_down", [D, DFF], F32, kind="ExternalInput").ap()
    c.clip_x = dt("clip_x", [1, 1], F32, kind="ExternalInput").ap()
    c.clip_down = dt("clip_down", [1, 1], F32, kind="ExternalInput").ap()
    c.c_ks = dt("c_ks", [NCORE, 128], F32, kind="ExternalInput").ap()
    c.wmod_ks = dt("wmod_ks", [128, MD], F32, kind="ExternalInput").ap()
    c.ident = dt("ident", [128, 128], F32, kind="ExternalInput").ap()
    c.ones1 = dt("ones1", [1, 128], F32, kind="ExternalInput").ap()

    c.out = dt("out", [SEQ, D], F32, kind="ExternalOutput").ap()

    skind = "ExternalOutput" if debug else "Internal"
    c.x2_s = dt("x2_s", [NS, 128, SUBT * D], F32, kind=skind).ap()
    c.h2_s = dt("h2_s", [NS, 128, KD * R], F32, kind=skind).ap()
    c.v_s = dt("v_s", [NS, 128, NIC * R], F32, kind=skind).ap()
    c.wqg_s = dt("wqg_s", [NIC * 128, KD * 128], BF16, kind="Internal").ap()
    c.wqu_s = dt("wqu_s", [NIC * 128, KD * 128], BF16, kind="Internal").ap()
    c.wqd_s = dt("wqd_s", [KD * 128, NIC * 128], BF16, kind="Internal").ap()
    c.wqd_t = dt("wqd_t", [NIC * 128, KD * 128], BF16, kind="Internal").ap()
    c.cc_mod_in = dt("cc_mod_in", [NCORE, MD], F32).ap()
    c.cc_mod_out = dt("cc_mod_out", [1, MD], F32).ap()
    c.cc_m1_in = dt("cc_m1_in", [1, 8], F32).ap()
    c.cc_m1_out = dt("cc_m1_out", [1, 8], F32).ap()
    c.cc_m2_in = dt("cc_m2_in", [1, 8], F32).ap()
    c.cc_m2_out = dt("cc_m2_out", [1, 8], F32).ap()
    if debug:
        c.dbg_mod = dt("dbg_mod", [128, MC], F32, kind="ExternalOutput").ap()
        c.dbg_u = dt("dbg_u", [128, SEQ], F32, kind="ExternalOutput").ap()
        c.dbg_hs = dt("dbg_hs", [128, SEQ], F32, kind="ExternalOutput").ap()
        c.dbg_h = dt("dbg_h", [128, KD * R], F32, kind="ExternalOutput").ap()
        c.dbg_sx = dt("dbg_sx", [1, 8], F32, kind="ExternalOutput").ap()
        c.dbg_sd = dt("dbg_sd", [1, 8], F32, kind="ExternalOutput").ap()

    c.sem_stack = ExitStack()

    al = nc.alloc_sbuf_tensor
    c.ID = al("ID", [128, 128], F32)
    c.ONES = al("ONES", [1, 128], F32)
    c.SUMX = al("SUMX", [128, 32], F32)
    c.SUMSQ = al("SUMSQ", [128, 32], F32)
    c.MU = al("MU", [128, 32], F32)
    c.INV = al("INV", [128, 32], F32)
    c.NMI = al("NMI", [128, 32], F32)
    c.MU2 = al("MU2", [128, 32], F32)
    c.INV2 = al("INV2", [128, 32], F32)
    c.NMI2 = al("NMI2", [128, 32], F32)
    c.STSC = al("STSC", [128, 8], F32)
    c.MOD = al("MOD", [128, MC], F32)
    c.ONE_PCA = al("ONE_PCA", [128, KD], F32)
    c.GA = al("GA", [128, KD], F32)
    c.ONE_PCB = al("ONE_PCB", [128, KD], F32)
    c.GB = al("GB", [128, KD], F32)
    c.BCD = al("BCD", [128, KD], F32)
    c.BIAS_B = al("BIAS_B", [DS, 1], F32)
    c.BIAS_S1 = al("BIAS_S1", [128, 4], F32)
    c.BIAS_S2 = al("BIAS_S2", [DS, 1], F32)
    c.BIAS_G = al("BIAS_G", [128, KD], F32)
    c.CB_T = al("CB_T", [128, KD], F32)
    c.DB_T = al("DB_T", [128, KD], F32)
    c.CLIPX = al("CLIPX", [1, 2], F32)
    c.CLIPD = al("CLIPD", [1, 2], F32)
    c.RMAX = al("RMAX", [128, 2], F32)
    c.SWG = al("SWG", [128, NIC], F32)
    c.SWU = al("SWU", [128, NIC], F32)
    c.SWD = al("SWD", [128, KD], F32)
    c.RSG = al("RSG", [128, NIC], F32)
    c.RSU = al("RSU", [128, NIC], F32)
    c.RSD = al("RSD", [128, KD], F32)
    c.SXG = al("SXG", [128, NIC], F32)
    c.SXU = al("SXU", [128, NIC], F32)
    c.SDD = al("SDD", [128, KD], F32)
    c.SXB = al("SXB", [128, 1], F32)
    c.ISXB = al("ISXB", [128, 1], F32)
    c.ISDB = al("ISDB", [128, 1], F32)
    c.SDB = al("SDB", [128, 1], F32)
    c.SCL = al("SCL", [1, 8], F32)
    c.CMAGIC = al("CMAGIC", [128, 1], F32)
    c.NMAGIC = al("NMAGIC", [128, 1], F32)
    c.LMX = al("LMX", [1, 8], F32)
    c.GMX = al("GMX", [1, 8], F32)
    c.H2MAX = al("H2MAX", [128, 64], F32)
    c.VMAX = al("VMAX", [128, NS * NIC], F32)
    c.REDT = al("REDT", [128, 2], F32)
    c.CSL = al("CSL", [NCORE, 128], F32)

    c.PS = [nc.alloc_psum_tensor(f"PS{i}", [128, 512], F32) for i in range(8)]

    c.stage1 = ExitStack()
    s1t = lambda name, shape, dtp=F32: c.stage1.enter_context(nc.sbuf_tensor(name, shape, dtp))
    c.X_TM = s1t("X_TM", [128, 2 * SUBT * D])
    c.XN = s1t("XN", [128, SUBT * D])
    c.H_FM = s1t("H_FM", [128, KD * R])
    c.U = s1t("U", [128, SEQ])
    c.HS = s1t("HS", [128, SEQ])
    c.SGT = s1t("SGT", [128, R])
    c.A_DIR = s1t("A_DIR", [128, 128])
    c.CSLT = s1t("CSLT", [128, NCORE])
    c.H_HI = c.stage1.enter_context(nc.sbuf_tensor("H_HI", [128, KD * R], BF16))
    c.H_LO = c.stage1.enter_context(nc.sbuf_tensor("H_LO", [128, KD * R], BF16))

    c.modstack = ExitStack()
    mst = lambda name, shape: c.modstack.enter_context(nc.sbuf_tensor(name, shape, F32))
    c.WMODS = mst("WMODS", [128, MD])
    c.MODP = mst("MODP", [NCORE, MD])
    c.MODROW = mst("MODROW", [1, MD])

    from types import SimpleNamespace
    sems = {}
    for name in ["dx", "dw", "dw2", "dw3", "dwl", "dws", "dxe", "dxo", "dsc", "dgp", "ct", "cv", "cs", "cg", "cc", "dst"]:
        sems[name] = Sem(c.sem_stack, nc, name)
    c.s = SimpleNamespace(**sems)
    c.m = {}

    with nc.Block() as block:
        _emit(c, block, upto)
    c.sem_stack.close()
    return nc, c




def _emit(c, bk, upto):
    nc = c.nc
    s = c.s
    m = c.m
    c.bank_free = [None] * 8
    pv = EngProxy(nc.vector)
    ps_ = EngProxy(nc.scalar)
    pg = EngProxy(nc.gpsimd)

    _bk0 = bk

    class BK:
        tensor = staticmethod(_bk0.tensor)
        sync = staticmethod(_bk0.sync)

        @staticmethod
        def vector(fn):
            _bk0.vector(lambda _e: fn(pv))

        @staticmethod
        def scalar(fn):
            _bk0.scalar(lambda _e: fn(ps_))

        @staticmethod
        def gpsimd(fn):
            _bk0.gpsimd(lambda _e: fn(pg))

    bk = BK

    def bank_wait(t, i):
        bf = c.bank_free[i]
        if bf is not None:
            t.wait_ge(bf[0].h, bf[1])
            c.bank_free[i] = None

    @bk.sync
    def _(e):
        for dst, src in [(c.ID[:], c.ident), (c.ONES[:], c.ones1), (c.A_DIR[:], c.A),
                         (c.CSL[:], c.c_ks), (c.BIAS_B[:], c.B_b), (c.BIAS_S1[:], c.s_b1),
                         (c.BIAS_S2[:], c.s_b2), (c.BIAS_G[:], c.g_b), (c.CB_T[:], c.C_b),
                         (c.DB_T[:], c.D_b), (c.CLIPX[0:1, 0:1], c.clip_x),
                         (c.CLIPD[0:1, 0:1], c.clip_down)]:
            s.dx.inc(e.dma_start(out=dst, in_=src), 16)
        m["ld_small"] = s.dx.n
        s.dx.inc(e.dma_start(out=c.WMODS[:], in_=c.wmod_ks), 16)
        m["ld_wmod"] = s.dx.n

    @bk.vector
    def _(v):
        v.memset(c.SCL[:], 0.0)
        v.memset(c.LMX[:], 0.0)
        v.memset(c.GMX[:], 0.0)
        v.memset(c.CMAGIC[:], MAGIC)
        v.memset(c.NMAGIC[:], -MAGIC)

    @bk.scalar
    def _(e):
        e.wait_ge(s.dx.h, s.dx.n)
        s.cs.inc(e.activation(c.CSL[:], c.CSL[:], AF.Silu))
        m["csl_silu"] = s.cs.n

    @bk.tensor
    def _(t):
        t.wait_ge(s.cs.h, m["csl_silu"])
        bank_wait(t, 4)
        s.ct.inc(t.transpose(c.PS[4][0:128, 0:NCORE], c.CSL[:], c.ID[0:NCORE, 0:NCORE]))
        m["cslt_t"] = s.ct.n

    @bk.vector
    def _(v):
        v.wait_ge(s.ct.h, m["cslt_t"])
        s.cv.inc(v.tensor_copy(c.CSLT[:], c.PS[4][:, 0:NCORE]))
        m["cslt"] = s.cv.n
    c.bank_free[4] = (s.cv, m["cslt"])

    for n in range(12):
        @bk.tensor
        def _(t, n=n):
            if n == 0:
                t.wait_ge(s.cv.h, m["cslt"])
                t.wait_ge(s.dx.h, s.dx.n)
                bank_wait(t, 4)
                bank_wait(t, 5)
            if n >= 2:
                t.wait_ge(s.cv.h, m["modp_e%d" % (n - 2)])
            s.ct.inc(t.matmul(c.PS[4 + n % 2][0:NCORE, 0:512], c.CSLT[:],
                              c.WMODS[:, n * 512:(n + 1) * 512], start=True, stop=True))
            m["modp_t%d" % n] = s.ct.n

        @bk.vector
        def _(v, n=n):
            v.wait_ge(s.ct.h, m["modp_t%d" % n])
            s.cv.inc(v.tensor_copy(c.MODP[:, n * 512:(n + 1) * 512], c.PS[4 + n % 2][0:NCORE, 0:512]))
            m["modp_e%d" % n] = s.cv.n
    m["modp"] = s.cv.n
    m["modp_mm"] = m["modp_t%d" % 11]
    c.bank_free[4] = (s.cv, m["modp"])
    c.bank_free[5] = (s.cv, m["modp"])

    @bk.gpsimd
    def _(g):
        g.wait_ge(s.cv.h, m["modp"])
        s.dgp.inc(g.dma_start(out=c.cc_mod_in, in_=c.MODP[:]), 16)
        g.wait_ge(s.dgp.h, s.dgp.n)
        s.cc.inc(g.collective_compute("ReduceScatter", ALU.add,
                                      replica_groups=[list(range(NCORE))],
                                      ins=[c.cc_mod_in], outs=[c.cc_mod_out]))
        g.wait_ge(s.cc.h, s.cc.n)
        s.dgp.inc(g.dma_start(out=c.MODROW[:], in_=c.cc_mod_out), 16)
        m["modrow"] = s.dgp.n

    c.modstack.close()
    c.powstack = ExitStack()
    pst = lambda name, shape: c.powstack.enter_context(nc.sbuf_tensor(name, shape, F32))
    c.M_ = [pst(f"Mp{k}", [128, 128]) for k in range(1, 9)]
    c.N_ = [pst(f"Np{k}", [128, 128]) for k in range(1, 9)]
    c.R_ = [pst(f"Rp{k}", [128, 128]) for k in range(1, 9)]
    c.Z_ = [pst(f"Zp{k}", [128, 128]) for k in range(1, 7)]
    c.TDIR = pst("TDIR", [128, 128])
    @bk.tensor
    def _(t):
        t.wait_ge(s.dx.h, s.dx.n)
        for _b in range(4):
            bank_wait(t, _b)
        s.ct.inc(t.matmul(c.PS[0][:, 0:128], c.A_DIR[:], c.ID[:], start=True, stop=True))
        m["pow_m1t"] = s.ct.n

    @bk.vector
    def _(v):
        v.wait_ge(s.ct.h, m["pow_m1t"])
        s.cv.inc(v.tensor_copy(c.M_[0][:], c.PS[0][:, 0:128]))
        m["pow_M1"] = s.cv.n

    for k in range(1, 8):
        @bk.tensor
        def _(t, k=k):
            t.wait_ge(s.cv.h, m["pow_M%d" % k])
            s.ct.inc(t.matmul(c.PS[k % 2][:, 0:128], c.A_DIR[:], c.M_[k - 1][:],
                              start=True, stop=True))
            m["pow_Mt%d" % (k + 1)] = s.ct.n

        @bk.vector
        def _(v, k=k):
            v.wait_ge(s.ct.h, m["pow_Mt%d" % (k + 1)])
            s.cv.inc(v.tensor_copy(c.M_[k][:], c.PS[k % 2][:, 0:128]))
            m["pow_M%d" % (k + 1)] = s.cv.n

    def build_level(tag, src_tile, src_ms, dst_list, nmax):
        @bk.tensor
        def _(t):
            t.wait_ge(s.cv.h, m[src_ms])
            s.ct.inc(t.transpose(c.PS[2][:, 0:128], src_tile[:], c.ID[:]))
            m[tag + "_tpt"] = s.ct.n

        @bk.vector
        def _(v):
            v.wait_ge(s.ct.h, m[tag + "_tpt"])
            v.tensor_copy(c.TDIR[:], c.PS[2][:, 0:128])
            s.cv.inc(v.tensor_copy(dst_list[0][:], src_tile[:]))
            m[tag + "_1"] = s.cv.n

        for k in range(1, nmax):
            @bk.tensor
            def _(t, k=k):
                t.wait_ge(s.cv.h, m[tag + "_%d" % k])
                s.ct.inc(t.matmul(c.PS[k % 2][:, 0:128], c.TDIR[:], dst_list[k - 1][:],
                                  start=True, stop=True))
                m[tag + "_t%d" % (k + 1)] = s.ct.n

            @bk.vector
            def _(v, k=k):
                v.wait_ge(s.ct.h, m[tag + "_t%d" % (k + 1)])
                s.cv.inc(v.tensor_copy(dst_list[k][:], c.PS[k % 2][:, 0:128]))
                m[tag + "_%d" % (k + 1)] = s.cv.n

    build_level("powN", c.M_[7], "pow_M8", c.N_, 8)
    build_level("powR", c.N_[7], "powN_8", c.R_, 8)
    build_level("powZ", c.R_[7], "powR_8", c.Z_, 6)
    m["powers_done"] = s.cv.n
    for _b in range(3):
        c.bank_free[_b] = (s.cv, m["powers_done"])


    @bk.tensor
    def _(t):
        t.wait_ge(s.dgp.h, s.dgp.n)
        for half in range(2):
            bank_wait(t, 6 + half)
            for jj in range(24):
                col = half * 24 + jj
                ins = t.transpose(c.PS[6 + half][:, jj:jj + 1],
                                  c.MODROW[0:1, col * 128:(col + 1) * 128], c.ID[0:1, 0:1])
            s.ct.inc(ins)
            m["mod_t%d" % half] = s.ct.n

    @bk.vector
    def _(v):
        v.wait_ge(s.ct.h, m["mod_t0"])
        v.wait_ge(s.ct.h, m["mod_t1"])
        v.tensor_copy(c.MOD[:, 0:24], c.PS[6][:, 0:24])
        s.cv.inc(v.tensor_copy(c.MOD[:, 24:48], c.PS[7][:, 0:24]))
        sa = c.MOD[:, 0:8]
        ca = c.MOD[:, 8:16]
        ga = c.MOD[:, 16:24]
        cb = c.MOD[:, 32:40]
        gb = c.MOD[:, 40:48]
        v.tensor_scalar(c.ONE_PCA[:], ca, 1.0, None, op0=ALU.add)
        v.tensor_scalar(c.ONE_PCB[:], cb, 1.0, None, op0=ALU.add)
        for src, dst in ((ga, c.GA), (gb, c.GB)):
            v.tensor_scalar(c.STSC[:], src, -0.25, 0.25, op0=ALU.max, op1=ALU.min)
            v.tensor_scalar(dst[:], src, 0.0, None, op0=ALU.is_ge)
            v.scalar_tensor_tensor(dst[:], dst[:], 0.5, src, op0=ALU.mult, op1=ALU.add)
            v.tensor_tensor(dst[:], dst[:], c.STSC[:], op=ALU.subtract)
            v.tensor_scalar(dst[:], dst[:], 0.25, None, op0=ALU.subtract)
        v.tensor_tensor(c.BCD[:], c.CB_T[:], c.DB_T[:], op=ALU.add)
        s.cv.inc(v.tensor_tensor(c.BCD[:], c.BCD[:], c.GA[:], op=ALU.mult))
        m["mod_done"] = s.cv.n
    c.bank_free[6] = (s.cv, m["mod_done"])
    c.bank_free[7] = (s.cv, m["mod_done"])

    if c.debug:
        @bk.sync
        def _(e):
            e.wait_ge(s.cv.h, m["mod_done"])
            s.dst.inc(e.dma_start(out=c.dbg_mod, in_=c.MOD[:]), 16)


    c.astack = ExitStack()
    sba = lambda name, shape: c.astack.enter_context(nc.sbuf_tensor(name, shape, F32))
    c.S_W1T = sba("S_W1T", [128, 32 * 128])
    c.S_W2T = sba("S_W2T", [128, 4 * 128])
    c.B_WT = sba("B_WT", [128, 8 * 128])
    c.WLD = sba("WLD", [128, D])
    c.S1A = sba("S1A", [128, 4 * R])

    def wprep(name, w_ap, nrc, ncc, dst, scale_ga, dst_lo=None):
        for rc in range(nrc):
            @bk.sync
            def _(e, rc=rc):
                if name == "sw1" and rc == 0:
                    e.wait_ge(s.ct.h, m["modp_mm"])
                    e.wait_ge(s.dgp.h, s.dgp.n - 16)
                if c.m.get("wld_free") is not None:
                    e.wait_ge(s.ct.h, c.m["wld_free"])
                s.dw.inc(e.dma_start(out=c.WLD[:, 0:ncc * 128],
                                     in_=w_ap[rc * 128:(rc + 1) * 128, :]), 16)
                c.m[name + "_ld%d" % rc] = s.dw.n

            if scale_ga:
                @bk.vector
                def _(v, rc=rc):
                    v.wait_ge(s.dw.h, s.dw.n)
                    v.wait_ge(s.cv.h, m["mod_done"])
                    s.cv.inc(v.tensor_scalar(c.WLD[:, 0:ncc * 128], c.WLD[:, 0:ncc * 128],
                                             c.GA[:, rc:rc + 1], None, op0=ALU.mult))
                    c.m[name + "_sc%d" % rc] = s.cv.n

            for cc2 in range(ncc):
                @bk.tensor
                def _(t, rc=rc, cc2=cc2):
                    if cc2 == 0:
                        if scale_ga:
                            t.wait_ge(s.cv.h, c.m[name + "_sc%d" % rc])
                        else:
                            t.wait_ge(s.dw.h, s.dw.n)
                    b = cc2 % 4
                    bank_wait(t, b)
                    key = name + "_evk%d" % b
                    if c.m.get(key) is not None:
                        t.wait_ge(s.cv.h, c.m[key])
                    s.ct.inc(t.transpose(c.PS[b][:, 0:128],
                                         c.WLD[:, cc2 * 128:(cc2 + 1) * 128], c.ID[:]))
                    c.m[name + "_tp_%d_%d" % (rc, cc2)] = s.ct.n
                    if cc2 == ncc - 1:
                        c.m["wld_free"] = s.ct.n

                @bk.vector
                def _(v, rc=rc, cc2=cc2):
                    v.wait_ge(s.ct.h, c.m[name + "_tp_%d_%d" % (rc, cc2)])
                    col = (cc2 * nrc + rc) * 128
                    if dst_lo is None:
                        s.cv.inc(v.tensor_copy(dst[:, col:col + 128], c.PS[cc2 % 4][:, 0:128]))
                    else:
                        v.tensor_copy(dst[:, col:col + 128], c.PS[cc2 % 4][:, 0:128])
                        s.cv.inc(v.tensor_tensor(dst_lo[:, col:col + 128], c.PS[cc2 % 4][:, 0:128],
                                                 dst[:, col:col + 128], op=ALU.subtract))
                    c.m[name + "_evk%d" % (cc2 % 4)] = s.cv.n

        c.m[name + "_done"] = s.cv.n
        c.bank_free[0] = (s.cv, s.cv.n)
        c.bank_free[1] = (s.cv, s.cv.n)

    wprep("sw1", c.s_w1, 4, 8, c.S_W1T, False)
    wprep("sw2", c.s_w2, 1, 4, c.S_W2T, False)
    wprep("bw", c.B_w, 1, 8, c.B_WT, False)
    m["wprep_done"] = s.cv.n

    c.qstack = ExitStack()
    qst = lambda name, shape, dtp: c.qstack.enter_context(nc.sbuf_tensor(name, shape, dtp))
    c.WST = qst("WST", [128, NIC * 128], F32)
    c.WQF = qst("WQF", [128, NIC * 128], F32)
    c.WQB = qst("WQB", [128, NIC * 128], BF16)

    def wq_load(name, w_ap, ncc, rc, first):
        @bk.sync
        def _(e):
            if first:
                e.wait_ge(s.ct.h, m["mod_t1"])
                e.wait_ge(s.dgp.h, s.dgp.n)
            if c.m.get("wst_free") is not None:
                e.wait_ge(s.cv.h, c.m["wst_free"])
            s.dwl.inc(e.dma_start(out=c.WST[:, 0:ncc * 128],
                                  in_=w_ap[rc * 128:(rc + 1) * 128, :]), 16)

    def wq_chunk(name, w_ap, ncc, rc, sw_tile, rs_tile, scratch, first):
        @bk.vector
        def _(v):
            v.wait_ge(s.dwl.h, s.dwl.n)
            v.reduce_max(c.RMAX[:, 0:1], c.WST[:, 0:ncc * 128], AXX,
                         apply_absolute_value=True)
            v.tensor_scalar(sw_tile[:, rc:rc + 1], c.RMAX[:, 0:1], 1.0 / QMAX, None,
                            op0=ALU.mult)
            v.reciprocal(c.RMAX[:, 1:2], c.RMAX[:, 0:1])
            s.cv.inc(v.tensor_scalar(rs_tile[:, rc:rc + 1], c.RMAX[:, 1:2], QMAX, None,
                                     op0=ALU.mult))
            s.cv.inc(v.tensor_scalar(c.WQF[:, 0:ncc * 128], c.WST[:, 0:ncc * 128],
                                     rs_tile[:, rc:rc + 1], None, op0=ALU.mult))
            c.m["wst_free"] = s.cv.n
            v.tensor_scalar(c.WQF[:, 0:ncc * 128], c.WQF[:, 0:ncc * 128], MAGIC, None,
                            op0=ALU.add)
            v.wait_ge(s.dws.h, s.dws.n)
            s.cv.inc(v.tensor_scalar(c.WQB[:, 0:ncc * 128], c.WQF[:, 0:ncc * 128],
                                     -MAGIC, None, op0=ALU.add))
            c.m["wq_b"] = s.cv.n

        @bk.sync
        def _(e):
            e.wait_ge(s.cv.h, c.m["wq_b"])
            s.dws.inc(e.dma_start(out=scratch[rc * 128:(rc + 1) * 128, :],
                                  in_=c.WQB[:, 0:ncc * 128]), 16)

    c.wq_chunks = [("wdq", c.w_down, NIC, rc, c.SWD, c.RSD, c.wqd_s) for rc in range(KD)]
    c.wq_pos = 0

    c.wq_ldpos = 0

    def emit_wq_load(n):
        while c.wq_ldpos < len(c.wq_chunks) and n > 0:
            nm, w_ap, ncc, rc, swt, rst, scr = c.wq_chunks[c.wq_ldpos]
            wq_load(nm, w_ap, ncc, rc, c.wq_ldpos == 0)
            c.wq_ldpos += 1
            n -= 1
    c.emit_wq_load = emit_wq_load

    def emit_wq(n):
        while c.wq_pos < len(c.wq_chunks) and n > 0:
            emit_wq_load(c.wq_pos + 1 - c.wq_ldpos)
            nm, w_ap, ncc, rc, swt, rst, scr = c.wq_chunks[c.wq_pos]
            wq_chunk(nm, w_ap, ncc, rc, swt, rst, scr, c.wq_pos == 0)
            c.wq_pos += 1
            n -= 1
    c.emit_wq = emit_wq

    x_sl = c.x.rearrange("(r e) d -> e r d", e=NS)

    def xbuf(j):
        return c.X_TM[:, (j % 2) * SUBT * D:(j % 2 + 1) * SUBT * D]
    c.xbuf = xbuf

    def a_xload(j):
        @bk.sync
        def _(e, j=j):
            if j >= 2:
                e.wait_ge(s.cs.h, c.m["A_aff%d" % (j - 1)])
            sp = s.dxe if j % 2 == 0 else s.dxo
            src2 = x_sl[j].rearrange("(t p) d -> p t d", p=128)
            dstv = xbuf(j).rearrange("p (t d) -> p t d", d=D)
            sp.inc(e.dma_start(out=dstv, in_=src2), 16)

    def a_vprep(j, sq_last):
        @bk.vector
        def _(v, j=j, sq_last=sq_last):
            sp = s.dxe if j % 2 == 0 else s.dxo
            v.wait_ge(sp.h, sp.n)
            xb = xbuf(j)
            for t in range(SUBT):
                ci = j * 4 + t
                s.cv.inc(v.reduce_sum(c.SUMX[:, ci:ci + 1], xb[:, t * D:(t + 1) * D], AXX))
            cl = slice(j * 4, j * 4 + 4)
            v.wait_ge(s.cs.h, sq_last)
            v.tensor_scalar(c.MU[:, cl], c.SUMX[:, cl], 1.0 / D, None, op0=ALU.mult)
            v.tensor_scalar(c.STSC[:, 0:4], c.SUMSQ[:, cl], 1.0 / D, None, op0=ALU.mult)
            v.tensor_tensor(c.STSC[:, 4:8], c.MU[:, cl], c.MU[:, cl], op=ALU.mult)
            v.tensor_tensor(c.STSC[:, 0:4], c.STSC[:, 0:4], c.STSC[:, 4:8], op=ALU.subtract)
            v.tensor_scalar(c.STSC[:, 0:4], c.STSC[:, 0:4], EPS, None, op0=ALU.add)
            s.cv.inc(v.reciprocal(c.STSC[:, 4:8], c.STSC[:, 0:4]))
            v.wait_ge(s.cs.h, sq_last + 1)
            s.cv.inc(v.scalar_tensor_tensor(c.NMI[:, cl], c.MU[:, cl], -1.0, c.INV[:, cl],
                                            op0=ALU.mult, op1=ALU.mult))
            c.m["A_nmi%d" % j] = s.cv.n

    def a_sprep(j, recip_ms, nmi_ms, tp_ms):
        @bk.scalar
        def _(e, j=j):
            sp = s.dxe if j % 2 == 0 else s.dxo
            e.wait_ge(sp.h, sp.n)
            xb = xbuf(j)
            for t in range(SUBT):
                ci = j * 4 + t
                s.cs.inc(e.activation(c.S1A[:, (t % 2) * D:(t % 2 + 1) * D],
                                      xb[:, t * D:(t + 1) * D], AF.Square,
                                      accum_out=c.SUMSQ[:, ci:ci + 1]))
            cl = slice(j * 4, j * 4 + 4)
            e.wait_ge(s.cv.h, recip_ms)
            s.cs.inc(e.activation(c.INV[:, cl], c.STSC[:, 4:8], AF.Sqrt))
            e.wait_ge(s.cv.h, nmi_ms)
            if tp_ms is not None:
                e.wait_ge(s.ct.h, tp_ms)
            else:
                e.wait_ge(s.cv.h, m["mod_done"])
            for t in range(SUBT):
                ci = j * 4 + t
                s.cs.inc(e.activation(c.XN[:, t * D:(t + 1) * D], xb[:, t * D:(t + 1) * D],
                                      AF.Identity, bias=c.NMI[:, ci:ci + 1],
                                      scale=c.INV[:, ci:ci + 1]))
            c.m["A_aff%d" % j] = s.cs.n

    a_xload(0)
    _sq0 = s.cs.n + 4
    _bv0 = s.cv.n
    a_vprep(0, _sq0)
    a_sprep(0, _bv0 + 5, _bv0 + 6, None)

    for j in range(NS):
        c.emit_wq_load(1)
        bsI = s.cs.n
        btI = s.ct.n
        bvI = s.cv.n
        has_prep = (j + 1) < NS
        silu_base = bsI + 8 + (9 if has_prep else 0)
        c.m["A_silu_pred%d" % j] = silu_base + 4

        @bk.tensor
        def _(t, j=j, bsI=bsI, bvI=bvI):
            if j == 0:
                t.wait_ge(s.cv.h, m["wprep_done"])
                for _b in range(8):
                    bank_wait(t, _b)
            t.wait_ge(s.cs.h, c.m["A_aff%d" % j])
            for k in range(KD):
                if k >= 2:
                    t.wait_ge(s.cs.h, bsI + k - 1)
                b = 6 + k % 2
                for tt_ in range(SUBT):
                    ins = t.transpose(c.PS[b][:, tt_ * 128:(tt_ + 1) * 128],
                                      c.XN[:, tt_ * D + k * 128:tt_ * D + (k + 1) * 128],
                                      c.ID[:])
                s.ct.inc(ins)
            t.wait_ge(s.cs.h, bsI + 8)
            for mm in range(4):
                for k in range(KD):
                    ins = t.matmul(c.PS[mm][:, 0:R],
                                   c.S_W1T[:, (k * 4 + mm) * 128:(k * 4 + mm + 1) * 128],
                                   c.H_FM[:, k * R:(k + 1) * R], start=(k == 0), stop=(k == KD - 1))
                s.ct.inc(ins)
            t.wait_ge(s.cs.h, c.m["A_silu_pred%d" % j])
            for k2 in range(4):
                ins = t.matmul(c.PS[4][:, 0:R], c.S_W2T[:, k2 * 128:(k2 + 1) * 128],
                               c.S1A[:, k2 * R:(k2 + 1) * R], start=(k2 == 0), stop=(k2 == 3))
            s.ct.inc(ins)
            if j > 0:
                t.wait_ge(s.cv.h, c.m["A_u%d" % (j - 1)])
            for k in range(KD):
                ins = t.matmul(c.PS[5][:, 0:R], c.B_WT[:, k * 128:(k + 1) * 128],
                               c.H_FM[:, k * R:(k + 1) * R], start=(k == 0), stop=(k == KD - 1))
            s.ct.inc(ins)

        if j + 1 < NS:
            a_xload(j + 1)

        @bk.scalar
        def _(e, j=j, btI=btI):
            if c.debug and j == 1:
                e.wait_ge(s.dst.h, s.dst.n)
            for k in range(KD):
                e.wait_ge(s.ct.h, btI + k + 1)
                s.cs.inc(e.activation(c.H_FM[:, k * R:(k + 1) * R], c.PS[6 + k % 2][:, 0:R],
                                      AF.Identity, bias=c.MOD[:, k:k + 1],
                                      scale=c.ONE_PCA[:, k:k + 1]))
            c.m["A_evac%d" % j] = s.cs.n

        if j + 1 < NS:
            _sq = s.cs.n + 4
            _bv = s.cv.n
            a_vprep(j + 1, _sq)
            a_sprep(j + 1, _bv + 5, _bv + 6, btI + 8)

        @bk.scalar
        def _(e, j=j, btI=btI):
            for mm in range(4):
                e.wait_ge(s.ct.h, btI + 9 + mm)
                s.cs.inc(e.activation(c.S1A[:, mm * R:(mm + 1) * R], c.PS[mm][:, 0:R], AF.Silu,
                                      bias=c.BIAS_S1[:, mm:mm + 1]))
            e.wait_ge(s.ct.h, btI + 13)
            if j > 0:
                e.wait_ge(s.cv.h, c.m["A_u%d" % (j - 1)])
            s.cs.inc(e.activation(c.SGT[:], c.PS[4][:, 0:R], AF.Sigmoid, bias=c.BIAS_S2[:]))
            c.m["A_sgt%d" % j] = s.cs.n

        @bk.vector
        def _(v, j=j, btI=btI):
            v.wait_ge(s.ct.h, btI + 14)
            v.wait_ge(s.cs.h, c.m["A_sgt%d" % j])
            s.cv.inc(v.scalar_tensor_tensor(c.U[:, j * R:(j + 1) * R], c.PS[5][:, 0:R],
                                            c.BIAS_B[:], c.SGT[:], op0=ALU.add, op1=ALU.mult))
            c.m["A_u%d" % j] = s.cv.n
        c.m["A_bt%d" % j] = s.ct.n
        c.m["A_xnlast%d" % j] = c.m["A_aff%d" % j]

        if c.debug and j == 0:
            @bk.sync
            def _(e, j=j):
                e.wait_ge(s.cs.h, c.m["A_silu_pred%d" % j])
                s.dst.inc(e.dma_start(out=c.dbg_h, in_=c.H_FM[:]), 16)

        c.emit_wq(1)

    for _b in range(6):
        c.bank_free[_b] = (s.cv, s.cv.n)
    c.bank_free[6] = (s.cs, s.cs.n)
    c.bank_free[7] = (s.cs, s.cs.n)
    m["A_done"] = s.cv.n
    c.emit_wq(999)
    m["wqd_all"] = s.dws.n
    c.qstack.close()
    c.astack.close()
    c.scanstack = ExitStack()
    sst = lambda name, shape: c.scanstack.enter_context(nc.sbuf_tensor(name, shape, F32))
    c.MIX2 = sst("MIX2", [128, 512])
    c.MIX3 = sst("MIX3", [128, 64])
    c.H4S = sst("H4S", [128, 8])
    c.H3S = sst("H3S", [128, 64])
    c.H2SS = sst("H2SS", [128, 512])


    @bk.tensor
    def _(t):
        t.wait_ge(s.cv.h, m["A_done"])
        for b in range(8):
            bank_wait(t, b)
        for tl in range(8):
            for jj in range(tl + 1):
                lhsT = c.ID if jj == tl else c.M_[tl - jj - 1]
                ins = t.matmul(c.PS[tl][:, 0:R], lhsT[:], c.U[:, jj * R:(jj + 1) * R],
                               start=(jj == 0), stop=(jj == tl))
        s.ct.inc(ins)
        m["sc_l1"] = s.ct.n

    @bk.vector
    def _(v):
        v.wait_ge(s.ct.h, m["sc_l1"])
        for tl in range(8):
            s.cv.inc(v.tensor_copy(c.HS[:, tl * R:(tl + 1) * R], c.PS[tl][:, 0:R]))
        m["sc_l1e"] = s.cv.n

    @bk.tensor
    def _(t):
        t.wait_ge(s.cv.h, m["sc_l1e"])
        T1 = c.HS[:, 7 * R:8 * R].rearrange("p (b e) -> p e b", e=8)
        for tl in range(8):
            for jj in range(tl + 1):
                lhsT = c.ID if jj == tl else c.N_[tl - jj - 1]
                ins = t.matmul(c.PS[tl][:, 0:64], lhsT[:], T1[:, jj, :],
                               start=(jj == 0), stop=(jj == tl))
        s.ct.inc(ins)
        m["sc_l2"] = s.ct.n

    @bk.vector
    def _(v):
        v.wait_ge(s.ct.h, m["sc_l2"])
        for tl in range(8):
            s.cv.inc(v.tensor_copy(c.MIX2[:, tl * 64:(tl + 1) * 64], c.PS[tl][:, 0:64]))
        m["sc_l2e"] = s.cv.n

    @bk.tensor
    def _(t):
        t.wait_ge(s.cv.h, m["sc_l2e"])
        T2 = c.MIX2[:, 7 * 64:8 * 64].rearrange("p (b e) -> p e b", e=8)
        for tl in range(8):
            for jj in range(tl + 1):
                lhsT = c.ID if jj == tl else c.R_[tl - jj - 1]
                ins = t.matmul(c.PS[tl][:, 0:8], lhsT[:], T2[:, jj, :],
                               start=(jj == 0), stop=(jj == tl))
        s.ct.inc(ins)
        m["sc_l3"] = s.ct.n

    @bk.vector
    def _(v):
        v.wait_ge(s.ct.h, m["sc_l3"])
        for tl in range(8):
            s.cv.inc(v.tensor_copy(c.MIX3[:, tl * 8:(tl + 1) * 8], c.PS[tl][:, 0:8]))
        s.cv.inc(v.memset(c.H4S[:, 0:1], 0.0))
        m["sc_l3e"] = s.cv.n

    @bk.tensor
    def _(t):
        t.wait_ge(s.cv.h, m["sc_l3e"])
        for cc2 in range(1, 8):
            for jj in range(cc2):
                p = cc2 - 1 - jj
                lhsT = c.ID if p == 0 else c.Z_[p - 1]
                ins = t.matmul(c.PS[cc2][:, 0:1], lhsT[:], c.MIX3[:, 56 + jj:57 + jj],
                               start=(jj == 0), stop=(jj == cc2 - 1))
        s.ct.inc(ins)
        m["sc_l4"] = s.ct.n

    @bk.vector
    def _(v):
        v.wait_ge(s.ct.h, m["sc_l4"])
        for cc2 in range(1, 8):
            s.cv.inc(v.tensor_copy(c.H4S[:, cc2:cc2 + 1], c.PS[cc2][:, 0:1]))
        m["sc_h4"] = s.cv.n

    @bk.tensor
    def _(t):
        t.wait_ge(s.cv.h, m["sc_h4"])
        for tl in range(8):
            lhsT = c.ID if tl == 0 else c.R_[tl - 1]
            ins = t.matmul(c.PS[tl][:, 0:8], lhsT[:], c.H4S[:], start=True, stop=True)
        s.ct.inc(ins)
        m["sc_e3t"] = s.ct.n

    @bk.vector
    def _(v):
        v.wait_ge(s.ct.h, m["sc_e3t"])
        for tl in range(8):
            if tl == 0:
                s.cv.inc(v.tensor_copy(c.H3S[:, 0:8], c.PS[0][:, 0:8]))
            else:
                s.cv.inc(v.tensor_tensor(c.H3S[:, tl * 8:(tl + 1) * 8], c.PS[tl][:, 0:8],
                                         c.MIX3[:, (tl - 1) * 8:tl * 8], op=ALU.add))
        m["sc_e3"] = s.cv.n

    @bk.tensor
    def _(t):
        t.wait_ge(s.cv.h, m["sc_e3"])
        e3nat = c.H3S[:, 0:64].rearrange("p (t b) -> p b t", t=8)
        for tl in range(8):
            lhsT = c.ID if tl == 0 else c.N_[tl - 1]
            ins = t.matmul(c.PS[tl][:, 0:64], lhsT[:], e3nat, start=True, stop=True)
        s.ct.inc(ins)
        m["sc_e2t"] = s.ct.n

    @bk.vector
    def _(v):
        v.wait_ge(s.ct.h, m["sc_e2t"])
        for tl in range(8):
            if tl == 0:
                s.cv.inc(v.tensor_copy(c.H2SS[:, 0:64], c.PS[0][:, 0:64]))
            else:
                s.cv.inc(v.tensor_tensor(c.H2SS[:, tl * 64:(tl + 1) * 64], c.PS[tl][:, 0:64],
                                         c.MIX2[:, (tl - 1) * 64:tl * 64], op=ALU.add))
        m["sc_e2"] = s.cv.n

    @bk.tensor
    def _(t):
        t.wait_ge(s.cv.h, m["sc_e2"])
        e2nat = c.H2SS[:, 0:512].rearrange("p (t b) -> p b t", t=8)
        for tl in range(8):
            ins = t.matmul(c.PS[tl][:, 0:R], c.M_[tl][:], e2nat, start=True, stop=True)
        s.ct.inc(ins)
        m["sc_corrt"] = s.ct.n

    @bk.vector
    def _(v):
        v.wait_ge(s.ct.h, m["sc_corrt"])
        for tl in range(8):
            s.cv.inc(v.tensor_tensor(c.HS[:, tl * R:(tl + 1) * R],
                                     c.HS[:, tl * R:(tl + 1) * R], c.PS[tl][:, 0:R], op=ALU.add))
        m["sc_done"] = s.cv.n
    for k in range(8):
        c.bank_free[k] = (s.cv, m["sc_done"])

    if c.debug:
        @bk.sync
        def _(e):
            e.wait_ge(s.cv.h, m["A_done"])
            s.dst.inc(e.dma_start(out=c.dbg_u, in_=c.U[:]), 16)

        @bk.sync
        def _(e):
            e.wait_ge(s.cv.h, m["sc_done"])
            s.dst.inc(e.dma_start(out=c.dbg_hs, in_=c.HS[:]), 16)

    if upto == "A":
        _finish(c, bk)
        return

    c.scanstack.close()
    c.powstack.close()
    c.cdgstack = ExitStack()
    cst = lambda name, shape, dtp: c.cdgstack.enter_context(nc.sbuf_tensor(name, shape, dtp))
    c.C_WT = cst("C_WT", [128, 8 * 128], F32)
    c.D_HI = cst("D_HI", [128, 64 * 128], BF16)
    c.D_LO = cst("D_LO", [128, 64 * 128], BF16)
    c.G_HI = cst("G_HI", [128, 64 * 128], BF16)
    c.G_LO = cst("G_LO", [128, 64 * 128], BF16)
    c.WLD = cst("WLDC", [128, D], F32)
    c.WSTC = cst("WSTC", [128, D], F32)
    c.WQFC = cst("WQFC", [128, D], F32)
    c.WQBC = cst("WQBC", [128, 2 * D], BF16)
    c.TBNC = cst("TBNC", [128, KD * 128], BF16)

    @bk.sync
    def _(e):
        e.wait_ge(s.ct.h, m["sc_corrt"])
        e.wait_ge(s.cv.h, m["sc_done"])
        e.wait_ge(s.dws.h, s.dws.n)
        e.wait_ge(s.dwl.h, s.dwl.n)

    @bk.vector
    def _(v):
        v.wait_ge(s.dws.h, s.dws.n)
        v.wait_ge(s.dwl.h, s.dwl.n)
        v.wait_ge(s.cs.h, s.cs.n)
    c.m["wld_free"] = m["sc_corrt"]

    wprep("cw", c.C_w, 8, 1, c.C_WT, True)
    wprep("dw", c.D_w, 8, 8, c.D_HI, True, dst_lo=c.D_LO)
    wprep("gw", c.g_w, 8, 8, c.G_HI, False, dst_lo=c.G_LO)
    m["wprepC_done"] = s.cv.n

    H2FM = c.U
    c.bankC = list(c.bank_free)
    c.bank_free = [None] * 8

    def cbw(t, b):
        e_ = c.bankC[b]
        if e_ is not None:
            t.wait_ge(e_[0].h, e_[1])
            c.bankC[b] = None

    c.wq2_chunks = ([(c.w_gate, rc, c.SWG, c.RSG, c.wqg_s) for rc in range(NIC)]
                    + [(c.w_up, rc, c.SWU, c.RSU, c.wqu_s) for rc in range(NIC)])
    c.wq2_pos = 0

    def wq2_batch(qs):
        for bi, q in enumerate(qs):
            @bk.sync
            def _(e, q=q):
                w_ap, rc, sw, rs, scr = c.wq2_chunks[q]
                if q >= 1:
                    e.wait_ge(s.cv.h, c.m["wq2_round%d" % (q - 1)])
                e.wait_ge(s.dwl.h, s.dwl.n)
                s.dwl.inc(e.dma_start(out=c.WSTC[:],
                                      in_=w_ap[rc * 128:(rc + 1) * 128, :]), 16)

            @bk.vector
            def _(v, q=q):
                w_ap, rc, sw, rs, scr = c.wq2_chunks[q]
                v.wait_ge(s.dwl.h, s.dwl.n)
                v.reduce_max(c.RMAX[:, 0:1], c.WSTC[:], AXX, apply_absolute_value=True)
                v.tensor_scalar(sw[:, rc:rc + 1], c.RMAX[:, 0:1], 1.0 / QMAX, None,
                                op0=ALU.mult)
                v.reciprocal(c.RMAX[:, 1:2], c.RMAX[:, 0:1])
                s.cv.inc(v.tensor_scalar(rs[:, rc:rc + 1], c.RMAX[:, 1:2], QMAX, None,
                                         op0=ALU.mult))
                s.cv.inc(v.tensor_scalar(c.WQFC[:], c.WSTC[:], rs[:, rc:rc + 1], None,
                                         op0=ALU.mult))
                c.m["wq2_round%d" % q] = s.cv.n
                v.tensor_scalar(c.WQFC[:], c.WQFC[:], MAGIC, None, op0=ALU.add)
                if q >= 2:
                    v.wait_ge(s.dws.h, c.m["wq2_st%d" % (q - 2)])
                s.cv.inc(v.tensor_scalar(c.WQBC[:, (q % 2) * D:(q % 2 + 1) * D],
                                         c.WQFC[:], -MAGIC, None, op0=ALU.add))
                c.m["wq2_ub%d" % q] = s.cv.n

        @bk.sync
        def _(e, qs=qs):
            e.wait_ge(s.dws.h, s.dws.n)
            for q in qs:
                w_ap, rc, sw, rs, scr = c.wq2_chunks[q]
                e.wait_ge(s.cv.h, c.m["wq2_ub%d" % q])
                s.dws.inc(e.dma_start(out=scr[rc * 128:(rc + 1) * 128, :],
                                      in_=c.WQBC[:, (q % 2) * D:(q % 2 + 1) * D]), 16)
            for q in qs:
                c.m["wq2_st%d" % q] = s.dws.n

    def emit_wq2(n):
        while c.wq2_pos < len(c.wq2_chunks) and n > 0:
            take = min(2, n, len(c.wq2_chunks) - c.wq2_pos)
            wq2_batch(list(range(c.wq2_pos, c.wq2_pos + take)))
            c.wq2_pos += take
            n -= take

    c.bnc_pos = 0

    def emit_bounce(n):
        while c.bnc_pos < NIC and n > 0:
            ic = c.bnc_pos

            @bk.sync
            def _(e, ic=ic):
                e.wait_ge(s.dws.h, m["wqd_all"])
                if ic >= 1:
                    e.wait_ge(s.dws.h, c.m["bnc_st%d" % (ic - 1)])
                e.wait_ge(s.dw3.h, s.dw3.n)
                s.dw3.inc(e.dma_start(out=c.TBNC[:],
                                      in_=c.wqd_s[:, ic * 128:(ic + 1) * 128],
                                      transpose=True), 16)
                e.wait_ge(s.dw3.h, s.dw3.n)
                e.wait_ge(s.dws.h, s.dws.n)
                s.dws.inc(e.dma_start(out=c.wqd_t[ic * 128:(ic + 1) * 128, :],
                                      in_=c.TBNC[:]), 16)
                c.m["bnc_st%d" % ic] = s.dws.n
            c.bnc_pos += 1
            n -= 1

    @bk.sync
    def _(e):
        e.wait_ge(s.cs.h, c.m["A_xnlast%d" % (NS - 1)])
        src2 = x_sl[0].rearrange("(t p) d -> p t d", p=128)
        dstv = c.xbuf(0).rearrange("p (t d) -> p t d", d=D)
        s.dxe.inc(e.dma_start(out=dstv, in_=src2), 16)

    c.mmrot = 0
    for j in range(NS):
        TAIL = j >= 1
        bsI = s.cs.n
        btI = s.ct.n
        bvI = s.cv.n
        offA = bsI + 4 + (8 if TAIL else 0)
        offB = offA + 8
        sq_last_p = offB + 4
        sqrt_p = offB + 5
        aff2_last_p = offB + 9
        yev_base = offB + (9 if TAIL else 0)
        h2ev_base = yev_base + 8
        t1tp_base = btI + 8
        y_base = t1tp_base + (8 if TAIL else 0)
        h2tp_base = y_base + 8
        hsplit_p = bvI + (8 if TAIL else 0) + 1
        x2add_base = hsplit_p

        @bk.scalar
        def _(e, j=j, TAIL=TAIL):
            sp = s.dxe if j % 2 == 0 else s.dxo
            e.wait_ge(sp.h, sp.n)
            if j == 0:
                e.wait_ge(s.cv.h, m["sc_done"])
                e.wait_ge(s.ct.h, m["sc_corrt"])
            elif j == 1:
                e.wait_ge(s.ct.h, c.m["C_tpL0"])
            else:
                e.wait_ge(s.ct.h, c.m["C_h2tpL%d" % (j - 2)])
            for t in range(SUBT):
                ci = j * 4 + t
                s.cs.inc(e.activation(c.XN[:, t * D:(t + 1) * D],
                                      c.xbuf(j)[:, t * D:(t + 1) * D], AF.Identity,
                                      bias=c.NMI[:, ci:ci + 1], scale=c.INV[:, ci:ci + 1]))
            c.m["C_aff1_%d" % j] = s.cs.n
            if TAIL:
                if j == 1:
                    if c.debug:
                        e.wait_ge(s.dst.h, s.dst.n)
                else:
                    e.wait_ge(s.cv.h, c.m["C_h2maxL%d" % (j - 2)])
                    e.wait_ge(s.dsc.h, c.m["C_h2st%d" % (j - 2)])
                for mc, b_, ctv in c.gch:
                    e.wait_ge(s.ct.h, ctv)
                    s.cs.inc(e.activation(H2FM[:, mc * R:(mc + 1) * R], c.PS[b_][:, 0:R],
                                          AF.Sigmoid, bias=c.BIAS_G[:, mc:mc + 1]))
                c.m["C_gevL%d" % (j - 1)] = s.cs.n
        assert s.cs.n == offA

        if TAIL:
            @bk.vector
            def _(v, j=j, bsI=bsI):
                for k in range(KD):
                    v.wait_ge(s.cs.h, bsI + 4 + k + 1)
                    gt = H2FM[:, k * R:(k + 1) * R]
                    s.cv.inc(v.tensor_tensor(gt, gt, c.H_FM[:, k * R:(k + 1) * R],
                                             op=ALU.mult))
                c.m["C_t1L%d" % (j - 1)] = s.cv.n

        @bk.tensor
        def _(t, j=j, TAIL=TAIL, bvI=bvI, offA=offA, x2add_base=x2add_base):
            t.wait_ge(s.cs.h, c.m["C_aff1_%d" % j])
            for k in range(KD):
                b = 6 + k % 2
                if k < 2:
                    if j == 0:
                        cbw(t, b)
                    else:
                        t.wait_ge(s.cs.h, c.m["C_gevL%d" % (j - 1)])
                else:
                    t.wait_ge(s.cs.h, offA + (k - 2) + 1)
                for tt_ in range(SUBT):
                    ins = t.transpose(c.PS[b][:, tt_ * 128:(tt_ + 1) * 128],
                                      c.XN[:, tt_ * D + k * 128:tt_ * D + (k + 1) * 128],
                                      c.ID[:])
                s.ct.inc(ins)
            c.m["C_tpL%d" % j] = s.ct.n
            if TAIL:
                for k in range(KD):
                    b = 6 + k % 2
                    t.wait_ge(s.cv.h, bvI + k + 1)
                    if k < 2:
                        t.wait_ge(s.cs.h, offA + (6 + k) + 1)
                    else:
                        t.wait_ge(s.cv.h, x2add_base + (k - 2) + 1)
                    for tt_ in range(SUBT):
                        ins = t.transpose(c.PS[b][:, tt_ * 128:(tt_ + 1) * 128],
                                          H2FM[:, k * R + tt_ * 128:k * R + (tt_ + 1) * 128],
                                          c.ID[:])
                    s.ct.inc(ins)
                c.m["C_t1tpL%d" % (j - 1)] = s.ct.n
        assert s.ct.n == t1tp_base + (8 if TAIL else 0)

        @bk.vector
        def _(v, j=j, TAIL=TAIL, offA=offA, sq_last_p=sq_last_p, sqrt_p=sqrt_p,
              t1tp_base=t1tp_base):
            v.wait_ge(s.cs.h, offA + 8)
            if TAIL:
                v.wait_ge(s.ct.h, c.m["C_gL%d" % (j - 1)])
            v.tensor_copy(c.H_HI[:], c.H_FM[:])
            s.cv.inc(v.tensor_tensor(c.H_LO[:], c.H_FM[:], c.H_HI[:], op=ALU.subtract))
            c.m["C_hsplit%d" % j] = s.cv.n
            if TAIL:
                for k in range(KD):
                    v.wait_ge(s.ct.h, t1tp_base + k + 1)
                    src = c.PS[6 + k % 2][:, 0:SUBT * 128].rearrange("p (t q) -> p t q", q=128)
                    dst = c.xbuf(j - 1).rearrange("p (t d) -> p t d", d=D)[:, :, k * 128:(k + 1) * 128]
                    s.cv.inc(v.tensor_tensor(dst, dst, src, op=ALU.add))
                for t in range(SUBT):
                    ci = (j - 1) * 4 + t
                    s.cv.inc(v.reduce_sum(c.SUMX[:, ci:ci + 1],
                                          c.xbuf(j - 1)[:, t * D:(t + 1) * D], AXX))
                cl = slice((j - 1) * 4, (j - 1) * 4 + 4)
                v.wait_ge(s.cs.h, sq_last_p)
                v.tensor_scalar(c.MU2[:, cl], c.SUMX[:, cl], 1.0 / D, None, op0=ALU.mult)
                v.tensor_scalar(c.STSC[:, 0:4], c.SUMSQ[:, cl], 1.0 / D, None, op0=ALU.mult)
                v.tensor_tensor(c.STSC[:, 4:8], c.MU2[:, cl], c.MU2[:, cl], op=ALU.mult)
                v.tensor_tensor(c.STSC[:, 0:4], c.STSC[:, 0:4], c.STSC[:, 4:8], op=ALU.subtract)
                v.tensor_scalar(c.STSC[:, 0:4], c.STSC[:, 0:4], EPS, None, op0=ALU.add)
                s.cv.inc(v.reciprocal(c.STSC[:, 4:8], c.STSC[:, 0:4]))
                v.wait_ge(s.cs.h, sqrt_p)
                s.cv.inc(v.scalar_tensor_tensor(c.NMI2[:, cl], c.MU2[:, cl], -1.0,
                                                c.INV2[:, cl], op0=ALU.mult, op1=ALU.mult))
                c.m["C_nmiL%d" % (j - 1)] = s.cv.n
        assert s.cv.n == x2add_base + (14 if TAIL else 0)

        @bk.tensor
        def _(t, j=j, TAIL=TAIL, yev_base=yev_base, h2ev_base=h2ev_base, offA=offA,
              aff2_last_p=aff2_last_p, x2add_base=x2add_base):
            t.wait_ge(s.cv.h, c.m["C_hsplit%d" % j])
            if j == 0:
                t.wait_ge(s.cv.h, m["wprepC_done"])
            c.ych = []
            for i in range(8):
                b_ = (c.mmrot + i) % 6
                if i < 6:
                    if TAIL:
                        t.wait_ge(s.cs.h, c.gevpos[i])
                    else:
                        cbw(t, b_)
                else:
                    t.wait_ge(s.cs.h, yev_base + (i - 6) + 1)
                t.matmul(c.PS[b_][:, 0:R], c.C_WT[:, i * 128:(i + 1) * 128],
                         c.HS[:, j * R:(j + 1) * R], start=True, stop=False)
                for k in range(KD):
                    col = (k * 8 + i) * 128
                    t.matmul(c.PS[b_][:, 0:R], c.D_HI[:, col:col + 128],
                             c.H_HI[:, k * R:(k + 1) * R], start=False, stop=False)
                    t.matmul(c.PS[b_][:, 0:R], c.D_LO[:, col:col + 128],
                             c.H_HI[:, k * R:(k + 1) * R], start=False, stop=False)
                    ins = t.matmul(c.PS[b_][:, 0:R], c.D_HI[:, col:col + 128],
                                   c.H_LO[:, k * R:(k + 1) * R], start=False, stop=(k == KD - 1))
                s.ct.inc(ins)
                c.ych.append((i, b_, s.ct.n))
            if TAIL:
                t.wait_ge(s.cs.h, aff2_last_p)
                for k in range(KD):
                    b = 6 + k % 2
                    if k < 2:
                        t.wait_ge(s.cv.h, x2add_base + (6 + k) + 1)
                    else:
                        t.wait_ge(s.cs.h, h2ev_base + (k - 2) + 1)
                    for tt_ in range(SUBT):
                        ins = t.transpose(c.PS[b][:, tt_ * 128:(tt_ + 1) * 128],
                                          c.XN[:, tt_ * D + k * 128:tt_ * D + (k + 1) * 128],
                                          c.ID[:])
                    s.ct.inc(ins)
                c.m["C_h2tpL%d" % (j - 1)] = s.ct.n
            c.gch = []
            for i in range(8):
                if i < 6:
                    b_ = (c.mmrot + 8 + i) % 6
                    t.wait_ge(s.cs.h, yev_base + (i + 2) + 1)
                else:
                    b_ = i
                    if TAIL:
                        t.wait_ge(s.cs.h, h2ev_base + i + 1)
                    else:
                        t.wait_ge(s.cs.h, offA + (i + 1))
                for k in range(KD):
                    col = (k * 8 + i) * 128
                    t.matmul(c.PS[b_][:, 0:R], c.G_HI[:, col:col + 128],
                             c.H_HI[:, k * R:(k + 1) * R], start=(k == 0), stop=False)
                    t.matmul(c.PS[b_][:, 0:R], c.G_LO[:, col:col + 128],
                             c.H_HI[:, k * R:(k + 1) * R], start=False, stop=False)
                    ins = t.matmul(c.PS[b_][:, 0:R], c.G_HI[:, col:col + 128],
                                   c.H_LO[:, k * R:(k + 1) * R], start=False, stop=(k == KD - 1))
                s.ct.inc(ins)
                c.gch.append((i, b_, s.ct.n))
            c.mmrot += 14
            c.m["C_gL%d" % j] = s.ct.n

        @bk.scalar
        def _(e, j=j, TAIL=TAIL, btI=btI, x2add_base=x2add_base):
            if TAIL:
                e.wait_ge(s.cv.h, c.m["C_t1L%d" % (j - 1)])
            for k in range(KD):
                e.wait_ge(s.ct.h, btI + k + 1)
                s.cs.inc(e.activation(c.H_FM[:, k * R:(k + 1) * R], c.PS[6 + k % 2][:, 0:R],
                                      AF.Identity, bias=c.MOD[:, k:k + 1],
                                      scale=c.ONE_PCA[:, k:k + 1]))
            c.m["C_hevL%d" % j] = s.cs.n
            if TAIL:
                e.wait_ge(s.cv.h, x2add_base + 8)
                e.wait_ge(s.dsc.h, s.dsc.n)
                s.dsc.inc(e.dma_start(out=c.x2_s[j - 1], in_=c.xbuf(j - 1)), 16)
                c.m["C_x2st%d" % (j - 1)] = s.dsc.n
                for t in range(SUBT):
                    ci = (j - 1) * 4 + t
                    s.cs.inc(e.activation(c.H_FM[:, (t % 2) * D:(t % 2 + 1) * D],
                                          c.xbuf(j - 1)[:, t * D:(t + 1) * D], AF.Square,
                                          accum_out=c.SUMSQ[:, ci:ci + 1]))
                cl = slice((j - 1) * 4, (j - 1) * 4 + 4)
                e.wait_ge(s.cv.h, x2add_base + 13)
                s.cs.inc(e.activation(c.INV2[:, cl], c.STSC[:, 4:8], AF.Sqrt))
                e.wait_ge(s.cv.h, x2add_base + 14)
                e.wait_ge(s.ct.h, c.m["C_tpL%d" % j])
                for t in range(SUBT):
                    ci = (j - 1) * 4 + t
                    s.cs.inc(e.activation(c.XN[:, t * D:(t + 1) * D],
                                          c.xbuf(j - 1)[:, t * D:(t + 1) * D], AF.Identity,
                                          bias=c.NMI2[:, ci:ci + 1], scale=c.INV2[:, ci:ci + 1]))
                c.m["C_aff2L%d" % (j - 1)] = s.cs.n
            e.wait_ge(s.cv.h, c.m["C_hsplit%d" % j])
            for mc, b_, ctv in c.ych:
                e.wait_ge(s.ct.h, ctv)
                s.cs.inc(e.activation(c.H_FM[:, mc * R:(mc + 1) * R], c.PS[b_][:, 0:R],
                                      AF.Identity, bias=c.BCD[:, mc:mc + 1]))
            c.m["C_yevL%d" % j] = s.cs.n
            if TAIL:
                for k in range(KD):
                    e.wait_ge(s.ct.h, c.m["C_h2tpL%d" % (j - 1)] - 8 + k + 1)
                    s.cs.inc(e.activation(H2FM[:, k * R:(k + 1) * R], c.PS[6 + k % 2][:, 0:R],
                                          AF.Identity, bias=c.MOD[:, 24 + k:24 + k + 1],
                                          scale=c.ONE_PCB[:, k:k + 1]))
                c.m["C_h2evL%d" % (j - 1)] = s.cs.n
                e.wait_ge(s.dsc.h, s.dsc.n)
                s.dsc.inc(e.dma_start(out=c.h2_s[j - 1], in_=H2FM[:]), 16)
                c.m["C_h2st%d" % (j - 1)] = s.dsc.n
        assert s.cs.n == h2ev_base + (8 if TAIL else 0), (j, s.cs.n, h2ev_base)
        assert s.ct.n == h2tp_base + (8 if TAIL else 0) + 8
        c.gevpos = [s.cs.n + 4 + i + 1 for i in range(8)]

        if TAIL:
            @bk.vector
            def _(v, j=j, h2ev_base=h2ev_base):
                for k in range(KD):
                    v.wait_ge(s.cs.h, h2ev_base + k + 1)
                    s.cv.inc(v.reduce_max(c.H2MAX[:, (j - 1) * 8 + k:(j - 1) * 8 + k + 1],
                                          H2FM[:, k * R:(k + 1) * R], AXX,
                                          apply_absolute_value=True))
                c.m["C_h2maxL%d" % (j - 1)] = s.cv.n

        if j + 1 < NS:
            @bk.sync
            def _(e, j=j, TAIL=TAIL):
                if TAIL:
                    e.wait_ge(s.cs.h, c.m["C_aff2L%d" % (j - 1)])
                    e.wait_ge(s.cv.h, c.m["C_nmiL%d" % (j - 1)])
                    e.wait_ge(s.dsc.h, c.m["C_x2st%d" % (j - 1)])
                else:
                    e.wait_ge(s.cs.h, c.m["A_xnlast%d" % (NS - 1)])
                sp = s.dxe if (j + 1) % 2 == 0 else s.dxo
                src2 = x_sl[j + 1].rearrange("(t p) d -> p t d", p=128)
                dstv = c.xbuf(j + 1).rearrange("p (t d) -> p t d", d=D)
                sp.inc(e.dma_start(out=dstv, in_=src2), 16)
        emit_wq2(6)
        emit_bounce(3)

    bsE = s.cs.n
    btE = s.ct.n
    bvE = s.cv.n

    @bk.scalar
    def _(e):
        e.wait_ge(s.cv.h, c.m["C_h2maxL%d" % (NS - 2)])
        e.wait_ge(s.dsc.h, c.m["C_h2st%d" % (NS - 2)])
        for mc, b_, ctv in c.gch:
            e.wait_ge(s.ct.h, ctv)
            s.cs.inc(e.activation(H2FM[:, mc * R:(mc + 1) * R], c.PS[b_][:, 0:R],
                                  AF.Sigmoid, bias=c.BIAS_G[:, mc:mc + 1]))
        c.m["C_gevL%d" % (NS - 1)] = s.cs.n

    @bk.vector
    def _(v):
        for k in range(KD):
            v.wait_ge(s.cs.h, bsE + k + 1)
            gt = H2FM[:, k * R:(k + 1) * R]
            s.cv.inc(v.tensor_tensor(gt, gt, c.H_FM[:, k * R:(k + 1) * R], op=ALU.mult))
        c.m["C_t1L%d" % (NS - 1)] = s.cv.n

    @bk.tensor
    def _(t):
        for k in range(KD):
            b = 6 + k % 2
            t.wait_ge(s.cv.h, bvE + k + 1)
            if k < 2:
                t.wait_ge(s.cs.h, c.m["C_gevL%d" % (NS - 1)])
            else:
                t.wait_ge(s.cv.h, bvE + 8 + (k - 2) + 1)
            for tt_ in range(SUBT):
                ins = t.transpose(c.PS[b][:, tt_ * 128:(tt_ + 1) * 128],
                                  H2FM[:, k * R + tt_ * 128:k * R + (tt_ + 1) * 128],
                                  c.ID[:])
            s.ct.inc(ins)

    @bk.vector
    def _(v):
        for k in range(KD):
            v.wait_ge(s.ct.h, btE + k + 1)
            src = c.PS[6 + k % 2][:, 0:SUBT * 128].rearrange("p (t q) -> p t q", q=128)
            dst = c.xbuf(NS - 1).rearrange("p (t d) -> p t d", d=D)[:, :, k * 128:(k + 1) * 128]
            s.cv.inc(v.tensor_tensor(dst, dst, src, op=ALU.add))
        for t in range(SUBT):
            ci = (NS - 1) * 4 + t
            s.cv.inc(v.reduce_sum(c.SUMX[:, ci:ci + 1],
                                  c.xbuf(NS - 1)[:, t * D:(t + 1) * D], AXX))
        cl = slice((NS - 1) * 4, (NS - 1) * 4 + 4)
        v.wait_ge(s.cs.h, bsE + 8 + 4)
        v.tensor_scalar(c.MU2[:, cl], c.SUMX[:, cl], 1.0 / D, None, op0=ALU.mult)
        v.tensor_scalar(c.STSC[:, 0:4], c.SUMSQ[:, cl], 1.0 / D, None, op0=ALU.mult)
        v.tensor_tensor(c.STSC[:, 4:8], c.MU2[:, cl], c.MU2[:, cl], op=ALU.mult)
        v.tensor_tensor(c.STSC[:, 0:4], c.STSC[:, 0:4], c.STSC[:, 4:8], op=ALU.subtract)
        v.tensor_scalar(c.STSC[:, 0:4], c.STSC[:, 0:4], EPS, None, op0=ALU.add)
        s.cv.inc(v.reciprocal(c.STSC[:, 4:8], c.STSC[:, 0:4]))
        v.wait_ge(s.cs.h, bsE + 8 + 5)
        s.cv.inc(v.scalar_tensor_tensor(c.NMI2[:, cl], c.MU2[:, cl], -1.0, c.INV2[:, cl],
                                        op0=ALU.mult, op1=ALU.mult))

    @bk.scalar
    def _(e):
        e.wait_ge(s.cv.h, bvE + 16)
        e.wait_ge(s.dsc.h, s.dsc.n)
        s.dsc.inc(e.dma_start(out=c.x2_s[NS - 1], in_=c.xbuf(NS - 1)), 16)
        c.m["C_x2st%d" % (NS - 1)] = s.dsc.n
        for t in range(SUBT):
            ci = (NS - 1) * 4 + t
            s.cs.inc(e.activation(c.H_FM[:, (t % 2) * D:(t % 2 + 1) * D],
                                  c.xbuf(NS - 1)[:, t * D:(t + 1) * D], AF.Square,
                                  accum_out=c.SUMSQ[:, ci:ci + 1]))
        cl = slice((NS - 1) * 4, (NS - 1) * 4 + 4)
        e.wait_ge(s.cv.h, bvE + 21)
        s.cs.inc(e.activation(c.INV2[:, cl], c.STSC[:, 4:8], AF.Sqrt))
        e.wait_ge(s.cv.h, bvE + 22)
        for t in range(SUBT):
            ci = (NS - 1) * 4 + t
            s.cs.inc(e.activation(c.XN[:, t * D:(t + 1) * D],
                                  c.xbuf(NS - 1)[:, t * D:(t + 1) * D], AF.Identity,
                                  bias=c.NMI2[:, ci:ci + 1], scale=c.INV2[:, ci:ci + 1]))
        c.m["C_aff2L%d" % (NS - 1)] = s.cs.n
    assert s.cs.n == bsE + 17

    @bk.tensor
    def _(t):
        t.wait_ge(s.cs.h, c.m["C_aff2L%d" % (NS - 1)])
        for k in range(KD):
            b = 6 + k % 2
            if k < 2:
                t.wait_ge(s.cv.h, bvE + 8 + (6 + k) + 1)
            else:
                t.wait_ge(s.cs.h, bsE + 17 + (k - 2) + 1)
            for tt_ in range(SUBT):
                ins = t.transpose(c.PS[b][:, tt_ * 128:(tt_ + 1) * 128],
                                  c.XN[:, tt_ * D + k * 128:tt_ * D + (k + 1) * 128],
                                  c.ID[:])
            s.ct.inc(ins)
        c.m["C_h2tpL%d" % (NS - 1)] = s.ct.n

    @bk.scalar
    def _(e):
        for k in range(KD):
            e.wait_ge(s.ct.h, btE + 8 + k + 1)
            s.cs.inc(e.activation(H2FM[:, k * R:(k + 1) * R], c.PS[6 + k % 2][:, 0:R],
                                  AF.Identity, bias=c.MOD[:, 24 + k:24 + k + 1],
                                  scale=c.ONE_PCB[:, k:k + 1]))
        c.m["C_h2evL%d" % (NS - 1)] = s.cs.n
        e.wait_ge(s.dsc.h, s.dsc.n)
        s.dsc.inc(e.dma_start(out=c.h2_s[NS - 1], in_=H2FM[:]), 16)
        c.m["C_h2st%d" % (NS - 1)] = s.dsc.n

    @bk.vector
    def _(v):
        for k in range(KD):
            v.wait_ge(s.cs.h, bsE + 17 + k + 1)
            s.cv.inc(v.reduce_max(c.H2MAX[:, (NS - 1) * 8 + k:(NS - 1) * 8 + k + 1],
                                  H2FM[:, k * R:(k + 1) * R], AXX,
                                  apply_absolute_value=True))
        c.m["C_h2maxL%d" % (NS - 1)] = s.cv.n

    c.bank_free = [(s.cs, s.cs.n) for _ in range(8)]
    m["C_done_v"] = s.cv.n
    m["C_done_t"] = s.ct.n
    m["C_done_s"] = s.cs.n
    m["C_done_st"] = s.dsc.n

    if upto == "C":
        _finish(c, bk)
        return

    @bk.vector
    def _(v):
        v.wait_ge(s.cv.h, m["C_done_v"])
        s.cv.inc(v.reduce_max(c.REDT[:, 0:1], c.H2MAX[:], AXX))
        m["sx_red"] = s.cv.n

    @bk.tensor
    def _(t):
        t.wait_ge(s.cv.h, m["sx_red"])
        bank_wait(t, 0)
        s.ct.inc(t.transpose(c.PS[0][0:1, 0:128], c.REDT[:, 0:1], c.ID[:]))
        m["sx_tp"] = s.ct.n

    @bk.vector
    def _(v):
        v.wait_ge(s.ct.h, m["sx_tp"])
        v.memset(c.LMX[:], 0.0)
        s.cv.inc(v.reduce_max(c.LMX[0:1, 0:1], c.PS[0][0:1, 0:128], AXX))
        m["sx_loc"] = s.cv.n
    c.bank_free[0] = (s.cv, m["sx_loc"])

    @bk.gpsimd
    def _(g):
        g.wait_ge(s.cv.h, m["sx_loc"])
        s.dgp.inc(g.dma_start(out=c.cc_m1_in, in_=c.LMX[:]), 16)
        g.wait_ge(s.dgp.h, s.dgp.n)
        s.cc.inc(g.collective_compute("AllReduce", ALU.max,
                                      replica_groups=[list(range(NCORE))],
                                      ins=[c.cc_m1_in], outs=[c.cc_m1_out]))
        g.wait_ge(s.cc.h, s.cc.n)
        s.dgp.inc(g.dma_start(out=c.GMX[:], in_=c.cc_m1_out), 16)
        m["sx_gmax"] = s.dgp.n

    @bk.scalar
    def _(e):
        e.wait_ge(s.dx.h, s.dx.n)
        e.activation(c.CLIPX[0:1, 1:2], c.CLIPX[0:1, 0:1], AF.Sigmoid)
        s.cs.inc(e.activation(c.CLIPD[0:1, 1:2], c.CLIPD[0:1, 0:1], AF.Sigmoid))
        m["clip_sig"] = s.cs.n

    @bk.vector
    def _(v):
        v.wait_ge(s.dgp.h, s.dgp.n)
        v.wait_ge(s.cs.h, m["clip_sig"])
        v.tensor_tensor(c.SCL[0:1, 0:1], c.GMX[0:1, 0:1], c.CLIPX[0:1, 1:2], op=ALU.mult)
        v.tensor_scalar(c.SCL[0:1, 0:1], c.SCL[0:1, 0:1], 1.0 / QMAX, None, op0=ALU.mult)
        s.cv.inc(v.reciprocal(c.SCL[0:1, 1:2], c.SCL[0:1, 0:1]))
        m["sx_val"] = s.cv.n

    @bk.tensor
    def _(t):
        t.wait_ge(s.cv.h, m["sx_val"])
        bank_wait(t, 1)
        t.matmul(c.PS[1][:, 0:1], c.ONES[:], c.SCL[0:1, 0:1], start=True, stop=True)
        s.ct.inc(t.matmul(c.PS[1][:, 1:2], c.ONES[:], c.SCL[0:1, 1:2], start=True, stop=True))
        m["sx_bc"] = s.ct.n

    @bk.vector
    def _(v):
        v.wait_ge(s.ct.h, m["sx_bc"])
        v.tensor_copy(c.SXB[:], c.PS[1][:, 0:1])
        s.cv.inc(v.tensor_copy(c.ISXB[:], c.PS[1][:, 1:2]))
        v.wait_ge(s.dw.h, s.dw.n)
        v.tensor_scalar(c.SXG[:], c.SWG[:], c.SXB[:, 0:1], None, op0=ALU.mult)
        s.cv.inc(v.tensor_scalar(c.SXU[:], c.SWU[:], c.SXB[:, 0:1], None, op0=ALU.mult))
        m["sx_done"] = s.cv.n
    c.bank_free[1] = (s.cv, m["sx_done"])

    if c.debug:
        @bk.sync
        def _(e):
            e.wait_ge(s.cv.h, m["sx_done"])
            s.dst.inc(e.dma_start(out=c.dbg_sx, in_=c.SCL[0:1, :]), 16)

    if upto == "SX":
        _finish(c, bk)
        return

    c.cdgstack.close()
    c.stage1.close()
    c.p2stack = ExitStack()
    sb2 = lambda name, shape, dtp: c.p2stack.enter_context(nc.sbuf_tensor(name, shape, dtp))
    c.H2RD = sb2("H2RD", [128, 2 * KD * R], F32)
    c.XQF = sb2("XQF", [128, KD * R], F32)
    c.XQB = sb2("XQB", [128, 2 * KD * R], BF16)
    c.SGW = sb2("SGW", [128, 2 * R], F32)
    c.VT = sb2("VT", [128, NIC * R], F32)
    c.WGT = sb2("WGT", [128, NIC * KD * 128], BF16)
    c.WUT = sb2("WUT", [128, NIC * KD * 128], BF16)

    @bk.sync
    def _(e):
        e.wait_ge(s.ct.h, c.m["C_gL%d" % (NS - 1)])
        e.wait_ge(s.dws.h, c.m["wq2_st%d" % (NIC - 1)])
        e.wait_ge(s.dw.h, s.dw.n)
        for k in range(KD):
            s.dw.inc(e.dma_start(out=c.WGT[:, k * NIC * 128:(k + 1) * NIC * 128],
                                 in_=c.wqg_s[:, k * 128:(k + 1) * 128], transpose=True), 16)
        m["p2_wg"] = s.dw.n

    @bk.sync
    def _(e):
        e.wait_ge(s.dws.h, s.dws.n)
        e.wait_ge(s.dwl.h, s.dwl.n)
        for k in range(KD):
            s.dwl.inc(e.dma_start(out=c.WUT[:, k * NIC * 128:(k + 1) * NIC * 128],
                                  in_=c.wqu_s[:, k * 128:(k + 1) * 128], transpose=True), 16)
        m["p2_wu"] = s.dwl.n

    @bk.sync
    def _(e):
        e.wait_ge(s.ct.h, m["C_done_t"])
        e.wait_ge(s.cv.h, m["C_done_v"])
        e.wait_ge(s.cs.h, m["C_done_s"])
        e.wait_ge(s.dsc.h, s.dsc.n)
        s.dxe.inc(e.dma_start(out=c.H2RD[:, 0:KD * R], in_=c.h2_s[0]), 16)
        s.dxo.inc(e.dma_start(out=c.H2RD[:, KD * R:2 * KD * R], in_=c.h2_s[1]), 16)

    def p2_quant(e, j):
        sp = s.dxe if j % 2 == 0 else s.dxo
        e.wait_ge(sp.h, sp.n)
        if j >= 1:
            e.wait_ge(s.cv.h, c.m["p2_clip%d" % (j - 1)])
        s.cs.inc(e.activation(c.XQF[:], c.H2RD[:, (j % 2) * KD * R:(j % 2 + 1) * KD * R],
                              AF.Identity, scale=c.ISXB[:, 0:1],
                              bias=c.CMAGIC[:, 0:1]))
        c.m["p2_q%d" % j] = s.cs.n

    def p2_clip(v, j):
        xqb = c.XQB[:, (j % 2) * KD * R:(j % 2 + 1) * KD * R]
        v.tensor_scalar(c.XQF[:], c.XQF[:], -MAGIC, None, op0=ALU.add)
        s.cv.inc(v.tensor_scalar(xqb, c.XQF[:], -128.0, 127.0,
                                 op0=ALU.max, op1=ALU.min))
        c.m["p2_clip%d" % j] = s.cv.n

    @bk.scalar
    def _(e):
        e.wait_ge(s.cv.h, m["sx_done"])
        e.wait_ge(s.dsc.h, s.dsc.n)
        p2_quant(e, 0)

    @bk.vector
    def _(v):
        v.wait_ge(s.cs.h, c.m["p2_q0"])
        v.wait_ge(s.dsc.h, s.dsc.n)
        p2_clip(v, 0)

    @bk.scalar
    def _(e):
        p2_quant(e, 1)

    @bk.vector
    def _(v):
        v.wait_ge(s.cs.h, c.m["p2_q1"])
        p2_clip(v, 1)

    for j in range(NS):
        bs = s.cs.n
        bt = s.ct.n
        bv = s.cv.n
        c.m["p2_bt%d" % j] = bt
        has_nx = j + 2 < NS
        qoff = 1 if has_nx else 0

        ldq = [2, 3] if j == 0 else ([j + 3] if j + 3 < NS else [])
        for q_ in ldq:
            @bk.sync
            def _(e, q_=q_):
                e.wait_ge(s.cs.h, c.m["p2_q%d" % (q_ - 2)])
                sp = s.dxe if q_ % 2 == 0 else s.dxo
                sp.inc(e.dma_start(out=c.H2RD[:, (q_ % 2) * KD * R:(q_ % 2 + 1) * KD * R],
                                   in_=c.h2_s[q_]), 16)

        @bk.scalar
        def _(e, j=j, bs=bs, bt=bt, bv=bv, has_nx=has_nx):
            if has_nx:
                p2_quant(e, j + 2)
            for ic in range(NIC):
                e.wait_ge(s.ct.h, bt + 2 * ic + 1)
                if ic >= 2:
                    e.wait_ge(s.cv.h, bv + (ic - 1))
                s.cs.inc(e.activation(c.SGW[:, (ic % 2) * R:(ic % 2 + 1) * R],
                                      c.PS[(2 * ic) % 8][:, 0:R],
                                      AF.Silu, scale=c.SXG[:, ic:ic + 1]))

        @bk.tensor
        def _(t, j=j, bs=bs, bt=bt, bv=bv):
            if j == 0:
                t.wait_ge(s.dw.h, m["p2_wg"])
            t.wait_ge(s.cv.h, c.m["p2_clip%d" % j])
            xqb = c.XQB[:, (j % 2) * KD * R:(j % 2 + 1) * KD * R]
            for ic in range(NIC):
                gb_, ub_ = (2 * ic) % 8, (2 * ic + 1) % 8
                if ic < 4:
                    bank_wait(t, gb_)
                    bank_wait(t, ub_)
                else:
                    t.wait_ge(s.cv.h, bv + (ic - 3))
                for k in range(KD):
                    ins = t.matmul(c.PS[gb_][:, 0:R],
                                   c.WGT[:, (k * NIC + ic) * 128:(k * NIC + ic + 1) * 128],
                                   xqb[:, k * R:(k + 1) * R], start=(k == 0), stop=(k == KD - 1))
                s.ct.inc(ins)
                if j == 0 and ic == 0:
                    t.wait_ge(s.dwl.h, m["p2_wu"])
                for k in range(KD):
                    ins = t.matmul(c.PS[ub_][:, 0:R],
                                   c.WUT[:, (k * NIC + ic) * 128:(k * NIC + ic + 1) * 128],
                                   xqb[:, k * R:(k + 1) * R], start=(k == 0), stop=(k == KD - 1))
                s.ct.inc(ins)

        @bk.vector
        def _(v, j=j, bs=bs, bt=bt, bv=bv, has_nx=has_nx, qoff=qoff):
            v.wait_ge(s.dsc.h, s.dsc.n)
            for ic in range(NIC):
                ub_ = (2 * ic + 1) % 8
                v.wait_ge(s.cs.h, bs + qoff + ic + 1)
                v.wait_ge(s.ct.h, bt + 2 * ic + 2)
                vt = c.VT[:, ic * R:(ic + 1) * R]
                v.scalar_tensor_tensor(vt, c.PS[ub_][:, 0:R], c.SXU[:, ic:ic + 1],
                                       c.SGW[:, (ic % 2) * R:(ic % 2 + 1) * R],
                                       op0=ALU.mult, op1=ALU.mult)
                s.cv.inc(v.reduce_max(c.VMAX[:, j * NIC + ic:j * NIC + ic + 1], vt, AXX,
                                      apply_absolute_value=True))
            c.m["p2_stt_last%d" % j] = s.cv.n
            if has_nx:
                v.wait_ge(s.cs.h, bs + 1)
                v.wait_ge(s.ct.h, bt + 2 * NIC)
                p2_clip(v, j + 2)
            c.m["p2_v_last%d" % j] = s.cv.n
        for b in range(8):
            c.bank_free[b] = (s.cv, c.m["p2_stt_last%d" % j])

        @bk.scalar
        def _(e, j=j, bv=bv):
            e.wait_ge(s.cv.h, bv + NIC)
            e.wait_ge(s.dsc.h, s.dsc.n)
            s.dsc.inc(e.dma_start(out=c.v_s[j], in_=c.VT[:]), 16)
            c.m["p2_vst%d" % j] = s.dsc.n

    m["p2_done_v"] = s.cv.n
    m["p2_done_st"] = s.dsc.n
    m["p2_done_t"] = s.ct.n
    m["p2_done_s"] = s.cs.n

    @bk.vector
    def _(v):
        v.wait_ge(s.cv.h, m["p2_done_v"])
        s.cv.inc(v.reduce_max(c.REDT[:, 1:2], c.VMAX[:, 0:NS * NIC], AXX))
        m["sd_red"] = s.cv.n

    @bk.tensor
    def _(t):
        t.wait_ge(s.cv.h, m["sd_red"])
        bank_wait(t, 0)
        s.ct.inc(t.transpose(c.PS[0][0:1, 0:128], c.REDT[:, 1:2], c.ID[:]))
        m["sd_tp"] = s.ct.n

    @bk.vector
    def _(v):
        v.wait_ge(s.ct.h, m["sd_tp"])
        v.memset(c.LMX[:], 0.0)
        s.cv.inc(v.reduce_max(c.LMX[0:1, 0:1], c.PS[0][0:1, 0:128], AXX))
        m["sd_loc"] = s.cv.n
    c.bank_free[0] = (s.cv, m["sd_loc"])

    @bk.gpsimd
    def _(g):
        g.wait_ge(s.cv.h, m["sd_loc"])
        s.dgp.inc(g.dma_start(out=c.cc_m2_in, in_=c.LMX[:]), 16)
        g.wait_ge(s.dgp.h, s.dgp.n)
        s.cc.inc(g.collective_compute("AllReduce", ALU.max,
                                      replica_groups=[list(range(NCORE))],
                                      ins=[c.cc_m2_in], outs=[c.cc_m2_out]))
        g.wait_ge(s.cc.h, s.cc.n)
        s.dgp.inc(g.dma_start(out=c.GMX[:], in_=c.cc_m2_out), 16)
        m["sd_gmax"] = s.dgp.n

    @bk.vector
    def _(v):
        v.wait_ge(s.dgp.h, s.dgp.n)
        v.wait_ge(s.cs.h, m["clip_sig"])
        if c.debug:
            v.wait_ge(s.dst.h, s.dst.n)
        v.tensor_tensor(c.SCL[0:1, 2:3], c.GMX[0:1, 0:1], c.CLIPD[0:1, 1:2], op=ALU.mult)
        v.tensor_scalar(c.SCL[0:1, 2:3], c.SCL[0:1, 2:3], 1.0 / QMAX, None, op0=ALU.mult)
        s.cv.inc(v.reciprocal(c.SCL[0:1, 3:4], c.SCL[0:1, 2:3]))
        m["sd_val"] = s.cv.n

    @bk.tensor
    def _(t):
        t.wait_ge(s.cv.h, m["sd_val"])
        bank_wait(t, 1)
        t.matmul(c.PS[1][:, 0:1], c.ONES[:], c.SCL[0:1, 2:3], start=True, stop=True)
        s.ct.inc(t.matmul(c.PS[1][:, 1:2], c.ONES[:], c.SCL[0:1, 3:4], start=True, stop=True))
        m["sd_bc"] = s.ct.n

    @bk.vector
    def _(v):
        v.wait_ge(s.ct.h, m["sd_bc"])
        v.tensor_copy(c.SDB[:], c.PS[1][:, 0:1])
        s.cv.inc(v.tensor_copy(c.ISDB[:], c.PS[1][:, 1:2]))
        v.wait_ge(s.dw.h, s.dw.n)
        v.tensor_scalar(c.SDD[:], c.SWD[:], c.SDB[:, 0:1], None, op0=ALU.mult)
        s.cv.inc(v.tensor_tensor(c.SDD[:], c.SDD[:], c.GB[:], op=ALU.mult))
        m["sd_done"] = s.cv.n
    c.bank_free[1] = (s.cv, m["sd_done"])

    if c.debug:
        @bk.sync
        def _(e):
            e.wait_ge(s.cv.h, m["sd_done"])
            s.dst.inc(e.dma_start(out=c.dbg_sd, in_=c.SCL[0:1, :]), 16)

    c.p2stack.close()
    c.p3stack = ExitStack()
    sb3 = lambda name, shape, dtp: c.p3stack.enter_context(nc.sbuf_tensor(name, shape, dtp))
    HNC = NIC // 2 + 1
    c.WDT = sb3("WDT", [128, NIC * KD * 128], BF16)
    c.V3H = sb3("V3H", [128, 2 * 11 * R], F32)
    c.V3F = sb3("V3F", [128, 2 * 11 * R], F32)
    c.V3B = sb3("V3B", [128, 2 * 11 * R], BF16)
    c.OFM = sb3("OFM", [128, KD * R], F32)
    c.X3T = sb3("X3T", [128, SUBT * D], F32)

    wdt_src = c.wqd_t.rearrange("(ic p) d -> p ic d", p=128)
    wdt_dst = c.WDT.rearrange("p (ic d) -> p ic d", d=KD * 128)

    @bk.sync
    def _(e):
        e.wait_ge(s.ct.h, m["p2_done_t"])
        e.wait_ge(s.cv.h, m["p2_done_v"])
        e.wait_ge(s.cs.h, m["p2_done_s"])
        e.wait_ge(s.dst.h, s.dst.n)
        e.wait_ge(s.dsc.h, m["C_done_st"])
        s.dx.inc(e.dma_start(out=c.X3T[:], in_=c.x2_s[0]), 16)
        e.wait_ge(s.dws.h, s.dws.n)
        e.wait_ge(s.dw.h, s.dw.n)
        s.dw.inc(e.dma_start(out=wdt_dst[:, 0:11], in_=wdt_src[:, 0:11]), 16)
        m["p3_wld_a"] = s.dw.n

    @bk.sync
    def _(e):
        e.wait_ge(s.dwl.h, s.dwl.n)
        s.dwl.inc(e.dma_start(out=wdt_dst[:, 11:NIC], in_=wdt_src[:, 11:NIC]), 16)
        m["p3_wld_b"] = s.dwl.n

    @bk.sync
    def _(e):
        e.wait_ge(s.dsc.h, s.dsc.n)
        s.dxe.inc(e.dma_start(out=c.V3H[:, 0:11 * R], in_=c.v_s[0][:, 0:11 * R]), 16)
        s.dxo.inc(e.dma_start(out=c.V3H[:, 11 * R:22 * R], in_=c.v_s[0][:, 11 * R:22 * R]), 16)

    def p3_quant(e, q):
        sp = s.dxe if q % 2 == 0 else s.dxo
        e.wait_ge(sp.h, sp.n)
        if q >= 2:
            e.wait_ge(s.cv.h, c.m["p3_clip%d" % (q - 2)])
        s.cs.inc(e.activation(c.V3F[:, (q % 2) * 11 * R:(q % 2 + 1) * 11 * R],
                              c.V3H[:, (q % 2) * 11 * R:(q % 2 + 1) * 11 * R],
                              AF.Identity, scale=c.ISDB[:, 0:1], bias=c.CMAGIC[:, 0:1]))
        c.m["p3_q%d" % q] = s.cs.n

    def p3_clip(v, q):
        v3f = c.V3F[:, (q % 2) * 11 * R:(q % 2 + 1) * 11 * R]
        v.tensor_scalar(v3f, v3f, -MAGIC, None, op0=ALU.add)
        s.cv.inc(v.tensor_scalar(c.V3B[:, (q % 2) * 11 * R:(q % 2 + 1) * 11 * R],
                                 v3f, -128.0, 127.0, op0=ALU.max, op1=ALU.min))
        c.m["p3_clip%d" % q] = s.cv.n

    @bk.scalar
    def _(e):
        e.wait_ge(s.cv.h, m["sd_done"])
        e.wait_ge(s.dsc.h, s.dsc.n)
        p3_quant(e, 0)
        p3_quant(e, 1)

    @bk.vector
    def _(v):
        v.wait_ge(s.cs.h, c.m["p3_q0"])
        p3_clip(v, 0)
        v.wait_ge(s.cs.h, c.m["p3_q1"])
        p3_clip(v, 1)

    for j in range(NS):
        bs3 = s.cs.n
        bt3 = s.ct.n
        bv3 = s.cv.n
        has_nx3 = j + 1 < NS
        nq = 2 if has_nx3 else 0

        if has_nx3:
            @bk.sync
            def _(e, j=j):
                e.wait_ge(s.dsc.h, c.m["p2_vst%d" % (NS - 1)])
                e.wait_ge(s.cs.h, c.m["p3_q%d" % (2 * j)])
                s.dxe.inc(e.dma_start(out=c.V3H[:, 0:11 * R],
                                      in_=c.v_s[j + 1][:, 0:11 * R]), 16)
                e.wait_ge(s.cs.h, c.m["p3_q%d" % (2 * j + 1)])
                s.dxo.inc(e.dma_start(out=c.V3H[:, 11 * R:22 * R],
                                      in_=c.v_s[j + 1][:, 11 * R:22 * R]), 16)

        @bk.scalar
        def _(e, j=j, bs3=bs3, bt3=bt3, has_nx3=has_nx3, nq=nq):
            if has_nx3:
                p3_quant(e, 2 * j + 2)
                p3_quant(e, 2 * j + 3)
            if j == 0:
                e.wait_ge(s.dsc.h, c.m["p2_vst%d" % (NS - 1)])
            for mc in range(8):
                e.wait_ge(s.ct.h, bt3 + 8 + mc + 1)
                s.cs.inc(e.activation(c.OFM[:, mc * R:(mc + 1) * R], c.PS[mc][:, 0:R],
                                      AF.Identity, scale=c.SDD[:, mc:mc + 1]))

        @bk.tensor
        def _(t, j=j, bt3=bt3, bv3=bv3, has_nx3=has_nx3):
            if j == 0:
                t.wait_ge(s.dw.h, m["p3_wld_a"])
                t.wait_ge(s.dwl.h, m["p3_wld_b"])
            for h in range(2):
                t.wait_ge(s.cv.h, c.m["p3_clip%d" % (2 * j + h)])
                for mc in range(8):
                    if j == 0:
                        if h == 0:
                            bank_wait(t, mc)
                    elif h == 0:
                        t.wait_ge(s.cv.h, c.m["p3_x3_%d_%d" % (j - 1, mc)])
                    for il in range(11):
                        ic = h * 11 + il
                        ins = t.matmul(c.PS[mc][:, 0:R],
                                       c.WDT[:, (ic * KD + mc) * 128:(ic * KD + mc + 1) * 128],
                                       c.V3B[:, (h % 2) * 11 * R + il * R:
                                              (h % 2) * 11 * R + (il + 1) * R],
                                       start=(ic == 0), stop=(ic == NIC - 1))
                    s.ct.inc(ins)
            for k in range(KD):
                t.wait_ge(s.cs.h, bs3 + (2 if has_nx3 else 0) + k + 1)
                for t_ in range(SUBT):
                    ins = t.transpose(c.PS[k][:, t_ * 128:(t_ + 1) * 128],
                                      c.OFM[:, k * R + t_ * 128:k * R + (t_ + 1) * 128],
                                      c.ID[:])
                s.ct.inc(ins)

        @bk.vector
        def _(v, j=j, bs3=bs3, bt3=bt3, has_nx3=has_nx3):
            if has_nx3:
                v.wait_ge(s.cs.h, bs3 + 1)
                v.wait_ge(s.ct.h, bt3 + 8)
                p3_clip(v, 2 * j + 2)
                v.wait_ge(s.cs.h, bs3 + 2)
                v.wait_ge(s.ct.h, bt3 + 16)
                p3_clip(v, 2 * j + 3)
            v.wait_ge(s.dx.h, s.dx.n)
            for k in range(KD):
                v.wait_ge(s.ct.h, bt3 + 16 + k + 1)
                src2 = c.PS[k][:, 0:SUBT * 128].rearrange("p (t q) -> p t q", q=128)
                dst2 = c.X3T[:].rearrange("p (t d) -> p t d", d=D)[:, :, k * 128:(k + 1) * 128]
                s.cv.inc(v.tensor_tensor(dst2, dst2, src2, op=ALU.add))
                c.m["p3_x3_%d_%d" % (j, k)] = s.cv.n

        @bk.scalar
        def _(e, j=j):
            e.wait_ge(s.cv.h, c.m["p3_x3_%d_%d" % (j, KD - 1)])
            dstv = c.out.rearrange("(r e) d -> e r d", e=NS)[j].rearrange("(t p) d -> p t d", p=128)
            srcv = c.X3T[:].rearrange("p (t d) -> p t d", d=D)
            s.dst.inc(e.dma_start(out=dstv, in_=srcv), 16)
            c.m["p3_out%d" % j] = s.dst.n

        if has_nx3:
            @bk.sync
            def _(e, j=j):
                e.wait_ge(s.dst.h, c.m["p3_out%d" % j])
                s.dx.inc(e.dma_start(out=c.X3T[:], in_=c.x2_s[j + 1]), 16)

    c.p3stack.close()
    _finish(c, bk)


def _finish(c, bk):
    s = c.s

    @bk.sync
    def _(e):
        e.wait_ge(s.dst.h, s.dst.n)
        e.wait_ge(s.dsc.h, s.dsc.n)
        e.wait_ge(s.dgp.h, s.dgp.n)
        e.wait_ge(s.dw2.h, s.dw2.n)
        e.wait_ge(s.dw3.h, s.dw3.n)




def _bias_fm(vec):
    n = vec.shape[0] // 128
    return np.ascontiguousarray(vec.reshape(n, 128).T)


def _prep_inputs(inputs):
    x = np.ascontiguousarray(np.asarray(inputs["L0_attn_x"], np.float32))
    c_full = np.asarray(inputs["L0_c"], np.float32)
    wmod = np.asarray(inputs["L0_wmod"], np.float32)
    base = {
        "A": np.ascontiguousarray(np.asarray(inputs["L0_attn_A"], np.float32)),
        "B_w": np.ascontiguousarray(np.asarray(inputs["L0_attn_B_w"], np.float32)),
        "B_b": np.ascontiguousarray(np.asarray(inputs["L0_attn_B_b"], np.float32).reshape(DS, 1)),
        "C_w": np.ascontiguousarray(np.asarray(inputs["L0_attn_C_w"], np.float32)),
        "C_b": _bias_fm(np.asarray(inputs["L0_attn_C_b"], np.float32)),
        "D_w": np.ascontiguousarray(np.asarray(inputs["L0_attn_D_w"], np.float32)),
        "D_b": _bias_fm(np.asarray(inputs["L0_attn_D_b"], np.float32)),
        "s_w1": np.ascontiguousarray(np.asarray(inputs["L0_attn_s_w1"], np.float32)),
        "s_b1": _bias_fm(np.asarray(inputs["L0_attn_s_b1"], np.float32)),
        "s_w2": np.ascontiguousarray(np.asarray(inputs["L0_attn_s_w2"], np.float32)),
        "s_b2": np.ascontiguousarray(np.asarray(inputs["L0_attn_s_b2"], np.float32).reshape(DS, 1)),
        "g_w": np.ascontiguousarray(np.asarray(inputs["L0_attn_g_w"], np.float32)),
        "g_b": _bias_fm(np.asarray(inputs["L0_attn_g_b"], np.float32)),
        "w_gate": np.ascontiguousarray(np.asarray(inputs["L0_ffn_w_gate"], np.float32)),
        "w_up": np.ascontiguousarray(np.asarray(inputs["L0_ffn_w_up"], np.float32)),
        "w_down": np.ascontiguousarray(np.asarray(inputs["L0_ffn_w_down"], np.float32)),
        "clip_x": np.asarray(inputs["L0_ffn_clip_x"], np.float32).reshape(1, 1),
        "clip_down": np.asarray(inputs["L0_ffn_clip_down"], np.float32).reshape(1, 1),
        "ident": np.eye(128, dtype=np.float32),
        "ones1": np.ones((1, 128), dtype=np.float32),
    }
    in_maps = []
    for i in range(NCORE):
        im = dict(base)
        im["x"] = np.ascontiguousarray(x[i])
        im["c_ks"] = np.ascontiguousarray(c_full[:, i * 128:(i + 1) * 128])
        im["wmod_ks"] = np.ascontiguousarray(wmod[i * 128:(i + 1) * 128, :])
        in_maps.append(im)
    return in_maps


def kernel(**inputs):
    in_maps = _prep_inputs(inputs)
    nc, _ = build(debug=False, upto="full")
    res = run_bass_kernel_spmd(nc, in_maps, core_ids=list(range(NCORE)))
    return np.stack([np.asarray(r["out"]) for r in res.results], axis=0)
